# version 08
# baseline (speedup 1.0000x reference)
import numpy as np
import ml_dtypes
from contextlib import ExitStack

import concourse.bass as bass
import concourse.mybir as mybir
from concourse.bass_utils import run_bass_kernel_spmd

F32 = mybir.dt.float32
BF16 = mybir.dt.bfloat16
I32 = mybir.dt.int32
AF = mybir.ActivationFunctionType
ALU = mybir.AluOpType
AXX = mybir.AxisListType.X
BF = ml_dtypes.bfloat16

NCORE = 8
TPC = 2048
NTT = TPC // 128
D = 1024
SC = 512
NSC = TPC // SC
H0, HK0, HV0 = 4, 256, 512
H1, DH1 = 16, 64
NE0, CAP0, PAD0 = 16, 1280, 1281
NE1, CAP1, PAD1 = 8, 5120, 5121
EPS = 1e-5

ENGINES = ("sync", "vector", "tensor", "scalar", "gpsimd")


class Ctr:
    __slots__ = ("h", "n", "name")

    def __init__(self, h, name):
        self.h = h
        self.n = 0
        self.name = name


class Ring:
    def __init__(self, ctrs):
        self.ctrs = ctrs
        self.i = 0

    def next(self):
        c = self.ctrs[self.i]
        self.i = (self.i + 1) % len(self.ctrs)
        return c

    def final_tokens(self):
        return [(c, c.n) for c in self.ctrs if c.n > 0]


class KB:

    def __init__(self, nc, ctx):
        self.nc = nc
        self.ctx = ctx
        self.ops = {e: [] for e in ENGINES}
        self.e = {}
        for e in ENGINES:
            self.e[e] = self.sem("c_" + e)

    def sem(self, name):
        return Ctr(self.ctx.enter_context(self.nc.semaphore(name)), name)

    def ring(self, name, n):
        return Ring([self.sem(f"{name}{i}") for i in range(n)])

    def op(self, eng, meth, *args, wait=(), inc="auto", incn=1, **kw):
        wl = list(wait)
        if eng in ("vector", "scalar", "gpsimd") and meth not in (
                "dma_start", "indirect_dma_start"):
            wl.append((self.e[eng], self.e[eng].n))
        best = {}
        for w in wl:
            if isinstance(w, tuple):
                c, val = w
            else:
                c, val = w, w.n
            if val > 0 and val > best.get(id(c), (None, -1))[1]:
                best[id(c)] = (c.h, val)
        waits = list(best.values())
        if inc == "auto":
            inc = self.e[eng]
        if inc is not None:
            inc.n += incn
            inc_spec = (inc.h, incn)
        else:
            inc_spec = None
        self.ops[eng].append((waits, meth, args, kw, inc_spec))

    def dma(self, eng, out, in_, ring, wait=(), **kw):
        ctr = ring.next()
        wl = list(wait)
        if ctr.n > 0:
            wl.append((ctr, ctr.n))
        self.op(eng, "dma_start", wait=wl, inc=ctr, incn=16, out=out, in_=in_, **kw)
        return (ctr, ctr.n)

    def idma(self, eng, out, out_off, in_, in_off, ring=None, wait=(), ctr=None,
             **kw):
        wl = list(wait)
        if ring is not None:
            ctr = ring.next()
            if ctr.n > 0:
                wl.append((ctr, ctr.n))
        self.op(eng, "indirect_dma_start", out, out_off, in_, in_off,
                wait=wl, inc=ctr, incn=16, **kw)
        return (ctr, ctr.n)

    def fence(self):
        return [(c, c.n) for c in self.e.values() if c.n > 0]

    def build(self, final_rings=()):
        waits = []
        for r in final_rings:
            waits += r.final_tokens()
        waits += self.fence()
        self.op("sync", "nop", wait=waits, inc=None)
        with self.nc.Block() as block:
            for eng in ENGINES:
                ops = self.ops[eng]
                if not ops:
                    continue

                def body(q, ops=ops):
                    for waits, meth, args, kw, inc_spec in ops:
                        for h, v in waits:
                            q.wait_ge(h, v)
                        ins = getattr(q, meth)(*args, **kw)
                        if inc_spec is not None:
                            ins.then_inc(inc_spec[0], inc_spec[1])

                getattr(block, eng)(body)


def fm_dram(ap):
    return ap.rearrange("(c p) t -> p c t", p=128)


class Pool:

    def __init__(self, nc):
        self.nc = nc
        self.es = ExitStack()

    def sb(self, name, shape, dt=F32):
        return self.es.enter_context(self.nc.sbuf_tensor(name, shape, dt))

    def ps(self, name, shape, dt=F32):
        return self.es.enter_context(self.nc.psum_tensor(name, shape, dt))

    def close(self):
        self.es.close()



def emit_ln_tile(kb, xb, xn_tile, red, xs_sc, wait_x, epsc, war=()):
    v, s = kb.e["vector"], kb.e["scalar"]
    kb.op("vector", "reduce_sum", red[:, 0:1], xb[:], AXX, wait=list(wait_x) + [s])
    kb.op("vector", "tensor_scalar_mul", red[:, 1:2], red[:, 0:1], -1.0 / D)
    kb.op("scalar", "activation", xs_sc[:], xb[:], AF.Square,
          accum_out=red[:, 2:3], wait=list(wait_x) + [v])
    kb.op("vector", "tensor_mul", red[:, 3:4], red[:, 1:2], red[:, 1:2], wait=[s])
    kb.op("vector", "tensor_scalar", red[:, 4:5], red[:, 2:3], 1.0 / D,
          red[:, 3:4], ALU.mult, ALU.subtract)
    kb.op("scalar", "activation", red[:, 5:6], red[:, 4:5], AF.Ln, bias=epsc, wait=[v])
    kb.op("scalar", "activation", red[:, 6:7], red[:, 5:6], AF.Exp, scale=-0.5)
    kb.op("vector", "tensor_scalar", xn_tile[:], xb[:], red[:, 1:2],
          red[:, 6:7], ALU.add, ALU.mult, wait=[s, kb.e["tensor"]] + list(war))


def emit_transpose_to_fm(kb, src_tile, dstT, tile_idx, ident, pT, itok,
                         nchunks=8, tcols=TPC, war=None):
    v, t = kb.e["vector"], kb.e["tensor"]
    banks = pT if isinstance(pT, list) else [pT]
    if war is None:
        war = {}
    for c in range(nchunks):
        b = c % len(banks)
        bw = [(v, v.n)] if c < len(banks) else war.get(b, [(v, v.n)])
        kb.op("tensor", "transpose", banks[b][:],
              src_tile[:, c * 128:(c + 1) * 128],
              ident[:], wait=[itok] + bw)
        kb.op("vector", "tensor_copy",
              dstT[:, c * tcols + tile_idx * 128: c * tcols + (tile_idx + 1) * 128],
              banks[b][:], wait=[(t, t.n)])
        war[b] = [(v, v.n)]



def build_launch_A():
    nc = bass.Bass("TRN2", target_bir_lowering=False, debug=False)
    ap = {}

    def din(name, shape, dt=F32):
        ap[name] = nc.dram_tensor(name, shape, dt, kind="ExternalInput").ap()

    def dout(name, shape, dt=F32):
        ap[name] = nc.dram_tensor(name, shape, dt, kind="ExternalOutput").ap()

    din("x", [TPC, D])
    din("WqT", [D, D]); din("WkT", [D, D])
    din("WvT", [D, 2048]); din("WgT", [D, 2048])
    din("ident", [128, 128], BF16)
    din("r0_cos", [128, TPC]); din("r0_sin", [128, TPC])
    din("r0_cos_q", [128, TPC]); din("r0_sin_q", [128, TPC])
    din("maskT", [H0 * SC, SC], BF16)
    din("kdec", [128, H0 * 4])
    din("crossT", [H0 * 128, SC], BF16)
    dout("qTd", [D, TPC], BF16)
    dout("kTd", [D, TPC], BF16)
    dout("vd", [TPC, 2048], BF16)
    dout("gTd", [2048, TPC], BF16)
    dout("o_part", [2048, TPC], BF16)
    dout("P", [256, H0 * HV0])

    gamma = (1.0 - np.exp2(-5.0 - np.arange(H0, dtype=np.float64)))
    gSC = [float(g) ** SC for g in gamma]

    ctx = ExitStack()
    with ctx:
        kb = KB(nc, ctx)
        ldr_ = kb.ring("ld", 4)
        wlr = kb.ring("wl", 2)
        str_ = kb.ring("st", 4)
        v, t, s = kb.e["vector"], kb.e["tensor"], kb.e["scalar"]

        ident = ctx.enter_context(nc.sbuf_tensor("identB", [128, 128], BF16))
        itok = kb.dma("sync", ident[:], ap["ident"], ldr_)
        S_f = ctx.enter_context(nc.sbuf_tensor("S_f", [128, H0 * 2 * 512], F32))
        S_b = ctx.enter_context(nc.sbuf_tensor("S_b", [128, 2 * 512], BF16))

        p12 = Pool(nc)
        xnT = p12.sb("xnT", [128, 8 * TPC], BF16)
        wbuf = p12.sb("wbuf", [128, 8 * 2048], BF16)
        ph1 = p12.sb("ph1", [128, 4 * 512], F32)
        x_a = p12.sb("x_a", [128, D], F32)
        x_b = p12.sb("x_b", [128, D], F32)
        xs_sc = p12.sb("xs_sc", [128, D], F32)
        red = p12.sb("red", [128, 8], F32)
        epsc = p12.sb("epsc", [128, 1], F32)
        xn_tile = p12.sb("xn_tile", [128, D], BF16)
        rc = p12.sb("rc", [128, TPC], F32)
        rs = p12.sb("rs", [128, TPC], F32)
        evb = p12.sb("evb", [128, 4 * 512], BF16)
        pT = [p12.ps(f"pT{i}", [128, 128], BF16) for i in range(2)]
        pj = [p12.ps(f"pj{i}", [128, 512], F32) for i in range(4)]
        kb.op("vector", "memset", epsc[:], EPS)

        twar = {}
        for i in range(NTT):
            xb = x_a if i % 2 == 0 else x_b
            xtok = kb.dma("sync", xb[:], ap["x"][i * 128:(i + 1) * 128, :], ldr_,
                          wait=[v] if i >= 2 else [])
            emit_ln_tile(kb, xb, xn_tile, red, xs_sc, [xtok], epsc[:, 0:1])
            emit_transpose_to_fm(kb, xn_tile, xnT, i, ident, pT, itok, war=twar)

        def load_w(name, ncols):
            w3 = ap[name].rearrange("(c p) o -> p c o", p=128)
            dst = wbuf[:].rearrange("p (c o) -> p c o", c=8)[:, :, 0:ncols]
            return kb.dma("gpsimd", dst, w3, wlr, wait=[t])

        def wtile(c, o):
            return wbuf[:, c * 2048 + o * 128: c * 2048 + (o + 1) * 128]

        ev_tok = {}

        def proj_pair_rope(ncols, cos_ap, sin_ap, out_dram, wtok):
            ctok = kb.dma("sync", rc[:], cos_ap, ldr_, wait=[v])
            stok = kb.dma("sync", rs[:], sin_ap, ldr_, wait=[v])
            for pair in range(ncols // 256):
                for cg in range(4):
                    pa, pb = pj[2 * (cg % 2)], pj[2 * (cg % 2) + 1]
                    bw = rope_war.get(cg % 2, [(v, v.n)])
                    for (oc, pp) in ((2 * pair, pa), (2 * pair + 1, pb)):
                        for c in range(8):
                            kb.op("tensor", "matmul", pp[:], wtile(c, oc),
                                  xnT[:, c * TPC + cg * 512: c * TPC + (cg + 1) * 512],
                                  start=(c == 0), stop=(c == 7),
                                  wait=[wtok] + bw if c == 0 else [])
                    cs = rc[:, cg * 512:(cg + 1) * 512]
                    sn = rs[:, cg * 512:(cg + 1) * 512]
                    t1, t2 = ph1[:, 0:512], ph1[:, 512:1024]
                    t3, t4 = ph1[:, 1024:1536], ph1[:, 1536:2048]
                    kb.op("vector", "tensor_mul", t1, pa[:], cs, wait=[t, ctok, stok])
                    kb.op("vector", "tensor_mul", t2, pb[:], sn)
                    kb.op("vector", "tensor_mul", t3, pa[:], sn)
                    kb.op("vector", "tensor_mul", t4, pb[:], cs)
                    sl = (cg % 2) * 1024
                    ea, eb = evb[:, sl:sl + 512], evb[:, sl + 512:sl + 1024]
                    kb.op("vector", "tensor_sub", ea, t1, t2,
                          wait=ev_tok.get(sl, []))
                    kb.op("vector", "tensor_add", eb, t4, t3,
                          wait=ev_tok.get(sl + 512, []))
                    d3 = fm_dram(out_dram)
                    ta = kb.dma("scalar", d3[:, 2 * pair, cg * 512:(cg + 1) * 512],
                                ea, str_, wait=[v])
                    tb = kb.dma("scalar",
                                d3[:, 2 * pair + 1, cg * 512:(cg + 1) * 512],
                                eb, str_, wait=[v])
                    ev_tok[sl] = [ta]
                    ev_tok[sl + 512] = [tb]
                    rope_war[cg % 2] = [(v, v.n)]

        rope_war = {}
        wtok = load_w("WqT", 1024)
        proj_pair_rope(1024, ap["r0_cos_q"], ap["r0_sin_q"], ap["qTd"], wtok)
        wtok = load_w("WkT", 1024)
        proj_pair_rope(1024, ap["r0_cos"], ap["r0_sin"], ap["kTd"], wtok)

        wtok = load_w("WvT", 2048)
        pj_war = {}
        for i in range(NTT):
            for og in range(4):
                pp = pj[og]
                for c in range(8):
                    kb.op("tensor", "matmul", pp[:],
                          xnT[:, c * TPC + i * 128: c * TPC + (i + 1) * 128],
                          wbuf[:, c * 2048 + og * 512: c * 2048 + (og + 1) * 512],
                          start=(c == 0), stop=(c == 7),
                          wait=[wtok] + pj_war.get(og, [(v, v.n)])
                          if c == 0 else [])
                sl = (og % 2) * 512
                ea = evb[:, sl:sl + 512]
                war = ev_tok.get(sl, [])
                kb.op("vector", "tensor_copy", ea, pp[:], wait=[(t, t.n)] + war)
                pj_war[og] = [(v, v.n)]
                ev_tok[sl] = [kb.dma(
                    "scalar", ap["vd"][i * 128:(i + 1) * 128, og * 512:(og + 1) * 512],
                    ea, str_, wait=[v])]

        wtok = load_w("WgT", 2048)
        pj_war = {}
        for oc in range(16):
            for cg in range(4):
                pp = pj[cg % 4]
                for c in range(8):
                    kb.op("tensor", "matmul", pp[:], wtile(c, oc),
                          xnT[:, c * TPC + cg * 512: c * TPC + (cg + 1) * 512],
                          start=(c == 0), stop=(c == 7),
                          wait=[wtok] + pj_war.get(cg % 4, [(v, v.n)])
                          if c == 0 else [])
                sl = (cg % 2) * 512
                ea = evb[:, sl:sl + 512]
                war = ev_tok.get(sl, [])
                kb.op("vector", "tensor_copy", ea, pp[:], wait=[(t, t.n)] + war)
                pj_war[cg % 4] = [(v, v.n)]
                d3 = fm_dram(ap["gTd"])
                ev_tok[sl] = [kb.dma("scalar", d3[:, oc, cg * 512:(cg + 1) * 512],
                                     ea, str_, wait=[v])]

        p12.close()

        fence = kb.fence() + str_.final_tokens() + ldr_.final_tokens() \
            + wlr.final_tokens()
        p3 = Pool(nc)
        qsb_d = p3.sb("qsb", [128, 2 * 2 * 512], BF16)
        ksb_d = p3.sb("ksb", [128, 2 * 2 * 512], BF16)
        vsb_d = p3.sb("vsb", [128, 2 * 4 * 512], BF16)
        khat_d = p3.sb("khat", [128, 2 * 4 * 256], BF16)
        qhat_d = p3.sb("qhat", [128, 2 * 2 * 512], BF16)
        attnT_d = p3.sb("attnT", [128, 2 * 4 * 512], BF16)
        maskb = p3.sb("maskb", [128, H0 * 4 * 512], BF16)
        crossb = p3.sb("crossb", [128, H0 * 512], BF16)
        kdecb = p3.sb("kdecb", [128, H0 * 4], F32)
        oev = p3.sb("oev", [128, 4 * 512], BF16)
        pTr = [p3.ps(f"pTr{i}", [128, 128], BF16) for i in range(2)]
        psc = [p3.ps(f"psc{i}", [128, 512], F32) for i in range(2)]
        po = [p3.ps(f"po{i}", [128, 512], F32) for i in range(2)]
        pA = [p3.ps(f"pA{i}", [128, 512], F32) for i in range(2)]

        mk3 = ap["maskT"].rearrange("(h J p) i -> p (h J) i", p=128, h=H0)
        mtok = kb.dma("sync", maskb[:].rearrange("p (x i) -> p x i", i=512), mk3,
                      ldr_, wait=fence)
        cr3 = ap["crossT"].rearrange("(h p) i -> p h i", p=128)
        crtok = kb.dma("sync", crossb[:].rearrange("p (h i) -> p h i", h=H0), cr3,
                       ldr_, wait=fence)
        kdtok = kb.dma("sync", kdecb[:], ap["kdec"], ldr_, wait=fence)
        kb.op("vector", "memset", S_f[:], 0.0, wait=fence)

        q3 = fm_dram(ap["qTd"])
        k3 = fm_dram(ap["kTd"])
        ost = {}
        tr_war = {}
        sc_war = {}
        po_war = {}
        pA_war = {}
        iter_tok = {}
        for sc in range(NSC):
            for h in range(H0):
                it = sc * H0 + h
                bsel = it % 2
                qsb = qsb_d[:, bsel * 1024:(bsel + 1) * 1024]
                ksb = ksb_d[:, bsel * 1024:(bsel + 1) * 1024]
                vsb = vsb_d[:, bsel * 2048:(bsel + 1) * 2048]
                khat = khat_d[:, bsel * 1024:(bsel + 1) * 1024]
                qhat = qhat_d[:, bsel * 1024:(bsel + 1) * 1024]
                attnT = attnT_d[:, bsel * 2048:(bsel + 1) * 2048]
                lwar = iter_tok.get(bsel, [])
                toks = []
                for c in range(2):
                    toks.append(kb.dma("sync", qsb[:, c * 512:(c + 1) * 512],
                                       q3[:, h * 2 + c, sc * 512:(sc + 1) * 512],
                                       ldr_, wait=lwar))
                    toks.append(kb.dma("sync", ksb[:, c * 512:(c + 1) * 512],
                                       k3[:, h * 2 + c, sc * 512:(sc + 1) * 512],
                                       ldr_, wait=lwar))
                for jt in range(4):
                    toks.append(kb.dma("sync", vsb[:, jt * 512:(jt + 1) * 512],
                                       ap["vd"][sc * 512 + jt * 128:
                                                sc * 512 + (jt + 1) * 128,
                                                h * 512:(h + 1) * 512],
                                       ldr_, wait=lwar))

                for jt in range(4):
                    for c in range(2):
                        b = (jt * 2 + c) % 2
                        kb.op("tensor", "transpose", pTr[b][:],
                              ksb[:, c * 512 + jt * 128: c * 512 + (jt + 1) * 128],
                              ident[:],
                              wait=toks + [itok] + tr_war.get(b, [(v, v.n)]))
                        kb.op("vector", "tensor_scalar_mul",
                              khat[:, jt * 256 + c * 128: jt * 256 + (c + 1) * 128],
                              pTr[b][:], kdecb[:, h * 4 + jt: h * 4 + jt + 1],
                              wait=[(t, t.n), kdtok])
                        tr_war[b] = [(v, v.n)]

                kb.op("vector", "tensor_copy", S_b[:, 0:512],
                      S_f[:, (h * 2) * 512:(h * 2 + 1) * 512], wait=[t])
                kb.op("vector", "tensor_copy", S_b[:, 512:1024],
                      S_f[:, (h * 2 + 1) * 512:(h * 2 + 2) * 512])
                for c in range(2):
                    kb.op("vector", "tensor_mul", qhat[:, c * 512:(c + 1) * 512],
                          qsb[:, c * 512:(c + 1) * 512],
                          crossb[:, h * 512:(h + 1) * 512], wait=toks + [crtok])

                for jt in range(4):
                    b = jt % 2
                    for c in range(2):
                        kb.op("tensor", "matmul", psc[b][:],
                              ksb[:, c * 512 + jt * 128: c * 512 + (jt + 1) * 128],
                              qsb[:, c * 512:(c + 1) * 512],
                              start=(c == 0), stop=(c == 1),
                              wait=sc_war.get(b, [(v, v.n)]) + toks
                              if c == 0 else [])
                    kb.op("vector", "tensor_mul", attnT[:, jt * 512:(jt + 1) * 512],
                          psc[b][:],
                          maskb[:, (h * 4 + jt) * 512:(h * 4 + jt + 1) * 512],
                          wait=[(t, t.n), mtok])
                    sc_war[b] = [(v, v.n)]

                for dvc in range(4):
                    b = dvc % 2
                    pp = po[b]
                    pw = [(v, v.n)] if dvc < 2 else po_war.get(b, [(v, v.n)])
                    for jt in range(4):
                        kb.op("tensor", "matmul", pp[:],
                              vsb[:, jt * 512 + dvc * 128: jt * 512 + (dvc + 1) * 128],
                              attnT[:, jt * 512:(jt + 1) * 512],
                              start=(jt == 0), stop=False,
                              wait=pw if jt == 0 else [])
                    for c in range(2):
                        kb.op("tensor", "matmul", pp[:],
                              S_b[:, c * 512 + dvc * 128: c * 512 + (dvc + 1) * 128],
                              qhat[:, c * 512:(c + 1) * 512],
                              start=False, stop=(c == 1))
                    war = [ost[dvc]] if dvc in ost else []
                    kb.op("vector", "tensor_copy", oev[:, dvc * 512:(dvc + 1) * 512],
                          pp[:], wait=[(t, t.n)] + war)
                    po_war[b] = [(v, v.n)]
                    d3 = fm_dram(ap["o_part"])
                    ost[dvc] = kb.dma("scalar",
                                      d3[:, h * 4 + dvc, sc * 512:(sc + 1) * 512],
                                      oev[:, dvc * 512:(dvc + 1) * 512], str_,
                                      wait=[v])

                for c in range(2):
                    pp = pA[c]
                    for jt in range(4):
                        kb.op("tensor", "matmul", pp[:],
                              khat[:, jt * 256 + c * 128: jt * 256 + (c + 1) * 128],
                              vsb[:, jt * 512:(jt + 1) * 512],
                              start=(jt == 0), stop=(jt == 3),
                              wait=[(v, v.n)] if jt == 0 else [])
                    kb.op("vector", "scalar_tensor_tensor",
                          S_f[:, (h * 2 + c) * 512:(h * 2 + c + 1) * 512],
                          S_f[:, (h * 2 + c) * 512:(h * 2 + c + 1) * 512],
                          gSC[h], pp[:], ALU.mult, ALU.add, wait=[(t, t.n)])
                    pA_war[c] = [(v, v.n)]
                iter_tok[bsel] = [(t, t.n), (v, v.n)]

        pd3 = ap["P"].rearrange("(c p) o -> p c o", p=128)
        for h in range(H0):
            for c in range(2):
                kb.dma("scalar", pd3[:, c, h * 512:(h + 1) * 512],
                       S_f[:, (h * 2 + c) * 512:(h * 2 + c + 1) * 512], str_,
                       wait=[v])

        kb.build(final_rings=[ldr_, wlr, str_])
        p3.close()
    return nc, ap




def build_launch_B():
    nc = bass.Bass("TRN2", target_bir_lowering=False, debug=False)
    ap = {}

    def din(name, shape, dt=F32):
        ap[name] = nc.dram_tensor(name, shape, dt, kind="ExternalInput").ap()

    def dout(name, shape, dt=F32):
        ap[name] = nc.dram_tensor(name, shape, dt, kind="ExternalOutput").ap()

    din("x", [TPC, D])
    din("o_part", [2048, TPC], BF16)
    din("qTd", [D, TPC], BF16)
    din("gTd", [2048, TPC], BF16)
    din("P_all", [NCORE * 256, H0 * HV0])
    din("coefP", [128, NCORE * H0])
    din("WoT", [2048, D])
    din("gwT", [D, NE0])
    din("gnw", [128, 4])
    din("crossfullT", [H0 * 128, TPC], BF16)
    din("ident", [128, 128], BF16)
    din("onesf", [128, 128])
    dout("x1", [TPC, D])
    dout("xn2", [TPC, D], BF16)
    dout("M_em", [NE0, TPC], BF16)
    dout("w_tm", [128, NTT])

    ctx = ExitStack()
    with ctx:
        kb = KB(nc, ctx)
        ldr_ = kb.ring("ld", 4)
        wlr = kb.ring("wl", 2)
        str_ = kb.ring("st", 4)
        v, t, s = kb.e["vector"], kb.e["tensor"], kb.e["scalar"]

        ident = ctx.enter_context(nc.sbuf_tensor("identB", [128, 128], BF16))
        itok = kb.dma("sync", ident[:], ap["ident"], ldr_)
        onesf = ctx.enter_context(nc.sbuf_tensor("onesf_sb", [128, 128], F32))
        otok = kb.dma("sync", onesf[:], ap["onesf"], ldr_)
        epsc = ctx.enter_context(nc.sbuf_tensor("epsc", [128, 1], F32))
        kb.op("vector", "memset", epsc[:], EPS)

        og_d = nc.dram_tensor("og_d", [16 * 128, TPC], BF16, kind="Internal").ap()

        p1 = Pool(nc)
        S_in = p1.sb("S_in", [128, H0 * 2 * 512], F32)
        S_inb_d = p1.sb("S_inb", [128, 2 * 2 * 512], BF16)
        ptmp = p1.sb("ptmp", [128, 2 * 2048], F32)
        coefb = p1.sb("coefb", [128, NCORE * H0], F32)
        gnwb = p1.sb("gnwb", [128, 4], F32)
        cstok = kb.dma("sync", coefb[:], ap["coefP"], ldr_)
        gntok = kb.dma("sync", gnwb[:], ap["gnw"], ldr_)
        kb.op("vector", "memset", S_in[:], 0.0)
        pa3 = ap["P_all"].rearrange("(j c p) o -> p (j c) o", p=128, c=2)
        pt_war = {}
        for j in range(NCORE):
            for c in range(2):
                i2 = j * 2 + c
                half = i2 % 2
                pt = ptmp[:, half * 2048:(half + 1) * 2048]
                ptok = kb.dma("sync" if i2 % 4 < 2 else "scalar", pt,
                              pa3[:, i2, :], ldr_ if i2 % 4 < 2 else str_,
                              wait=pt_war.get(half, []))
                for h in range(H0):
                    kb.op("vector", "scalar_tensor_tensor",
                          S_in[:, (h * 2 + c) * 512:(h * 2 + c + 1) * 512],
                          pt[:, h * 512:(h + 1) * 512],
                          coefb[:, j * H0 + h: j * H0 + h + 1],
                          S_in[:, (h * 2 + c) * 512:(h * 2 + c + 1) * 512],
                          ALU.mult, ALU.add, wait=[ptok, cstok])
                pt_war[half] = [(v, v.n)]

        qhat_d = p1.sb("qhat", [128, 2 * 2 * TPC], BF16)
        crb_d = p1.sb("crb", [128, 2 * TPC], BF16)
        of_d = p1.sb("of", [128, 2 * 4 * TPC], F32)
        sq1 = p1.sb("sq1", [128, 512], F32)
        ogt = p1.sb("ogt", [128, 2 * TPC], BF16)
        opb = p1.sb("opb", [128, 2 * TPC], BF16)
        gob = p1.sb("gob", [128, 2 * TPC], BF16)
        gsil = p1.sb("gsil", [128, TPC], BF16)
        tmpf = p1.sb("tmpf", [128, TPC], F32)
        vr = p1.sb("vr", [1, 2 * TPC], F32)
        varr = vr[0:1, 0:TPC]
        rstd = vr[0:1, TPC:2 * TPC]
        rbc = p1.sb("rbc", [128, TPC], F32)
        pox = [p1.ps(f"pox{i}", [128, 512]) for i in range(4)]
        pvar = p1.ps("pvar", [1, 512])
        pbc = p1.ps("pbc", [128, 512])

        q3 = fm_dram(ap["qTd"])
        cf3 = ap["crossfullT"].rearrange("(h p) t -> p h t", p=128)
        op3 = fm_dram(ap["o_part"])
        g3 = fm_dram(ap["gTd"])
        ogt_war = {}
        qh_war = {}
        si_war = {}
        of_war = {}
        op_war = {}
        go_war = {}
        pox_war = {}
        for h in range(H0):
            hb = h % 2
            qhat = qhat_d[:, hb * 2 * TPC:(hb + 1) * 2 * TPC]
            crb = crb_d[:, hb * TPC:(hb + 1) * TPC]
            S_inb = S_inb_d[:, hb * 1024:(hb + 1) * 1024]
            of = of_d[:, hb * 4 * TPC:(hb + 1) * 4 * TPC]
            qtok = kb.dma("sync", qhat.rearrange("p (c t) -> p c t", c=2),
                          q3[:, h * 2:h * 2 + 2, :], ldr_,
                          wait=qh_war.get(hb, []))
            crtok = kb.dma("sync", crb, cf3[:, h, :], ldr_,
                           wait=qh_war.get(hb, []))
            for c in range(2):
                kb.op("vector", "tensor_mul", qhat[:, c * TPC:(c + 1) * TPC],
                      qhat[:, c * TPC:(c + 1) * TPC], crb,
                      wait=[qtok, crtok])
                kb.op("vector", "tensor_copy", S_inb[:, c * 512:(c + 1) * 512],
                      S_in[:, (h * 2 + c) * 512:(h * 2 + c + 1) * 512],
                      wait=si_war.get(hb, []))
            vq_tok = [(v, v.n)]
            ofw = of_war.get(hb, [])
            for dvc in range(4):
                oh = dvc % 2
                optok = kb.dma("sync", opb[:, oh * TPC:(oh + 1) * TPC],
                               op3[:, h * 4 + dvc, :], ldr_,
                               wait=op_war.get(oh, []))
                for tc in range(4):
                    bk = (dvc * 4 + tc) % 4
                    pp = pox[bk]
                    for c in range(2):
                        kb.op("tensor", "matmul",
                              pp[:],
                              S_inb[:, c * 512 + dvc * 128: c * 512 + (dvc + 1) * 128],
                              qhat[:, c * TPC + tc * 512: c * TPC + (tc + 1) * 512],
                              start=(c == 0), stop=(c == 1),
                              wait=vq_tok + pox_war.get(bk, [])
                              if c == 0 else [])
                    kb.op("vector", "scalar_tensor_tensor",
                          of[:, dvc * TPC + tc * 512: dvc * TPC + (tc + 1) * 512],
                          opb[:, oh * TPC + tc * 512: oh * TPC + (tc + 1) * 512],
                          0.0, pp[:],
                          ALU.bypass, ALU.add, wait=[(t, t.n), optok] + ofw)
                    pox_war[bk] = [(v, v.n)]
                op_war[oh] = [(v, v.n)]
            for tc in range(4):
                for dvc in range(4):
                    kb.op("scalar", "activation", sq1[:],
                          of[:, dvc * TPC + tc * 512: dvc * TPC + (tc + 1) * 512],
                          AF.Square, wait=[(v, v.n), (t, t.n)])
                    kb.op("tensor", "matmul", pvar[:], onesf[:, 0:1], sq1[:],
                          start=(dvc == 0), stop=(dvc == 3),
                          wait=[(s, s.n), otok] + ([(v, v.n)] if dvc == 0 else []))
                kb.op("vector", "tensor_copy", varr[:, tc * 512:(tc + 1) * 512],
                      pvar[:], wait=[(t, t.n)])
            kb.op("scalar", "activation", rstd[:], varr[:], AF.Ln,
                  bias=epsc[0:1, 0:1], scale=1.0 / HV0, wait=[(v, v.n)])
            kb.op("scalar", "activation", varr[:], rstd[:], AF.Exp, scale=-0.5)
            for tc in range(4):
                kb.op("tensor", "matmul", pbc[:], onesf[0:1, :],
                      varr[:, tc * 512:(tc + 1) * 512], start=True, stop=True,
                      wait=[(s, s.n), (v, v.n)])
                kb.op("vector", "tensor_copy", rbc[:, tc * 512:(tc + 1) * 512],
                      pbc[:], wait=[(t, t.n)])
            for dvc in range(4):
                oh = dvc % 2
                gtok = kb.dma("sync", gob[:, oh * TPC:(oh + 1) * TPC],
                              g3[:, h * 4 + dvc, :], ldr_,
                              wait=go_war.get(oh, []))
                kb.op("scalar", "activation", gsil[:],
                      gob[:, oh * TPC:(oh + 1) * TPC], AF.Silu,
                      wait=[gtok])
                go_war[oh] = [(s, s.n)]
                kb.op("vector", "scalar_tensor_tensor", tmpf[:],
                      of[:, dvc * TPC:(dvc + 1) * TPC],
                      gnwb[:, dvc:dvc + 1], rbc[:],
                      ALU.mult, ALU.mult, wait=[gntok])
                kb.op("vector", "tensor_mul", ogt[:, oh * TPC:(oh + 1) * TPC],
                      tmpf[:], gsil[:],
                      wait=ogt_war.get(oh, []) + [(s, s.n)])
                ogt_war[oh] = [kb.dma(
                    "scalar", fm_dram(og_d)[:, h * 4 + dvc, :],
                    ogt[:, oh * TPC:(oh + 1) * TPC], str_, wait=[v])]
            qh_war[hb] = [(t, t.n), (v, v.n)]
            si_war[hb] = [(t, t.n)]
            of_war[hb] = [(s, s.n), (v, v.n)]
        p1.close()

        fence = kb.fence() + ldr_.final_tokens() + str_.final_tokens()
        og_fence = fence
        p2 = Pool(nc)
        wbuf = p2.sb("wbuf", [128, 16 * 1024], BF16)
        ogt2 = p2.sb("ogt2", [128, 16 * 128], BF16)
        gwb = p2.sb("gwb", [128, 8 * NE0], BF16)
        x_a = p2.sb("x_a", [128, D], F32)
        x_b = p2.sb("x_b", [128, D], F32)
        x1t = p2.sb("x1t", [128, D], F32)
        xn_tile = p2.sb("xn_tile", [128, D], BF16)
        red = p2.sb("red", [128, 8], F32)
        xs_sc = p2.sb("xs_sc", [128, D], F32)
        xnTt = p2.sb("xnTt", [128, 8 * 128], BF16)
        lg = p2.sb("lg", [128, NE0], F32)
        eqb = p2.sb("eqb", [128, NE0], F32)
        cumb = p2.sb("cumb", [128, NE0], F32)
        cum1 = p2.sb("cum1", [128, NE0], F32)
        m_bf = p2.sb("m_bf", [128, NE0], BF16)
        wsb = p2.sb("wsb", [128, NTT], F32)
        Mem = p2.sb("Mem", [16, TPC], BF16)
        pj2 = [p2.ps(f"pj2{i}", [128, 512]) for i in range(2)]
        pT = p2.ps("pT", [128, 128], BF16)
        pr = p2.ps("pr", [128, NE0])

        w3 = ap["WoT"].rearrange("(c p) o -> p c o", p=128)
        wotok = kb.dma("gpsimd", wbuf[:].rearrange("p (c o) -> p c o", c=16),
                       w3, wlr, wait=fence)
        gw3 = ap["gwT"].rearrange("(c p) e -> p c e", p=128)
        gwtok = kb.dma("gpsimd", gwb[:].rearrange("p (c e) -> p c e", c=8),
                       gw3, wlr, wait=fence)

        x1_war = []
        xn2_war = []
        for i in range(NTT):
            xb = x_a if i % 2 == 0 else x_b
            xtok = kb.dma("sync", xb[:], ap["x"][i * 128:(i + 1) * 128, :], ldr_,
                          wait=[v] if i >= 2 else fence)
            ogtok = kb.dma("sync", ogt2[:].rearrange("p (c w) -> p c w", c=16),
                           fm_dram(og_d)[:, :, i * 128:(i + 1) * 128], ldr_,
                           wait=[t] + (og_fence if i < 2 else []))
            for og2 in range(2):
                pp = pj2[og2]
                for c in range(16):
                    kb.op("tensor", "matmul", pp[:],
                          ogt2[:, c * 128:(c + 1) * 128],
                          wbuf[:, c * 1024 + og2 * 512: c * 1024 + (og2 + 1) * 512],
                          start=(c == 0), stop=(c == 15),
                          wait=[wotok, v, ogtok] if c == 0 else [])
                kb.op("vector", "scalar_tensor_tensor",
                      x1t[:, og2 * 512:(og2 + 1) * 512],
                      xb[:, og2 * 512:(og2 + 1) * 512], 0.0, pp[:],
                      ALU.bypass, ALU.add, wait=[t, xtok, s] + x1_war)
            x1_war = [kb.dma("scalar", ap["x1"][i * 128:(i + 1) * 128, :],
                             x1t[:], str_, wait=[v])]
            emit_ln_tile(kb, x1t, xn_tile, red, xs_sc, [], epsc[:, 0:1],
                         war=xn2_war)
            xn2_war = [kb.dma("scalar", ap["xn2"][i * 128:(i + 1) * 128, :],
                              xn_tile[:], str_, wait=[v])]
            emit_transpose_to_fm(kb, xn_tile, xnTt, 0, ident, pT, itok, tcols=128)
            for c in range(8):
                kb.op("tensor", "matmul", pr[:], xnTt[:, c * 128:(c + 1) * 128],
                      gwb[:, c * NE0:(c + 1) * NE0],
                      start=(c == 0), stop=(c == 7),
                      wait=[v, gwtok] if c == 0 else [])
            kb.op("vector", "tensor_copy", lg[:], pr[:], wait=[t])
            kb.op("vector", "reduce_max", red[:, 0:1], lg[:], AXX)
            kb.op("vector", "tensor_scalar_mul", red[:, 1:2], red[:, 0:1], -1.0)
            kb.op("scalar", "activation", eqb[:], lg[:], AF.Exp,
                  bias=red[:, 1:2], accum_out=red[:, 2:3], wait=[v])
            kb.op("vector", "reciprocal", wsb[:, i:i + 1], red[:, 2:3], wait=[s])
            kb.op("vector", "tensor_scalar", eqb[:], lg[:], red[:, 0:1], None,
                  ALU.is_equal)
            kb.op("vector", "tensor_tensor_scan", cumb[:], eqb[:], eqb[:], 0.0,
                  ALU.add, ALU.bypass)
            kb.op("vector", "tensor_single_scalar", cum1[:], cumb[:], 1.0,
                  ALU.is_equal)
            kb.op("vector", "tensor_mul", m_bf[:], eqb[:], cum1[:])
            kb.op("tensor", "transpose", pT[0:16, :], m_bf[:], ident[:],
                  wait=[v, itok])
            kb.op("vector", "tensor_copy", Mem[0:16, i * 128:(i + 1) * 128],
                  pT[0:16, :], wait=[t])
        kb.dma("scalar", ap["M_em"], Mem[0:16, :], str_, wait=[v])
        kb.dma("scalar", ap["w_tm"], wsb[:], str_, wait=[v])
        kb.build(final_rings=[ldr_, wlr, str_])
        p2.close()
    return nc, ap


def host_constants(core):
    c = {}
    c["ident"] = np.eye(128, dtype=np.float32).astype(BF)
    tloc = np.arange(TPC, dtype=np.float32)
    pos = (core * TPC + tloc) % 8192

    inv0 = 1.0 / 10000.0 ** (np.arange(0, HK0, 2, dtype=np.float32) / HK0)
    fr = inv0[:, None] * pos[None, :]
    c["r0_cos"] = np.cos(fr).astype(np.float32)
    c["r0_sin"] = np.sin(fr).astype(np.float32)
    s0 = HK0 ** -0.5
    c["r0_cos_q"] = (np.cos(fr) * s0).astype(np.float32)
    c["r0_sin_q"] = (np.sin(fr) * s0).astype(np.float32)

    gamma = (1.0 - np.exp2(-5.0 - np.arange(H0, dtype=np.float32)))
    i512 = np.arange(SC, dtype=np.float32)
    dif = i512[None, :] - i512[:, None]
    mT = np.stack([np.where(dif >= 0, g ** np.clip(dif, 0, None), 0.0) for g in gamma])
    c["maskT"] = mT.reshape(H0 * SC, SC).astype(BF)
    kd = np.zeros((128, H0 * 4), np.float32)
    for h in range(H0):
        for jt in range(4):
            kd[:, h * 4 + jt] = gamma[h] ** (SC - 1 - (jt * 128 + np.arange(128)))
    c["kdec"] = kd
    cr = np.stack([g ** (i512 + 1) for g in gamma])
    c["crossT"] = np.repeat(cr[:, None, :], 128, axis=1).reshape(H0 * 128, SC).astype(BF)
    return c


def prep_launch_A(inputs, core):
    d = inputs
    im = host_constants(core)
    im["x"] = np.ascontiguousarray(d["L0_attn_x"].reshape(-1, D)[core * TPC:(core + 1) * TPC])
    im["WqT"] = np.ascontiguousarray(np.asarray(d["L0_attn_Wq"]).T)
    im["WkT"] = np.ascontiguousarray(np.asarray(d["L0_attn_Wk"]).T)
    im["WvT"] = np.ascontiguousarray(np.asarray(d["L0_attn_Wv"]).T)
    im["WgT"] = np.ascontiguousarray(np.asarray(d["L0_attn_Wg"]).T)
    keep = {"x", "WqT", "WkT", "WvT", "WgT", "ident", "r0_cos", "r0_sin",
            "r0_cos_q", "r0_sin_q", "maskT", "kdec", "crossT"}
    return {k: im[k] for k in keep}





def emit_wload_cast(kb, w_src, dst_bf, stage, ldr_, n_cols, tag_toks):
    s = kb.e["scalar"]
    nch = w_src.shape[0] // 128
    w3 = w_src.rearrange("(c p) f -> p c f", p=128)
    W = stage[0].shape[1]
    st_war = [[(s, s.n)], [(s, s.n)]]
    i = 0
    if n_cols <= W:
        per = W // n_cols
        for c0 in range(0, nch, per):
            buf = stage[i % 2]
            cw = min(per, nch - c0)
            tok = kb.dma("sync", buf[:, 0:cw * n_cols].rearrange(
                "p (c f) -> p c f", c=cw), w3[:, c0:c0 + cw, :], ldr_,
                wait=st_war[i % 2] + tag_toks)
            kb.op("scalar", "activation",
                  dst_bf[:, c0 * n_cols:(c0 + cw) * n_cols],
                  buf[:, 0:cw * n_cols], AF.Copy, wait=[tok])
            st_war[i % 2] = [(s, s.n)]
            i += 1
    else:
        for c in range(nch):
            for off in range(0, n_cols, W):
                buf = stage[i % 2]
                w = min(W, n_cols - off)
                tok = kb.dma("sync", buf[:, 0:w], w3[:, c, off:off + w], ldr_,
                             wait=st_war[i % 2] + tag_toks)
                kb.op("scalar", "activation",
                      dst_bf[:, c * n_cols + off: c * n_cols + off + w],
                      buf[:, 0:w], AF.Copy, wait=[tok])
                st_war[i % 2] = [(s, s.n)]
                i += 1
    return (s, s.n)


def emit_ffn_blocks(kb, ap_in, invtab, eo_ap, w1b, w2b, bufs, act_fn,
                    blocks, w1tok, w2tok, fence, ldr_, gr, str_):
    v, t, s, g = kb.e["vector"], kb.e["tensor"], kb.e["scalar"], kb.e["gpsimd"]
    idxc, xg, xgT, hT, eot = (bufs["idxc"], bufs["xg"], bufs["xgT"],
                              bufs["hT"], bufs["eot"])
    pT2, ps1, ps2 = bufs["pT2"], bufs["ps1"], bufs["ps2"]
    ident = bufs["ident"]
    itok = bufs["itok"]
    ga_war = []
    eo_war = bufs.setdefault("_eo_war", {})
    for base, nt in blocks:
        gtoks = []
        for j in range(nt):
            itk = kb.dma("gpsimd", idxc[j][:],
                         invtab[base + j * 128:base + (j + 1) * 128, :], gr,
                         wait=fence + ga_war)
            gtoks.append(kb.idma("gpsimd", xg[j][:], None, ap_in,
                                 bass.IndirectOffsetOnAxis(ap=idxc[j][:], axis=0),
                                 gr, wait=[itk, t]))
        ga_war = list(gtoks)
        ns = nt * 128
        evac_tok = bufs.setdefault("_evac_tok", [[], []])
        for j in range(nt):
            for c in range(8):
                bank = (j * 8 + c) % 2
                kb.op("tensor", "transpose", pT2[bank][:],
                      xg[j][:, c * 128:(c + 1) * 128], ident[:],
                      wait=[gtoks[j], itok] + evac_tok[bank])
                kb.op("vector", "tensor_copy",
                      xgT[:, c * 512 + j * 128: c * 512 + (j + 1) * 128],
                      pT2[bank][:], wait=[(t, kb.e["tensor"].n)])
                evac_tok[bank] = [(v, v.n)]
        s1war = bufs.setdefault("_s1war", {})
        for fc in range(32):
            b1 = fc % 2
            pp = ps1[b1]
            w0 = [w1tok] + s1war.get(b1, [])
            if fc < 2:
                w0.append((v, v.n))
                w0.append((s, s.n))
            for c in range(8):
                kb.op("tensor", "matmul", pp[:, 0:ns],
                      w1b[:, c * 4096 + fc * 128: c * 4096 + (fc + 1) * 128],
                      xgT[:, c * 512: c * 512 + ns],
                      start=(c == 0), stop=(c == 7),
                      wait=w0 if c == 0 else [])
            kb.op("scalar", "activation", hT[:, fc * 512: fc * 512 + ns],
                  pp[:, 0:ns], act_fn, wait=[(t, kb.e["tensor"].n)])
            s1war[b1] = [(s, s.n)]
        s2war = bufs.setdefault("_s2war", {})
        for j in range(nt):
            for og2 in range(2):
                pp = ps2[og2]
                w0 = [w2tok] + s2war.get(og2, [])
                if j == 0:
                    w0.append((s, s.n))
                for fc in range(32):
                    kb.op("tensor", "matmul", pp[:],
                          hT[:, fc * 512 + j * 128: fc * 512 + (j + 1) * 128],
                          w2b[:, fc * D + og2 * 512: fc * D + (og2 + 1) * 512],
                          start=(fc == 0), stop=(fc == 31),
                          wait=w0 if fc == 0 else [])
                war = eo_war.get(og2, [])
                kb.op("vector", "tensor_copy",
                      eot[:, og2 * 512:(og2 + 1) * 512], pp[:],
                      wait=[(t, kb.e["tensor"].n)] + war)
                s2war[og2] = [(v, v.n)]
                eo_war[og2] = [kb.dma(
                    "scalar",
                    eo_ap[base + j * 128:base + (j + 1) * 128,
                          og2 * 512:(og2 + 1) * 512],
                    eot[:, og2 * 512:(og2 + 1) * 512], str_, wait=[v])]



def build_launch_C():
    nc = bass.Bass("TRN2", target_bir_lowering=False, debug=False)
    ap = {}

    def din(name, shape, dt=F32):
        ap[name] = nc.dram_tensor(name, shape, dt, kind="ExternalInput").ap()

    def dout(name, shape, dt=F32):
        ap[name] = nc.dram_tensor(name, shape, dt, kind="ExternalOutput").ap()

    NTOK = NCORE * TPC
    din("M_all", [NE0, NTOK], BF16)
    din("xn2_all", [NTOK, D], BF16)
    din("w1", [2 * D, 4096])
    din("w2", [2 * 4096, D])
    din("ident", [128, 128], BF16)
    din("iota_t", [128, 128], I32)
    din("eoff16", [16, 1])
    din("basec", [128, 1])
    din("trashc", [128, 128])
    dout("expert_out", [2 * PAD0, D])
    dout("gidx_row", [1, NTOK])
    dout("keep_row", [1, NTOK])

    invtab = nc.dram_tensor("invtab", [19072, 1], I32, kind="Internal").ap()

    ctx = ExitStack()
    with ctx:
        kb = KB(nc, ctx)
        ldr_ = kb.ring("ld", 4)
        wlr = kb.ring("wl", 2)
        str_ = kb.ring("st", 4)
        gr = kb.ring("gq", 4)
        v, t, s, g = kb.e["vector"], kb.e["tensor"], kb.e["scalar"], kb.e["gpsimd"]

        ident = ctx.enter_context(nc.sbuf_tensor("identB", [128, 128], BF16))
        itok = kb.dma("sync", ident[:], ap["ident"], ldr_)

        p1 = Pool(nc)
        Mem = p1.sb("Mem", [16, NTOK], BF16)
        cum = p1.sb("cum", [16, NTOK], F32)
        tmp_em = p1.sb("tmp_em", [16, NTOK], F32)
        eoffc = p1.sb("eoffc", [16, 1], F32)
        stg = p1.sb("stg", [1, 1024], F32)
        basec = p1.sb("basec_s", [128, 1], F32)
        iota = p1.sb("iota_s", [128, 128], I32)
        gidxf = p1.sb("gidxf", [128, 128], F32)
        negm = p1.sb("negm", [128, 128], F32)
        idxf2 = p1.sb("idxf2", [128, 128], I32)
        zs = p1.sb("zs", [128, 21], I32)
        trashc = p1.sb("trashc_s", [128, 128], F32)
        onesf = p1.sb("onesf_c", [128, 1], F32)
        pcs = p1.ps("pcs", [1, 512])

        mtok = kb.dma("sync", Mem[:], ap["M_all"], ldr_)
        etok = kb.dma("sync", eoffc[:], ap["eoff16"], ldr_)
        btok = kb.dma("sync", basec[:], ap["basec"], ldr_)
        ittok = kb.dma("sync", iota[:], ap["iota_t"], ldr_)
        kb.op("vector", "memset", onesf[:], 1.0)
        ttok2 = kb.dma("sync", trashc[:], ap["trashc"], ldr_)
        kb.op("gpsimd", "memset", zs[:], 0)
        ztok = kb.dma("gpsimd",
                      invtab[0:2688, :].rearrange("(p k) one -> p (k one)", p=128),
                      zs[:], gr, wait=[g])
        kb.op("vector", "tensor_tensor_scan", cum[:], Mem[:], Mem[:], 0.0,
              ALU.add, ALU.bypass, wait=[mtok])
        kb.op("vector", "tensor_scalar", cum[:], cum[:], -1.0, float(CAP0),
              ALU.add, ALU.min)
        kb.op("vector", "tensor_scalar", tmp_em[:], cum[:], eoffc[0:16, 0:1],
              None, ALU.add, wait=[etok])
        kb.op("vector", "tensor_mul", tmp_em[:], tmp_em[:], Mem[:])
        st_tok = []
        for cg in range(NTOK // 512):
            kb.op("tensor", "matmul", pcs[:], onesf[0:16, 0:1],
                  tmp_em[0:16, cg * 512:(cg + 1) * 512], start=True, stop=True,
                  wait=[(v, v.n)])
            sl = (cg % 2) * 512
            kb.op("vector", "tensor_copy", stg[:, sl:sl + 512], pcs[:],
                  wait=[(t, kb.e["tensor"].n)] + st_tok)
            st_tok = [kb.dma("scalar",
                             ap["gidx_row"][0:1, cg * 512:(cg + 1) * 512],
                             stg[:, sl:sl + 512], str_, wait=[v])]
        kb.op("vector", "tensor_single_scalar", tmp_em[:], cum[:],
              float(CAP0) - 0.5, ALU.is_lt)
        kb.op("vector", "tensor_mul", tmp_em[:], tmp_em[:], Mem[:])
        for cg in range(NTOK // 512):
            kb.op("tensor", "matmul", pcs[:], onesf[0:16, 0:1],
                  tmp_em[0:16, cg * 512:(cg + 1) * 512], start=True, stop=True,
                  wait=[(v, v.n)])
            sl = (cg % 2) * 512
            kb.op("vector", "tensor_copy", stg[:, sl:sl + 512], pcs[:],
                  wait=[(t, kb.e["tensor"].n)] + st_tok)
            st_tok = [kb.dma("scalar",
                             ap["keep_row"][0:1, cg * 512:(cg + 1) * 512],
                             stg[:, sl:sl + 512], str_, wait=[v])]
        gxt = kb.dma("sync", gidxf[:],
                     ap["gidx_row"].rearrange("one (a b) -> (one a) b", a=128),
                     ldr_, wait=st_tok)
        kb.op("vector", "tensor_scalar", gidxf[:], gidxf[:], basec[:, 0:1],
              None, ALU.subtract, wait=[gxt, btok])
        kb.op("vector", "tensor_single_scalar", negm[:], gidxf[:], 0.0,
              ALU.is_lt)
        kb.op("vector", "scalar_tensor_tensor", gidxf[:], negm[:], 1e6,
              gidxf[:], ALU.mult, ALU.add)
        kb.op("vector", "tensor_single_scalar", negm[:], gidxf[:], 2561.5,
              ALU.is_lt)
        kb.op("vector", "tensor_mul", gidxf[:], gidxf[:], negm[:])
        kb.op("vector", "tensor_scalar", negm[:], negm[:], -1.0, 1.0,
              ALU.mult, ALU.add)
        kb.op("vector", "tensor_mul", negm[:], negm[:], trashc[:],
              wait=[ttok2])
        kb.op("vector", "tensor_add", gidxf[:], gidxf[:], negm[:])
        kb.op("vector", "tensor_copy", idxf2[:], gidxf[:])
        vsc = (v, v.n)
        scr = kb.ring("scrC", 8)
        for tt_ in range(128):
            kb.idma("gpsimd", invtab,
                    bass.IndirectOffsetOnAxis(ap=idxf2[:, tt_:tt_ + 1], axis=0),
                    iota[:, tt_:tt_ + 1], None, ring=scr,
                    wait=[vsc, ittok, ztok] if tt_ == 0 else [])
        scb_tok = scr.final_tokens()
        p1.close()

        fence_w = kb.fence() + gr.final_tokens() + ldr_.final_tokens()
        fence = fence_w + scb_tok
        p2 = Pool(nc)
        w1b = p2.sb("w1b", [128, 8 * 4096], BF16)
        w2b = p2.sb("w2b", [128, 32 * D], BF16)
        wstage = [p2.sb(f"wst{i}", [128, 2048], F32) for i in range(2)]
        bufs = dict(
            idxc=[p2.sb(f"idxc{i}", [128, 1], I32) for i in range(4)],
            xg=[p2.sb(f"xg{i}", [128, D], BF16) for i in range(4)],
            xgT=p2.sb("xgT", [128, 8 * 512], BF16),
            hT=p2.sb("hT", [128, 32 * 512], BF16),
            eot=p2.sb("eot", [128, 2 * 512], F32),
            pT2=[p2.ps(f"pT2{i}", [128, 128], BF16) for i in range(2)],
            ps1=[p2.ps(f"ps1{i}", [128, 512]) for i in range(2)],
            ps2=[p2.ps(f"ps2{i}", [128, 512]) for i in range(2)],
            ident=ident, itok=itok,
        )
        zr = p2.sb("zr", [1, D], F32)
        kb.op("vector", "memset", zr[:], 0.0, wait=fence)
        kb.dma("scalar", ap["expert_out"][PAD0 - 1:PAD0, :], zr[:], str_,
               wait=[v])
        kb.dma("scalar", ap["expert_out"][2 * PAD0 - 1:2 * PAD0, :], zr[:],
               str_, wait=[v])

        for e in range(2):
            w1tok = emit_wload_cast(kb, ap["w1"][e * D:(e + 1) * D, :], w1b,
                                    wstage, ldr_, 4096, fence + [t])
            w2tok = emit_wload_cast(kb, ap["w2"][e * 4096:(e + 1) * 4096, :],
                                    w2b, wstage, ldr_, D, fence + [t])
            blocks = [(e * PAD0, 4), (e * PAD0 + 512, 4), (e * PAD0 + 1024, 2)]
            emit_ffn_blocks(kb, ap["xn2_all"], invtab, ap["expert_out"],
                            w1b, w2b, bufs, AF.Gelu_apprx_tanh, blocks,
                            w1tok, w2tok, fence, ldr_, gr, str_)
            fence = kb.fence()
        kb.build(final_rings=[ldr_, wlr, str_, gr])
        p2.close()
    return nc, ap


def prep_launch_C(inputs, core, stB):
    d = inputs
    im = {}
    im["M_all"] = np.concatenate([stB[j]["M_em"] for j in range(NCORE)], 1)
    im["xn2_all"] = np.concatenate([stB[j]["xn2"] for j in range(NCORE)], 0)
    im["w1"] = np.ascontiguousarray(
        np.asarray(d["L0_ffn_w1"][2 * core:2 * core + 2]).reshape(2 * D, 4096))
    im["w2"] = np.ascontiguousarray(
        np.asarray(d["L0_ffn_w2"][2 * core:2 * core + 2]).reshape(2 * 4096, D))
    im["ident"] = np.eye(128, dtype=np.float32).astype(BF)
    p = np.arange(128)
    im["iota_t"] = (p[:, None] * 128 + np.arange(128)[None, :]).astype(np.int32)
    im["eoff16"] = (np.arange(NE0) * PAD0).astype(np.float32).reshape(16, 1)
    im["basec"] = np.full((128, 1), 2 * core * PAD0, np.float32)
    im["trashc"] = (2688 + np.arange(128)[:, None] * 128
                    + np.arange(128)[None, :]).astype(np.float32)
    return im




def build_launch_D():
    nc = bass.Bass("TRN2", target_bir_lowering=False, debug=False)
    ap = {}

    def din(name, shape, dt=F32):
        ap[name] = nc.dram_tensor(name, shape, dt, kind="ExternalInput").ap()

    def dout(name, shape, dt=F32):
        ap[name] = nc.dram_tensor(name, shape, dt, kind="ExternalOutput").ap()

    din("x1", [TPC, D])
    din("expert_out_all", [NE0 * PAD0, D])
    din("gidx", [128, NTT])
    din("keep", [128, NTT])
    din("w_tm", [128, NTT])
    din("WqT1", [D, D]); din("WkT1", [D, D]); din("WvT1", [D, D])
    din("bq", [128, 8]); din("bk", [128, 8]); din("bv", [128, 8])
    din("r1_cos", [128, TPC], BF16); din("r1_sin", [128, TPC], BF16)
    din("r1_cos_q", [128, TPC], BF16); din("r1_sin_q", [128, TPC], BF16)
    din("rot_permT", [128, 128], BF16)
    din("ident", [128, 128], BF16)
    dout("x2", [TPC, D])
    dout("qTd1", [D, TPC], BF16)
    dout("kTd1", [D, TPC], BF16)
    dout("v_aug", [TPC, H1 * 65], BF16)

    ctx = ExitStack()
    with ctx:
        kb = KB(nc, ctx)
        ldr_ = kb.ring("ld", 4)
        wlr = kb.ring("wl", 2)
        str_ = kb.ring("st", 4)
        gr = kb.ring("gq", 4)
        v, t, s, g = kb.e["vector"], kb.e["tensor"], kb.e["scalar"], kb.e["gpsimd"]

        ident = ctx.enter_context(nc.sbuf_tensor("identB", [128, 128], BF16))
        itok = kb.dma("sync", ident[:], ap["ident"], ldr_)
        epsc = ctx.enter_context(nc.sbuf_tensor("epsc", [128, 1], F32))
        kb.op("vector", "memset", epsc[:], EPS)
        xn3T = ctx.enter_context(nc.sbuf_tensor("xn3T", [128, 8 * TPC], BF16))

        p1 = Pool(nc)
        x_a = p1.sb("x_a", [128, D], F32)
        x_b = p1.sb("x_b", [128, D], F32)
        gk = p1.sb("gk", [128, 3 * NTT], F32)
        idxc = p1.sb("idxc", [128, 1], I32)
        yg = p1.sb("yg", [128, D], F32)
        x2t = p1.sb("x2t", [128, D], F32)
        xn_tile = p1.sb("xn_tile", [128, D], BF16)
        red = p1.sb("red", [128, 8], F32)
        xs_sc = p1.sb("xs_sc", [128, D], F32)
        wk2 = p1.sb("wk2", [128, 1], F32)
        pT = p1.ps("pT", [128, 128], BF16)

        gtok1 = kb.dma("sync", gk[:, 0:NTT], ap["gidx"], ldr_)
        gtok2 = kb.dma("sync", gk[:, NTT:2 * NTT], ap["keep"], ldr_)
        gtok3 = kb.dma("sync", gk[:, 2 * NTT:3 * NTT], ap["w_tm"], ldr_)
        ga_war = []
        for i in range(NTT):
            xb = x_a if i % 2 == 0 else x_b
            xtok = kb.dma("sync", xb[:], ap["x1"][i * 128:(i + 1) * 128, :], ldr_,
                          wait=[v] if i >= 2 else [])
            kb.op("vector", "tensor_copy", idxc[:], gk[:, i:i + 1],
                  wait=[gtok1] + ga_war)
            gtok = kb.idma("gpsimd", yg[:], None, ap["expert_out_all"],
                           bass.IndirectOffsetOnAxis(ap=idxc[:], axis=0),
                           gr, wait=[v, s])
            ga_war[:] = [gtok]
            kb.op("vector", "tensor_mul", wk2[:], gk[:, NTT + i:NTT + i + 1],
                  gk[:, 2 * NTT + i:2 * NTT + i + 1], wait=[gtok2, gtok3])
            kb.op("vector", "scalar_tensor_tensor", x2t[:], yg[:], wk2[:, 0:1],
                  xb[:], ALU.mult, ALU.add, wait=[gtok, xtok, s])
            kb.dma("scalar", ap["x2"][i * 128:(i + 1) * 128, :], x2t[:], str_,
                   wait=[v])
            emit_ln_tile(kb, x2t, xn_tile, red, xs_sc, [], epsc[:, 0:1])
            emit_transpose_to_fm(kb, xn_tile, xn3T, i, ident, pT, itok)
        p1.close()

        fence = kb.fence() + ldr_.final_tokens() + gr.final_tokens() \
            + str_.final_tokens()
        p2 = Pool(nc)
        wbuf = p2.sb("wbuf", [128, 8 * 1024], BF16)
        bb = p2.sb("bb", [128, 3 * 8], F32)
        rc = p2.sb("rc", [128, TPC], BF16)
        rs = p2.sb("rs", [128, TPC], BF16)
        rperm = p2.sb("rperm", [128, 128], BF16)
        qb_bf = p2.sb("qb_bf", [128, 512], BF16)
        t1 = p2.sb("t1", [128, 512], F32)
        t2 = p2.sb("t2", [128, 512], F32)
        evb = p2.sb("evb", [128, 4 * 512], BF16)
        vfm = p2.sb("vfm", [128, 8 * TPC], BF16)
        vag = p2.sb("vag", [128, H1 * 65], BF16)
        psq = [p2.ps(f"psq{i}", [128, 512]) for i in range(2)]
        pro = [p2.ps(f"pro{i}", [128, 512]) for i in range(2)]
        pT2 = p2.ps("pT2", [128, 128], BF16)

        btok = kb.dma("sync", bb[:, 0:8], ap["bq"], ldr_, wait=fence)
        kb.dma("sync", bb[:, 8:16], ap["bk"], ldr_)
        kb.dma("sync", bb[:, 16:24], ap["bv"], ldr_)
        ptok = kb.dma("sync", rperm[:], ap["rot_permT"], ldr_)

        def load_w(name):
            w3 = ap[name].rearrange("(c p) o -> p c o", p=128)
            return kb.dma("gpsimd", wbuf[:].rearrange("p (c o) -> p c o", c=8),
                          w3, wlr, wait=fence + [t])

        ev_war = {}

        def proj_rope(cos_ap, sin_ap, out_dram, bcol_off, wtok):
            ctok = kb.dma("sync", rc[:], cos_ap, ldr_, wait=[v])
            stok = kb.dma("sync", rs[:], sin_ap, ldr_, wait=[v])
            for oc in range(8):
                for cg in range(4):
                    pq = psq[cg % 2]
                    pr2 = pro[cg % 2]
                    for c in range(8):
                        kb.op("tensor", "matmul", pq[:],
                              wbuf[:, c * 1024 + oc * 128: c * 1024 + (oc + 1) * 128],
                              xn3T[:, c * TPC + cg * 512: c * TPC + (cg + 1) * 512],
                              start=(c == 0), stop=(c == 7),
                              wait=[wtok, v] if c == 0 else [])
                    kb.op("vector", "tensor_scalar",
                          qb_bf[:], pq[:], bb[:, bcol_off + oc:bcol_off + oc + 1],
                          None, ALU.add, wait=[t, btok, ctok, stok])
                    kb.op("tensor", "matmul", pr2[:], rperm[:], qb_bf[:],
                          start=True, stop=True, wait=[v, ptok])
                    kb.op("vector", "tensor_mul", t1[:], qb_bf[:],
                          rc[:, cg * 512:(cg + 1) * 512])
                    kb.op("vector", "tensor_mul", t2[:], pr2[:],
                          rs[:, cg * 512:(cg + 1) * 512], wait=[t])
                    sl = (cg % 2) * 512
                    kb.op("vector", "tensor_add", evb[:, sl:sl + 512], t1[:],
                          t2[:], wait=ev_war.get(sl, []))
                    d3 = fm_dram(out_dram)
                    ev_war[sl] = [kb.dma("scalar",
                                         d3[:, oc, cg * 512:(cg + 1) * 512],
                                         evb[:, sl:sl + 512], str_, wait=[v])]

        wtok = load_w("WqT1")
        proj_rope(ap["r1_cos_q"], ap["r1_sin_q"], ap["qTd1"], 0, wtok)
        wtok = load_w("WkT1")
        proj_rope(ap["r1_cos"], ap["r1_sin"], ap["kTd1"], 8, wtok)

        wtok = load_w("WvT1")
        for oc in range(8):
            for cg in range(4):
                pq = psq[cg % 2]
                for c in range(8):
                    kb.op("tensor", "matmul", pq[:],
                          wbuf[:, c * 1024 + oc * 128: c * 1024 + (oc + 1) * 128],
                          xn3T[:, c * TPC + cg * 512: c * TPC + (cg + 1) * 512],
                          start=(c == 0), stop=(c == 7),
                          wait=[wtok, v] if c == 0 else [])
                kb.op("vector", "tensor_scalar",
                      vfm[:, oc * TPC + cg * 512: oc * TPC + (cg + 1) * 512],
                      pq[:], bb[:, 16 + oc:16 + oc + 1], None, ALU.add,
                      wait=[t])
        vw_war = []
        for i in range(NTT):
            kb.op("vector", "memset", vag[:, 64::65], 1.0, wait=vw_war)
            for hp in range(8):
                kb.op("tensor", "transpose", pT2[:],
                      vfm[:, hp * TPC + i * 128: hp * TPC + (i + 1) * 128],
                      ident[:], wait=[v, itok])
                kb.op("vector", "tensor_copy",
                      vag[:, (2 * hp) * 65:(2 * hp) * 65 + 64], pT2[:, 0:64],
                      wait=[t])
                kb.op("vector", "tensor_copy",
                      vag[:, (2 * hp + 1) * 65:(2 * hp + 1) * 65 + 64],
                      pT2[:, 64:128])
            vw_war[:] = [kb.dma("scalar", ap["v_aug"][i * 128:(i + 1) * 128, :],
                                vag[:], str_, wait=[v])]
        kb.build(final_rings=[ldr_, wlr, str_, gr])
        p2.close()
    return nc, ap


def prep_launch_D(inputs, core, stB, stC):
    d = inputs
    im = {}
    im["x1"] = stB[core]["x1"]
    im["expert_out_all"] = np.concatenate(
        [stC[j]["expert_out"] for j in range(NCORE)], 0)
    gr_ = np.asarray(stC[0]["gidx_row"]).reshape(-1)[core * TPC:(core + 1) * TPC]
    im["gidx"] = np.ascontiguousarray(gr_.reshape(NTT, 128).T)
    kr_ = np.asarray(stC[0]["keep_row"]).reshape(-1)[core * TPC:(core + 1) * TPC]
    im["keep"] = np.ascontiguousarray(kr_.reshape(NTT, 128).T)
    im["w_tm"] = stB[core]["w_tm"]
    im["WqT1"] = np.ascontiguousarray(np.asarray(d["L1_attn_Wq"]).T)
    im["WkT1"] = np.ascontiguousarray(np.asarray(d["L1_attn_Wk"]).T)
    im["WvT1"] = np.ascontiguousarray(np.asarray(d["L1_attn_Wv"]).T)
    im["bq"] = np.ascontiguousarray(np.asarray(d["L1_attn_bq"]).reshape(8, 128).T)
    im["bk"] = np.ascontiguousarray(np.asarray(d["L1_attn_bk"]).reshape(8, 128).T)
    im["bv"] = np.ascontiguousarray(np.asarray(d["L1_attn_bv"]).reshape(8, 128).T)
    tloc = np.arange(TPC, dtype=np.float32)
    pos = (core * TPC + tloc) % 8192
    inv1 = 1.0 / 10000.0 ** (np.arange(0, DH1, 2, dtype=np.float32) / DH1)
    fr1 = np.concatenate([inv1, inv1])[:, None] * pos[None, :]
    cos2 = np.concatenate([np.cos(fr1)] * 2, 0)
    sin2 = np.concatenate([np.sin(fr1)] * 2, 0)
    s1 = DH1 ** -0.5
    im["r1_cos"] = cos2.astype(BF)
    im["r1_sin"] = sin2.astype(BF)
    im["r1_cos_q"] = (cos2 * s1).astype(BF)
    im["r1_sin_q"] = (sin2 * s1).astype(BF)
    P = np.zeros((128, 128), np.float32)
    for hb in range(2):
        o = hb * 64
        for j in range(32):
            P[o + j, o + 32 + j] = -1.0
            P[o + 32 + j, o + j] = 1.0
    im["rot_permT"] = P.T.copy().astype(BF)
    im["ident"] = np.eye(128, dtype=np.float32).astype(BF)
    return im




def build_launch_E():
    nc = bass.Bass("TRN2", target_bir_lowering=False, debug=False)
    ap = {}

    def din(name, shape, dt=F32):
        ap[name] = nc.dram_tensor(name, shape, dt, kind="ExternalInput").ap()

    def dout(name, shape, dt=F32):
        ap[name] = nc.dram_tensor(name, shape, dt, kind="ExternalOutput").ap()

    din("qT", [D, TPC], BF16)
    din("kT_all", [D, 8192], BF16)
    din("v_aug_all", [8192, H1 * 65], BF16)
    din("ident", [128, 128], BF16)
    din("identf", [128, 128])
    din("onesf", [128, 128])
    dout("attT", [D, TPC], BF16)

    NKT = 8192 // 128

    ctx = ExitStack()
    with ctx:
        kb = KB(nc, ctx)
        ldr_ = kb.ring("ld", 6)
        str_ = kb.ring("st", 4)
        v, t, s = kb.e["vector"], kb.e["tensor"], kb.e["scalar"]

        ident = ctx.enter_context(nc.sbuf_tensor("identB", [128, 128], BF16))
        itok = kb.dma("sync", ident[:], ap["ident"], ldr_)
        identf = ctx.enter_context(nc.sbuf_tensor("identF", [128, 128], F32))
        iftok = kb.dma("sync", identf[:], ap["identf"], ldr_)
        onesf = ctx.enter_context(nc.sbuf_tensor("onesf_sb", [128, 128], F32))
        otok = kb.dma("sync", onesf[:], ap["onesf"], ldr_)

        p = Pool(nc)
        qsb = [p.sb(f"qsb{i}", [128, TPC], BF16) for i in range(2)]
        ksb = [p.sb(f"ksb{i}", [128, 8192], BF16) for i in range(2)]
        vsb = [p.sb(f"vsb{i}", [128, 64 * 130], BF16) for i in range(2)]
        PTA = [p.sb(f"PTA{i}", [128, 1024], BF16) for i in range(3)]
        PTB = [p.sb(f"PTB{i}", [128, 1024], BF16) for i in range(3)]
        mcol = p.sb("mcol", [128, 8], F32)
        rec_sb = p.sb("rec_sb", [1, TPC], F32)
        rb_sb = p.sb("rb_sb", [64, TPC], F32)
        att_sb = p.sb("att_sb", [128, TPC], BF16)
        psA = p.ps("psA", [128, 1024])
        psB = p.ps("psB", [128, 1024])
        pvA = p.ps("pvA", [128, 1024])
        pvB = p.ps("pvB", [128, 1024])

        q3 = fm_dram(ap["qT"])
        k3 = fm_dram(ap["kT_all"])
        v3 = ap["v_aug_all"].rearrange("(kt p) c -> p kt c", p=128)
        a3 = fm_dram(ap["attT"])

        ld_war = [[], []]
        att_war = []
        ld_toks = {}

        def issue_loads(ch):
            bslot = ch % 2
            t1 = kb.dma("sync", qsb[bslot][:], q3[:, ch, :], ldr_,
                        wait=ld_war[bslot])
            t2 = kb.dma("sync", ksb[bslot][:], k3[:, ch, :], ldr_,
                        wait=ld_war[bslot])
            t3 = kb.dma("sync", vsb[bslot][:].rearrange("p (kt c) -> p kt c", c=130),
                        v3[:, :, 2 * ch * 65:(2 * ch + 2) * 65], ldr_,
                        wait=ld_war[bslot])
            ld_toks[ch] = (t1, t2, t3)

        issue_loads(0)
        for ch in range(H1 // 2):
            bslot = ch % 2
            qtok, ktok, vtok = ld_toks[ch]
            qs, ks, vs = qsb[bslot], ksb[bslot], vsb[bslot]
            if ch + 1 < H1 // 2:
                issue_loads(ch + 1)

            for i, kt in enumerate((0, 32)):
                for half in range(2):
                    kb.op("tensor", "matmul", psA[:, half * 512:(half + 1) * 512],
                          ks[0:64, kt * 128:(kt + 1) * 128],
                          qs[0:64, half * 512:(half + 1) * 512],
                          start=True, stop=True, tile_position=(0, 0),
                          wait=[qtok, ktok, s, v] if half == 0 else [])
                    kb.op("tensor", "matmul", psB[:, half * 512:(half + 1) * 512],
                          ks[64:128, kt * 128:(kt + 1) * 128],
                          qs[64:128, half * 512:(half + 1) * 512],
                          start=True, stop=True, tile_position=(64, 0))
                kb.op("vector", "reduce_max", mcol[:, 2 * i:2 * i + 1], psA[:],
                      AXX, wait=[t])
                kb.op("vector", "reduce_max", mcol[:, 2 * i + 1:2 * i + 2], psB[:],
                      AXX)
            kb.op("vector", "tensor_max", mcol[:, 4:5], mcol[:, 0:1], mcol[:, 2:3])
            kb.op("vector", "tensor_max", mcol[:, 5:6], mcol[:, 1:2], mcol[:, 3:4])
            kb.op("tensor", "transpose", psA[0:2, 0:128], mcol[:, 4:6],
                  identf[:], wait=[v, iftok])
            kb.op("vector", "reduce_max", mcol[0:2, 6:7], psA[0:2, 0:128], AXX,
                  wait=[t])
            kb.op("vector", "tensor_scalar_mul", mcol[0:2, 7:8], mcol[0:2, 6:7],
                  -1.0)
            kb.op("vector", "tensor_scalar", mcol[0:2, 4:6], identf[0:2, 0:2],
                  mcol[0:2, 7:8], None, ALU.mult)
            kb.op("tensor", "matmul", psA[0:128, 0:2], onesf[0:2, 0:128],
                  mcol[0:2, 4:6], start=True, stop=True, wait=[v, otok])
            kb.op("vector", "tensor_copy", mcol[:, 6:8], psA[:, 0:2], wait=[t])

            for qc in range(2):
                qof = qc * 1024
                expA = [0] * NKT
                expB = [0] * NKT
                for kt in range(NKT):
                    wA = [(s, expA[kt - 1])] if kt >= 1 else [v, s]
                    wB = [(s, expB[kt - 1])] if kt >= 1 else []
                    kb.op("tensor", "matmul", psA[:, 0:512],
                          ks[0:64, kt * 128:(kt + 1) * 128],
                          qs[0:64, qof:qof + 512],
                          start=True, stop=True, tile_position=(0, 0), wait=wA)
                    kb.op("tensor", "matmul", psA[:, 512:1024],
                          ks[0:64, kt * 128:(kt + 1) * 128],
                          qs[0:64, qof + 512:qof + 1024],
                          start=True, stop=True, tile_position=(0, 0))
                    tA = kb.e["tensor"].n
                    kb.op("tensor", "matmul", psB[:, 0:512],
                          ks[64:128, kt * 128:(kt + 1) * 128],
                          qs[64:128, qof:qof + 512],
                          start=True, stop=True, tile_position=(64, 0), wait=wB)
                    kb.op("tensor", "matmul", psB[:, 512:1024],
                          ks[64:128, kt * 128:(kt + 1) * 128],
                          qs[64:128, qof + 512:qof + 1024],
                          start=True, stop=True, tile_position=(64, 0))
                    tB = kb.e["tensor"].n
                    kb.op("scalar", "activation", PTA[kt % 3][:], psA[:], AF.Exp,
                          bias=mcol[:, 6:7], wait=[(t, tA)])
                    expA[kt] = s.n
                    kb.op("scalar", "activation", PTB[kt % 3][:], psB[:], AF.Exp,
                          bias=mcol[:, 7:8], wait=[(t, tB)])
                    expB[kt] = s.n
                    if kt >= 1:
                        for half in range(2):
                            kb.op("tensor", "matmul",
                                  pvA[0:65, half * 512:(half + 1) * 512],
                                  vs[:, (kt - 1) * 130:(kt - 1) * 130 + 65],
                                  PTA[(kt - 1) % 3][:, half * 512:(half + 1) * 512],
                                  start=(kt == 1), stop=False,
                                  wait=([vtok, (s, expA[kt - 1])] if kt == 1
                                        else [(s, expA[kt - 1])])
                                  if half == 0 else [])
                        for half in range(2):
                            kb.op("tensor", "matmul",
                                  pvB[0:65, half * 512:(half + 1) * 512],
                                  vs[:, (kt - 1) * 130 + 65:kt * 130],
                                  PTB[(kt - 1) % 3][:, half * 512:(half + 1) * 512],
                                  start=(kt == 1), stop=False,
                                  wait=[(s, expB[kt - 1])] if half == 0 else [])
                for half in range(2):
                    kb.op("tensor", "matmul",
                          pvA[0:65, half * 512:(half + 1) * 512],
                          vs[:, (NKT - 1) * 130:(NKT - 1) * 130 + 65],
                          PTA[(NKT - 1) % 3][:, half * 512:(half + 1) * 512],
                          start=False, stop=True,
                          wait=[(s, expA[NKT - 1])] if half == 0 else [])
                for half in range(2):
                    kb.op("tensor", "matmul",
                          pvB[0:65, half * 512:(half + 1) * 512],
                          vs[:, (NKT - 1) * 130 + 65:NKT * 130],
                          PTB[(NKT - 1) % 3][:, half * 512:(half + 1) * 512],
                          start=False, stop=True,
                          wait=[(s, expB[NKT - 1])] if half == 0 else [])

                for hh, pvX, sub in ((0, pvA, 0), (1, pvB, 64)):
                    kb.op("scalar", "activation", rec_sb[:, qof:qof + 1024],
                          pvX[64:65, :], AF.Ln, wait=[t, v])
                    kb.op("scalar", "activation", rec_sb[:, qof:qof + 1024],
                          rec_sb[:, qof:qof + 1024], AF.Exp, scale=-1.0)
                    for sl in range(2):
                        kb.op("tensor", "matmul",
                              psA[0:64, sl * 512:(sl + 1) * 512],
                              onesf[0:1, 0:64],
                              rec_sb[:, qof + sl * 512:qof + (sl + 1) * 512],
                              start=True, stop=True, wait=[s, v])
                    kb.op("vector", "tensor_copy", rb_sb[:, qof:qof + 1024],
                          psA[0:64, :], wait=[t])
                    kb.op("vector", "tensor_mul",
                          att_sb[sub:sub + 64, qof:qof + 1024],
                          pvX[0:64, :], rb_sb[:, qof:qof + 1024],
                          wait=att_war)
            ld_war[bslot] = [(t, t.n), (s, s.n)]
            att_war = [kb.dma("scalar", a3[:, ch, :], att_sb[:], str_,
                              wait=[v])]

        kb.build(final_rings=[ldr_, str_])
        p.close()
    return nc, ap


def prep_launch_E(inputs, core, stD):
    b0 = (core // 4) * 4
    im = {}
    im["identf"] = np.eye(128, dtype=np.float32)
    im["qT"] = stD[core]["qTd1"]
    im["kT_all"] = np.concatenate(
        [stD[j]["kTd1"] for j in range(b0, b0 + 4)], 1)
    im["v_aug_all"] = np.concatenate(
        [stD[j]["v_aug"] for j in range(b0, b0 + 4)], 0)
    im["ident"] = np.eye(128, dtype=np.float32).astype(BF)
    im["onesf"] = np.ones((128, 128), np.float32)
    return im




def build_launch_F():
    nc = bass.Bass("TRN2", target_bir_lowering=False, debug=False)
    ap = {}

    def din(name, shape, dt=F32):
        ap[name] = nc.dram_tensor(name, shape, dt, kind="ExternalInput").ap()

    def dout(name, shape, dt=F32):
        ap[name] = nc.dram_tensor(name, shape, dt, kind="ExternalOutput").ap()

    din("x2", [TPC, D])
    din("attT", [D, TPC], BF16)
    din("WoT1", [D, D])
    din("bo_b", [128, D])
    din("Wr", [D, NE1])
    din("ident", [128, 128], BF16)
    din("identf", [128, 128])
    dout("x3", [TPC, D])
    dout("xn4", [TPC, D], BF16)
    dout("G_em", [NE1, TPC])
    dout("M1_em", [NE1, TPC])
    dout("M2_em", [NE1, TPC])

    ctx = ExitStack()
    with ctx:
        kb = KB(nc, ctx)
        ldr_ = kb.ring("ld", 4)
        wlr = kb.ring("wl", 2)
        str_ = kb.ring("st", 4)
        v, t, s = kb.e["vector"], kb.e["tensor"], kb.e["scalar"]

        ident = ctx.enter_context(nc.sbuf_tensor("identB", [128, 128], BF16))
        itok = kb.dma("sync", ident[:], ap["ident"], ldr_)
        identf = ctx.enter_context(nc.sbuf_tensor("identF", [128, 128], F32))
        iftok = kb.dma("sync", identf[:], ap["identf"], ldr_)
        epsc = ctx.enter_context(nc.sbuf_tensor("epsc", [128, 1], F32))
        kb.op("vector", "memset", epsc[:], EPS)

        p = Pool(nc)
        wbuf = p.sb("wbuf", [128, 8 * 1024], BF16)
        wrb = p.sb("wrb", [128, 8 * NE1], BF16)
        bob = p.sb("bob", [128, D], F32)
        atile = p.sb("atile", [128, 8 * 128], BF16)
        x_a = p.sb("x_a", [128, D], F32)
        x_b = p.sb("x_b", [128, D], F32)
        x3t = p.sb("x3t", [128, D], F32)
        xn_tile = p.sb("xn_tile", [128, D], BF16)
        red = p.sb("red", [128, 8], F32)
        xs_sc = p.sb("xs_sc", [128, D], F32)
        xnTt = p.sb("xnTt", [128, 8 * 128], BF16)
        lg = p.sb("lg", [128, NE1], F32)
        ex = p.sb("ex", [128, NE1], F32)
        pb = p.sb("pb", [128, NE1], F32)
        t8 = p.sb("t8", [128, 8], F32)
        sc1 = p.sb("sc1", [128, 6 * NE1], F32)
        Gm = p.sb("Gm", [NE1, TPC], F32)
        M1m = p.sb("M1m", [NE1, TPC], F32)
        M2m = p.sb("M2m", [NE1, TPC], F32)
        pj2 = [p.ps(f"pj2{i}", [128, 512]) for i in range(2)]
        pT = p.ps("pT", [128, 128], BF16)
        pr = p.ps("pr", [128, NE1])
        pG = p.ps("pG", [128, 128])

        w3 = ap["WoT1"].rearrange("(c p) o -> p c o", p=128)
        wotok = kb.dma("gpsimd", wbuf[:].rearrange("p (c o) -> p c o", c=8),
                       w3, wlr)
        wr3 = ap["Wr"].rearrange("(c p) e -> p c e", p=128)
        wrtok = kb.dma("gpsimd", wrb[:].rearrange("p (c e) -> p c e", c=8),
                       wr3, wlr)
        botok = kb.dma("sync", bob[:], ap["bo_b"], ldr_)

        a3 = fm_dram(ap["attT"])
        for i in range(NTT):
            xb = x_a if i % 2 == 0 else x_b
            xtok = kb.dma("sync", xb[:], ap["x2"][i * 128:(i + 1) * 128, :], ldr_,
                          wait=[v] if i >= 2 else [])
            atok = kb.dma("sync", atile[:].rearrange("p (c w) -> p c w", c=8),
                          a3[:, :, i * 128:(i + 1) * 128], ldr_, wait=[t])
            for og2 in range(2):
                pp = pj2[og2]
                for c in range(8):
                    kb.op("tensor", "matmul", pp[:],
                          atile[:, c * 128:(c + 1) * 128],
                          wbuf[:, c * 1024 + og2 * 512: c * 1024 + (og2 + 1) * 512],
                          start=(c == 0), stop=(c == 7),
                          wait=[wotok, v, atok] if c == 0 else [])
                kb.op("vector", "scalar_tensor_tensor",
                      x3t[:, og2 * 512:(og2 + 1) * 512],
                      xb[:, og2 * 512:(og2 + 1) * 512], 0.0, pp[:],
                      ALU.bypass, ALU.add, wait=[t, xtok, s])
                kb.op("vector", "tensor_add", x3t[:, og2 * 512:(og2 + 1) * 512],
                      x3t[:, og2 * 512:(og2 + 1) * 512],
                      bob[:, og2 * 512:(og2 + 1) * 512], wait=[botok])
            kb.dma("scalar", ap["x3"][i * 128:(i + 1) * 128, :], x3t[:], str_,
                   wait=[v])
            emit_ln_tile(kb, x3t, xn_tile, red, xs_sc, [], epsc[:, 0:1])
            kb.dma("scalar", ap["xn4"][i * 128:(i + 1) * 128, :], xn_tile[:], str_,
                   wait=[v])
            emit_transpose_to_fm(kb, xn_tile, xnTt, 0, ident, pT, itok, tcols=128)
            for c in range(8):
                kb.op("tensor", "matmul", pr[:], xnTt[:, c * 128:(c + 1) * 128],
                      wrb[:, c * NE1:(c + 1) * NE1],
                      start=(c == 0), stop=(c == 7),
                      wait=[v, wrtok] if c == 0 else [])
            kb.op("vector", "tensor_copy", lg[:], pr[:], wait=[t])
            kb.op("vector", "reduce_max", red[:, 0:1], lg[:], AXX)
            kb.op("vector", "tensor_scalar_mul", red[:, 1:2], red[:, 0:1], -1.0)
            kb.op("scalar", "activation", ex[:], lg[:], AF.Exp,
                  bias=red[:, 1:2], accum_out=red[:, 2:3], wait=[v])
            kb.op("vector", "reciprocal", red[:, 3:4], red[:, 2:3], wait=[s])
            kb.op("vector", "tensor_scalar", pb[:], ex[:], red[:, 3:4], None,
                  ALU.mult)
            kb.op("vector", "max", t8[:], pb[:])
            eq1 = sc1[:, 0:8]; f1 = sc1[:, 8:16]; m1 = sc1[:, 16:24]
            cand = sc1[:, 24:32]; m2 = sc1[:, 32:40]; mor = sc1[:, 40:48]
            kb.op("vector", "tensor_scalar", eq1, pb[:], t8[:, 0:1], None,
                  ALU.is_equal)
            kb.op("vector", "tensor_tensor_scan", f1, eq1, eq1, 0.0,
                  ALU.add, ALU.bypass)
            kb.op("vector", "tensor_single_scalar", f1, f1, 1.0, ALU.is_equal)
            kb.op("vector", "tensor_mul", m1, eq1, f1)
            kb.op("vector", "tensor_scalar", eq1, pb[:], t8[:, 1:2], None,
                  ALU.is_equal)
            kb.op("vector", "tensor_scalar", cand, m1, -1.0, 1.0,
                  ALU.mult, ALU.add)
            kb.op("vector", "tensor_mul", cand, cand, eq1)
            kb.op("vector", "tensor_tensor_scan", f1, cand, cand, 0.0,
                  ALU.add, ALU.bypass)
            kb.op("vector", "tensor_single_scalar", f1, f1, 1.0, ALU.is_equal)
            kb.op("vector", "tensor_mul", m2, cand, f1)
            kb.op("vector", "tensor_add", mor, m1, m2)
            kb.op("vector", "tensor_mul", mor, mor, pb[:])
            kb.op("tensor", "transpose", pG[0:NE1, :], mor, identf[:],
                  wait=[v, iftok])
            kb.op("vector", "tensor_copy", Gm[0:NE1, i * 128:(i + 1) * 128],
                  pG[0:NE1, 0:128], wait=[t])
            kb.op("tensor", "transpose", pG[0:NE1, :], m1, identf[:],
                  wait=[v, iftok])
            kb.op("vector", "tensor_copy", M1m[0:NE1, i * 128:(i + 1) * 128],
                  pG[0:NE1, 0:128], wait=[t])
            kb.op("tensor", "transpose", pG[0:NE1, :], m2, identf[:],
                  wait=[v, iftok])
            kb.op("vector", "tensor_copy", M2m[0:NE1, i * 128:(i + 1) * 128],
                  pG[0:NE1, 0:128], wait=[t])
        kb.dma("scalar", ap["G_em"], Gm[:], str_, wait=[v])
        kb.dma("scalar", ap["M1_em"], M1m[:], str_, wait=[v])
        kb.dma("scalar", ap["M2_em"], M2m[:], str_, wait=[v])
        kb.build(final_rings=[ldr_, wlr, str_])
        p.close()
    return nc, ap


def prep_launch_F(inputs, core, stD, stE):
    d = inputs
    im = {}
    im["x2"] = stD[core]["x2"]
    im["attT"] = stE[core]["attT"]
    im["WoT1"] = np.ascontiguousarray(np.asarray(d["L1_attn_Wo"]).T)
    im["bo_b"] = np.broadcast_to(np.asarray(d["L1_attn_bo"]), (128, D)).copy()
    im["Wr"] = np.ascontiguousarray(np.asarray(d["L1_ffn_Wr"]))
    im["ident"] = np.eye(128, dtype=np.float32).astype(BF)
    im["identf"] = np.eye(128, dtype=np.float32)
    return im



def build_launch_G():
    nc = bass.Bass("TRN2", target_bir_lowering=False, debug=False)
    ap = {}

    def din(name, shape, dt=F32):
        ap[name] = nc.dram_tensor(name, shape, dt, kind="ExternalInput").ap()

    def dout(name, shape, dt=F32):
        ap[name] = nc.dram_tensor(name, shape, dt, kind="ExternalOutput").ap()

    NTOK = NCORE * TPC
    din("g_own", [128, 128])
    din("g_row", [1, NTOK])
    din("xn4_all", [NTOK, D], BF16)
    din("W1", [D, 4096])
    din("W2", [4096, D])
    din("ident", [128, 128], BF16)
    din("onesb", [128, 128], BF16)
    din("iota_t", [128, 128], I32)
    din("trashg", [128, 128])
    dout("expert_out", [PAD1, D])
    dout("theta", [1, 1])

    invtab = nc.dram_tensor("invtabG", [21632, 1], I32, kind="Internal").ap()
    rowb = nc.dram_tensor("rowbG", [NTOK], F32, kind="Internal").ap()

    ctx = ExitStack()
    with ctx:
        kb = KB(nc, ctx)
        ldr_ = kb.ring("ld", 4)
        wlr = kb.ring("wl", 2)
        str_ = kb.ring("st", 4)
        gr = kb.ring("gq", 4)
        v, t, s, g = kb.e["vector"], kb.e["tensor"], kb.e["scalar"], kb.e["gpsimd"]

        ident = ctx.enter_context(nc.sbuf_tensor("identB", [128, 128], BF16))
        itok = kb.dma("sync", ident[:], ap["ident"], ldr_)
        onesb = ctx.enter_context(nc.sbuf_tensor("onesB", [128, 128], BF16))
        obtok = kb.dma("sync", onesb[:], ap["onesb"], ldr_)

        p1 = Pool(nc)
        go = p1.sb("go", [128, 128], F32)
        mk = p1.sb("mk", [128, 128], F32)
        bi = p1.sb("bi", [128, 8], F32)
        pcb = p1.sb("pcb", [128, 1], BF16)
        selb = p1.sb("selb", [1, 1], BF16)
        selm = p1.sb("selm", [128, 1], I32)
        pcnt = p1.ps("pcnt", [1, 1])
        pbc2 = p1.ps("pbc2", [128, 1])

        gotok = kb.dma("sync", go[:], ap["g_own"], ldr_)
        kb.op("vector", "memset", bi[:, 0:1], 0.0)
        kb.op("vector", "memset", bi[:, 1:2], 1.0)
        kb.op("vector", "memset", bi[:, 2:3], 0.5)
        for it in range(45):
            kb.op("vector", "tensor_scalar", mk[:], go[:], bi[:, 2:3], None,
                  ALU.is_gt, ALU.add, accum_out=bi[:, 4:5], wait=[gotok])
            kb.op("vector", "tensor_copy", pcb[:], bi[:, 4:5])
            kb.op("tensor", "matmul", pcnt[:], pcb[:], onesb[:, 0:1],
                  start=True, stop=True, wait=[v, obtok])
            kb.op("vector", "tensor_single_scalar", bi[0:1, 5:6], pcnt[0:1, 0:1],
                  float(CAP1) + 0.5, ALU.is_le, wait=[t])
            kb.op("vector", "tensor_copy", selb[:], bi[0:1, 5:6])
            kb.op("tensor", "matmul", pbc2[:], onesb[0:1, :], selb[:],
                  start=True, stop=True, wait=[v])
            kb.op("vector", "tensor_copy", selm[:], pbc2[:, 0:1], wait=[t])
            kb.op("vector", "select", bi[:, 7:8], selm[:], bi[:, 2:3],
                  bi[:, 1:2])
            kb.op("vector", "select", bi[:, 0:1], selm[:], bi[:, 0:1],
                  bi[:, 2:3])
            kb.op("vector", "tensor_copy", bi[:, 1:2], bi[:, 7:8])
            kb.op("vector", "tensor_add", bi[:, 2:3], bi[:, 0:1], bi[:, 1:2])
            kb.op("vector", "tensor_scalar_mul", bi[:, 2:3], bi[:, 2:3], 0.5)
        kb.dma("scalar", ap["theta"], bi[0:1, 1:2], str_, wait=[v])

        grow = p1.sb("grow", [1, NTOK], F32)
        mrow = p1.sb("mrow", [1, NTOK], F32)
        crow = p1.sb("crow", [1, NTOK], F32)
        rtok = kb.dma("sync", grow[:], ap["g_row"], ldr_)
        kb.op("vector", "tensor_scalar", mrow[:], grow[:], bi[0:1, 1:2], None,
              ALU.is_gt, wait=[rtok])
        kb.op("vector", "tensor_tensor_scan", crow[:], mrow[:], mrow[:], 0.0,
              ALU.add, ALU.bypass)
        kb.op("vector", "tensor_sub", crow[:], crow[:], mrow[:])
        kb.op("vector", "tensor_scalar", mrow[:], mrow[:], -1e6, 1e6,
              ALU.mult, ALU.add)
        kb.op("vector", "tensor_add", crow[:], crow[:], mrow[:])
        rbtok = kb.dma("scalar", rowb.rearrange("(a x) -> a x", a=1), crow[:],
                       str_, wait=[v])

        iota = p1.sb("iotaG", [128, 128], I32)
        zs = p1.sb("zsG", [128, 41], I32)
        idxf = p1.sb("idxf", [128, 128], F32)
        negi = p1.sb("negiG", [128, 128], F32)
        idxi = p1.sb("idxiG", [128, 128], I32)
        trashg = p1.sb("trashg_s", [128, 128], F32)
        ittok = kb.dma("sync", iota[:], ap["iota_t"], ldr_)
        tgt = kb.dma("sync", trashg[:], ap["trashg"], ldr_)
        kb.op("gpsimd", "memset", zs[:], 0)
        ztok = kb.dma("gpsimd",
                      invtab[0:5248, :].rearrange("(p k) one -> p (k one)", p=128),
                      zs[:], gr, wait=[g])
        ixtok = kb.dma("sync", idxf[:], rowb.rearrange("(a b) -> a b", a=128),
                       ldr_, wait=[rbtok])
        kb.op("vector", "tensor_single_scalar", negi[:], idxf[:], 5120.5,
              ALU.is_lt, wait=[ixtok])
        kb.op("vector", "tensor_mul", idxf[:], idxf[:], negi[:])
        kb.op("vector", "tensor_scalar", negi[:], negi[:], -1.0, 1.0,
              ALU.mult, ALU.add)
        kb.op("vector", "tensor_mul", negi[:], negi[:], trashg[:], wait=[tgt])
        kb.op("vector", "tensor_add", idxf[:], idxf[:], negi[:])
        kb.op("vector", "tensor_copy", idxi[:], idxf[:])
        vsc = (v, v.n)
        scr = kb.ring("scrG", 8)
        for tt_ in range(128):
            kb.idma("gpsimd", invtab,
                    bass.IndirectOffsetOnAxis(ap=idxi[:, tt_:tt_ + 1], axis=0),
                    iota[:, tt_:tt_ + 1], None, ring=scr,
                    wait=[vsc, ittok, ztok] if tt_ == 0 else [])
        scb_tok = scr.final_tokens()
        p1.close()

        fence_w = kb.fence() + gr.final_tokens() + ldr_.final_tokens() \
            + str_.final_tokens()
        fence = fence_w + scb_tok
        p2 = Pool(nc)
        w1b = p2.sb("w1b", [128, 8 * 4096], BF16)
        w2b = p2.sb("w2b", [128, 32 * D], BF16)
        wstage = [p2.sb(f"wst{i}", [128, 2048], F32) for i in range(2)]
        bufs = dict(
            idxc=[p2.sb(f"idxc{i}", [128, 1], I32) for i in range(4)],
            xg=[p2.sb(f"xg{i}", [128, D], BF16) for i in range(4)],
            xgT=p2.sb("xgT", [128, 8 * 512], BF16),
            hT=p2.sb("hT", [128, 32 * 512], BF16),
            eot=p2.sb("eot", [128, 2 * 512], F32),
            pT2=[p2.ps(f"pT2{i}", [128, 128], BF16) for i in range(2)],
            ps1=[p2.ps(f"ps1{i}", [128, 512]) for i in range(2)],
            ps2=[p2.ps(f"ps2{i}", [128, 512]) for i in range(2)],
            ident=ident, itok=itok,
        )
        zr = p2.sb("zr", [1, D], F32)
        kb.op("vector", "memset", zr[:], 0.0, wait=fence)
        kb.dma("scalar", ap["expert_out"][PAD1 - 1:PAD1, :], zr[:], str_,
               wait=[v])
        w1tok = emit_wload_cast(kb, ap["W1"], w1b, wstage, ldr_, 4096, fence)
        w2tok = emit_wload_cast(kb, ap["W2"], w2b, wstage, ldr_, D, fence)
        blocks = [(b * 512, 4) for b in range(10)]
        emit_ffn_blocks(kb, ap["xn4_all"], invtab, ap["expert_out"],
                        w1b, w2b, bufs, AF.Relu, blocks,
                        w1tok, w2tok, fence, ldr_, gr, str_)
        kb.build(final_rings=[ldr_, wlr, str_, gr])
        p2.close()
    return nc, ap


def prep_launch_G(inputs, core, stF):
    d = inputs
    im = {}
    G_all = np.concatenate([stF[j]["G_em"] for j in range(NCORE)], 1)
    im["g_own"] = np.ascontiguousarray(G_all[core].reshape(128, 128))
    im["g_row"] = np.ascontiguousarray(G_all[core].reshape(1, -1))
    im["xn4_all"] = np.concatenate([stF[j]["xn4"] for j in range(NCORE)], 0)
    im["W1"] = np.ascontiguousarray(np.asarray(d["L1_ffn_W1"][core]))
    im["W2"] = np.ascontiguousarray(np.asarray(d["L1_ffn_W2"][core]))
    im["ident"] = np.eye(128, dtype=np.float32).astype(BF)
    im["onesb"] = np.ones((128, 128), np.float32).astype(BF)
    p = np.arange(128)
    im["iota_t"] = (p[:, None] * 128 + np.arange(128)[None, :]).astype(np.int32)
    im["trashg"] = (5152 + np.arange(128)[:, None] * 128
                    + np.arange(128)[None, :]).astype(np.float32)
    return im



def build_launch_Ha():
    nc = bass.Bass("TRN2", target_bir_lowering=False, debug=False)
    ap = {}

    def din(name, shape, dt=F32):
        ap[name] = nc.dram_tensor(name, shape, dt, kind="ExternalInput").ap()

    def dout(name, shape, dt=F32):
        ap[name] = nc.dram_tensor(name, shape, dt, kind="ExternalOutput").ap()

    NTOK = NCORE * TPC
    din("G_all", [NE1, NTOK])
    din("M1_all", [NE1, NTOK])
    din("M2_all", [NE1, NTOK])
    din("theta_all", [NE1, 1])
    din("eoff8", [NE1, 1])
    dout("slota_row", [1, NTOK]); dout("wa_row", [1, NTOK])
    dout("slotb_row", [1, NTOK]); dout("wb_row", [1, NTOK])

    SEG = 4096

    ctx = ExitStack()
    with ctx:
        kb = KB(nc, ctx)
        ldr_ = kb.ring("ld", 6)
        str_ = kb.ring("st", 4)
        v, t, s = kb.e["vector"], kb.e["tensor"], kb.e["scalar"]

        p = Pool(nc)
        Gs = p.sb("Gs", [NE1, SEG], F32)
        M1s = p.sb("M1s", [NE1, SEG], F32)
        M2s = p.sb("M2s", [NE1, SEG], F32)
        Mk = p.sb("Mk", [NE1, SEG], F32)
        cum = p.sb("cum", [NE1, SEG], F32)
        wsel = p.sb("wsel", [NE1, SEG], F32)
        tmp = p.sb("tmp", [NE1, SEG], F32)
        thc = p.sb("thc", [NE1, 1], F32)
        eoffc = p.sb("eoffc8", [NE1, 1], F32)
        carry = p.sb("carry", [NE1, 2], F32)
        onesf = p.sb("onesf_h", [NE1, 1], F32)
        stg = p.sb("stg", [1, 1024], F32)
        pcs = p.ps("pcs", [1, 512])

        ttok = kb.dma("sync", thc[:], ap["theta_all"], ldr_)
        etok = kb.dma("sync", eoffc[:], ap["eoff8"], ldr_)
        kb.op("vector", "memset", onesf[:], 1.0)
        kb.op("vector", "memset", carry[:, 0:1], 0.0)
        st_tok = []

        def colsums(src_ap, out_dram, seg):
            nonlocal st_tok
            for cg in range(SEG // 512):
                kb.op("tensor", "matmul", pcs[:], onesf[:],
                      src_ap[0:NE1, cg * 512:(cg + 1) * 512],
                      start=True, stop=True, wait=[(v, v.n)])
                sl = (cg % 2) * 512
                kb.op("vector", "tensor_copy", stg[:, sl:sl + 512], pcs[:],
                      wait=[(t, kb.e["tensor"].n)] + st_tok)
                st_tok = [kb.dma(
                    "scalar",
                    out_dram[0:1, seg * SEG + cg * 512: seg * SEG + (cg + 1) * 512],
                    stg[:, sl:sl + 512], str_, wait=[v])]

        for seg in range(NTOK // SEG):
            gt = kb.dma("sync", Gs[:], ap["G_all"][:, seg * SEG:(seg + 1) * SEG],
                        ldr_, wait=[v, t])
            m1t = kb.dma("sync", M1s[:], ap["M1_all"][:, seg * SEG:(seg + 1) * SEG],
                         ldr_, wait=[v, t])
            m2t = kb.dma("sync", M2s[:], ap["M2_all"][:, seg * SEG:(seg + 1) * SEG],
                         ldr_, wait=[v, t])
            kb.op("vector", "tensor_scalar", Mk[:], Gs[:], thc[:, 0:1], None,
                  ALU.is_gt, wait=[gt, ttok])
            kb.op("vector", "tensor_tensor_scan", cum[:], Mk[:], Mk[:],
                  carry[:, 0:1], ALU.add, ALU.bypass)
            kb.op("vector", "tensor_copy", carry[:, 0:1],
                  cum[:, SEG - 1:SEG])
            kb.op("vector", "tensor_sub", cum[:], cum[:], Mk[:])
            kb.op("vector", "tensor_scalar", cum[:], cum[:], eoffc[:, 0:1],
                  None, ALU.add, wait=[etok])
            kb.op("vector", "tensor_mul", wsel[:], M1s[:], Mk[:], wait=[m1t])
            kb.op("vector", "tensor_mul", tmp[:], wsel[:], cum[:])
            colsums(tmp, ap["slota_row"], seg)
            kb.op("vector", "tensor_mul", tmp[:], wsel[:], Gs[:])
            colsums(tmp, ap["wa_row"], seg)
            kb.op("vector", "tensor_mul", wsel[:], M2s[:], Mk[:], wait=[m2t])
            kb.op("vector", "tensor_mul", tmp[:], wsel[:], cum[:])
            colsums(tmp, ap["slotb_row"], seg)
            kb.op("vector", "tensor_mul", tmp[:], wsel[:], Gs[:])
            colsums(tmp, ap["wb_row"], seg)
        kb.build(final_rings=[ldr_, str_])
        p.close()
    return nc, ap


def prep_launch_Ha(inputs, core, stF, stG):
    im = {}
    im["G_all"] = np.concatenate([stF[j]["G_em"] for j in range(NCORE)], 1)
    im["M1_all"] = np.concatenate([stF[j]["M1_em"] for j in range(NCORE)], 1)
    im["M2_all"] = np.concatenate([stF[j]["M2_em"] for j in range(NCORE)], 1)
    im["theta_all"] = np.concatenate([stG[j]["theta"] for j in range(NCORE)], 0)
    im["eoff8"] = (np.arange(NE1) * PAD1).astype(np.float32).reshape(NE1, 1)
    return im



def build_launch_Hb():
    nc = bass.Bass("TRN2", target_bir_lowering=False, debug=False)
    ap = {}

    def din(name, shape, dt=F32):
        ap[name] = nc.dram_tensor(name, shape, dt, kind="ExternalInput").ap()

    def dout(name, shape, dt=F32):
        ap[name] = nc.dram_tensor(name, shape, dt, kind="ExternalOutput").ap()

    din("x3", [TPC, D])
    din("expert_out_all", [NE1 * PAD1, D])
    din("slota", [128, NTT]); din("wa", [128, NTT])
    din("slotb", [128, NTT]); din("wb", [128, NTT])
    dout("out", [TPC, D])

    ctx = ExitStack()
    with ctx:
        kb = KB(nc, ctx)
        ldr_ = kb.ring("ld", 4)
        str_ = kb.ring("st", 4)
        gr = kb.ring("gq", 4)
        v, t, s, g = kb.e["vector"], kb.e["tensor"], kb.e["scalar"], kb.e["gpsimd"]

        p = Pool(nc)
        x_a = p.sb("x_a", [128, D], F32)
        x_b = p.sb("x_b", [128, D], F32)
        tabs = p.sb("tabs", [128, 4 * NTT], F32)
        idxa = p.sb("idxa", [128, 1], I32)
        idxb = p.sb("idxb", [128, 1], I32)
        ya = p.sb("ya", [128, D], F32)
        yb = p.sb("yb", [128, D], F32)
        acc = p.sb("acc", [128, D], F32)
        outt = p.sb("outt", [128, D], F32)

        ta = kb.dma("sync", tabs[:, 0:NTT], ap["slota"], ldr_)
        tb = kb.dma("sync", tabs[:, NTT:2 * NTT], ap["slotb"], ldr_)
        tc = kb.dma("sync", tabs[:, 2 * NTT:3 * NTT], ap["wa"], ldr_)
        td = kb.dma("sync", tabs[:, 3 * NTT:4 * NTT], ap["wb"], ldr_)
        ga_war = []
        for i in range(NTT):
            xb = x_a if i % 2 == 0 else x_b
            xtok = kb.dma("sync", xb[:], ap["x3"][i * 128:(i + 1) * 128, :], ldr_,
                          wait=[v] if i >= 2 else [])
            kb.op("vector", "tensor_copy", idxa[:], tabs[:, i:i + 1],
                  wait=[ta] + ga_war)
            kb.op("vector", "tensor_copy", idxb[:], tabs[:, NTT + i:NTT + i + 1],
                  wait=[tb])
            gt1 = kb.idma("gpsimd", ya[:], None, ap["expert_out_all"],
                          bass.IndirectOffsetOnAxis(ap=idxa[:], axis=0),
                          gr, wait=[v, s])
            gt2 = kb.idma("gpsimd", yb[:], None, ap["expert_out_all"],
                          bass.IndirectOffsetOnAxis(ap=idxb[:], axis=0),
                          gr, wait=[v, s])
            ga_war = [gt1, gt2]
            kb.op("vector", "tensor_scalar", acc[:], ya[:],
                  tabs[:, 2 * NTT + i:2 * NTT + i + 1], None, ALU.mult,
                  wait=[gt1, tc, s])
            kb.op("vector", "scalar_tensor_tensor", acc[:], yb[:],
                  tabs[:, 3 * NTT + i:3 * NTT + i + 1], acc[:],
                  ALU.mult, ALU.add, wait=[gt2, td])
            kb.op("vector", "tensor_add", outt[:], acc[:], xb[:],
                  wait=[xtok])
            kb.dma("scalar", ap["out"][i * 128:(i + 1) * 128, :], outt[:], str_,
                   wait=[v])
        kb.build(final_rings=[ldr_, str_, gr])
        p.close()
    return nc, ap


def prep_launch_Hb(inputs, core, stF2, stG, tabs):
    im = {}
    im["x3"] = stF2[core]["x3"]
    im["expert_out_all"] = np.concatenate(
        [stG[j]["expert_out"] for j in range(NCORE)], 0)
    for k2, row in (("slota", "slota_row"), ("wa", "wa_row"),
                    ("slotb", "slotb_row"), ("wb", "wb_row")):
        r = np.asarray(tabs[row]).reshape(-1)[core * TPC:(core + 1) * TPC]
        im[k2] = np.ascontiguousarray(r.reshape(NTT, 128).T)
    return im


GAMMA0 = (1.0 - np.exp2(-5.0 - np.arange(H0, dtype=np.float64)))


def prep_launch_B(inputs, core, stA):
    d = inputs
    im = {}
    im["x"] = np.ascontiguousarray(d["L0_attn_x"].reshape(-1, D)[core * TPC:(core + 1) * TPC])
    im["o_part"] = stA[core]["o_part"]
    im["qTd"] = stA[core]["qTd"]
    im["gTd"] = stA[core]["gTd"]
    im["P_all"] = np.concatenate([stA[j]["P"] for j in range(NCORE)], 0)
    coef = np.zeros((128, NCORE * H0), np.float32)
    b0 = (core // 4) * 4
    for j in range(b0, core):
        for h in range(H0):
            coef[:, j * H0 + h] = (GAMMA0[h] ** SC) ** (4 * (core - 1 - j))
    im["coefP"] = coef
    im["WoT"] = np.ascontiguousarray(np.asarray(d["L0_attn_Wo"]).T)
    im["gwT"] = np.ascontiguousarray(np.asarray(d["L0_ffn_gate_w"]).T)
    im["gnw"] = np.ascontiguousarray(np.asarray(d["L0_attn_gnorm_w"]).reshape(4, 128).T)
    tloc = np.arange(TPC, dtype=np.float32)
    cf = np.stack([g ** (tloc + 1) for g in GAMMA0.astype(np.float32)])
    im["crossfullT"] = np.repeat(cf[:, None, :], 128, axis=1).reshape(H0 * 128, TPC).astype(BF)
    im["ident"] = np.eye(128, dtype=np.float32).astype(BF)
    im["onesf"] = np.ones((128, 128), np.float32)
    return im


def kernel(**inputs):
    cores = list(range(NCORE))
    times = []

    def run(nc, ims):
        r = run_bass_kernel_spmd(nc, ims, core_ids=cores)
        times.append(r.exec_time_ns)
        try:
            with open("/tmp/klaunch_times.txt", "w") as f:
                f.write(repr(times))
        except OSError:
            pass
        return r.results

    ncA, _ = build_launch_A()
    stA = run(ncA, [prep_launch_A(inputs, c) for c in cores])
    ncB, _ = build_launch_B()
    stB = run(ncB, [prep_launch_B(inputs, c, stA) for c in cores])
    ncC, _ = build_launch_C()
    stC = run(ncC, [prep_launch_C(inputs, c, stB) for c in cores])
    ncD, _ = build_launch_D()
    stD = run(ncD, [prep_launch_D(inputs, c, stB, stC) for c in cores])
    ncE, _ = build_launch_E()
    stE = run(ncE, [prep_launch_E(inputs, c, stD) for c in cores])
    ncF, _ = build_launch_F()
    stF = run(ncF, [prep_launch_F(inputs, c, stD, stE) for c in cores])
    ncG, _ = build_launch_G()
    stG = run(ncG, [prep_launch_G(inputs, c, stF) for c in cores])
    ncHa, _ = build_launch_Ha()
    stHa = run(ncHa, [prep_launch_Ha(inputs, c, stF, stG) for c in cores])
    tabs = stHa[0]
    ncHb, _ = build_launch_Hb()
    stHb = run(ncHb, [prep_launch_Hb(inputs, c, stF, stG, tabs) for c in cores])
    out = np.concatenate([stHb[c]["out"] for c in cores], 0)
    return np.ascontiguousarray(out.reshape(2, 8192, D).astype(np.float32))


# version 09
# speedup vs baseline: 1.0347x; 1.0347x over previous
import numpy as np
import ml_dtypes
from contextlib import ExitStack

import concourse.bass as bass
import concourse.mybir as mybir
from concourse.bass_utils import run_bass_kernel_spmd

F32 = mybir.dt.float32
BF16 = mybir.dt.bfloat16
I32 = mybir.dt.int32
AF = mybir.ActivationFunctionType
ALU = mybir.AluOpType
AXX = mybir.AxisListType.X
BF = ml_dtypes.bfloat16

NCORE = 8
TPC = 2048
NTT = TPC // 128
D = 1024
SC = 512
NSC = TPC // SC
H0, HK0, HV0 = 4, 256, 512
H1, DH1 = 16, 64
NE0, CAP0, PAD0 = 16, 1280, 1281
NE1, CAP1, PAD1 = 8, 5120, 5121
EPS = 1e-5

ENGINES = ("sync", "vector", "tensor", "scalar", "gpsimd")


class Ctr:
    __slots__ = ("h", "n", "name")

    def __init__(self, h, name):
        self.h = h
        self.n = 0
        self.name = name


class Ring:
    def __init__(self, ctrs):
        self.ctrs = ctrs
        self.i = 0

    def next(self):
        c = self.ctrs[self.i]
        self.i = (self.i + 1) % len(self.ctrs)
        return c

    def final_tokens(self):
        return [(c, c.n) for c in self.ctrs if c.n > 0]


class KB:

    def __init__(self, nc, ctx):
        self.nc = nc
        self.ctx = ctx
        self.ops = {e: [] for e in ENGINES}
        self.e = {}
        for e in ENGINES:
            self.e[e] = self.sem("c_" + e)

    def sem(self, name):
        return Ctr(self.ctx.enter_context(self.nc.semaphore(name)), name)

    def ring(self, name, n):
        return Ring([self.sem(f"{name}{i}") for i in range(n)])

    def op(self, eng, meth, *args, wait=(), inc="auto", incn=1, **kw):
        wl = list(wait)
        if eng in ("vector", "scalar", "gpsimd") and meth not in (
                "dma_start", "indirect_dma_start"):
            wl.append((self.e[eng], self.e[eng].n))
        best = {}
        for w in wl:
            if isinstance(w, tuple):
                c, val = w
            else:
                c, val = w, w.n
            if val > 0 and val > best.get(id(c), (None, -1))[1]:
                best[id(c)] = (c.h, val)
        waits = list(best.values())
        if inc == "auto":
            inc = self.e[eng]
        if inc is not None:
            inc.n += incn
            inc_spec = (inc.h, incn)
        else:
            inc_spec = None
        self.ops[eng].append((waits, meth, args, kw, inc_spec))

    def dma(self, eng, out, in_, ring, wait=(), **kw):
        ctr = ring.next()
        wl = list(wait)
        if ctr.n > 0:
            wl.append((ctr, ctr.n))
        self.op(eng, "dma_start", wait=wl, inc=ctr, incn=16, out=out, in_=in_, **kw)
        return (ctr, ctr.n)

    def idma(self, eng, out, out_off, in_, in_off, ring=None, wait=(), ctr=None,
             **kw):
        wl = list(wait)
        if ring is not None:
            ctr = ring.next()
            if ctr.n > 0:
                wl.append((ctr, ctr.n))
        self.op(eng, "indirect_dma_start", out, out_off, in_, in_off,
                wait=wl, inc=ctr, incn=16, **kw)
        return (ctr, ctr.n)

    def fence(self):
        return [(c, c.n) for c in self.e.values() if c.n > 0]

    def build(self, final_rings=()):
        waits = []
        for r in final_rings:
            waits += r.final_tokens()
        waits += self.fence()
        self.op("sync", "nop", wait=waits, inc=None)
        with self.nc.Block() as block:
            for eng in ENGINES:
                ops = self.ops[eng]
                if not ops:
                    continue

                def body(q, ops=ops):
                    for waits, meth, args, kw, inc_spec in ops:
                        for h, v in waits:
                            q.wait_ge(h, v)
                        ins = getattr(q, meth)(*args, **kw)
                        if inc_spec is not None:
                            ins.then_inc(inc_spec[0], inc_spec[1])

                getattr(block, eng)(body)


def fm_dram(ap):
    return ap.rearrange("(c p) t -> p c t", p=128)


class Pool:

    def __init__(self, nc):
        self.nc = nc
        self.es = ExitStack()

    def sb(self, name, shape, dt=F32):
        return self.es.enter_context(self.nc.sbuf_tensor(name, shape, dt))

    def ps(self, name, shape, dt=F32):
        return self.es.enter_context(self.nc.psum_tensor(name, shape, dt))

    def close(self):
        self.es.close()



def emit_ln_tile(kb, xb, xn_tile, red, xs_sc, wait_x, epsc, war=()):
    v, s = kb.e["vector"], kb.e["scalar"]
    kb.op("vector", "reduce_sum", red[:, 0:1], xb[:], AXX, wait=list(wait_x) + [s])
    kb.op("vector", "tensor_scalar_mul", red[:, 1:2], red[:, 0:1], -1.0 / D)
    kb.op("scalar", "activation", xs_sc[:], xb[:], AF.Square,
          accum_out=red[:, 2:3], wait=list(wait_x) + [v])
    kb.op("vector", "tensor_mul", red[:, 3:4], red[:, 1:2], red[:, 1:2], wait=[s])
    kb.op("vector", "tensor_scalar", red[:, 4:5], red[:, 2:3], 1.0 / D,
          red[:, 3:4], ALU.mult, ALU.subtract)
    kb.op("scalar", "activation", red[:, 5:6], red[:, 4:5], AF.Ln, bias=epsc, wait=[v])
    kb.op("scalar", "activation", red[:, 6:7], red[:, 5:6], AF.Exp, scale=-0.5)
    kb.op("vector", "tensor_scalar", xn_tile[:], xb[:], red[:, 1:2],
          red[:, 6:7], ALU.add, ALU.mult, wait=[s, kb.e["tensor"]] + list(war))


def emit_transpose_to_fm(kb, src_tile, dstT, tile_idx, ident, pT, itok,
                         nchunks=8, tcols=TPC, war=None):
    v, t = kb.e["vector"], kb.e["tensor"]
    banks = pT if isinstance(pT, list) else [pT]
    if war is None:
        war = {}
    for c in range(nchunks):
        b = c % len(banks)
        bw = [(v, v.n)] if c < len(banks) else war.get(b, [(v, v.n)])
        kb.op("tensor", "transpose", banks[b][:],
              src_tile[:, c * 128:(c + 1) * 128],
              ident[:], wait=[itok] + bw)
        kb.op("vector", "tensor_copy",
              dstT[:, c * tcols + tile_idx * 128: c * tcols + (tile_idx + 1) * 128],
              banks[b][:], wait=[(t, t.n)])
        war[b] = [(v, v.n)]



def build_launch_A():
    nc = bass.Bass("TRN2", target_bir_lowering=False, debug=False)
    ap = {}

    def din(name, shape, dt=F32):
        ap[name] = nc.dram_tensor(name, shape, dt, kind="ExternalInput").ap()

    def dout(name, shape, dt=F32):
        ap[name] = nc.dram_tensor(name, shape, dt, kind="ExternalOutput").ap()

    din("x", [TPC, D])
    din("WqT", [D, D]); din("WkT", [D, D])
    din("WvT", [D, 2048]); din("WgT", [D, 2048])
    din("ident", [128, 128], BF16)
    din("r0_cos", [128, TPC]); din("r0_sin", [128, TPC])
    din("r0_cos_q", [128, TPC]); din("r0_sin_q", [128, TPC])
    din("maskT", [H0 * SC, SC], BF16)
    din("kdec", [128, H0 * 4])
    din("crossT", [H0 * 128, SC], BF16)
    dout("qTd", [D, TPC], BF16)
    dout("kTd", [D, TPC], BF16)
    dout("vd", [TPC, 2048], BF16)
    dout("gTd", [2048, TPC], BF16)
    dout("o_part", [2048, TPC], BF16)
    dout("P", [256, H0 * HV0])

    gamma = (1.0 - np.exp2(-5.0 - np.arange(H0, dtype=np.float64)))
    gSC = [float(g) ** SC for g in gamma]

    ctx = ExitStack()
    with ctx:
        kb = KB(nc, ctx)
        ldr_ = kb.ring("ld", 4)
        wlr = kb.ring("wl", 2)
        str_ = kb.ring("st", 4)
        v, t, s = kb.e["vector"], kb.e["tensor"], kb.e["scalar"]

        ident = ctx.enter_context(nc.sbuf_tensor("identB", [128, 128], BF16))
        itok = kb.dma("sync", ident[:], ap["ident"], ldr_)
        S_f = ctx.enter_context(nc.sbuf_tensor("S_f", [128, H0 * 2 * 512], F32))
        S_b = ctx.enter_context(nc.sbuf_tensor("S_b", [128, 2 * 512], BF16))

        p12 = Pool(nc)
        xnT = p12.sb("xnT", [128, 8 * TPC], BF16)
        wbuf = p12.sb("wbuf", [128, 8 * 2048], BF16)
        ph1 = p12.sb("ph1", [128, 4 * 512], F32)
        x_a = p12.sb("x_a", [128, D], F32)
        x_b = p12.sb("x_b", [128, D], F32)
        xs_sc = p12.sb("xs_sc", [128, D], F32)
        red = p12.sb("red", [128, 8], F32)
        epsc = p12.sb("epsc", [128, 1], F32)
        xn_tile = p12.sb("xn_tile", [128, D], BF16)
        rc = p12.sb("rc", [128, TPC], F32)
        rs = p12.sb("rs", [128, TPC], F32)
        evb = p12.sb("evb", [128, 4 * 512], BF16)
        pT = [p12.ps(f"pT{i}", [128, 128], BF16) for i in range(2)]
        pj = [p12.ps(f"pj{i}", [128, 512], F32) for i in range(4)]
        kb.op("vector", "memset", epsc[:], EPS)

        twar = {}
        for i in range(NTT):
            xb = x_a if i % 2 == 0 else x_b
            xtok = kb.dma("sync", xb[:], ap["x"][i * 128:(i + 1) * 128, :], ldr_,
                          wait=[v] if i >= 2 else [])
            emit_ln_tile(kb, xb, xn_tile, red, xs_sc, [xtok], epsc[:, 0:1])
            emit_transpose_to_fm(kb, xn_tile, xnT, i, ident, pT, itok, war=twar)

        def load_w(name, ncols):
            w3 = ap[name].rearrange("(c p) o -> p c o", p=128)
            dst = wbuf[:].rearrange("p (c o) -> p c o", c=8)[:, :, 0:ncols]
            return kb.dma("gpsimd", dst, w3, wlr, wait=[t])

        def wtile(c, o):
            return wbuf[:, c * 2048 + o * 128: c * 2048 + (o + 1) * 128]

        ev_tok = {}

        def proj_pair_rope(ncols, cos_ap, sin_ap, out_dram, wtok):
            ctok = kb.dma("sync", rc[:], cos_ap, ldr_, wait=[v])
            stok = kb.dma("sync", rs[:], sin_ap, ldr_, wait=[v])
            for pair in range(ncols // 256):
                for cg in range(4):
                    pa, pb = pj[2 * (cg % 2)], pj[2 * (cg % 2) + 1]
                    bw = rope_war.get(cg % 2, [(v, v.n)])
                    for (oc, pp) in ((2 * pair, pa), (2 * pair + 1, pb)):
                        for c in range(8):
                            kb.op("tensor", "matmul", pp[:], wtile(c, oc),
                                  xnT[:, c * TPC + cg * 512: c * TPC + (cg + 1) * 512],
                                  start=(c == 0), stop=(c == 7),
                                  wait=[wtok] + bw if c == 0 else [])
                    cs = rc[:, cg * 512:(cg + 1) * 512]
                    sn = rs[:, cg * 512:(cg + 1) * 512]
                    t1, t2 = ph1[:, 0:512], ph1[:, 512:1024]
                    t3, t4 = ph1[:, 1024:1536], ph1[:, 1536:2048]
                    kb.op("vector", "tensor_mul", t1, pa[:], cs, wait=[t, ctok, stok])
                    kb.op("vector", "tensor_mul", t2, pb[:], sn)
                    kb.op("vector", "tensor_mul", t3, pa[:], sn)
                    kb.op("vector", "tensor_mul", t4, pb[:], cs)
                    sl = (cg % 2) * 1024
                    ea, eb = evb[:, sl:sl + 512], evb[:, sl + 512:sl + 1024]
                    kb.op("vector", "tensor_sub", ea, t1, t2,
                          wait=ev_tok.get(sl, []))
                    kb.op("vector", "tensor_add", eb, t4, t3,
                          wait=ev_tok.get(sl + 512, []))
                    d3 = fm_dram(out_dram)
                    ta = kb.dma("scalar", d3[:, 2 * pair, cg * 512:(cg + 1) * 512],
                                ea, str_, wait=[v])
                    tb = kb.dma("scalar",
                                d3[:, 2 * pair + 1, cg * 512:(cg + 1) * 512],
                                eb, str_, wait=[v])
                    ev_tok[sl] = [ta]
                    ev_tok[sl + 512] = [tb]
                    rope_war[cg % 2] = [(v, v.n)]

        rope_war = {}
        wtok = load_w("WqT", 1024)
        proj_pair_rope(1024, ap["r0_cos_q"], ap["r0_sin_q"], ap["qTd"], wtok)
        wtok = load_w("WkT", 1024)
        proj_pair_rope(1024, ap["r0_cos"], ap["r0_sin"], ap["kTd"], wtok)

        wtok = load_w("WvT", 2048)
        pj_war = {}
        for i in range(NTT):
            for og in range(4):
                pp = pj[og]
                for c in range(8):
                    kb.op("tensor", "matmul", pp[:],
                          xnT[:, c * TPC + i * 128: c * TPC + (i + 1) * 128],
                          wbuf[:, c * 2048 + og * 512: c * 2048 + (og + 1) * 512],
                          start=(c == 0), stop=(c == 7),
                          wait=[wtok] + pj_war.get(og, [(v, v.n)])
                          if c == 0 else [])
                sl = (og % 2) * 512
                ea = evb[:, sl:sl + 512]
                war = ev_tok.get(sl, [])
                kb.op("vector", "tensor_copy", ea, pp[:], wait=[(t, t.n)] + war)
                pj_war[og] = [(v, v.n)]
                ev_tok[sl] = [kb.dma(
                    "scalar", ap["vd"][i * 128:(i + 1) * 128, og * 512:(og + 1) * 512],
                    ea, str_, wait=[v])]

        wtok = load_w("WgT", 2048)
        pj_war = {}
        for oc in range(16):
            for cg in range(4):
                pp = pj[cg % 4]
                for c in range(8):
                    kb.op("tensor", "matmul", pp[:], wtile(c, oc),
                          xnT[:, c * TPC + cg * 512: c * TPC + (cg + 1) * 512],
                          start=(c == 0), stop=(c == 7),
                          wait=[wtok] + pj_war.get(cg % 4, [(v, v.n)])
                          if c == 0 else [])
                sl = (cg % 2) * 512
                ea = evb[:, sl:sl + 512]
                war = ev_tok.get(sl, [])
                kb.op("vector", "tensor_copy", ea, pp[:], wait=[(t, t.n)] + war)
                pj_war[cg % 4] = [(v, v.n)]
                d3 = fm_dram(ap["gTd"])
                ev_tok[sl] = [kb.dma("scalar", d3[:, oc, cg * 512:(cg + 1) * 512],
                                     ea, str_, wait=[v])]

        p12.close()

        fence = kb.fence() + str_.final_tokens() + ldr_.final_tokens() \
            + wlr.final_tokens()
        p3 = Pool(nc)
        qsb_d = p3.sb("qsb", [128, 2 * 2 * 512], BF16)
        ksb_d = p3.sb("ksb", [128, 2 * 2 * 512], BF16)
        vsb_d = p3.sb("vsb", [128, 2 * 4 * 512], BF16)
        khat_d = p3.sb("khat", [128, 2 * 4 * 256], BF16)
        qhat_d = p3.sb("qhat", [128, 2 * 2 * 512], BF16)
        attnT_d = p3.sb("attnT", [128, 2 * 4 * 512], BF16)
        maskb = p3.sb("maskb", [128, H0 * 4 * 512], BF16)
        crossb = p3.sb("crossb", [128, H0 * 512], BF16)
        kdecb = p3.sb("kdecb", [128, H0 * 4], F32)
        oev = p3.sb("oev", [128, 4 * 512], BF16)
        pTr = [p3.ps(f"pTr{i}", [128, 128], BF16) for i in range(2)]
        psc = [p3.ps(f"psc{i}", [128, 512], F32) for i in range(2)]
        po = [p3.ps(f"po{i}", [128, 512], F32) for i in range(2)]
        pA = [p3.ps(f"pA{i}", [128, 512], F32) for i in range(2)]

        mk3 = ap["maskT"].rearrange("(h J p) i -> p (h J) i", p=128, h=H0)
        mtok = kb.dma("sync", maskb[:].rearrange("p (x i) -> p x i", i=512), mk3,
                      ldr_, wait=fence)
        cr3 = ap["crossT"].rearrange("(h p) i -> p h i", p=128)
        crtok = kb.dma("sync", crossb[:].rearrange("p (h i) -> p h i", h=H0), cr3,
                       ldr_, wait=fence)
        kdtok = kb.dma("sync", kdecb[:], ap["kdec"], ldr_, wait=fence)
        kb.op("vector", "memset", S_f[:], 0.0, wait=fence)

        q3 = fm_dram(ap["qTd"])
        k3 = fm_dram(ap["kTd"])
        ost = {}
        tr_war = {}
        sc_war = {}
        po_war = {}
        pA_war = {}
        iter_tok = {}
        for sc in range(NSC):
            for h in range(H0):
                it = sc * H0 + h
                bsel = it % 2
                qsb = qsb_d[:, bsel * 1024:(bsel + 1) * 1024]
                ksb = ksb_d[:, bsel * 1024:(bsel + 1) * 1024]
                vsb = vsb_d[:, bsel * 2048:(bsel + 1) * 2048]
                khat = khat_d[:, bsel * 1024:(bsel + 1) * 1024]
                qhat = qhat_d[:, bsel * 1024:(bsel + 1) * 1024]
                attnT = attnT_d[:, bsel * 2048:(bsel + 1) * 2048]
                lwar = iter_tok.get(bsel, [])
                toks = []
                for c in range(2):
                    toks.append(kb.dma("sync", qsb[:, c * 512:(c + 1) * 512],
                                       q3[:, h * 2 + c, sc * 512:(sc + 1) * 512],
                                       ldr_, wait=lwar))
                    toks.append(kb.dma("sync", ksb[:, c * 512:(c + 1) * 512],
                                       k3[:, h * 2 + c, sc * 512:(sc + 1) * 512],
                                       ldr_, wait=lwar))
                for jt in range(4):
                    toks.append(kb.dma("sync", vsb[:, jt * 512:(jt + 1) * 512],
                                       ap["vd"][sc * 512 + jt * 128:
                                                sc * 512 + (jt + 1) * 128,
                                                h * 512:(h + 1) * 512],
                                       ldr_, wait=lwar))

                for jt in range(4):
                    for c in range(2):
                        b = (jt * 2 + c) % 2
                        kb.op("tensor", "transpose", pTr[b][:],
                              ksb[:, c * 512 + jt * 128: c * 512 + (jt + 1) * 128],
                              ident[:],
                              wait=toks + [itok] + tr_war.get(b, [(v, v.n)]))
                        kb.op("vector", "tensor_scalar_mul",
                              khat[:, jt * 256 + c * 128: jt * 256 + (c + 1) * 128],
                              pTr[b][:], kdecb[:, h * 4 + jt: h * 4 + jt + 1],
                              wait=[(t, t.n), kdtok])
                        tr_war[b] = [(v, v.n)]

                kb.op("vector", "tensor_copy", S_b[:, 0:512],
                      S_f[:, (h * 2) * 512:(h * 2 + 1) * 512], wait=[t])
                kb.op("vector", "tensor_copy", S_b[:, 512:1024],
                      S_f[:, (h * 2 + 1) * 512:(h * 2 + 2) * 512])
                for c in range(2):
                    kb.op("vector", "tensor_mul", qhat[:, c * 512:(c + 1) * 512],
                          qsb[:, c * 512:(c + 1) * 512],
                          crossb[:, h * 512:(h + 1) * 512], wait=toks + [crtok])

                for jt in range(4):
                    b = jt % 2
                    for c in range(2):
                        kb.op("tensor", "matmul", psc[b][:],
                              ksb[:, c * 512 + jt * 128: c * 512 + (jt + 1) * 128],
                              qsb[:, c * 512:(c + 1) * 512],
                              start=(c == 0), stop=(c == 1),
                              wait=sc_war.get(b, [(v, v.n)]) + toks
                              if c == 0 else [])
                    kb.op("vector", "tensor_mul", attnT[:, jt * 512:(jt + 1) * 512],
                          psc[b][:],
                          maskb[:, (h * 4 + jt) * 512:(h * 4 + jt + 1) * 512],
                          wait=[(t, t.n), mtok])
                    sc_war[b] = [(v, v.n)]

                for dvc in range(4):
                    b = dvc % 2
                    pp = po[b]
                    pw = [(v, v.n)] if dvc < 2 else po_war.get(b, [(v, v.n)])
                    for jt in range(4):
                        kb.op("tensor", "matmul", pp[:],
                              vsb[:, jt * 512 + dvc * 128: jt * 512 + (dvc + 1) * 128],
                              attnT[:, jt * 512:(jt + 1) * 512],
                              start=(jt == 0), stop=False,
                              wait=pw if jt == 0 else [])
                    for c in range(2):
                        kb.op("tensor", "matmul", pp[:],
                              S_b[:, c * 512 + dvc * 128: c * 512 + (dvc + 1) * 128],
                              qhat[:, c * 512:(c + 1) * 512],
                              start=False, stop=(c == 1))
                    war = [ost[dvc]] if dvc in ost else []
                    kb.op("vector", "tensor_copy", oev[:, dvc * 512:(dvc + 1) * 512],
                          pp[:], wait=[(t, t.n)] + war)
                    po_war[b] = [(v, v.n)]
                    d3 = fm_dram(ap["o_part"])
                    ost[dvc] = kb.dma("scalar",
                                      d3[:, h * 4 + dvc, sc * 512:(sc + 1) * 512],
                                      oev[:, dvc * 512:(dvc + 1) * 512], str_,
                                      wait=[v])

                for c in range(2):
                    pp = pA[c]
                    for jt in range(4):
                        kb.op("tensor", "matmul", pp[:],
                              khat[:, jt * 256 + c * 128: jt * 256 + (c + 1) * 128],
                              vsb[:, jt * 512:(jt + 1) * 512],
                              start=(jt == 0), stop=(jt == 3),
                              wait=[(v, v.n)] if jt == 0 else [])
                    kb.op("vector", "scalar_tensor_tensor",
                          S_f[:, (h * 2 + c) * 512:(h * 2 + c + 1) * 512],
                          S_f[:, (h * 2 + c) * 512:(h * 2 + c + 1) * 512],
                          gSC[h], pp[:], ALU.mult, ALU.add, wait=[(t, t.n)])
                    pA_war[c] = [(v, v.n)]
                iter_tok[bsel] = [(t, t.n), (v, v.n)]

        pd3 = ap["P"].rearrange("(c p) o -> p c o", p=128)
        for h in range(H0):
            for c in range(2):
                kb.dma("scalar", pd3[:, c, h * 512:(h + 1) * 512],
                       S_f[:, (h * 2 + c) * 512:(h * 2 + c + 1) * 512], str_,
                       wait=[v])

        kb.build(final_rings=[ldr_, wlr, str_])
        p3.close()
    return nc, ap




def build_launch_B():
    nc = bass.Bass("TRN2", target_bir_lowering=False, debug=False)
    ap = {}

    def din(name, shape, dt=F32):
        ap[name] = nc.dram_tensor(name, shape, dt, kind="ExternalInput").ap()

    def dout(name, shape, dt=F32):
        ap[name] = nc.dram_tensor(name, shape, dt, kind="ExternalOutput").ap()

    din("x", [TPC, D])
    din("o_part", [2048, TPC], BF16)
    din("qTd", [D, TPC], BF16)
    din("gTd", [2048, TPC], BF16)
    din("P_all", [NCORE * 256, H0 * HV0])
    din("coefP", [128, NCORE * H0])
    din("WoT", [2048, D])
    din("gwT", [D, NE0])
    din("gnw", [128, 4])
    din("crossfullT", [H0 * 128, TPC], BF16)
    din("ident", [128, 128], BF16)
    din("onesf", [128, 128])
    dout("x1", [TPC, D])
    dout("xn2", [TPC, D], BF16)
    dout("M_em", [NE0, TPC], BF16)
    dout("w_tm", [128, NTT])

    ctx = ExitStack()
    with ctx:
        kb = KB(nc, ctx)
        ldr_ = kb.ring("ld", 4)
        wlr = kb.ring("wl", 2)
        str_ = kb.ring("st", 4)
        v, t, s = kb.e["vector"], kb.e["tensor"], kb.e["scalar"]

        ident = ctx.enter_context(nc.sbuf_tensor("identB", [128, 128], BF16))
        itok = kb.dma("sync", ident[:], ap["ident"], ldr_)
        onesf = ctx.enter_context(nc.sbuf_tensor("onesf_sb", [128, 128], F32))
        otok = kb.dma("sync", onesf[:], ap["onesf"], ldr_)
        epsc = ctx.enter_context(nc.sbuf_tensor("epsc", [128, 1], F32))
        kb.op("vector", "memset", epsc[:], EPS)

        og_d = nc.dram_tensor("og_d", [16 * 128, TPC], BF16, kind="Internal").ap()

        p1 = Pool(nc)
        S_in = p1.sb("S_in", [128, H0 * 2 * 512], F32)
        S_inb_d = p1.sb("S_inb", [128, 2 * 2 * 512], BF16)
        ptmp = p1.sb("ptmp", [128, 2 * 2048], F32)
        coefb = p1.sb("coefb", [128, NCORE * H0], F32)
        gnwb = p1.sb("gnwb", [128, 4], F32)
        cstok = kb.dma("sync", coefb[:], ap["coefP"], ldr_)
        gntok = kb.dma("sync", gnwb[:], ap["gnw"], ldr_)
        kb.op("vector", "memset", S_in[:], 0.0)
        pa3 = ap["P_all"].rearrange("(j c p) o -> p (j c) o", p=128, c=2)
        pt_war = {}
        for j in range(NCORE):
            for c in range(2):
                i2 = j * 2 + c
                half = i2 % 2
                pt = ptmp[:, half * 2048:(half + 1) * 2048]
                ptok = kb.dma("sync" if i2 % 4 < 2 else "scalar", pt,
                              pa3[:, i2, :], ldr_ if i2 % 4 < 2 else str_,
                              wait=pt_war.get(half, []))
                for h in range(H0):
                    kb.op("vector", "scalar_tensor_tensor",
                          S_in[:, (h * 2 + c) * 512:(h * 2 + c + 1) * 512],
                          pt[:, h * 512:(h + 1) * 512],
                          coefb[:, j * H0 + h: j * H0 + h + 1],
                          S_in[:, (h * 2 + c) * 512:(h * 2 + c + 1) * 512],
                          ALU.mult, ALU.add, wait=[ptok, cstok])
                pt_war[half] = [(v, v.n)]

        qhat_d = p1.sb("qhat", [128, 2 * 2 * TPC], BF16)
        crb_d = p1.sb("crb", [128, 2 * TPC], BF16)
        of_d = p1.sb("of", [128, 2 * 4 * TPC], F32)
        sq1 = p1.sb("sq1", [128, 512], F32)
        ogt = p1.sb("ogt", [128, 2 * TPC], BF16)
        opb = p1.sb("opb", [128, 2 * TPC], BF16)
        gob = p1.sb("gob", [128, 2 * TPC], BF16)
        gsil = p1.sb("gsil", [128, TPC], BF16)
        tmpf = p1.sb("tmpf", [128, TPC], F32)
        vr = p1.sb("vr", [1, 2 * TPC], F32)
        varr = vr[0:1, 0:TPC]
        rstd = vr[0:1, TPC:2 * TPC]
        rbc = p1.sb("rbc", [128, TPC], F32)
        pox = [p1.ps(f"pox{i}", [128, 512]) for i in range(4)]
        pvar = p1.ps("pvar", [1, 512])
        pbc = p1.ps("pbc", [128, 512])

        q3 = fm_dram(ap["qTd"])
        cf3 = ap["crossfullT"].rearrange("(h p) t -> p h t", p=128)
        op3 = fm_dram(ap["o_part"])
        g3 = fm_dram(ap["gTd"])
        ogt_war = {}
        qh_war = {}
        si_war = {}
        of_war = {}
        op_war = {}
        go_war = {}
        pox_war = {}
        for h in range(H0):
            hb = h % 2
            qhat = qhat_d[:, hb * 2 * TPC:(hb + 1) * 2 * TPC]
            crb = crb_d[:, hb * TPC:(hb + 1) * TPC]
            S_inb = S_inb_d[:, hb * 1024:(hb + 1) * 1024]
            of = of_d[:, hb * 4 * TPC:(hb + 1) * 4 * TPC]
            qtok = kb.dma("sync", qhat.rearrange("p (c t) -> p c t", c=2),
                          q3[:, h * 2:h * 2 + 2, :], ldr_,
                          wait=qh_war.get(hb, []))
            crtok = kb.dma("sync", crb, cf3[:, h, :], ldr_,
                           wait=qh_war.get(hb, []))
            for c in range(2):
                kb.op("vector", "tensor_mul", qhat[:, c * TPC:(c + 1) * TPC],
                      qhat[:, c * TPC:(c + 1) * TPC], crb,
                      wait=[qtok, crtok])
                kb.op("vector", "tensor_copy", S_inb[:, c * 512:(c + 1) * 512],
                      S_in[:, (h * 2 + c) * 512:(h * 2 + c + 1) * 512],
                      wait=si_war.get(hb, []))
            vq_tok = [(v, v.n)]
            ofw = of_war.get(hb, [])
            for dvc in range(4):
                oh = dvc % 2
                optok = kb.dma("sync", opb[:, oh * TPC:(oh + 1) * TPC],
                               op3[:, h * 4 + dvc, :], ldr_,
                               wait=op_war.get(oh, []))
                for tc in range(4):
                    bk = (dvc * 4 + tc) % 4
                    pp = pox[bk]
                    for c in range(2):
                        kb.op("tensor", "matmul",
                              pp[:],
                              S_inb[:, c * 512 + dvc * 128: c * 512 + (dvc + 1) * 128],
                              qhat[:, c * TPC + tc * 512: c * TPC + (tc + 1) * 512],
                              start=(c == 0), stop=(c == 1),
                              wait=vq_tok + pox_war.get(bk, [])
                              if c == 0 else [])
                    kb.op("vector", "scalar_tensor_tensor",
                          of[:, dvc * TPC + tc * 512: dvc * TPC + (tc + 1) * 512],
                          opb[:, oh * TPC + tc * 512: oh * TPC + (tc + 1) * 512],
                          0.0, pp[:],
                          ALU.bypass, ALU.add, wait=[(t, t.n), optok] + ofw)
                    pox_war[bk] = [(v, v.n)]
                op_war[oh] = [(v, v.n)]
            for tc in range(4):
                for dvc in range(4):
                    kb.op("scalar", "activation", sq1[:],
                          of[:, dvc * TPC + tc * 512: dvc * TPC + (tc + 1) * 512],
                          AF.Square, wait=[(v, v.n), (t, t.n)])
                    kb.op("tensor", "matmul", pvar[:], onesf[:, 0:1], sq1[:],
                          start=(dvc == 0), stop=(dvc == 3),
                          wait=[(s, s.n), otok] + ([(v, v.n)] if dvc == 0 else []))
                kb.op("vector", "tensor_copy", varr[:, tc * 512:(tc + 1) * 512],
                      pvar[:], wait=[(t, t.n)])
            kb.op("scalar", "activation", rstd[:], varr[:], AF.Ln,
                  bias=epsc[0:1, 0:1], scale=1.0 / HV0, wait=[(v, v.n)])
            kb.op("scalar", "activation", varr[:], rstd[:], AF.Exp, scale=-0.5)
            for tc in range(4):
                kb.op("tensor", "matmul", pbc[:], onesf[0:1, :],
                      varr[:, tc * 512:(tc + 1) * 512], start=True, stop=True,
                      wait=[(s, s.n), (v, v.n)])
                kb.op("vector", "tensor_copy", rbc[:, tc * 512:(tc + 1) * 512],
                      pbc[:], wait=[(t, t.n)])
            for dvc in range(4):
                oh = dvc % 2
                gtok = kb.dma("sync", gob[:, oh * TPC:(oh + 1) * TPC],
                              g3[:, h * 4 + dvc, :], ldr_,
                              wait=go_war.get(oh, []))
                kb.op("scalar", "activation", gsil[:],
                      gob[:, oh * TPC:(oh + 1) * TPC], AF.Silu,
                      wait=[gtok])
                go_war[oh] = [(s, s.n)]
                kb.op("vector", "scalar_tensor_tensor", tmpf[:],
                      of[:, dvc * TPC:(dvc + 1) * TPC],
                      gnwb[:, dvc:dvc + 1], rbc[:],
                      ALU.mult, ALU.mult, wait=[gntok])
                kb.op("vector", "tensor_mul", ogt[:, oh * TPC:(oh + 1) * TPC],
                      tmpf[:], gsil[:],
                      wait=ogt_war.get(oh, []) + [(s, s.n)])
                ogt_war[oh] = [kb.dma(
                    "scalar", fm_dram(og_d)[:, h * 4 + dvc, :],
                    ogt[:, oh * TPC:(oh + 1) * TPC], str_, wait=[v])]
            qh_war[hb] = [(t, t.n), (v, v.n)]
            si_war[hb] = [(t, t.n)]
            of_war[hb] = [(s, s.n), (v, v.n)]
        p1.close()

        fence = kb.fence() + ldr_.final_tokens() + str_.final_tokens()
        og_fence = fence
        p2 = Pool(nc)
        wbuf = p2.sb("wbuf", [128, 16 * 1024], BF16)
        ogt2 = p2.sb("ogt2", [128, 16 * 128], BF16)
        gwb = p2.sb("gwb", [128, 8 * NE0], BF16)
        x_a = p2.sb("x_a", [128, D], F32)
        x_b = p2.sb("x_b", [128, D], F32)
        x1t = p2.sb("x1t", [128, D], F32)
        xn_tile = p2.sb("xn_tile", [128, D], BF16)
        red = p2.sb("red", [128, 8], F32)
        xs_sc = p2.sb("xs_sc", [128, D], F32)
        xnTt = p2.sb("xnTt", [128, 8 * 128], BF16)
        lg = p2.sb("lg", [128, NE0], F32)
        eqb = p2.sb("eqb", [128, NE0], F32)
        cumb = p2.sb("cumb", [128, NE0], F32)
        cum1 = p2.sb("cum1", [128, NE0], F32)
        m_bf = p2.sb("m_bf", [128, NE0], BF16)
        wsb = p2.sb("wsb", [128, NTT], F32)
        Mem = p2.sb("Mem", [16, TPC], BF16)
        pj2 = [p2.ps(f"pj2{i}", [128, 512]) for i in range(2)]
        pT = p2.ps("pT", [128, 128], BF16)
        pr = p2.ps("pr", [128, NE0])

        w3 = ap["WoT"].rearrange("(c p) o -> p c o", p=128)
        wotok = kb.dma("gpsimd", wbuf[:].rearrange("p (c o) -> p c o", c=16),
                       w3, wlr, wait=fence)
        gw3 = ap["gwT"].rearrange("(c p) e -> p c e", p=128)
        gwtok = kb.dma("gpsimd", gwb[:].rearrange("p (c e) -> p c e", c=8),
                       gw3, wlr, wait=fence)

        x1_war = []
        xn2_war = []
        for i in range(NTT):
            xb = x_a if i % 2 == 0 else x_b
            xtok = kb.dma("sync", xb[:], ap["x"][i * 128:(i + 1) * 128, :], ldr_,
                          wait=[v] if i >= 2 else fence)
            ogtok = kb.dma("sync", ogt2[:].rearrange("p (c w) -> p c w", c=16),
                           fm_dram(og_d)[:, :, i * 128:(i + 1) * 128], ldr_,
                           wait=[t] + (og_fence if i < 2 else []))
            for og2 in range(2):
                pp = pj2[og2]
                for c in range(16):
                    kb.op("tensor", "matmul", pp[:],
                          ogt2[:, c * 128:(c + 1) * 128],
                          wbuf[:, c * 1024 + og2 * 512: c * 1024 + (og2 + 1) * 512],
                          start=(c == 0), stop=(c == 15),
                          wait=[wotok, v, ogtok] if c == 0 else [])
                kb.op("vector", "scalar_tensor_tensor",
                      x1t[:, og2 * 512:(og2 + 1) * 512],
                      xb[:, og2 * 512:(og2 + 1) * 512], 0.0, pp[:],
                      ALU.bypass, ALU.add, wait=[t, xtok, s] + x1_war)
            x1_war = [kb.dma("scalar", ap["x1"][i * 128:(i + 1) * 128, :],
                             x1t[:], str_, wait=[v])]
            emit_ln_tile(kb, x1t, xn_tile, red, xs_sc, [], epsc[:, 0:1],
                         war=xn2_war)
            xn2_war = [kb.dma("scalar", ap["xn2"][i * 128:(i + 1) * 128, :],
                              xn_tile[:], str_, wait=[v])]
            emit_transpose_to_fm(kb, xn_tile, xnTt, 0, ident, pT, itok, tcols=128)
            for c in range(8):
                kb.op("tensor", "matmul", pr[:], xnTt[:, c * 128:(c + 1) * 128],
                      gwb[:, c * NE0:(c + 1) * NE0],
                      start=(c == 0), stop=(c == 7),
                      wait=[v, gwtok] if c == 0 else [])
            kb.op("vector", "tensor_copy", lg[:], pr[:], wait=[t])
            kb.op("vector", "reduce_max", red[:, 0:1], lg[:], AXX)
            kb.op("vector", "tensor_scalar_mul", red[:, 1:2], red[:, 0:1], -1.0)
            kb.op("scalar", "activation", eqb[:], lg[:], AF.Exp,
                  bias=red[:, 1:2], accum_out=red[:, 2:3], wait=[v])
            kb.op("vector", "reciprocal", wsb[:, i:i + 1], red[:, 2:3], wait=[s])
            kb.op("vector", "tensor_scalar", eqb[:], lg[:], red[:, 0:1], None,
                  ALU.is_equal)
            kb.op("vector", "tensor_tensor_scan", cumb[:], eqb[:], eqb[:], 0.0,
                  ALU.add, ALU.bypass)
            kb.op("vector", "tensor_single_scalar", cum1[:], cumb[:], 1.0,
                  ALU.is_equal)
            kb.op("vector", "tensor_mul", m_bf[:], eqb[:], cum1[:])
            kb.op("tensor", "transpose", pT[0:16, :], m_bf[:], ident[:],
                  wait=[v, itok])
            kb.op("vector", "tensor_copy", Mem[0:16, i * 128:(i + 1) * 128],
                  pT[0:16, :], wait=[t])
        kb.dma("scalar", ap["M_em"], Mem[0:16, :], str_, wait=[v])
        kb.dma("scalar", ap["w_tm"], wsb[:], str_, wait=[v])
        kb.build(final_rings=[ldr_, wlr, str_])
        p2.close()
    return nc, ap


def host_constants(core):
    c = {}
    c["ident"] = np.eye(128, dtype=np.float32).astype(BF)
    tloc = np.arange(TPC, dtype=np.float32)
    pos = (core * TPC + tloc) % 8192

    inv0 = 1.0 / 10000.0 ** (np.arange(0, HK0, 2, dtype=np.float32) / HK0)
    fr = inv0[:, None] * pos[None, :]
    c["r0_cos"] = np.cos(fr).astype(np.float32)
    c["r0_sin"] = np.sin(fr).astype(np.float32)
    s0 = HK0 ** -0.5
    c["r0_cos_q"] = (np.cos(fr) * s0).astype(np.float32)
    c["r0_sin_q"] = (np.sin(fr) * s0).astype(np.float32)

    gamma = (1.0 - np.exp2(-5.0 - np.arange(H0, dtype=np.float32)))
    i512 = np.arange(SC, dtype=np.float32)
    dif = i512[None, :] - i512[:, None]
    mT = np.stack([np.where(dif >= 0, g ** np.clip(dif, 0, None), 0.0) for g in gamma])
    c["maskT"] = mT.reshape(H0 * SC, SC).astype(BF)
    kd = np.zeros((128, H0 * 4), np.float32)
    for h in range(H0):
        for jt in range(4):
            kd[:, h * 4 + jt] = gamma[h] ** (SC - 1 - (jt * 128 + np.arange(128)))
    c["kdec"] = kd
    cr = np.stack([g ** (i512 + 1) for g in gamma])
    c["crossT"] = np.repeat(cr[:, None, :], 128, axis=1).reshape(H0 * 128, SC).astype(BF)
    return c


def prep_launch_A(inputs, core):
    d = inputs
    im = host_constants(core)
    im["x"] = np.ascontiguousarray(d["L0_attn_x"].reshape(-1, D)[core * TPC:(core + 1) * TPC])
    im["WqT"] = np.ascontiguousarray(np.asarray(d["L0_attn_Wq"]).T)
    im["WkT"] = np.ascontiguousarray(np.asarray(d["L0_attn_Wk"]).T)
    im["WvT"] = np.ascontiguousarray(np.asarray(d["L0_attn_Wv"]).T)
    im["WgT"] = np.ascontiguousarray(np.asarray(d["L0_attn_Wg"]).T)
    keep = {"x", "WqT", "WkT", "WvT", "WgT", "ident", "r0_cos", "r0_sin",
            "r0_cos_q", "r0_sin_q", "maskT", "kdec", "crossT"}
    return {k: im[k] for k in keep}





def emit_wload_cast(kb, w_src, dst_bf, stage, ldr_, n_cols, tag_toks):
    s = kb.e["scalar"]
    nch = w_src.shape[0] // 128
    w3 = w_src.rearrange("(c p) f -> p c f", p=128)
    W = stage[0].shape[1]
    st_war = [[(s, s.n)], [(s, s.n)]]
    i = 0
    if n_cols <= W:
        per = W // n_cols
        for c0 in range(0, nch, per):
            buf = stage[i % 2]
            cw = min(per, nch - c0)
            tok = kb.dma("sync", buf[:, 0:cw * n_cols].rearrange(
                "p (c f) -> p c f", c=cw), w3[:, c0:c0 + cw, :], ldr_,
                wait=st_war[i % 2] + tag_toks)
            kb.op("scalar", "activation",
                  dst_bf[:, c0 * n_cols:(c0 + cw) * n_cols],
                  buf[:, 0:cw * n_cols], AF.Copy, wait=[tok])
            st_war[i % 2] = [(s, s.n)]
            i += 1
    else:
        for c in range(nch):
            for off in range(0, n_cols, W):
                buf = stage[i % 2]
                w = min(W, n_cols - off)
                tok = kb.dma("sync", buf[:, 0:w], w3[:, c, off:off + w], ldr_,
                             wait=st_war[i % 2] + tag_toks)
                kb.op("scalar", "activation",
                      dst_bf[:, c * n_cols + off: c * n_cols + off + w],
                      buf[:, 0:w], AF.Copy, wait=[tok])
                st_war[i % 2] = [(s, s.n)]
                i += 1
    return (s, s.n)


def emit_ffn_blocks(kb, ap_in, invtab, eo_ap, w1b, w2b, bufs, act_fn,
                    blocks, w1tok, w2tok, fence, ldr_, gr, str_):
    v, t, s, g = kb.e["vector"], kb.e["tensor"], kb.e["scalar"], kb.e["gpsimd"]
    idxc, xg, xgT, hT, eot = (bufs["idxc"], bufs["xg"], bufs["xgT"],
                              bufs["hT"], bufs["eot"])
    pT2, ps1, ps2 = bufs["pT2"], bufs["ps1"], bufs["ps2"]
    ident = bufs["ident"]
    itok = bufs["itok"]
    ga_war = []
    eo_war = bufs.setdefault("_eo_war", {})
    for base, nt in blocks:
        gtoks = []
        for j in range(nt):
            itk = kb.dma("gpsimd", idxc[j][:],
                         invtab[base + j * 128:base + (j + 1) * 128, :], gr,
                         wait=fence + ga_war)
            gtoks.append(kb.idma("gpsimd", xg[j][:], None, ap_in,
                                 bass.IndirectOffsetOnAxis(ap=idxc[j][:], axis=0),
                                 gr, wait=[itk, t]))
        ga_war = list(gtoks)
        ns = nt * 128
        evac_tok = bufs.setdefault("_evac_tok", [[], []])
        for j in range(nt):
            for c in range(8):
                bank = (j * 8 + c) % 2
                kb.op("tensor", "transpose", pT2[bank][:],
                      xg[j][:, c * 128:(c + 1) * 128], ident[:],
                      wait=[gtoks[j], itok] + evac_tok[bank])
                kb.op("vector", "tensor_copy",
                      xgT[:, c * 512 + j * 128: c * 512 + (j + 1) * 128],
                      pT2[bank][:], wait=[(t, kb.e["tensor"].n)])
                evac_tok[bank] = [(v, v.n)]
        s1war = bufs.setdefault("_s1war", {})
        for fc in range(32):
            b1 = fc % 2
            pp = ps1[b1]
            w0 = [w1tok] + s1war.get(b1, [])
            if fc < 2:
                w0.append((v, v.n))
                w0.append((s, s.n))
            for c in range(8):
                kb.op("tensor", "matmul", pp[:, 0:ns],
                      w1b[:, c * 4096 + fc * 128: c * 4096 + (fc + 1) * 128],
                      xgT[:, c * 512: c * 512 + ns],
                      start=(c == 0), stop=(c == 7),
                      wait=w0 if c == 0 else [])
            kb.op("scalar", "activation", hT[:, fc * 512: fc * 512 + ns],
                  pp[:, 0:ns], act_fn, wait=[(t, kb.e["tensor"].n)])
            s1war[b1] = [(s, s.n)]
        s2war = bufs.setdefault("_s2war", {})
        for j in range(nt):
            for og2 in range(2):
                pp = ps2[og2]
                w0 = [w2tok] + s2war.get(og2, [])
                if j == 0:
                    w0.append((s, s.n))
                for fc in range(32):
                    kb.op("tensor", "matmul", pp[:],
                          hT[:, fc * 512 + j * 128: fc * 512 + (j + 1) * 128],
                          w2b[:, fc * D + og2 * 512: fc * D + (og2 + 1) * 512],
                          start=(fc == 0), stop=(fc == 31),
                          wait=w0 if fc == 0 else [])
                war = eo_war.get(og2, [])
                kb.op("vector", "tensor_copy",
                      eot[:, og2 * 512:(og2 + 1) * 512], pp[:],
                      wait=[(t, kb.e["tensor"].n)] + war)
                s2war[og2] = [(v, v.n)]
                eo_war[og2] = [kb.dma(
                    "scalar",
                    eo_ap[base + j * 128:base + (j + 1) * 128,
                          og2 * 512:(og2 + 1) * 512],
                    eot[:, og2 * 512:(og2 + 1) * 512], str_, wait=[v])]



def build_launch_C():
    nc = bass.Bass("TRN2", target_bir_lowering=False, debug=False)
    ap = {}

    def din(name, shape, dt=F32):
        ap[name] = nc.dram_tensor(name, shape, dt, kind="ExternalInput").ap()

    def dout(name, shape, dt=F32):
        ap[name] = nc.dram_tensor(name, shape, dt, kind="ExternalOutput").ap()

    NTOK = NCORE * TPC
    din("M_all", [NE0, NTOK], BF16)
    din("xn2_all", [NTOK, D], BF16)
    din("w1", [2 * D, 4096])
    din("w2", [2 * 4096, D])
    din("ident", [128, 128], BF16)
    din("iota_t", [128, 128], I32)
    din("eoff16", [16, 1])
    din("basec", [128, 1])
    din("trashc", [128, 128])
    dout("expert_out", [2 * PAD0, D])
    dout("gidx_row", [1, NTOK])
    dout("keep_row", [1, NTOK])

    invtab = nc.dram_tensor("invtab", [19072, 1], I32, kind="Internal").ap()

    ctx = ExitStack()
    with ctx:
        kb = KB(nc, ctx)
        ldr_ = kb.ring("ld", 4)
        wlr = kb.ring("wl", 2)
        str_ = kb.ring("st", 4)
        gr = kb.ring("gq", 4)
        v, t, s, g = kb.e["vector"], kb.e["tensor"], kb.e["scalar"], kb.e["gpsimd"]

        ident = ctx.enter_context(nc.sbuf_tensor("identB", [128, 128], BF16))
        itok = kb.dma("sync", ident[:], ap["ident"], ldr_)

        p1 = Pool(nc)
        Mem = p1.sb("Mem", [16, NTOK], BF16)
        cum = p1.sb("cum", [16, NTOK], F32)
        tmp_em = p1.sb("tmp_em", [16, NTOK], F32)
        eoffc = p1.sb("eoffc", [16, 1], F32)
        stg = p1.sb("stg", [1, 1024], F32)
        basec = p1.sb("basec_s", [128, 1], F32)
        iota = p1.sb("iota_s", [128, 128], I32)
        gidxf = p1.sb("gidxf", [128, 128], F32)
        negm = p1.sb("negm", [128, 128], F32)
        idxf2 = p1.sb("idxf2", [128, 128], I32)
        zs = p1.sb("zs", [128, 21], I32)
        trashc = p1.sb("trashc_s", [128, 128], F32)
        onesf = p1.sb("onesf_c", [128, 1], F32)
        pcs = p1.ps("pcs", [1, 512])

        mtok = kb.dma("sync", Mem[:], ap["M_all"], ldr_)
        etok = kb.dma("sync", eoffc[:], ap["eoff16"], ldr_)
        btok = kb.dma("sync", basec[:], ap["basec"], ldr_)
        ittok = kb.dma("sync", iota[:], ap["iota_t"], ldr_)
        kb.op("vector", "memset", onesf[:], 1.0)
        ttok2 = kb.dma("sync", trashc[:], ap["trashc"], ldr_)
        kb.op("gpsimd", "memset", zs[:], 0)
        ztok = kb.dma("gpsimd",
                      invtab[0:2688, :].rearrange("(p k) one -> p (k one)", p=128),
                      zs[:], gr, wait=[g])
        kb.op("vector", "tensor_tensor_scan", cum[:], Mem[:], Mem[:], 0.0,
              ALU.add, ALU.bypass, wait=[mtok])
        kb.op("vector", "tensor_scalar", cum[:], cum[:], -1.0, float(CAP0),
              ALU.add, ALU.min)
        kb.op("vector", "tensor_scalar", tmp_em[:], cum[:], eoffc[0:16, 0:1],
              None, ALU.add, wait=[etok])
        kb.op("vector", "tensor_mul", tmp_em[:], tmp_em[:], Mem[:])
        st_tok = []
        for cg in range(NTOK // 512):
            kb.op("tensor", "matmul", pcs[:], onesf[0:16, 0:1],
                  tmp_em[0:16, cg * 512:(cg + 1) * 512], start=True, stop=True,
                  wait=[(v, v.n)])
            sl = (cg % 2) * 512
            kb.op("vector", "tensor_copy", stg[:, sl:sl + 512], pcs[:],
                  wait=[(t, kb.e["tensor"].n)] + st_tok)
            st_tok = [kb.dma("scalar",
                             ap["gidx_row"][0:1, cg * 512:(cg + 1) * 512],
                             stg[:, sl:sl + 512], str_, wait=[v])]
        kb.op("vector", "tensor_single_scalar", tmp_em[:], cum[:],
              float(CAP0) - 0.5, ALU.is_lt)
        kb.op("vector", "tensor_mul", tmp_em[:], tmp_em[:], Mem[:])
        for cg in range(NTOK // 512):
            kb.op("tensor", "matmul", pcs[:], onesf[0:16, 0:1],
                  tmp_em[0:16, cg * 512:(cg + 1) * 512], start=True, stop=True,
                  wait=[(v, v.n)])
            sl = (cg % 2) * 512
            kb.op("vector", "tensor_copy", stg[:, sl:sl + 512], pcs[:],
                  wait=[(t, kb.e["tensor"].n)] + st_tok)
            st_tok = [kb.dma("scalar",
                             ap["keep_row"][0:1, cg * 512:(cg + 1) * 512],
                             stg[:, sl:sl + 512], str_, wait=[v])]
        gxt = kb.dma("sync", gidxf[:],
                     ap["gidx_row"].rearrange("one (a b) -> (one a) b", a=128),
                     ldr_, wait=st_tok)
        kb.op("vector", "tensor_scalar", gidxf[:], gidxf[:], basec[:, 0:1],
              None, ALU.subtract, wait=[gxt, btok])
        kb.op("vector", "tensor_single_scalar", negm[:], gidxf[:], 0.0,
              ALU.is_lt)
        kb.op("vector", "scalar_tensor_tensor", gidxf[:], negm[:], 1e6,
              gidxf[:], ALU.mult, ALU.add)
        kb.op("vector", "tensor_single_scalar", negm[:], gidxf[:], 2561.5,
              ALU.is_lt)
        kb.op("vector", "tensor_mul", gidxf[:], gidxf[:], negm[:])
        kb.op("vector", "tensor_scalar", negm[:], negm[:], -1.0, 1.0,
              ALU.mult, ALU.add)
        kb.op("vector", "tensor_mul", negm[:], negm[:], trashc[:],
              wait=[ttok2])
        kb.op("vector", "tensor_add", gidxf[:], gidxf[:], negm[:])
        kb.op("vector", "tensor_copy", idxf2[:], gidxf[:])
        vsc = (v, v.n)
        scr = kb.ring("scrC", 8)
        for tt_ in range(128):
            kb.idma("gpsimd", invtab,
                    bass.IndirectOffsetOnAxis(ap=idxf2[:, tt_:tt_ + 1], axis=0),
                    iota[:, tt_:tt_ + 1], None, ring=scr,
                    wait=[vsc, ittok, ztok] if tt_ == 0 else [])
        scb_tok = scr.final_tokens()
        p1.close()

        fence_w = kb.fence() + gr.final_tokens() + ldr_.final_tokens()
        fence = fence_w + scb_tok
        p2 = Pool(nc)
        w1b = p2.sb("w1b", [128, 8 * 4096], BF16)
        w2b = p2.sb("w2b", [128, 32 * D], BF16)
        wstage = [p2.sb(f"wst{i}", [128, 2048], F32) for i in range(2)]
        bufs = dict(
            idxc=[p2.sb(f"idxc{i}", [128, 1], I32) for i in range(4)],
            xg=[p2.sb(f"xg{i}", [128, D], BF16) for i in range(4)],
            xgT=p2.sb("xgT", [128, 8 * 512], BF16),
            hT=p2.sb("hT", [128, 32 * 512], BF16),
            eot=p2.sb("eot", [128, 2 * 512], F32),
            pT2=[p2.ps(f"pT2{i}", [128, 128], BF16) for i in range(2)],
            ps1=[p2.ps(f"ps1{i}", [128, 512]) for i in range(2)],
            ps2=[p2.ps(f"ps2{i}", [128, 512]) for i in range(2)],
            ident=ident, itok=itok,
        )
        zr = p2.sb("zr", [1, D], F32)
        kb.op("vector", "memset", zr[:], 0.0, wait=fence)
        kb.dma("scalar", ap["expert_out"][PAD0 - 1:PAD0, :], zr[:], str_,
               wait=[v])
        kb.dma("scalar", ap["expert_out"][2 * PAD0 - 1:2 * PAD0, :], zr[:],
               str_, wait=[v])

        for e in range(2):
            w1tok = emit_wload_cast(kb, ap["w1"][e * D:(e + 1) * D, :], w1b,
                                    wstage, ldr_, 4096, fence + [t])
            w2tok = emit_wload_cast(kb, ap["w2"][e * 4096:(e + 1) * 4096, :],
                                    w2b, wstage, ldr_, D, fence + [t])
            blocks = [(e * PAD0, 4), (e * PAD0 + 512, 4), (e * PAD0 + 1024, 2)]
            emit_ffn_blocks(kb, ap["xn2_all"], invtab, ap["expert_out"],
                            w1b, w2b, bufs, AF.Gelu_apprx_tanh, blocks,
                            w1tok, w2tok, fence, ldr_, gr, str_)
            fence = kb.fence()
        kb.build(final_rings=[ldr_, wlr, str_, gr])
        p2.close()
    return nc, ap


def prep_launch_C(inputs, core, stB):
    d = inputs
    im = {}
    im["M_all"] = np.concatenate([stB[j]["M_em"] for j in range(NCORE)], 1)
    im["xn2_all"] = np.concatenate([stB[j]["xn2"] for j in range(NCORE)], 0)
    im["w1"] = np.ascontiguousarray(
        np.asarray(d["L0_ffn_w1"][2 * core:2 * core + 2]).reshape(2 * D, 4096))
    im["w2"] = np.ascontiguousarray(
        np.asarray(d["L0_ffn_w2"][2 * core:2 * core + 2]).reshape(2 * 4096, D))
    im["ident"] = np.eye(128, dtype=np.float32).astype(BF)
    p = np.arange(128)
    im["iota_t"] = (p[:, None] * 128 + np.arange(128)[None, :]).astype(np.int32)
    im["eoff16"] = (np.arange(NE0) * PAD0).astype(np.float32).reshape(16, 1)
    im["basec"] = np.full((128, 1), 2 * core * PAD0, np.float32)
    im["trashc"] = (2688 + np.arange(128)[:, None] * 128
                    + np.arange(128)[None, :]).astype(np.float32)
    return im




def build_launch_D():
    nc = bass.Bass("TRN2", target_bir_lowering=False, debug=False)
    ap = {}

    def din(name, shape, dt=F32):
        ap[name] = nc.dram_tensor(name, shape, dt, kind="ExternalInput").ap()

    def dout(name, shape, dt=F32):
        ap[name] = nc.dram_tensor(name, shape, dt, kind="ExternalOutput").ap()

    din("x1", [TPC, D])
    din("expert_out_all", [NE0 * PAD0, D])
    din("gidx", [128, NTT])
    din("keep", [128, NTT])
    din("w_tm", [128, NTT])
    din("WqT1", [D, D]); din("WkT1", [D, D]); din("WvT1", [D, D])
    din("bq", [128, 8]); din("bk", [128, 8]); din("bv", [128, 8])
    din("r1_cos", [128, TPC], BF16); din("r1_sin", [128, TPC], BF16)
    din("r1_cos_q", [128, TPC], BF16); din("r1_sin_q", [128, TPC], BF16)
    din("rot_permT", [128, 128], BF16)
    din("ident", [128, 128], BF16)
    dout("x2", [TPC, D])
    dout("qTd1", [D, TPC], BF16)
    dout("kTd1", [D, TPC], BF16)
    dout("v_aug", [TPC, H1 * 65], BF16)

    ctx = ExitStack()
    with ctx:
        kb = KB(nc, ctx)
        ldr_ = kb.ring("ld", 4)
        wlr = kb.ring("wl", 2)
        str_ = kb.ring("st", 4)
        gr = kb.ring("gq", 4)
        v, t, s, g = kb.e["vector"], kb.e["tensor"], kb.e["scalar"], kb.e["gpsimd"]

        ident = ctx.enter_context(nc.sbuf_tensor("identB", [128, 128], BF16))
        itok = kb.dma("sync", ident[:], ap["ident"], ldr_)
        epsc = ctx.enter_context(nc.sbuf_tensor("epsc", [128, 1], F32))
        kb.op("vector", "memset", epsc[:], EPS)
        xn3T = ctx.enter_context(nc.sbuf_tensor("xn3T", [128, 8 * TPC], BF16))

        p1 = Pool(nc)
        x_a = p1.sb("x_a", [128, D], F32)
        x_b = p1.sb("x_b", [128, D], F32)
        gk = p1.sb("gk", [128, 3 * NTT], F32)
        idxc = p1.sb("idxc", [128, 2], I32)
        yg = p1.sb("yg", [128, 2 * D], F32)
        x2t = p1.sb("x2t", [128, D], F32)
        xn_tile = p1.sb("xn_tile", [128, D], BF16)
        red = p1.sb("red", [128, 8], F32)
        xs_sc = p1.sb("xs_sc", [128, D], F32)
        wk2 = p1.sb("wk2", [128, 1], F32)
        pT = p1.ps("pT", [128, 128], BF16)

        gtok1 = kb.dma("sync", gk[:, 0:NTT], ap["gidx"], ldr_)
        gtok2 = kb.dma("sync", gk[:, NTT:2 * NTT], ap["keep"], ldr_)
        gtok3 = kb.dma("sync", gk[:, 2 * NTT:3 * NTT], ap["w_tm"], ldr_)
        x2_war = []
        g_tok = {}
        stt_tok = {}

        def emit_gather(i):
            b = i % 2
            kb.op("vector", "tensor_copy", idxc[:, b:b + 1], gk[:, i:i + 1],
                  wait=[gtok1] + stt_tok.get(b, []))
            g_tok[b] = kb.idma("gpsimd", yg[:, b * D:(b + 1) * D], None,
                               ap["expert_out_all"],
                               bass.IndirectOffsetOnAxis(ap=idxc[:, b:b + 1],
                                                         axis=0),
                               gr, wait=[(v, v.n)])

        emit_gather(0)
        tw = {}
        for i in range(NTT):
            b = i % 2
            xb = x_a if i % 2 == 0 else x_b
            xtok = kb.dma("sync", xb[:], ap["x1"][i * 128:(i + 1) * 128, :], ldr_,
                          wait=[v] if i >= 2 else [])
            if i + 1 < NTT:
                emit_gather(i + 1)
            kb.op("vector", "tensor_mul", wk2[:], gk[:, NTT + i:NTT + i + 1],
                  gk[:, 2 * NTT + i:2 * NTT + i + 1], wait=[gtok2, gtok3])
            kb.op("vector", "scalar_tensor_tensor", x2t[:], yg[:, b * D:(b + 1) * D],
                  wk2[:, 0:1], xb[:], ALU.mult, ALU.add,
                  wait=[g_tok[b], xtok, s] + x2_war)
            stt_tok[b] = [(v, v.n)]
            x2_war = [kb.dma("scalar", ap["x2"][i * 128:(i + 1) * 128, :],
                             x2t[:], str_, wait=[v])]
            emit_ln_tile(kb, x2t, xn_tile, red, xs_sc, [], epsc[:, 0:1])
            emit_transpose_to_fm(kb, xn_tile, xn3T, i, ident, pT, itok, war=tw)
        p1.close()

        fence = kb.fence() + ldr_.final_tokens() + gr.final_tokens() \
            + str_.final_tokens()
        p2 = Pool(nc)
        wbuf_d = p2.sb("wbuf", [128, 2 * 8 * 1024], BF16)
        bb = p2.sb("bb", [128, 3 * 8], F32)
        rc = p2.sb("rc", [128, TPC], BF16)
        rs = p2.sb("rs", [128, TPC], BF16)
        rperm = p2.sb("rperm", [128, 128], BF16)
        qb_bf = p2.sb("qb_bf", [128, 2 * 512], BF16)
        t1 = p2.sb("t1", [128, 2 * 512], F32)
        t2 = p2.sb("t2", [128, 2 * 512], F32)
        evb = p2.sb("evb", [128, 4 * 512], BF16)
        vfm = p2.sb("vfm", [128, 8 * TPC], BF16)
        vag = p2.sb("vag", [128, H1 * 65], BF16)
        psq = [p2.ps(f"psq{i}", [128, 512]) for i in range(2)]
        pro = [p2.ps(f"pro{i}", [128, 512]) for i in range(2)]
        pT2 = [p2.ps(f"pT2{i}", [128, 128], BF16) for i in range(2)]

        btok = kb.dma("sync", bb[:, 0:8], ap["bq"], ldr_, wait=fence)
        kb.dma("sync", bb[:, 8:16], ap["bk"], ldr_)
        kb.dma("sync", bb[:, 16:24], ap["bv"], ldr_)
        ptok = kb.dma("sync", rperm[:], ap["rot_permT"], ldr_)

        wslot = [0]

        def load_w(name):
            b = wslot[0] % 2
            wslot[0] += 1
            wb = wbuf_d[:, b * 8192:(b + 1) * 8192]
            tok = kb.dma("gpsimd", wb.rearrange("p (c o) -> p c o", c=8),
                         ap[name].rearrange("(c p) o -> p c o", p=128),
                         wlr, wait=fence + ([t] if wslot[0] > 2 else []))
            return wb, tok

        ev_war = {}

        pq_war = {}
        pr_war = {}
        qb_war = {}

        def proj_rope(cos_ap, sin_ap, out_dram, bcol_off, wb, wtok):
            ctok = kb.dma("sync", rc[:], cos_ap, ldr_, wait=[v])
            stok = kb.dma("sync", rs[:], sin_ap, ldr_, wait=[v])
            for oc in range(8):
                for cg in range(4):
                    b = cg % 2
                    pq = psq[b]
                    pr2 = pro[b]
                    for c in range(8):
                        kb.op("tensor", "matmul", pq[:],
                              wb[:, c * 1024 + oc * 128: c * 1024 + (oc + 1) * 128],
                              xn3T[:, c * TPC + cg * 512: c * TPC + (cg + 1) * 512],
                              start=(c == 0), stop=(c == 7),
                              wait=[wtok] + pq_war.get(b, [(v, v.n)])
                              if c == 0 else [])
                    kb.op("vector", "tensor_scalar",
                          qb_bf[:, b * 512:(b + 1) * 512], pq[:],
                          bb[:, bcol_off + oc:bcol_off + oc + 1],
                          None, ALU.add,
                          wait=[(t, t.n), btok, ctok, stok] + qb_war.get(b, []))
                    pq_war[b] = [(v, v.n)]
                    kb.op("tensor", "matmul", pr2[:], rperm[:],
                          qb_bf[:, b * 512:(b + 1) * 512],
                          start=True, stop=True,
                          wait=[(v, v.n), ptok] + pr_war.get(b, []))
                    kb.op("vector", "tensor_mul", t1[:, b * 512:(b + 1) * 512],
                          qb_bf[:, b * 512:(b + 1) * 512],
                          rc[:, cg * 512:(cg + 1) * 512])
                    kb.op("vector", "tensor_mul", t2[:, b * 512:(b + 1) * 512],
                          pr2[:],
                          rs[:, cg * 512:(cg + 1) * 512], wait=[(t, t.n)])
                    pr_war[b] = [(v, v.n)]
                    sl = b * 512
                    kb.op("vector", "tensor_add", evb[:, sl:sl + 512],
                          t1[:, b * 512:(b + 1) * 512],
                          t2[:, b * 512:(b + 1) * 512],
                          wait=ev_war.get(sl, []))
                    qb_war[b] = [(v, v.n)]
                    d3 = fm_dram(out_dram)
                    ev_war[sl] = [kb.dma("scalar",
                                         d3[:, oc, cg * 512:(cg + 1) * 512],
                                         evb[:, sl:sl + 512], str_, wait=[v])]

        wb, wtok = load_w("WqT1")
        proj_rope(ap["r1_cos_q"], ap["r1_sin_q"], ap["qTd1"], 0, wb, wtok)
        wb, wtok = load_w("WkT1")
        proj_rope(ap["r1_cos"], ap["r1_sin"], ap["kTd1"], 8, wb, wtok)

        wb, wtok = load_w("WvT1")
        for oc in range(8):
            for cg in range(4):
                b = cg % 2
                pq = psq[b]
                for c in range(8):
                    kb.op("tensor", "matmul", pq[:],
                          wb[:, c * 1024 + oc * 128: c * 1024 + (oc + 1) * 128],
                          xn3T[:, c * TPC + cg * 512: c * TPC + (cg + 1) * 512],
                          start=(c == 0), stop=(c == 7),
                          wait=[wtok] + pq_war.get(b, [(v, v.n)])
                          if c == 0 else [])
                kb.op("vector", "tensor_scalar",
                      vfm[:, oc * TPC + cg * 512: oc * TPC + (cg + 1) * 512],
                      pq[:], bb[:, 16 + oc:16 + oc + 1], None, ALU.add,
                      wait=[(t, t.n)])
                pq_war[b] = [(v, v.n)]
        vw_war = []
        tp_war = {}
        for i in range(NTT):
            kb.op("vector", "memset", vag[:, 64::65], 1.0, wait=vw_war)
            for hp in range(8):
                b = hp % 2
                kb.op("tensor", "transpose", pT2[b][:],
                      vfm[:, hp * TPC + i * 128: hp * TPC + (i + 1) * 128],
                      ident[:],
                      wait=[itok] + ([(v, v.n)] if hp < 2
                                     else tp_war.get(b, [(v, v.n)])))
                kb.op("vector", "tensor_copy",
                      vag[:, (2 * hp) * 65:(2 * hp) * 65 + 64], pT2[b][:, 0:64],
                      wait=[(t, t.n)])
                kb.op("vector", "tensor_copy",
                      vag[:, (2 * hp + 1) * 65:(2 * hp + 1) * 65 + 64],
                      pT2[b][:, 64:128])
                tp_war[b] = [(v, v.n)]
            vw_war[:] = [kb.dma("scalar", ap["v_aug"][i * 128:(i + 1) * 128, :],
                                vag[:], str_, wait=[v])]
        kb.build(final_rings=[ldr_, wlr, str_, gr])
        p2.close()
    return nc, ap


def prep_launch_D(inputs, core, stB, stC):
    d = inputs
    im = {}
    im["x1"] = stB[core]["x1"]
    im["expert_out_all"] = np.concatenate(
        [stC[j]["expert_out"] for j in range(NCORE)], 0)
    gr_ = np.asarray(stC[0]["gidx_row"]).reshape(-1)[core * TPC:(core + 1) * TPC]
    im["gidx"] = np.ascontiguousarray(gr_.reshape(NTT, 128).T)
    kr_ = np.asarray(stC[0]["keep_row"]).reshape(-1)[core * TPC:(core + 1) * TPC]
    im["keep"] = np.ascontiguousarray(kr_.reshape(NTT, 128).T)
    im["w_tm"] = stB[core]["w_tm"]
    im["WqT1"] = np.ascontiguousarray(np.asarray(d["L1_attn_Wq"]).T)
    im["WkT1"] = np.ascontiguousarray(np.asarray(d["L1_attn_Wk"]).T)
    im["WvT1"] = np.ascontiguousarray(np.asarray(d["L1_attn_Wv"]).T)
    im["bq"] = np.ascontiguousarray(np.asarray(d["L1_attn_bq"]).reshape(8, 128).T)
    im["bk"] = np.ascontiguousarray(np.asarray(d["L1_attn_bk"]).reshape(8, 128).T)
    im["bv"] = np.ascontiguousarray(np.asarray(d["L1_attn_bv"]).reshape(8, 128).T)
    tloc = np.arange(TPC, dtype=np.float32)
    pos = (core * TPC + tloc) % 8192
    inv1 = 1.0 / 10000.0 ** (np.arange(0, DH1, 2, dtype=np.float32) / DH1)
    fr1 = np.concatenate([inv1, inv1])[:, None] * pos[None, :]
    cos2 = np.concatenate([np.cos(fr1)] * 2, 0)
    sin2 = np.concatenate([np.sin(fr1)] * 2, 0)
    s1 = DH1 ** -0.5
    im["r1_cos"] = cos2.astype(BF)
    im["r1_sin"] = sin2.astype(BF)
    im["r1_cos_q"] = (cos2 * s1).astype(BF)
    im["r1_sin_q"] = (sin2 * s1).astype(BF)
    P = np.zeros((128, 128), np.float32)
    for hb in range(2):
        o = hb * 64
        for j in range(32):
            P[o + j, o + 32 + j] = -1.0
            P[o + 32 + j, o + j] = 1.0
    im["rot_permT"] = P.T.copy().astype(BF)
    im["ident"] = np.eye(128, dtype=np.float32).astype(BF)
    return im




def build_launch_E():
    nc = bass.Bass("TRN2", target_bir_lowering=False, debug=False)
    ap = {}

    def din(name, shape, dt=F32):
        ap[name] = nc.dram_tensor(name, shape, dt, kind="ExternalInput").ap()

    def dout(name, shape, dt=F32):
        ap[name] = nc.dram_tensor(name, shape, dt, kind="ExternalOutput").ap()

    din("qT", [D, TPC], BF16)
    din("kT_all", [D, 8192], BF16)
    din("v_aug_all", [8192, H1 * 65], BF16)
    din("ident", [128, 128], BF16)
    din("identf", [128, 128])
    din("onesf", [128, 128])
    dout("attT", [D, TPC], BF16)

    NKT = 8192 // 128

    ctx = ExitStack()
    with ctx:
        kb = KB(nc, ctx)
        ldr_ = kb.ring("ld", 6)
        str_ = kb.ring("st", 4)
        v, t, s = kb.e["vector"], kb.e["tensor"], kb.e["scalar"]

        ident = ctx.enter_context(nc.sbuf_tensor("identB", [128, 128], BF16))
        itok = kb.dma("sync", ident[:], ap["ident"], ldr_)
        identf = ctx.enter_context(nc.sbuf_tensor("identF", [128, 128], F32))
        iftok = kb.dma("sync", identf[:], ap["identf"], ldr_)
        onesf = ctx.enter_context(nc.sbuf_tensor("onesf_sb", [128, 128], F32))
        otok = kb.dma("sync", onesf[:], ap["onesf"], ldr_)

        p = Pool(nc)
        qsb = [p.sb(f"qsb{i}", [128, TPC], BF16) for i in range(2)]
        ksb = [p.sb(f"ksb{i}", [128, 8192], BF16) for i in range(2)]
        vsb = [p.sb(f"vsb{i}", [128, 64 * 130], BF16) for i in range(2)]
        PTA = [p.sb(f"PTA{i}", [128, 1024], BF16) for i in range(3)]
        PTB = [p.sb(f"PTB{i}", [128, 1024], BF16) for i in range(3)]
        mcol = p.sb("mcol", [128, 8], F32)
        rec_sb = p.sb("rec_sb", [1, TPC], F32)
        rb_sb = p.sb("rb_sb", [64, TPC], F32)
        att_sb = p.sb("att_sb", [128, TPC], BF16)
        psA = p.ps("psA", [128, 1024])
        psB = p.ps("psB", [128, 1024])
        pvA = p.ps("pvA", [128, 1024])
        pvB = p.ps("pvB", [128, 1024])

        q3 = fm_dram(ap["qT"])
        k3 = fm_dram(ap["kT_all"])
        v3 = ap["v_aug_all"].rearrange("(kt p) c -> p kt c", p=128)
        a3 = fm_dram(ap["attT"])

        ld_war = [[], []]
        att_war = []
        ld_toks = {}

        def issue_loads(ch):
            bslot = ch % 2
            t1 = kb.dma("sync", qsb[bslot][:], q3[:, ch, :], ldr_,
                        wait=ld_war[bslot])
            t2 = kb.dma("sync", ksb[bslot][:], k3[:, ch, :], ldr_,
                        wait=ld_war[bslot])
            t3 = kb.dma("sync", vsb[bslot][:].rearrange("p (kt c) -> p kt c", c=130),
                        v3[:, :, 2 * ch * 65:(2 * ch + 2) * 65], ldr_,
                        wait=ld_war[bslot])
            ld_toks[ch] = (t1, t2, t3)

        issue_loads(0)
        for ch in range(H1 // 2):
            bslot = ch % 2
            qtok, ktok, vtok = ld_toks[ch]
            qs, ks, vs = qsb[bslot], ksb[bslot], vsb[bslot]
            if ch + 1 < H1 // 2:
                issue_loads(ch + 1)

            for i, kt in enumerate((0, 32)):
                for half in range(2):
                    kb.op("tensor", "matmul", psA[:, half * 512:(half + 1) * 512],
                          ks[0:64, kt * 128:(kt + 1) * 128],
                          qs[0:64, half * 512:(half + 1) * 512],
                          start=True, stop=True, tile_position=(0, 0),
                          wait=[qtok, ktok, s, v] if half == 0 else [])
                    kb.op("tensor", "matmul", psB[:, half * 512:(half + 1) * 512],
                          ks[64:128, kt * 128:(kt + 1) * 128],
                          qs[64:128, half * 512:(half + 1) * 512],
                          start=True, stop=True, tile_position=(64, 0))
                kb.op("vector", "reduce_max", mcol[:, 2 * i:2 * i + 1], psA[:],
                      AXX, wait=[t])
                kb.op("vector", "reduce_max", mcol[:, 2 * i + 1:2 * i + 2], psB[:],
                      AXX)
            kb.op("vector", "tensor_max", mcol[:, 4:5], mcol[:, 0:1], mcol[:, 2:3])
            kb.op("vector", "tensor_max", mcol[:, 5:6], mcol[:, 1:2], mcol[:, 3:4])
            kb.op("tensor", "transpose", psA[0:2, 0:128], mcol[:, 4:6],
                  identf[:], wait=[v, iftok])
            kb.op("vector", "reduce_max", mcol[0:2, 6:7], psA[0:2, 0:128], AXX,
                  wait=[t])
            kb.op("vector", "tensor_scalar_mul", mcol[0:2, 7:8], mcol[0:2, 6:7],
                  -1.0)
            kb.op("vector", "tensor_scalar", mcol[0:2, 4:6], identf[0:2, 0:2],
                  mcol[0:2, 7:8], None, ALU.mult)
            kb.op("tensor", "matmul", psA[0:128, 0:2], onesf[0:2, 0:128],
                  mcol[0:2, 4:6], start=True, stop=True, wait=[v, otok])
            kb.op("vector", "tensor_copy", mcol[:, 6:8], psA[:, 0:2], wait=[t])

            for qc in range(2):
                qof = qc * 1024
                expA = [0] * NKT
                expB = [0] * NKT
                for kt in range(NKT):
                    wA = [(s, expA[kt - 1])] if kt >= 1 else [v, s]
                    wB = [(s, expB[kt - 1])] if kt >= 1 else []
                    kb.op("tensor", "matmul", psA[:, 0:512],
                          ks[0:64, kt * 128:(kt + 1) * 128],
                          qs[0:64, qof:qof + 512],
                          start=True, stop=True, tile_position=(0, 0), wait=wA)
                    kb.op("tensor", "matmul", psA[:, 512:1024],
                          ks[0:64, kt * 128:(kt + 1) * 128],
                          qs[0:64, qof + 512:qof + 1024],
                          start=True, stop=True, tile_position=(0, 0))
                    tA = kb.e["tensor"].n
                    kb.op("tensor", "matmul", psB[:, 0:512],
                          ks[64:128, kt * 128:(kt + 1) * 128],
                          qs[64:128, qof:qof + 512],
                          start=True, stop=True, tile_position=(64, 0), wait=wB)
                    kb.op("tensor", "matmul", psB[:, 512:1024],
                          ks[64:128, kt * 128:(kt + 1) * 128],
                          qs[64:128, qof + 512:qof + 1024],
                          start=True, stop=True, tile_position=(64, 0))
                    tB = kb.e["tensor"].n
                    kb.op("scalar", "activation", PTA[kt % 3][:], psA[:], AF.Exp,
                          bias=mcol[:, 6:7], wait=[(t, tA)])
                    expA[kt] = s.n
                    kb.op("scalar", "activation", PTB[kt % 3][:], psB[:], AF.Exp,
                          bias=mcol[:, 7:8], wait=[(t, tB)])
                    expB[kt] = s.n
                    if kt >= 1:
                        for half in range(2):
                            kb.op("tensor", "matmul",
                                  pvA[0:65, half * 512:(half + 1) * 512],
                                  vs[:, (kt - 1) * 130:(kt - 1) * 130 + 65],
                                  PTA[(kt - 1) % 3][:, half * 512:(half + 1) * 512],
                                  start=(kt == 1), stop=False,
                                  wait=([vtok, (s, expA[kt - 1])] if kt == 1
                                        else [(s, expA[kt - 1])])
                                  if half == 0 else [])
                        for half in range(2):
                            kb.op("tensor", "matmul",
                                  pvB[0:65, half * 512:(half + 1) * 512],
                                  vs[:, (kt - 1) * 130 + 65:kt * 130],
                                  PTB[(kt - 1) % 3][:, half * 512:(half + 1) * 512],
                                  start=(kt == 1), stop=False,
                                  wait=[(s, expB[kt - 1])] if half == 0 else [])
                for half in range(2):
                    kb.op("tensor", "matmul",
                          pvA[0:65, half * 512:(half + 1) * 512],
                          vs[:, (NKT - 1) * 130:(NKT - 1) * 130 + 65],
                          PTA[(NKT - 1) % 3][:, half * 512:(half + 1) * 512],
                          start=False, stop=True,
                          wait=[(s, expA[NKT - 1])] if half == 0 else [])
                for half in range(2):
                    kb.op("tensor", "matmul",
                          pvB[0:65, half * 512:(half + 1) * 512],
                          vs[:, (NKT - 1) * 130 + 65:NKT * 130],
                          PTB[(NKT - 1) % 3][:, half * 512:(half + 1) * 512],
                          start=False, stop=True,
                          wait=[(s, expB[NKT - 1])] if half == 0 else [])

                for hh, pvX, sub in ((0, pvA, 0), (1, pvB, 64)):
                    kb.op("scalar", "activation", rec_sb[:, qof:qof + 1024],
                          pvX[64:65, :], AF.Ln, wait=[t, v])
                    kb.op("scalar", "activation", rec_sb[:, qof:qof + 1024],
                          rec_sb[:, qof:qof + 1024], AF.Exp, scale=-1.0)
                    for sl in range(2):
                        kb.op("tensor", "matmul",
                              psA[0:64, sl * 512:(sl + 1) * 512],
                              onesf[0:1, 0:64],
                              rec_sb[:, qof + sl * 512:qof + (sl + 1) * 512],
                              start=True, stop=True, wait=[s, v])
                    kb.op("vector", "tensor_copy", rb_sb[:, qof:qof + 1024],
                          psA[0:64, :], wait=[t])
                    kb.op("vector", "tensor_mul",
                          att_sb[sub:sub + 64, qof:qof + 1024],
                          pvX[0:64, :], rb_sb[:, qof:qof + 1024],
                          wait=att_war)
            ld_war[bslot] = [(t, t.n), (s, s.n)]
            att_war = [kb.dma("scalar", a3[:, ch, :], att_sb[:], str_,
                              wait=[v])]

        kb.build(final_rings=[ldr_, str_])
        p.close()
    return nc, ap


def prep_launch_E(inputs, core, stD):
    b0 = (core // 4) * 4
    im = {}
    im["identf"] = np.eye(128, dtype=np.float32)
    im["qT"] = stD[core]["qTd1"]
    im["kT_all"] = np.concatenate(
        [stD[j]["kTd1"] for j in range(b0, b0 + 4)], 1)
    im["v_aug_all"] = np.concatenate(
        [stD[j]["v_aug"] for j in range(b0, b0 + 4)], 0)
    im["ident"] = np.eye(128, dtype=np.float32).astype(BF)
    im["onesf"] = np.ones((128, 128), np.float32)
    return im




def build_launch_F():
    nc = bass.Bass("TRN2", target_bir_lowering=False, debug=False)
    ap = {}

    def din(name, shape, dt=F32):
        ap[name] = nc.dram_tensor(name, shape, dt, kind="ExternalInput").ap()

    def dout(name, shape, dt=F32):
        ap[name] = nc.dram_tensor(name, shape, dt, kind="ExternalOutput").ap()

    din("x2", [TPC, D])
    din("attT", [D, TPC], BF16)
    din("WoT1", [D, D])
    din("bo_b", [128, D])
    din("Wr", [D, NE1])
    din("ident", [128, 128], BF16)
    din("identf", [128, 128])
    dout("x3", [TPC, D])
    dout("xn4", [TPC, D], BF16)
    dout("G_em", [NE1, TPC])
    dout("M1_em", [NE1, TPC])
    dout("M2_em", [NE1, TPC])

    ctx = ExitStack()
    with ctx:
        kb = KB(nc, ctx)
        ldr_ = kb.ring("ld", 4)
        wlr = kb.ring("wl", 2)
        str_ = kb.ring("st", 4)
        v, t, s = kb.e["vector"], kb.e["tensor"], kb.e["scalar"]

        ident = ctx.enter_context(nc.sbuf_tensor("identB", [128, 128], BF16))
        itok = kb.dma("sync", ident[:], ap["ident"], ldr_)
        identf = ctx.enter_context(nc.sbuf_tensor("identF", [128, 128], F32))
        iftok = kb.dma("sync", identf[:], ap["identf"], ldr_)
        epsc = ctx.enter_context(nc.sbuf_tensor("epsc", [128, 1], F32))
        kb.op("vector", "memset", epsc[:], EPS)

        p = Pool(nc)
        wbuf = p.sb("wbuf", [128, 8 * 1024], BF16)
        wrb = p.sb("wrb", [128, 8 * NE1], BF16)
        bob = p.sb("bob", [128, D], F32)
        atile = p.sb("atile", [128, 8 * 128], BF16)
        x_a = p.sb("x_a", [128, D], F32)
        x_b = p.sb("x_b", [128, D], F32)
        x3t = p.sb("x3t", [128, D], F32)
        xn_tile = p.sb("xn_tile", [128, D], BF16)
        red = p.sb("red", [128, 8], F32)
        xs_sc = p.sb("xs_sc", [128, D], F32)
        xnTt = p.sb("xnTt", [128, 8 * 128], BF16)
        lg = p.sb("lg", [128, NE1], F32)
        ex = p.sb("ex", [128, NE1], F32)
        pb = p.sb("pb", [128, NE1], F32)
        t8 = p.sb("t8", [128, 8], F32)
        sc1 = p.sb("sc1", [128, 6 * NE1], F32)
        Gm = p.sb("Gm", [NE1, TPC], F32)
        M1m = p.sb("M1m", [NE1, TPC], F32)
        M2m = p.sb("M2m", [NE1, TPC], F32)
        pj2 = [p.ps(f"pj2{i}", [128, 512]) for i in range(2)]
        pT = p.ps("pT", [128, 128], BF16)
        pr = p.ps("pr", [128, NE1])
        pG = p.ps("pG", [128, 128])

        w3 = ap["WoT1"].rearrange("(c p) o -> p c o", p=128)
        wotok = kb.dma("gpsimd", wbuf[:].rearrange("p (c o) -> p c o", c=8),
                       w3, wlr)
        wr3 = ap["Wr"].rearrange("(c p) e -> p c e", p=128)
        wrtok = kb.dma("gpsimd", wrb[:].rearrange("p (c e) -> p c e", c=8),
                       wr3, wlr)
        botok = kb.dma("sync", bob[:], ap["bo_b"], ldr_)

        a3 = fm_dram(ap["attT"])
        for i in range(NTT):
            xb = x_a if i % 2 == 0 else x_b
            xtok = kb.dma("sync", xb[:], ap["x2"][i * 128:(i + 1) * 128, :], ldr_,
                          wait=[v] if i >= 2 else [])
            atok = kb.dma("sync", atile[:].rearrange("p (c w) -> p c w", c=8),
                          a3[:, :, i * 128:(i + 1) * 128], ldr_, wait=[t])
            for og2 in range(2):
                pp = pj2[og2]
                for c in range(8):
                    kb.op("tensor", "matmul", pp[:],
                          atile[:, c * 128:(c + 1) * 128],
                          wbuf[:, c * 1024 + og2 * 512: c * 1024 + (og2 + 1) * 512],
                          start=(c == 0), stop=(c == 7),
                          wait=[wotok, v, atok] if c == 0 else [])
                kb.op("vector", "scalar_tensor_tensor",
                      x3t[:, og2 * 512:(og2 + 1) * 512],
                      xb[:, og2 * 512:(og2 + 1) * 512], 0.0, pp[:],
                      ALU.bypass, ALU.add, wait=[t, xtok, s])
                kb.op("vector", "tensor_add", x3t[:, og2 * 512:(og2 + 1) * 512],
                      x3t[:, og2 * 512:(og2 + 1) * 512],
                      bob[:, og2 * 512:(og2 + 1) * 512], wait=[botok])
            kb.dma("scalar", ap["x3"][i * 128:(i + 1) * 128, :], x3t[:], str_,
                   wait=[v])
            emit_ln_tile(kb, x3t, xn_tile, red, xs_sc, [], epsc[:, 0:1])
            kb.dma("scalar", ap["xn4"][i * 128:(i + 1) * 128, :], xn_tile[:], str_,
                   wait=[v])
            emit_transpose_to_fm(kb, xn_tile, xnTt, 0, ident, pT, itok, tcols=128)
            for c in range(8):
                kb.op("tensor", "matmul", pr[:], xnTt[:, c * 128:(c + 1) * 128],
                      wrb[:, c * NE1:(c + 1) * NE1],
                      start=(c == 0), stop=(c == 7),
                      wait=[v, wrtok] if c == 0 else [])
            kb.op("vector", "tensor_copy", lg[:], pr[:], wait=[t])
            kb.op("vector", "reduce_max", red[:, 0:1], lg[:], AXX)
            kb.op("vector", "tensor_scalar_mul", red[:, 1:2], red[:, 0:1], -1.0)
            kb.op("scalar", "activation", ex[:], lg[:], AF.Exp,
                  bias=red[:, 1:2], accum_out=red[:, 2:3], wait=[v])
            kb.op("vector", "reciprocal", red[:, 3:4], red[:, 2:3], wait=[s])
            kb.op("vector", "tensor_scalar", pb[:], ex[:], red[:, 3:4], None,
                  ALU.mult)
            kb.op("vector", "max", t8[:], pb[:])
            eq1 = sc1[:, 0:8]; f1 = sc1[:, 8:16]; m1 = sc1[:, 16:24]
            cand = sc1[:, 24:32]; m2 = sc1[:, 32:40]; mor = sc1[:, 40:48]
            kb.op("vector", "tensor_scalar", eq1, pb[:], t8[:, 0:1], None,
                  ALU.is_equal)
            kb.op("vector", "tensor_tensor_scan", f1, eq1, eq1, 0.0,
                  ALU.add, ALU.bypass)
            kb.op("vector", "tensor_single_scalar", f1, f1, 1.0, ALU.is_equal)
            kb.op("vector", "tensor_mul", m1, eq1, f1)
            kb.op("vector", "tensor_scalar", eq1, pb[:], t8[:, 1:2], None,
                  ALU.is_equal)
            kb.op("vector", "tensor_scalar", cand, m1, -1.0, 1.0,
                  ALU.mult, ALU.add)
            kb.op("vector", "tensor_mul", cand, cand, eq1)
            kb.op("vector", "tensor_tensor_scan", f1, cand, cand, 0.0,
                  ALU.add, ALU.bypass)
            kb.op("vector", "tensor_single_scalar", f1, f1, 1.0, ALU.is_equal)
            kb.op("vector", "tensor_mul", m2, cand, f1)
            kb.op("vector", "tensor_add", mor, m1, m2)
            kb.op("vector", "tensor_mul", mor, mor, pb[:])
            kb.op("tensor", "transpose", pG[0:NE1, :], mor, identf[:],
                  wait=[v, iftok])
            kb.op("vector", "tensor_copy", Gm[0:NE1, i * 128:(i + 1) * 128],
                  pG[0:NE1, 0:128], wait=[t])
            kb.op("tensor", "transpose", pG[0:NE1, :], m1, identf[:],
                  wait=[v, iftok])
            kb.op("vector", "tensor_copy", M1m[0:NE1, i * 128:(i + 1) * 128],
                  pG[0:NE1, 0:128], wait=[t])
            kb.op("tensor", "transpose", pG[0:NE1, :], m2, identf[:],
                  wait=[v, iftok])
            kb.op("vector", "tensor_copy", M2m[0:NE1, i * 128:(i + 1) * 128],
                  pG[0:NE1, 0:128], wait=[t])
        kb.dma("scalar", ap["G_em"], Gm[:], str_, wait=[v])
        kb.dma("scalar", ap["M1_em"], M1m[:], str_, wait=[v])
        kb.dma("scalar", ap["M2_em"], M2m[:], str_, wait=[v])
        kb.build(final_rings=[ldr_, wlr, str_])
        p.close()
    return nc, ap


def prep_launch_F(inputs, core, stD, stE):
    d = inputs
    im = {}
    im["x2"] = stD[core]["x2"]
    im["attT"] = stE[core]["attT"]
    im["WoT1"] = np.ascontiguousarray(np.asarray(d["L1_attn_Wo"]).T)
    im["bo_b"] = np.broadcast_to(np.asarray(d["L1_attn_bo"]), (128, D)).copy()
    im["Wr"] = np.ascontiguousarray(np.asarray(d["L1_ffn_Wr"]))
    im["ident"] = np.eye(128, dtype=np.float32).astype(BF)
    im["identf"] = np.eye(128, dtype=np.float32)
    return im



def build_launch_G():
    nc = bass.Bass("TRN2", target_bir_lowering=False, debug=False)
    ap = {}

    def din(name, shape, dt=F32):
        ap[name] = nc.dram_tensor(name, shape, dt, kind="ExternalInput").ap()

    def dout(name, shape, dt=F32):
        ap[name] = nc.dram_tensor(name, shape, dt, kind="ExternalOutput").ap()

    NTOK = NCORE * TPC
    din("g_own", [128, 128])
    din("g_row", [1, NTOK])
    din("xn4_all", [NTOK, D], BF16)
    din("W1", [D, 4096])
    din("W2", [4096, D])
    din("ident", [128, 128], BF16)
    din("onesb", [128, 128], BF16)
    din("iota_t", [128, 128], I32)
    din("trashg", [128, 128])
    dout("expert_out", [PAD1, D])
    dout("theta", [1, 1])

    invtab = nc.dram_tensor("invtabG", [21632, 1], I32, kind="Internal").ap()
    rowb = nc.dram_tensor("rowbG", [NTOK], F32, kind="Internal").ap()

    ctx = ExitStack()
    with ctx:
        kb = KB(nc, ctx)
        ldr_ = kb.ring("ld", 4)
        wlr = kb.ring("wl", 2)
        str_ = kb.ring("st", 4)
        gr = kb.ring("gq", 4)
        v, t, s, g = kb.e["vector"], kb.e["tensor"], kb.e["scalar"], kb.e["gpsimd"]

        ident = ctx.enter_context(nc.sbuf_tensor("identB", [128, 128], BF16))
        itok = kb.dma("sync", ident[:], ap["ident"], ldr_)
        onesb = ctx.enter_context(nc.sbuf_tensor("onesB", [128, 128], BF16))
        obtok = kb.dma("sync", onesb[:], ap["onesb"], ldr_)

        p1 = Pool(nc)
        go = p1.sb("go", [128, 128], F32)
        mk = p1.sb("mk", [128, 128], F32)
        bi = p1.sb("bi", [128, 8], F32)
        pcb = p1.sb("pcb", [128, 1], BF16)
        selb = p1.sb("selb", [1, 1], BF16)
        selm = p1.sb("selm", [128, 1], I32)
        pcnt = p1.ps("pcnt", [1, 1])
        pbc2 = p1.ps("pbc2", [128, 1])

        gotok = kb.dma("sync", go[:], ap["g_own"], ldr_)
        kb.op("vector", "memset", bi[:, 0:1], 0.0)
        kb.op("vector", "memset", bi[:, 1:2], 1.0)
        kb.op("vector", "memset", bi[:, 2:3], 0.5)
        for it in range(45):
            kb.op("vector", "tensor_scalar", mk[:], go[:], bi[:, 2:3], None,
                  ALU.is_gt, ALU.add, accum_out=bi[:, 4:5], wait=[gotok])
            kb.op("vector", "tensor_copy", pcb[:], bi[:, 4:5])
            kb.op("tensor", "matmul", pcnt[:], pcb[:], onesb[:, 0:1],
                  start=True, stop=True, wait=[v, obtok])
            kb.op("vector", "tensor_single_scalar", bi[0:1, 5:6], pcnt[0:1, 0:1],
                  float(CAP1) + 0.5, ALU.is_le, wait=[t])
            kb.op("vector", "tensor_copy", selb[:], bi[0:1, 5:6])
            kb.op("tensor", "matmul", pbc2[:], onesb[0:1, :], selb[:],
                  start=True, stop=True, wait=[v])
            kb.op("vector", "tensor_copy", selm[:], pbc2[:, 0:1], wait=[t])
            kb.op("vector", "select", bi[:, 7:8], selm[:], bi[:, 2:3],
                  bi[:, 1:2])
            kb.op("vector", "select", bi[:, 0:1], selm[:], bi[:, 0:1],
                  bi[:, 2:3])
            kb.op("vector", "tensor_copy", bi[:, 1:2], bi[:, 7:8])
            kb.op("vector", "tensor_add", bi[:, 2:3], bi[:, 0:1], bi[:, 1:2])
            kb.op("vector", "tensor_scalar_mul", bi[:, 2:3], bi[:, 2:3], 0.5)
        kb.dma("scalar", ap["theta"], bi[0:1, 1:2], str_, wait=[v])

        grow = p1.sb("grow", [1, NTOK], F32)
        mrow = p1.sb("mrow", [1, NTOK], F32)
        crow = p1.sb("crow", [1, NTOK], F32)
        rtok = kb.dma("sync", grow[:], ap["g_row"], ldr_)
        kb.op("vector", "tensor_scalar", mrow[:], grow[:], bi[0:1, 1:2], None,
              ALU.is_gt, wait=[rtok])
        kb.op("vector", "tensor_tensor_scan", crow[:], mrow[:], mrow[:], 0.0,
              ALU.add, ALU.bypass)
        kb.op("vector", "tensor_sub", crow[:], crow[:], mrow[:])
        kb.op("vector", "tensor_scalar", mrow[:], mrow[:], -1e6, 1e6,
              ALU.mult, ALU.add)
        kb.op("vector", "tensor_add", crow[:], crow[:], mrow[:])
        rbtok = kb.dma("scalar", rowb.rearrange("(a x) -> a x", a=1), crow[:],
                       str_, wait=[v])

        iota = p1.sb("iotaG", [128, 128], I32)
        zs = p1.sb("zsG", [128, 41], I32)
        idxf = p1.sb("idxf", [128, 128], F32)
        negi = p1.sb("negiG", [128, 128], F32)
        idxi = p1.sb("idxiG", [128, 128], I32)
        trashg = p1.sb("trashg_s", [128, 128], F32)
        ittok = kb.dma("sync", iota[:], ap["iota_t"], ldr_)
        tgt = kb.dma("sync", trashg[:], ap["trashg"], ldr_)
        kb.op("gpsimd", "memset", zs[:], 0)
        ztok = kb.dma("gpsimd",
                      invtab[0:5248, :].rearrange("(p k) one -> p (k one)", p=128),
                      zs[:], gr, wait=[g])
        ixtok = kb.dma("sync", idxf[:], rowb.rearrange("(a b) -> a b", a=128),
                       ldr_, wait=[rbtok])
        kb.op("vector", "tensor_single_scalar", negi[:], idxf[:], 5120.5,
              ALU.is_lt, wait=[ixtok])
        kb.op("vector", "tensor_mul", idxf[:], idxf[:], negi[:])
        kb.op("vector", "tensor_scalar", negi[:], negi[:], -1.0, 1.0,
              ALU.mult, ALU.add)
        kb.op("vector", "tensor_mul", negi[:], negi[:], trashg[:], wait=[tgt])
        kb.op("vector", "tensor_add", idxf[:], idxf[:], negi[:])
        kb.op("vector", "tensor_copy", idxi[:], idxf[:])
        vsc = (v, v.n)
        scr = kb.ring("scrG", 8)
        for tt_ in range(128):
            kb.idma("gpsimd", invtab,
                    bass.IndirectOffsetOnAxis(ap=idxi[:, tt_:tt_ + 1], axis=0),
                    iota[:, tt_:tt_ + 1], None, ring=scr,
                    wait=[vsc, ittok, ztok] if tt_ == 0 else [])
        scb_tok = scr.final_tokens()
        p1.close()

        fence_w = kb.fence() + gr.final_tokens() + ldr_.final_tokens() \
            + str_.final_tokens()
        fence = fence_w + scb_tok
        p2 = Pool(nc)
        w1b = p2.sb("w1b", [128, 8 * 4096], BF16)
        w2b = p2.sb("w2b", [128, 32 * D], BF16)
        wstage = [p2.sb(f"wst{i}", [128, 2048], F32) for i in range(2)]
        bufs = dict(
            idxc=[p2.sb(f"idxc{i}", [128, 1], I32) for i in range(4)],
            xg=[p2.sb(f"xg{i}", [128, D], BF16) for i in range(4)],
            xgT=p2.sb("xgT", [128, 8 * 512], BF16),
            hT=p2.sb("hT", [128, 32 * 512], BF16),
            eot=p2.sb("eot", [128, 2 * 512], F32),
            pT2=[p2.ps(f"pT2{i}", [128, 128], BF16) for i in range(2)],
            ps1=[p2.ps(f"ps1{i}", [128, 512]) for i in range(2)],
            ps2=[p2.ps(f"ps2{i}", [128, 512]) for i in range(2)],
            ident=ident, itok=itok,
        )
        zr = p2.sb("zr", [1, D], F32)
        kb.op("vector", "memset", zr[:], 0.0, wait=fence)
        kb.dma("scalar", ap["expert_out"][PAD1 - 1:PAD1, :], zr[:], str_,
               wait=[v])
        w1tok = emit_wload_cast(kb, ap["W1"], w1b, wstage, ldr_, 4096, fence)
        w2tok = emit_wload_cast(kb, ap["W2"], w2b, wstage, ldr_, D, fence)
        blocks = [(b * 512, 4) for b in range(10)]
        emit_ffn_blocks(kb, ap["xn4_all"], invtab, ap["expert_out"],
                        w1b, w2b, bufs, AF.Relu, blocks,
                        w1tok, w2tok, fence, ldr_, gr, str_)
        kb.build(final_rings=[ldr_, wlr, str_, gr])
        p2.close()
    return nc, ap


def prep_launch_G(inputs, core, stF):
    d = inputs
    im = {}
    G_all = np.concatenate([stF[j]["G_em"] for j in range(NCORE)], 1)
    im["g_own"] = np.ascontiguousarray(G_all[core].reshape(128, 128))
    im["g_row"] = np.ascontiguousarray(G_all[core].reshape(1, -1))
    im["xn4_all"] = np.concatenate([stF[j]["xn4"] for j in range(NCORE)], 0)
    im["W1"] = np.ascontiguousarray(np.asarray(d["L1_ffn_W1"][core]))
    im["W2"] = np.ascontiguousarray(np.asarray(d["L1_ffn_W2"][core]))
    im["ident"] = np.eye(128, dtype=np.float32).astype(BF)
    im["onesb"] = np.ones((128, 128), np.float32).astype(BF)
    p = np.arange(128)
    im["iota_t"] = (p[:, None] * 128 + np.arange(128)[None, :]).astype(np.int32)
    im["trashg"] = (5152 + np.arange(128)[:, None] * 128
                    + np.arange(128)[None, :]).astype(np.float32)
    return im



def build_launch_Ha():
    nc = bass.Bass("TRN2", target_bir_lowering=False, debug=False)
    ap = {}

    def din(name, shape, dt=F32):
        ap[name] = nc.dram_tensor(name, shape, dt, kind="ExternalInput").ap()

    def dout(name, shape, dt=F32):
        ap[name] = nc.dram_tensor(name, shape, dt, kind="ExternalOutput").ap()

    NTOK = NCORE * TPC
    din("G_all", [NE1, NTOK])
    din("M1_all", [NE1, NTOK])
    din("M2_all", [NE1, NTOK])
    din("theta_all", [NE1, 1])
    din("eoff8", [NE1, 1])
    dout("slota_row", [1, NTOK]); dout("wa_row", [1, NTOK])
    dout("slotb_row", [1, NTOK]); dout("wb_row", [1, NTOK])

    SEG = 4096

    ctx = ExitStack()
    with ctx:
        kb = KB(nc, ctx)
        ldr_ = kb.ring("ld", 6)
        str_ = kb.ring("st", 4)
        v, t, s = kb.e["vector"], kb.e["tensor"], kb.e["scalar"]

        p = Pool(nc)
        Gs = p.sb("Gs", [NE1, SEG], F32)
        M1s = p.sb("M1s", [NE1, SEG], F32)
        M2s = p.sb("M2s", [NE1, SEG], F32)
        Mk = p.sb("Mk", [NE1, SEG], F32)
        cum = p.sb("cum", [NE1, SEG], F32)
        wsel = p.sb("wsel", [NE1, SEG], F32)
        tmp = p.sb("tmp", [NE1, SEG], F32)
        thc = p.sb("thc", [NE1, 1], F32)
        eoffc = p.sb("eoffc8", [NE1, 1], F32)
        carry = p.sb("carry", [NE1, 2], F32)
        onesf = p.sb("onesf_h", [NE1, 1], F32)
        stg = p.sb("stg", [1, 1024], F32)
        pcs = p.ps("pcs", [1, 512])

        ttok = kb.dma("sync", thc[:], ap["theta_all"], ldr_)
        etok = kb.dma("sync", eoffc[:], ap["eoff8"], ldr_)
        kb.op("vector", "memset", onesf[:], 1.0)
        kb.op("vector", "memset", carry[:, 0:1], 0.0)
        st_tok = []

        def colsums(src_ap, out_dram, seg):
            nonlocal st_tok
            for cg in range(SEG // 512):
                kb.op("tensor", "matmul", pcs[:], onesf[:],
                      src_ap[0:NE1, cg * 512:(cg + 1) * 512],
                      start=True, stop=True, wait=[(v, v.n)])
                sl = (cg % 2) * 512
                kb.op("vector", "tensor_copy", stg[:, sl:sl + 512], pcs[:],
                      wait=[(t, kb.e["tensor"].n)] + st_tok)
                st_tok = [kb.dma(
                    "scalar",
                    out_dram[0:1, seg * SEG + cg * 512: seg * SEG + (cg + 1) * 512],
                    stg[:, sl:sl + 512], str_, wait=[v])]

        for seg in range(NTOK // SEG):
            gt = kb.dma("sync", Gs[:], ap["G_all"][:, seg * SEG:(seg + 1) * SEG],
                        ldr_, wait=[v, t])
            m1t = kb.dma("sync", M1s[:], ap["M1_all"][:, seg * SEG:(seg + 1) * SEG],
                         ldr_, wait=[v, t])
            m2t = kb.dma("sync", M2s[:], ap["M2_all"][:, seg * SEG:(seg + 1) * SEG],
                         ldr_, wait=[v, t])
            kb.op("vector", "tensor_scalar", Mk[:], Gs[:], thc[:, 0:1], None,
                  ALU.is_gt, wait=[gt, ttok])
            kb.op("vector", "tensor_tensor_scan", cum[:], Mk[:], Mk[:],
                  carry[:, 0:1], ALU.add, ALU.bypass)
            kb.op("vector", "tensor_copy", carry[:, 0:1],
                  cum[:, SEG - 1:SEG])
            kb.op("vector", "tensor_sub", cum[:], cum[:], Mk[:])
            kb.op("vector", "tensor_scalar", cum[:], cum[:], eoffc[:, 0:1],
                  None, ALU.add, wait=[etok])
            kb.op("vector", "tensor_mul", wsel[:], M1s[:], Mk[:], wait=[m1t])
            kb.op("vector", "tensor_mul", tmp[:], wsel[:], cum[:])
            colsums(tmp, ap["slota_row"], seg)
            kb.op("vector", "tensor_mul", tmp[:], wsel[:], Gs[:])
            colsums(tmp, ap["wa_row"], seg)
            kb.op("vector", "tensor_mul", wsel[:], M2s[:], Mk[:], wait=[m2t])
            kb.op("vector", "tensor_mul", tmp[:], wsel[:], cum[:])
            colsums(tmp, ap["slotb_row"], seg)
            kb.op("vector", "tensor_mul", tmp[:], wsel[:], Gs[:])
            colsums(tmp, ap["wb_row"], seg)
        kb.build(final_rings=[ldr_, str_])
        p.close()
    return nc, ap


def prep_launch_Ha(inputs, core, stF, stG):
    im = {}
    im["G_all"] = np.concatenate([stF[j]["G_em"] for j in range(NCORE)], 1)
    im["M1_all"] = np.concatenate([stF[j]["M1_em"] for j in range(NCORE)], 1)
    im["M2_all"] = np.concatenate([stF[j]["M2_em"] for j in range(NCORE)], 1)
    im["theta_all"] = np.concatenate([stG[j]["theta"] for j in range(NCORE)], 0)
    im["eoff8"] = (np.arange(NE1) * PAD1).astype(np.float32).reshape(NE1, 1)
    return im



def build_launch_Hb():
    nc = bass.Bass("TRN2", target_bir_lowering=False, debug=False)
    ap = {}

    def din(name, shape, dt=F32):
        ap[name] = nc.dram_tensor(name, shape, dt, kind="ExternalInput").ap()

    def dout(name, shape, dt=F32):
        ap[name] = nc.dram_tensor(name, shape, dt, kind="ExternalOutput").ap()

    din("x3", [TPC, D])
    din("expert_out_all", [NE1 * PAD1, D])
    din("slota", [128, NTT]); din("wa", [128, NTT])
    din("slotb", [128, NTT]); din("wb", [128, NTT])
    dout("out", [TPC, D])

    ctx = ExitStack()
    with ctx:
        kb = KB(nc, ctx)
        ldr_ = kb.ring("ld", 4)
        str_ = kb.ring("st", 4)
        gr = kb.ring("gq", 4)
        v, t, s, g = kb.e["vector"], kb.e["tensor"], kb.e["scalar"], kb.e["gpsimd"]

        p = Pool(nc)
        x_a = p.sb("x_a", [128, D], F32)
        x_b = p.sb("x_b", [128, D], F32)
        tabs = p.sb("tabs", [128, 4 * NTT], F32)
        idxa = p.sb("idxa", [128, 1], I32)
        idxb = p.sb("idxb", [128, 1], I32)
        ya = p.sb("ya", [128, D], F32)
        yb = p.sb("yb", [128, D], F32)
        acc = p.sb("acc", [128, D], F32)
        outt = p.sb("outt", [128, D], F32)

        ta = kb.dma("sync", tabs[:, 0:NTT], ap["slota"], ldr_)
        tb = kb.dma("sync", tabs[:, NTT:2 * NTT], ap["slotb"], ldr_)
        tc = kb.dma("sync", tabs[:, 2 * NTT:3 * NTT], ap["wa"], ldr_)
        td = kb.dma("sync", tabs[:, 3 * NTT:4 * NTT], ap["wb"], ldr_)
        ga_war = []
        for i in range(NTT):
            xb = x_a if i % 2 == 0 else x_b
            xtok = kb.dma("sync", xb[:], ap["x3"][i * 128:(i + 1) * 128, :], ldr_,
                          wait=[v] if i >= 2 else [])
            kb.op("vector", "tensor_copy", idxa[:], tabs[:, i:i + 1],
                  wait=[ta] + ga_war)
            kb.op("vector", "tensor_copy", idxb[:], tabs[:, NTT + i:NTT + i + 1],
                  wait=[tb])
            gt1 = kb.idma("gpsimd", ya[:], None, ap["expert_out_all"],
                          bass.IndirectOffsetOnAxis(ap=idxa[:], axis=0),
                          gr, wait=[v, s])
            gt2 = kb.idma("gpsimd", yb[:], None, ap["expert_out_all"],
                          bass.IndirectOffsetOnAxis(ap=idxb[:], axis=0),
                          gr, wait=[v, s])
            ga_war = [gt1, gt2]
            kb.op("vector", "tensor_scalar", acc[:], ya[:],
                  tabs[:, 2 * NTT + i:2 * NTT + i + 1], None, ALU.mult,
                  wait=[gt1, tc, s])
            kb.op("vector", "scalar_tensor_tensor", acc[:], yb[:],
                  tabs[:, 3 * NTT + i:3 * NTT + i + 1], acc[:],
                  ALU.mult, ALU.add, wait=[gt2, td])
            kb.op("vector", "tensor_add", outt[:], acc[:], xb[:],
                  wait=[xtok])
            kb.dma("scalar", ap["out"][i * 128:(i + 1) * 128, :], outt[:], str_,
                   wait=[v])
        kb.build(final_rings=[ldr_, str_, gr])
        p.close()
    return nc, ap


def prep_launch_Hb(inputs, core, stF2, stG, tabs):
    im = {}
    im["x3"] = stF2[core]["x3"]
    im["expert_out_all"] = np.concatenate(
        [stG[j]["expert_out"] for j in range(NCORE)], 0)
    for k2, row in (("slota", "slota_row"), ("wa", "wa_row"),
                    ("slotb", "slotb_row"), ("wb", "wb_row")):
        r = np.asarray(tabs[row]).reshape(-1)[core * TPC:(core + 1) * TPC]
        im[k2] = np.ascontiguousarray(r.reshape(NTT, 128).T)
    return im


GAMMA0 = (1.0 - np.exp2(-5.0 - np.arange(H0, dtype=np.float64)))


def prep_launch_B(inputs, core, stA):
    d = inputs
    im = {}
    im["x"] = np.ascontiguousarray(d["L0_attn_x"].reshape(-1, D)[core * TPC:(core + 1) * TPC])
    im["o_part"] = stA[core]["o_part"]
    im["qTd"] = stA[core]["qTd"]
    im["gTd"] = stA[core]["gTd"]
    im["P_all"] = np.concatenate([stA[j]["P"] for j in range(NCORE)], 0)
    coef = np.zeros((128, NCORE * H0), np.float32)
    b0 = (core // 4) * 4
    for j in range(b0, core):
        for h in range(H0):
            coef[:, j * H0 + h] = (GAMMA0[h] ** SC) ** (4 * (core - 1 - j))
    im["coefP"] = coef
    im["WoT"] = np.ascontiguousarray(np.asarray(d["L0_attn_Wo"]).T)
    im["gwT"] = np.ascontiguousarray(np.asarray(d["L0_ffn_gate_w"]).T)
    im["gnw"] = np.ascontiguousarray(np.asarray(d["L0_attn_gnorm_w"]).reshape(4, 128).T)
    tloc = np.arange(TPC, dtype=np.float32)
    cf = np.stack([g ** (tloc + 1) for g in GAMMA0.astype(np.float32)])
    im["crossfullT"] = np.repeat(cf[:, None, :], 128, axis=1).reshape(H0 * 128, TPC).astype(BF)
    im["ident"] = np.eye(128, dtype=np.float32).astype(BF)
    im["onesf"] = np.ones((128, 128), np.float32)
    return im


def kernel(**inputs):
    cores = list(range(NCORE))
    times = []

    def run(nc, ims):
        r = run_bass_kernel_spmd(nc, ims, core_ids=cores)
        times.append(r.exec_time_ns)
        try:
            with open("/tmp/klaunch_times.txt", "w") as f:
                f.write(repr(times))
        except OSError:
            pass
        return r.results

    ncA, _ = build_launch_A()
    stA = run(ncA, [prep_launch_A(inputs, c) for c in cores])
    ncB, _ = build_launch_B()
    stB = run(ncB, [prep_launch_B(inputs, c, stA) for c in cores])
    ncC, _ = build_launch_C()
    stC = run(ncC, [prep_launch_C(inputs, c, stB) for c in cores])
    ncD, _ = build_launch_D()
    stD = run(ncD, [prep_launch_D(inputs, c, stB, stC) for c in cores])
    ncE, _ = build_launch_E()
    stE = run(ncE, [prep_launch_E(inputs, c, stD) for c in cores])
    ncF, _ = build_launch_F()
    stF = run(ncF, [prep_launch_F(inputs, c, stD, stE) for c in cores])
    ncG, _ = build_launch_G()
    stG = run(ncG, [prep_launch_G(inputs, c, stF) for c in cores])
    ncHa, _ = build_launch_Ha()
    stHa = run(ncHa, [prep_launch_Ha(inputs, c, stF, stG) for c in cores])
    tabs = stHa[0]
    ncHb, _ = build_launch_Hb()
    stHb = run(ncHb, [prep_launch_Hb(inputs, c, stF, stG, tabs) for c in cores])
    out = np.concatenate([stHb[c]["out"] for c in cores], 0)
    return np.ascontiguousarray(out.reshape(2, 8192, D).astype(np.float32))


# version 10
# speedup vs baseline: 1.2125x; 1.1718x over previous
import numpy as np
import ml_dtypes
from contextlib import ExitStack

import concourse.bass as bass
import concourse.mybir as mybir
from concourse.bass_utils import run_bass_kernel_spmd

F32 = mybir.dt.float32
BF16 = mybir.dt.bfloat16
I32 = mybir.dt.int32
AF = mybir.ActivationFunctionType
ALU = mybir.AluOpType
AXX = mybir.AxisListType.X
BF = ml_dtypes.bfloat16

NCORE = 8
TPC = 2048
NTT = TPC // 128
D = 1024
SC = 512
NSC = TPC // SC
H0, HK0, HV0 = 4, 256, 512
H1, DH1 = 16, 64
NE0, CAP0, PAD0 = 16, 1280, 1281
NE1, CAP1, PAD1 = 8, 5120, 5121
EPS = 1e-5

ENGINES = ("sync", "vector", "tensor", "scalar", "gpsimd")


class Ctr:
    __slots__ = ("h", "n", "name")

    def __init__(self, h, name):
        self.h = h
        self.n = 0
        self.name = name


class Ring:
    def __init__(self, ctrs):
        self.ctrs = ctrs
        self.i = 0

    def next(self):
        c = self.ctrs[self.i]
        self.i = (self.i + 1) % len(self.ctrs)
        return c

    def final_tokens(self):
        return [(c, c.n) for c in self.ctrs if c.n > 0]


class KB:

    def __init__(self, nc, ctx):
        self.nc = nc
        self.ctx = ctx
        self.ops = {e: [] for e in ENGINES}
        self.e = {}
        for e in ENGINES:
            self.e[e] = self.sem("c_" + e)

    def sem(self, name):
        return Ctr(self.ctx.enter_context(self.nc.semaphore(name)), name)

    def ring(self, name, n):
        return Ring([self.sem(f"{name}{i}") for i in range(n)])

    def op(self, eng, meth, *args, wait=(), inc="auto", incn=1, **kw):
        wl = list(wait)
        if eng in ("vector", "scalar", "gpsimd") and meth not in (
                "dma_start", "indirect_dma_start"):
            wl.append((self.e[eng], self.e[eng].n))
        best = {}
        for w in wl:
            if isinstance(w, tuple):
                c, val = w
            else:
                c, val = w, w.n
            if val > 0 and val > best.get(id(c), (None, -1))[1]:
                best[id(c)] = (c.h, val)
        waits = list(best.values())
        if inc == "auto":
            inc = self.e[eng]
        if inc is not None:
            inc.n += incn
            inc_spec = (inc.h, incn)
        else:
            inc_spec = None
        self.ops[eng].append((waits, meth, args, kw, inc_spec))

    def dma(self, eng, out, in_, ring, wait=(), **kw):
        ctr = ring.next()
        wl = list(wait)
        if ctr.n > 0:
            wl.append((ctr, ctr.n))
        self.op(eng, "dma_start", wait=wl, inc=ctr, incn=16, out=out, in_=in_, **kw)
        return (ctr, ctr.n)

    def idma(self, eng, out, out_off, in_, in_off, ring=None, wait=(), ctr=None,
             **kw):
        wl = list(wait)
        if ring is not None:
            ctr = ring.next()
            if ctr.n > 0:
                wl.append((ctr, ctr.n))
        self.op(eng, "indirect_dma_start", out, out_off, in_, in_off,
                wait=wl, inc=ctr, incn=16, **kw)
        return (ctr, ctr.n)

    def fence(self):
        return [(c, c.n) for c in self.e.values() if c.n > 0]

    def build(self, final_rings=()):
        waits = []
        for r in final_rings:
            waits += r.final_tokens()
        waits += self.fence()
        self.op("sync", "nop", wait=waits, inc=None)
        with self.nc.Block() as block:
            for eng in ENGINES:
                ops = self.ops[eng]
                if not ops:
                    continue

                def body(q, ops=ops):
                    for waits, meth, args, kw, inc_spec in ops:
                        for h, v in waits:
                            q.wait_ge(h, v)
                        ins = getattr(q, meth)(*args, **kw)
                        if inc_spec is not None:
                            ins.then_inc(inc_spec[0], inc_spec[1])

                getattr(block, eng)(body)


def fm_dram(ap):
    return ap.rearrange("(c p) t -> p c t", p=128)


class Pool:

    def __init__(self, nc):
        self.nc = nc
        self.es = ExitStack()

    def sb(self, name, shape, dt=F32):
        return self.es.enter_context(self.nc.sbuf_tensor(name, shape, dt))

    def ps(self, name, shape, dt=F32):
        return self.es.enter_context(self.nc.psum_tensor(name, shape, dt))

    def close(self):
        self.es.close()



def emit_ln_tile(kb, xb, xn_tile, red, xs_sc, wait_x, epsc, war=()):
    v, s = kb.e["vector"], kb.e["scalar"]
    kb.op("vector", "reduce_sum", red[:, 0:1], xb[:], AXX, wait=list(wait_x) + [s])
    kb.op("vector", "tensor_scalar_mul", red[:, 1:2], red[:, 0:1], -1.0 / D)
    kb.op("scalar", "activation", xs_sc[:], xb[:], AF.Square,
          accum_out=red[:, 2:3], wait=list(wait_x) + [v])
    kb.op("vector", "tensor_mul", red[:, 3:4], red[:, 1:2], red[:, 1:2], wait=[s])
    kb.op("vector", "tensor_scalar", red[:, 4:5], red[:, 2:3], 1.0 / D,
          red[:, 3:4], ALU.mult, ALU.subtract)
    kb.op("scalar", "activation", red[:, 5:6], red[:, 4:5], AF.Ln, bias=epsc, wait=[v])
    kb.op("scalar", "activation", red[:, 6:7], red[:, 5:6], AF.Exp, scale=-0.5)
    kb.op("vector", "tensor_scalar", xn_tile[:], xb[:], red[:, 1:2],
          red[:, 6:7], ALU.add, ALU.mult, wait=[s, kb.e["tensor"]] + list(war))


def emit_transpose_to_fm(kb, src_tile, dstT, tile_idx, ident, pT, itok,
                         nchunks=8, tcols=TPC, war=None):
    v, t = kb.e["vector"], kb.e["tensor"]
    banks = pT if isinstance(pT, list) else [pT]
    if war is None:
        war = {}
    for c in range(nchunks):
        b = c % len(banks)
        bw = [(v, v.n)] if c < len(banks) else war.get(b, [(v, v.n)])
        kb.op("tensor", "transpose", banks[b][:],
              src_tile[:, c * 128:(c + 1) * 128],
              ident[:], wait=[itok] + bw)
        kb.op("vector", "tensor_copy",
              dstT[:, c * tcols + tile_idx * 128: c * tcols + (tile_idx + 1) * 128],
              banks[b][:], wait=[(t, t.n)])
        war[b] = [(v, v.n)]



def build_launch_A():
    nc = bass.Bass("TRN2", target_bir_lowering=False, debug=False)
    ap = {}

    def din(name, shape, dt=F32):
        ap[name] = nc.dram_tensor(name, shape, dt, kind="ExternalInput").ap()

    def dout(name, shape, dt=F32):
        ap[name] = nc.dram_tensor(name, shape, dt, kind="ExternalOutput").ap()

    din("x", [TPC, D])
    din("WqT", [D, D]); din("WkT", [D, D])
    din("WvT", [D, 2048]); din("WgT", [D, 2048])
    din("ident", [128, 128], BF16)
    din("r0_cos", [128, TPC]); din("r0_sin", [128, TPC])
    din("r0_cos_q", [128, TPC]); din("r0_sin_q", [128, TPC])
    din("maskT", [H0 * SC, SC], BF16)
    din("kdec", [128, H0 * 4])
    din("crossT", [H0 * 128, SC], BF16)
    dout("qTd", [D, TPC], BF16)
    dout("kTd", [D, TPC], BF16)
    dout("vd", [TPC, 2048], BF16)
    dout("gTd", [2048, TPC], BF16)
    dout("o_part", [2048, TPC], BF16)
    dout("P", [256, H0 * HV0])

    gamma = (1.0 - np.exp2(-5.0 - np.arange(H0, dtype=np.float64)))
    gSC = [float(g) ** SC for g in gamma]

    ctx = ExitStack()
    with ctx:
        kb = KB(nc, ctx)
        ldr_ = kb.ring("ld", 4)
        wlr = kb.ring("wl", 2)
        str_ = kb.ring("st", 4)
        v, t, s = kb.e["vector"], kb.e["tensor"], kb.e["scalar"]

        ident = ctx.enter_context(nc.sbuf_tensor("identB", [128, 128], BF16))
        itok = kb.dma("sync", ident[:], ap["ident"], ldr_)
        S_f = ctx.enter_context(nc.sbuf_tensor("S_f", [128, H0 * 2 * 512], F32))
        S_b = ctx.enter_context(nc.sbuf_tensor("S_b", [128, 2 * 512], BF16))

        p12 = Pool(nc)
        xnT = p12.sb("xnT", [128, 8 * TPC], BF16)
        wbuf = p12.sb("wbuf", [128, 8 * 2048], BF16)
        ph1 = p12.sb("ph1", [128, 4 * 512], F32)
        x_a = p12.sb("x_a", [128, D], F32)
        x_b = p12.sb("x_b", [128, D], F32)
        xs_sc = p12.sb("xs_sc", [128, D], F32)
        red = p12.sb("red", [128, 8], F32)
        epsc = p12.sb("epsc", [128, 1], F32)
        xn_tile = p12.sb("xn_tile", [128, D], BF16)
        rc = p12.sb("rc", [128, TPC], F32)
        rs = p12.sb("rs", [128, TPC], F32)
        evb = p12.sb("evb", [128, 4 * 512], BF16)
        pT = [p12.ps(f"pT{i}", [128, 128], BF16) for i in range(2)]
        pj = [p12.ps(f"pj{i}", [128, 512], F32) for i in range(4)]
        kb.op("vector", "memset", epsc[:], EPS)

        twar = {}
        for i in range(NTT):
            xb = x_a if i % 2 == 0 else x_b
            xtok = kb.dma("sync", xb[:], ap["x"][i * 128:(i + 1) * 128, :], ldr_,
                          wait=[v] if i >= 2 else [])
            emit_ln_tile(kb, xb, xn_tile, red, xs_sc, [xtok], epsc[:, 0:1])
            emit_transpose_to_fm(kb, xn_tile, xnT, i, ident, pT, itok, war=twar)

        def load_w(name, ncols):
            w3 = ap[name].rearrange("(c p) o -> p c o", p=128)
            dst = wbuf[:].rearrange("p (c o) -> p c o", c=8)[:, :, 0:ncols]
            return kb.dma("gpsimd", dst, w3, wlr, wait=[t])

        def wtile(c, o):
            return wbuf[:, c * 2048 + o * 128: c * 2048 + (o + 1) * 128]

        ev_tok = {}

        def proj_pair_rope(ncols, cos_ap, sin_ap, out_dram, wtok):
            ctok = kb.dma("sync", rc[:], cos_ap, ldr_, wait=[v])
            stok = kb.dma("sync", rs[:], sin_ap, ldr_, wait=[v])
            for pair in range(ncols // 256):
                for cg in range(4):
                    pa, pb = pj[2 * (cg % 2)], pj[2 * (cg % 2) + 1]
                    bw = rope_war.get(cg % 2, [(v, v.n)])
                    for (oc, pp) in ((2 * pair, pa), (2 * pair + 1, pb)):
                        for c in range(8):
                            kb.op("tensor", "matmul", pp[:], wtile(c, oc),
                                  xnT[:, c * TPC + cg * 512: c * TPC + (cg + 1) * 512],
                                  start=(c == 0), stop=(c == 7),
                                  wait=[wtok] + bw if c == 0 else [])
                    cs = rc[:, cg * 512:(cg + 1) * 512]
                    sn = rs[:, cg * 512:(cg + 1) * 512]
                    t1, t2 = ph1[:, 0:512], ph1[:, 512:1024]
                    t3, t4 = ph1[:, 1024:1536], ph1[:, 1536:2048]
                    kb.op("vector", "tensor_mul", t1, pa[:], cs, wait=[t, ctok, stok])
                    kb.op("vector", "tensor_mul", t2, pb[:], sn)
                    kb.op("vector", "tensor_mul", t3, pa[:], sn)
                    kb.op("vector", "tensor_mul", t4, pb[:], cs)
                    sl = (cg % 2) * 1024
                    ea, eb = evb[:, sl:sl + 512], evb[:, sl + 512:sl + 1024]
                    kb.op("vector", "tensor_sub", ea, t1, t2,
                          wait=ev_tok.get(sl, []))
                    kb.op("vector", "tensor_add", eb, t4, t3,
                          wait=ev_tok.get(sl + 512, []))
                    d3 = fm_dram(out_dram)
                    ta = kb.dma("scalar", d3[:, 2 * pair, cg * 512:(cg + 1) * 512],
                                ea, str_, wait=[v])
                    tb = kb.dma("scalar",
                                d3[:, 2 * pair + 1, cg * 512:(cg + 1) * 512],
                                eb, str_, wait=[v])
                    ev_tok[sl] = [ta]
                    ev_tok[sl + 512] = [tb]
                    rope_war[cg % 2] = [(v, v.n)]

        rope_war = {}
        wtok = load_w("WqT", 1024)
        proj_pair_rope(1024, ap["r0_cos_q"], ap["r0_sin_q"], ap["qTd"], wtok)
        wtok = load_w("WkT", 1024)
        proj_pair_rope(1024, ap["r0_cos"], ap["r0_sin"], ap["kTd"], wtok)

        wtok = load_w("WvT", 2048)
        pj_war = {}
        for i in range(NTT):
            for og in range(4):
                pp = pj[og]
                for c in range(8):
                    kb.op("tensor", "matmul", pp[:],
                          xnT[:, c * TPC + i * 128: c * TPC + (i + 1) * 128],
                          wbuf[:, c * 2048 + og * 512: c * 2048 + (og + 1) * 512],
                          start=(c == 0), stop=(c == 7),
                          wait=[wtok] + pj_war.get(og, [(v, v.n)])
                          if c == 0 else [])
                sl = (og % 2) * 512
                ea = evb[:, sl:sl + 512]
                war = ev_tok.get(sl, [])
                kb.op("vector", "tensor_copy", ea, pp[:], wait=[(t, t.n)] + war)
                pj_war[og] = [(v, v.n)]
                ev_tok[sl] = [kb.dma(
                    "scalar", ap["vd"][i * 128:(i + 1) * 128, og * 512:(og + 1) * 512],
                    ea, str_, wait=[v])]

        wtok = load_w("WgT", 2048)
        pj_war = {}
        for oc in range(16):
            for cg in range(4):
                pp = pj[cg % 4]
                for c in range(8):
                    kb.op("tensor", "matmul", pp[:], wtile(c, oc),
                          xnT[:, c * TPC + cg * 512: c * TPC + (cg + 1) * 512],
                          start=(c == 0), stop=(c == 7),
                          wait=[wtok] + pj_war.get(cg % 4, [(v, v.n)])
                          if c == 0 else [])
                sl = (cg % 2) * 512
                ea = evb[:, sl:sl + 512]
                war = ev_tok.get(sl, [])
                kb.op("vector", "tensor_copy", ea, pp[:], wait=[(t, t.n)] + war)
                pj_war[cg % 4] = [(v, v.n)]
                d3 = fm_dram(ap["gTd"])
                ev_tok[sl] = [kb.dma("scalar", d3[:, oc, cg * 512:(cg + 1) * 512],
                                     ea, str_, wait=[v])]

        p12.close()

        fence = kb.fence() + str_.final_tokens() + ldr_.final_tokens() \
            + wlr.final_tokens()
        p3 = Pool(nc)
        qsb_d = p3.sb("qsb", [128, 2 * 2 * 512], BF16)
        ksb_d = p3.sb("ksb", [128, 2 * 2 * 512], BF16)
        vsb_d = p3.sb("vsb", [128, 2 * 4 * 512], BF16)
        khat_d = p3.sb("khat", [128, 2 * 4 * 256], BF16)
        qhat_d = p3.sb("qhat", [128, 2 * 2 * 512], BF16)
        attnT_d = p3.sb("attnT", [128, 2 * 4 * 512], BF16)
        maskb = p3.sb("maskb", [128, H0 * 4 * 512], BF16)
        crossb = p3.sb("crossb", [128, H0 * 512], BF16)
        kdecb = p3.sb("kdecb", [128, H0 * 4], F32)
        oev = p3.sb("oev", [128, 4 * 512], BF16)
        pTr = [p3.ps(f"pTr{i}", [128, 128], BF16) for i in range(2)]
        psc = [p3.ps(f"psc{i}", [128, 512], F32) for i in range(2)]
        po = [p3.ps(f"po{i}", [128, 512], F32) for i in range(2)]
        pA = [p3.ps(f"pA{i}", [128, 512], F32) for i in range(2)]

        mk3 = ap["maskT"].rearrange("(h J p) i -> p (h J) i", p=128, h=H0)
        mtok = kb.dma("sync", maskb[:].rearrange("p (x i) -> p x i", i=512), mk3,
                      ldr_, wait=fence)
        cr3 = ap["crossT"].rearrange("(h p) i -> p h i", p=128)
        crtok = kb.dma("sync", crossb[:].rearrange("p (h i) -> p h i", h=H0), cr3,
                       ldr_, wait=fence)
        kdtok = kb.dma("sync", kdecb[:], ap["kdec"], ldr_, wait=fence)
        kb.op("vector", "memset", S_f[:], 0.0, wait=fence)

        q3 = fm_dram(ap["qTd"])
        k3 = fm_dram(ap["kTd"])
        ost = {}
        tr_war = {}
        sc_war = {}
        po_war = {}
        pA_war = {}
        iter_tok = {}
        for sc in range(NSC):
            for h in range(H0):
                it = sc * H0 + h
                bsel = it % 2
                qsb = qsb_d[:, bsel * 1024:(bsel + 1) * 1024]
                ksb = ksb_d[:, bsel * 1024:(bsel + 1) * 1024]
                vsb = vsb_d[:, bsel * 2048:(bsel + 1) * 2048]
                khat = khat_d[:, bsel * 1024:(bsel + 1) * 1024]
                qhat = qhat_d[:, bsel * 1024:(bsel + 1) * 1024]
                attnT = attnT_d[:, bsel * 2048:(bsel + 1) * 2048]
                lwar = iter_tok.get(bsel, [])
                toks = []
                for c in range(2):
                    toks.append(kb.dma("sync", qsb[:, c * 512:(c + 1) * 512],
                                       q3[:, h * 2 + c, sc * 512:(sc + 1) * 512],
                                       ldr_, wait=lwar))
                    toks.append(kb.dma("sync", ksb[:, c * 512:(c + 1) * 512],
                                       k3[:, h * 2 + c, sc * 512:(sc + 1) * 512],
                                       ldr_, wait=lwar))
                for jt in range(4):
                    toks.append(kb.dma("sync", vsb[:, jt * 512:(jt + 1) * 512],
                                       ap["vd"][sc * 512 + jt * 128:
                                                sc * 512 + (jt + 1) * 128,
                                                h * 512:(h + 1) * 512],
                                       ldr_, wait=lwar))

                for jt in range(4):
                    for c in range(2):
                        b = (jt * 2 + c) % 2
                        kb.op("tensor", "transpose", pTr[b][:],
                              ksb[:, c * 512 + jt * 128: c * 512 + (jt + 1) * 128],
                              ident[:],
                              wait=toks + [itok] + tr_war.get(b, [(v, v.n)]))
                        kb.op("vector", "tensor_scalar_mul",
                              khat[:, jt * 256 + c * 128: jt * 256 + (c + 1) * 128],
                              pTr[b][:], kdecb[:, h * 4 + jt: h * 4 + jt + 1],
                              wait=[(t, t.n), kdtok])
                        tr_war[b] = [(v, v.n)]

                kb.op("vector", "tensor_copy", S_b[:, 0:512],
                      S_f[:, (h * 2) * 512:(h * 2 + 1) * 512], wait=[t])
                kb.op("vector", "tensor_copy", S_b[:, 512:1024],
                      S_f[:, (h * 2 + 1) * 512:(h * 2 + 2) * 512])
                for c in range(2):
                    kb.op("vector", "tensor_mul", qhat[:, c * 512:(c + 1) * 512],
                          qsb[:, c * 512:(c + 1) * 512],
                          crossb[:, h * 512:(h + 1) * 512], wait=toks + [crtok])

                for jt in range(4):
                    b = jt % 2
                    for c in range(2):
                        kb.op("tensor", "matmul", psc[b][:],
                              ksb[:, c * 512 + jt * 128: c * 512 + (jt + 1) * 128],
                              qsb[:, c * 512:(c + 1) * 512],
                              start=(c == 0), stop=(c == 1),
                              wait=sc_war.get(b, [(v, v.n)]) + toks
                              if c == 0 else [])
                    kb.op("vector", "tensor_mul", attnT[:, jt * 512:(jt + 1) * 512],
                          psc[b][:],
                          maskb[:, (h * 4 + jt) * 512:(h * 4 + jt + 1) * 512],
                          wait=[(t, t.n), mtok])
                    sc_war[b] = [(v, v.n)]

                for dvc in range(4):
                    b = dvc % 2
                    pp = po[b]
                    pw = [(v, v.n)] if dvc < 2 else po_war.get(b, [(v, v.n)])
                    for jt in range(4):
                        kb.op("tensor", "matmul", pp[:],
                              vsb[:, jt * 512 + dvc * 128: jt * 512 + (dvc + 1) * 128],
                              attnT[:, jt * 512:(jt + 1) * 512],
                              start=(jt == 0), stop=False,
                              wait=pw if jt == 0 else [])
                    for c in range(2):
                        kb.op("tensor", "matmul", pp[:],
                              S_b[:, c * 512 + dvc * 128: c * 512 + (dvc + 1) * 128],
                              qhat[:, c * 512:(c + 1) * 512],
                              start=False, stop=(c == 1))
                    war = [ost[dvc]] if dvc in ost else []
                    kb.op("vector", "tensor_copy", oev[:, dvc * 512:(dvc + 1) * 512],
                          pp[:], wait=[(t, t.n)] + war)
                    po_war[b] = [(v, v.n)]
                    d3 = fm_dram(ap["o_part"])
                    ost[dvc] = kb.dma("scalar",
                                      d3[:, h * 4 + dvc, sc * 512:(sc + 1) * 512],
                                      oev[:, dvc * 512:(dvc + 1) * 512], str_,
                                      wait=[v])

                for c in range(2):
                    pp = pA[c]
                    for jt in range(4):
                        kb.op("tensor", "matmul", pp[:],
                              khat[:, jt * 256 + c * 128: jt * 256 + (c + 1) * 128],
                              vsb[:, jt * 512:(jt + 1) * 512],
                              start=(jt == 0), stop=(jt == 3),
                              wait=[(v, v.n)] if jt == 0 else [])
                    kb.op("vector", "scalar_tensor_tensor",
                          S_f[:, (h * 2 + c) * 512:(h * 2 + c + 1) * 512],
                          S_f[:, (h * 2 + c) * 512:(h * 2 + c + 1) * 512],
                          gSC[h], pp[:], ALU.mult, ALU.add, wait=[(t, t.n)])
                    pA_war[c] = [(v, v.n)]
                iter_tok[bsel] = [(t, t.n), (v, v.n)]

        pd3 = ap["P"].rearrange("(c p) o -> p c o", p=128)
        for h in range(H0):
            for c in range(2):
                kb.dma("scalar", pd3[:, c, h * 512:(h + 1) * 512],
                       S_f[:, (h * 2 + c) * 512:(h * 2 + c + 1) * 512], str_,
                       wait=[v])

        kb.build(final_rings=[ldr_, wlr, str_])
        p3.close()
    return nc, ap




def build_launch_B():
    nc = bass.Bass("TRN2", target_bir_lowering=False, debug=False)
    ap = {}

    def din(name, shape, dt=F32):
        ap[name] = nc.dram_tensor(name, shape, dt, kind="ExternalInput").ap()

    def dout(name, shape, dt=F32):
        ap[name] = nc.dram_tensor(name, shape, dt, kind="ExternalOutput").ap()

    din("x", [TPC, D])
    din("o_part", [2048, TPC], BF16)
    din("qTd", [D, TPC], BF16)
    din("gTd", [2048, TPC], BF16)
    din("P_all", [NCORE * 256, H0 * HV0])
    din("coefP", [128, NCORE * H0])
    din("WoT", [2048, D])
    din("gwT", [D, NE0])
    din("gnw", [128, 4])
    din("crossfullT", [H0 * 128, TPC], BF16)
    din("ident", [128, 128], BF16)
    din("onesf", [128, 128])
    dout("x1", [TPC, D])
    dout("xn2", [TPC, D], BF16)
    dout("M_em", [NE0, TPC], BF16)
    dout("w_tm", [128, NTT])

    ctx = ExitStack()
    with ctx:
        kb = KB(nc, ctx)
        ldr_ = kb.ring("ld", 4)
        wlr = kb.ring("wl", 2)
        str_ = kb.ring("st", 4)
        v, t, s = kb.e["vector"], kb.e["tensor"], kb.e["scalar"]

        ident = ctx.enter_context(nc.sbuf_tensor("identB", [128, 128], BF16))
        itok = kb.dma("sync", ident[:], ap["ident"], ldr_)
        onesf = ctx.enter_context(nc.sbuf_tensor("onesf_sb", [128, 128], F32))
        otok = kb.dma("sync", onesf[:], ap["onesf"], ldr_)
        epsc = ctx.enter_context(nc.sbuf_tensor("epsc", [128, 1], F32))
        kb.op("vector", "memset", epsc[:], EPS)

        og_d = nc.dram_tensor("og_d", [16 * 128, TPC], BF16, kind="Internal").ap()

        p1 = Pool(nc)
        S_in = p1.sb("S_in", [128, H0 * 2 * 512], F32)
        S_inb_d = p1.sb("S_inb", [128, 2 * 2 * 512], BF16)
        ptmp = p1.sb("ptmp", [128, 2 * 2048], F32)
        coefb = p1.sb("coefb", [128, NCORE * H0], F32)
        gnwb = p1.sb("gnwb", [128, 4], F32)
        cstok = kb.dma("sync", coefb[:], ap["coefP"], ldr_)
        gntok = kb.dma("sync", gnwb[:], ap["gnw"], ldr_)
        kb.op("vector", "memset", S_in[:], 0.0)
        pa3 = ap["P_all"].rearrange("(j c p) o -> p (j c) o", p=128, c=2)
        pt_war = {}
        for j in range(NCORE):
            for c in range(2):
                i2 = j * 2 + c
                half = i2 % 2
                pt = ptmp[:, half * 2048:(half + 1) * 2048]
                ptok = kb.dma("sync" if i2 % 4 < 2 else "scalar", pt,
                              pa3[:, i2, :], ldr_ if i2 % 4 < 2 else str_,
                              wait=pt_war.get(half, []))
                for h in range(H0):
                    kb.op("vector", "scalar_tensor_tensor",
                          S_in[:, (h * 2 + c) * 512:(h * 2 + c + 1) * 512],
                          pt[:, h * 512:(h + 1) * 512],
                          coefb[:, j * H0 + h: j * H0 + h + 1],
                          S_in[:, (h * 2 + c) * 512:(h * 2 + c + 1) * 512],
                          ALU.mult, ALU.add, wait=[ptok, cstok])
                pt_war[half] = [(v, v.n)]

        qhat_d = p1.sb("qhat", [128, 2 * 2 * TPC], BF16)
        crb_d = p1.sb("crb", [128, 2 * TPC], BF16)
        of_d = p1.sb("of", [128, 2 * 4 * TPC], F32)
        sq1 = p1.sb("sq1", [128, 512], F32)
        ogt = p1.sb("ogt", [128, 2 * TPC], BF16)
        opb = p1.sb("opb", [128, 2 * TPC], BF16)
        gob = p1.sb("gob", [128, 2 * TPC], BF16)
        gsil = p1.sb("gsil", [128, TPC], BF16)
        tmpf = p1.sb("tmpf", [128, TPC], F32)
        vr = p1.sb("vr", [1, 2 * TPC], F32)
        varr = vr[0:1, 0:TPC]
        rstd = vr[0:1, TPC:2 * TPC]
        rbc = p1.sb("rbc", [128, TPC], F32)
        pox = [p1.ps(f"pox{i}", [128, 512]) for i in range(4)]
        pvar = p1.ps("pvar", [1, 512])
        pbc = p1.ps("pbc", [128, 512])

        q3 = fm_dram(ap["qTd"])
        cf3 = ap["crossfullT"].rearrange("(h p) t -> p h t", p=128)
        op3 = fm_dram(ap["o_part"])
        g3 = fm_dram(ap["gTd"])
        ogt_war = {}
        qh_war = {}
        si_war = {}
        of_war = {}
        op_war = {}
        go_war = {}
        pox_war = {}
        for h in range(H0):
            hb = h % 2
            qhat = qhat_d[:, hb * 2 * TPC:(hb + 1) * 2 * TPC]
            crb = crb_d[:, hb * TPC:(hb + 1) * TPC]
            S_inb = S_inb_d[:, hb * 1024:(hb + 1) * 1024]
            of = of_d[:, hb * 4 * TPC:(hb + 1) * 4 * TPC]
            qtok = kb.dma("sync", qhat.rearrange("p (c t) -> p c t", c=2),
                          q3[:, h * 2:h * 2 + 2, :], ldr_,
                          wait=qh_war.get(hb, []))
            crtok = kb.dma("sync", crb, cf3[:, h, :], ldr_,
                           wait=qh_war.get(hb, []))
            for c in range(2):
                kb.op("vector", "tensor_mul", qhat[:, c * TPC:(c + 1) * TPC],
                      qhat[:, c * TPC:(c + 1) * TPC], crb,
                      wait=[qtok, crtok])
                kb.op("vector", "tensor_copy", S_inb[:, c * 512:(c + 1) * 512],
                      S_in[:, (h * 2 + c) * 512:(h * 2 + c + 1) * 512],
                      wait=si_war.get(hb, []))
            vq_tok = [(v, v.n)]
            ofw = of_war.get(hb, [])
            for dvc in range(4):
                oh = dvc % 2
                optok = kb.dma("sync", opb[:, oh * TPC:(oh + 1) * TPC],
                               op3[:, h * 4 + dvc, :], ldr_,
                               wait=op_war.get(oh, []))
                for tc in range(4):
                    bk = (dvc * 4 + tc) % 4
                    pp = pox[bk]
                    for c in range(2):
                        kb.op("tensor", "matmul",
                              pp[:],
                              S_inb[:, c * 512 + dvc * 128: c * 512 + (dvc + 1) * 128],
                              qhat[:, c * TPC + tc * 512: c * TPC + (tc + 1) * 512],
                              start=(c == 0), stop=(c == 1),
                              wait=vq_tok + pox_war.get(bk, [])
                              if c == 0 else [])
                    kb.op("vector", "scalar_tensor_tensor",
                          of[:, dvc * TPC + tc * 512: dvc * TPC + (tc + 1) * 512],
                          opb[:, oh * TPC + tc * 512: oh * TPC + (tc + 1) * 512],
                          0.0, pp[:],
                          ALU.bypass, ALU.add, wait=[(t, t.n), optok] + ofw)
                    pox_war[bk] = [(v, v.n)]
                op_war[oh] = [(v, v.n)]
            for tc in range(4):
                for dvc in range(4):
                    kb.op("scalar", "activation", sq1[:],
                          of[:, dvc * TPC + tc * 512: dvc * TPC + (tc + 1) * 512],
                          AF.Square, wait=[(v, v.n), (t, t.n)])
                    kb.op("tensor", "matmul", pvar[:], onesf[:, 0:1], sq1[:],
                          start=(dvc == 0), stop=(dvc == 3),
                          wait=[(s, s.n), otok] + ([(v, v.n)] if dvc == 0 else []))
                kb.op("vector", "tensor_copy", varr[:, tc * 512:(tc + 1) * 512],
                      pvar[:], wait=[(t, t.n)])
            kb.op("scalar", "activation", rstd[:], varr[:], AF.Ln,
                  bias=epsc[0:1, 0:1], scale=1.0 / HV0, wait=[(v, v.n)])
            kb.op("scalar", "activation", varr[:], rstd[:], AF.Exp, scale=-0.5)
            for tc in range(4):
                kb.op("tensor", "matmul", pbc[:], onesf[0:1, :],
                      varr[:, tc * 512:(tc + 1) * 512], start=True, stop=True,
                      wait=[(s, s.n), (v, v.n)])
                kb.op("vector", "tensor_copy", rbc[:, tc * 512:(tc + 1) * 512],
                      pbc[:], wait=[(t, t.n)])
            for dvc in range(4):
                oh = dvc % 2
                gtok = kb.dma("sync", gob[:, oh * TPC:(oh + 1) * TPC],
                              g3[:, h * 4 + dvc, :], ldr_,
                              wait=go_war.get(oh, []))
                kb.op("scalar", "activation", gsil[:],
                      gob[:, oh * TPC:(oh + 1) * TPC], AF.Silu,
                      wait=[gtok])
                go_war[oh] = [(s, s.n)]
                kb.op("vector", "scalar_tensor_tensor", tmpf[:],
                      of[:, dvc * TPC:(dvc + 1) * TPC],
                      gnwb[:, dvc:dvc + 1], rbc[:],
                      ALU.mult, ALU.mult, wait=[gntok])
                kb.op("vector", "tensor_mul", ogt[:, oh * TPC:(oh + 1) * TPC],
                      tmpf[:], gsil[:],
                      wait=ogt_war.get(oh, []) + [(s, s.n)])
                ogt_war[oh] = [kb.dma(
                    "scalar", fm_dram(og_d)[:, h * 4 + dvc, :],
                    ogt[:, oh * TPC:(oh + 1) * TPC], str_, wait=[v])]
            qh_war[hb] = [(t, t.n), (v, v.n)]
            si_war[hb] = [(t, t.n)]
            of_war[hb] = [(s, s.n), (v, v.n)]
        p1.close()

        fence = kb.fence() + ldr_.final_tokens() + str_.final_tokens()
        og_fence = fence
        p2 = Pool(nc)
        wbuf = p2.sb("wbuf", [128, 16 * 1024], BF16)
        ogt2 = p2.sb("ogt2", [128, 16 * 128], BF16)
        gwb = p2.sb("gwb", [128, 8 * NE0], BF16)
        x_a = p2.sb("x_a", [128, D], F32)
        x_b = p2.sb("x_b", [128, D], F32)
        x1t = p2.sb("x1t", [128, D], F32)
        xn_tile = p2.sb("xn_tile", [128, D], BF16)
        red = p2.sb("red", [128, 8], F32)
        xs_sc = p2.sb("xs_sc", [128, D], F32)
        xnTt = p2.sb("xnTt", [128, 8 * 128], BF16)
        lg = p2.sb("lg", [128, NE0], F32)
        eqb = p2.sb("eqb", [128, NE0], F32)
        cumb = p2.sb("cumb", [128, NE0], F32)
        cum1 = p2.sb("cum1", [128, NE0], F32)
        m_bf = p2.sb("m_bf", [128, NE0], BF16)
        wsb = p2.sb("wsb", [128, NTT], F32)
        Mem = p2.sb("Mem", [16, TPC], BF16)
        pj2 = [p2.ps(f"pj2{i}", [128, 512]) for i in range(2)]
        pT = p2.ps("pT", [128, 128], BF16)
        pr = p2.ps("pr", [128, NE0])

        w3 = ap["WoT"].rearrange("(c p) o -> p c o", p=128)
        wotok = kb.dma("gpsimd", wbuf[:].rearrange("p (c o) -> p c o", c=16),
                       w3, wlr, wait=fence)
        gw3 = ap["gwT"].rearrange("(c p) e -> p c e", p=128)
        gwtok = kb.dma("gpsimd", gwb[:].rearrange("p (c e) -> p c e", c=8),
                       gw3, wlr, wait=fence)

        x1_war = []
        xn2_war = []
        for i in range(NTT):
            xb = x_a if i % 2 == 0 else x_b
            xtok = kb.dma("sync", xb[:], ap["x"][i * 128:(i + 1) * 128, :], ldr_,
                          wait=[v] if i >= 2 else fence)
            ogtok = kb.dma("sync", ogt2[:].rearrange("p (c w) -> p c w", c=16),
                           fm_dram(og_d)[:, :, i * 128:(i + 1) * 128], ldr_,
                           wait=[t] + (og_fence if i < 2 else []))
            for og2 in range(2):
                pp = pj2[og2]
                for c in range(16):
                    kb.op("tensor", "matmul", pp[:],
                          ogt2[:, c * 128:(c + 1) * 128],
                          wbuf[:, c * 1024 + og2 * 512: c * 1024 + (og2 + 1) * 512],
                          start=(c == 0), stop=(c == 15),
                          wait=[wotok, v, ogtok] if c == 0 else [])
                kb.op("vector", "scalar_tensor_tensor",
                      x1t[:, og2 * 512:(og2 + 1) * 512],
                      xb[:, og2 * 512:(og2 + 1) * 512], 0.0, pp[:],
                      ALU.bypass, ALU.add, wait=[t, xtok, s] + x1_war)
            x1_war = [kb.dma("scalar", ap["x1"][i * 128:(i + 1) * 128, :],
                             x1t[:], str_, wait=[v])]
            emit_ln_tile(kb, x1t, xn_tile, red, xs_sc, [], epsc[:, 0:1],
                         war=xn2_war)
            xn2_war = [kb.dma("scalar", ap["xn2"][i * 128:(i + 1) * 128, :],
                              xn_tile[:], str_, wait=[v])]
            emit_transpose_to_fm(kb, xn_tile, xnTt, 0, ident, pT, itok, tcols=128)
            for c in range(8):
                kb.op("tensor", "matmul", pr[:], xnTt[:, c * 128:(c + 1) * 128],
                      gwb[:, c * NE0:(c + 1) * NE0],
                      start=(c == 0), stop=(c == 7),
                      wait=[v, gwtok] if c == 0 else [])
            kb.op("vector", "tensor_copy", lg[:], pr[:], wait=[t])
            kb.op("vector", "reduce_max", red[:, 0:1], lg[:], AXX)
            kb.op("vector", "tensor_scalar_mul", red[:, 1:2], red[:, 0:1], -1.0)
            kb.op("scalar", "activation", eqb[:], lg[:], AF.Exp,
                  bias=red[:, 1:2], accum_out=red[:, 2:3], wait=[v])
            kb.op("vector", "reciprocal", wsb[:, i:i + 1], red[:, 2:3], wait=[s])
            kb.op("vector", "tensor_scalar", eqb[:], lg[:], red[:, 0:1], None,
                  ALU.is_equal)
            kb.op("vector", "tensor_tensor_scan", cumb[:], eqb[:], eqb[:], 0.0,
                  ALU.add, ALU.bypass)
            kb.op("vector", "tensor_single_scalar", cum1[:], cumb[:], 1.0,
                  ALU.is_equal)
            kb.op("vector", "tensor_mul", m_bf[:], eqb[:], cum1[:])
            kb.op("tensor", "transpose", pT[0:16, :], m_bf[:], ident[:],
                  wait=[v, itok])
            kb.op("vector", "tensor_copy", Mem[0:16, i * 128:(i + 1) * 128],
                  pT[0:16, :], wait=[t])
        kb.dma("scalar", ap["M_em"], Mem[0:16, :], str_, wait=[v])
        kb.dma("scalar", ap["w_tm"], wsb[:], str_, wait=[v])
        kb.build(final_rings=[ldr_, wlr, str_])
        p2.close()
    return nc, ap


def host_constants(core):
    c = {}
    c["ident"] = np.eye(128, dtype=np.float32).astype(BF)
    tloc = np.arange(TPC, dtype=np.float32)
    pos = (core * TPC + tloc) % 8192

    inv0 = 1.0 / 10000.0 ** (np.arange(0, HK0, 2, dtype=np.float32) / HK0)
    fr = inv0[:, None] * pos[None, :]
    c["r0_cos"] = np.cos(fr).astype(np.float32)
    c["r0_sin"] = np.sin(fr).astype(np.float32)
    s0 = HK0 ** -0.5
    c["r0_cos_q"] = (np.cos(fr) * s0).astype(np.float32)
    c["r0_sin_q"] = (np.sin(fr) * s0).astype(np.float32)

    gamma = (1.0 - np.exp2(-5.0 - np.arange(H0, dtype=np.float32)))
    i512 = np.arange(SC, dtype=np.float32)
    dif = i512[None, :] - i512[:, None]
    mT = np.stack([np.where(dif >= 0, g ** np.clip(dif, 0, None), 0.0) for g in gamma])
    c["maskT"] = mT.reshape(H0 * SC, SC).astype(BF)
    kd = np.zeros((128, H0 * 4), np.float32)
    for h in range(H0):
        for jt in range(4):
            kd[:, h * 4 + jt] = gamma[h] ** (SC - 1 - (jt * 128 + np.arange(128)))
    c["kdec"] = kd
    cr = np.stack([g ** (i512 + 1) for g in gamma])
    c["crossT"] = np.repeat(cr[:, None, :], 128, axis=1).reshape(H0 * 128, SC).astype(BF)
    return c


def prep_launch_A(inputs, core):
    d = inputs
    im = host_constants(core)
    im["x"] = np.ascontiguousarray(d["L0_attn_x"].reshape(-1, D)[core * TPC:(core + 1) * TPC])
    im["WqT"] = np.ascontiguousarray(np.asarray(d["L0_attn_Wq"]).T)
    im["WkT"] = np.ascontiguousarray(np.asarray(d["L0_attn_Wk"]).T)
    im["WvT"] = np.ascontiguousarray(np.asarray(d["L0_attn_Wv"]).T)
    im["WgT"] = np.ascontiguousarray(np.asarray(d["L0_attn_Wg"]).T)
    keep = {"x", "WqT", "WkT", "WvT", "WgT", "ident", "r0_cos", "r0_sin",
            "r0_cos_q", "r0_sin_q", "maskT", "kdec", "crossT"}
    return {k: im[k] for k in keep}





def emit_wload_g(kb, w_src, dst_bf, wlr, n_cols, waits):
    nch = w_src.shape[0] // 128
    w3 = w_src.rearrange("(c p) f -> p c f", p=128)
    toks = []
    for c0 in range(nch):
        toks.append(kb.dma("gpsimd", dst_bf[:, c0 * n_cols:(c0 + 1) * n_cols],
                           w3[:, c0, :], wlr, wait=waits))
    return toks


def emit_wload_cast(kb, w_src, dst_bf, stage, ldr_, n_cols, tag_toks):
    s = kb.e["scalar"]
    nch = w_src.shape[0] // 128
    w3 = w_src.rearrange("(c p) f -> p c f", p=128)
    W = stage[0].shape[1]
    st_war = [[(s, s.n)], [(s, s.n)]]
    i = 0
    if n_cols <= W:
        per = W // n_cols
        for c0 in range(0, nch, per):
            buf = stage[i % 2]
            cw = min(per, nch - c0)
            tok = kb.dma("sync", buf[:, 0:cw * n_cols].rearrange(
                "p (c f) -> p c f", c=cw), w3[:, c0:c0 + cw, :], ldr_,
                wait=st_war[i % 2] + tag_toks)
            kb.op("scalar", "activation",
                  dst_bf[:, c0 * n_cols:(c0 + cw) * n_cols],
                  buf[:, 0:cw * n_cols], AF.Copy, wait=[tok])
            st_war[i % 2] = [(s, s.n)]
            i += 1
    else:
        for c in range(nch):
            for off in range(0, n_cols, W):
                buf = stage[i % 2]
                w = min(W, n_cols - off)
                tok = kb.dma("sync", buf[:, 0:w], w3[:, c, off:off + w], ldr_,
                             wait=st_war[i % 2] + tag_toks)
                kb.op("scalar", "activation",
                      dst_bf[:, c * n_cols + off: c * n_cols + off + w],
                      buf[:, 0:w], AF.Copy, wait=[tok])
                st_war[i % 2] = [(s, s.n)]
                i += 1
    return (s, s.n)


def emit_ffn_blocks(kb, ap_in, invtab, eo_ap, w1b, w2b, bufs, act_fn,
                    blocks, w1tok, w2tok, fence, ldr_, gr, str_):
    v, t, s, g = kb.e["vector"], kb.e["tensor"], kb.e["scalar"], kb.e["gpsimd"]
    idxc, xg, xgT, hT, eot = (bufs["idxc"], bufs["xg"], bufs["xgT"],
                              bufs["hT"], bufs["eot"])
    pT2, ps1, ps2 = bufs["pT2"], bufs["ps1"], bufs["ps2"]
    ident = bufs["ident"]
    itok = bufs["itok"]
    ga_war = []
    eo_war = bufs.setdefault("_eo_war", {})
    for base, nt in blocks:
        gtoks = []
        for j in range(nt):
            itk = kb.dma("gpsimd", idxc[j][:],
                         invtab[base + j * 128:base + (j + 1) * 128, :], gr,
                         wait=fence + ga_war)
            gtoks.append(kb.idma("gpsimd", xg[j][:], None, ap_in,
                                 bass.IndirectOffsetOnAxis(ap=idxc[j][:], axis=0),
                                 gr, wait=[itk, t]))
        ga_war = list(gtoks)
        ns = nt * 128
        evac_tok = bufs.setdefault("_evac_tok", [[], []])
        for j in range(nt):
            for c in range(8):
                bank = (j * 8 + c) % 2
                kb.op("tensor", "transpose", pT2[bank][:],
                      xg[j][:, c * 128:(c + 1) * 128], ident[:],
                      wait=[gtoks[j], itok] + evac_tok[bank])
                kb.op("vector", "tensor_copy",
                      xgT[:, c * 512 + j * 128: c * 512 + (j + 1) * 128],
                      pT2[bank][:], wait=[(t, kb.e["tensor"].n)])
                evac_tok[bank] = [(v, v.n)]
        s1war = bufs.setdefault("_s1war", {})
        for fc in range(32):
            b1 = fc % 2
            pp = ps1[b1]
            w0 = list(w1tok) + s1war.get(b1, [])
            if fc < 2:
                w0.append((v, v.n))
                w0.append((s, s.n))
            for c in range(8):
                kb.op("tensor", "matmul", pp[:, 0:ns],
                      w1b[:, c * 4096 + fc * 128: c * 4096 + (fc + 1) * 128],
                      xgT[:, c * 512: c * 512 + ns],
                      start=(c == 0), stop=(c == 7),
                      wait=w0 if c == 0 else [])
            kb.op("scalar", "activation", hT[:, fc * 512: fc * 512 + ns],
                  pp[:, 0:ns], act_fn, wait=[(t, kb.e["tensor"].n)])
            s1war[b1] = [(s, s.n)]
        s2war = bufs.setdefault("_s2war", {})
        for j in range(nt):
            for og2 in range(2):
                pp = ps2[og2]
                w0 = list(w2tok) + s2war.get(og2, [])
                if j == 0:
                    w0.append((s, s.n))
                for fc in range(32):
                    kb.op("tensor", "matmul", pp[:],
                          hT[:, fc * 512 + j * 128: fc * 512 + (j + 1) * 128],
                          w2b[:, fc * D + og2 * 512: fc * D + (og2 + 1) * 512],
                          start=(fc == 0), stop=(fc == 31),
                          wait=w0 if fc == 0 else [])
                war = eo_war.get(og2, [])
                kb.op("vector", "tensor_copy",
                      eot[:, og2 * 512:(og2 + 1) * 512], pp[:],
                      wait=[(t, kb.e["tensor"].n)] + war)
                s2war[og2] = [(v, v.n)]
                eo_war[og2] = [kb.dma(
                    "scalar",
                    eo_ap[base + j * 128:base + (j + 1) * 128,
                          og2 * 512:(og2 + 1) * 512],
                    eot[:, og2 * 512:(og2 + 1) * 512], str_, wait=[v])]



def build_launch_C():
    nc = bass.Bass("TRN2", target_bir_lowering=False, debug=False)
    ap = {}

    def din(name, shape, dt=F32):
        ap[name] = nc.dram_tensor(name, shape, dt, kind="ExternalInput").ap()

    def dout(name, shape, dt=F32):
        ap[name] = nc.dram_tensor(name, shape, dt, kind="ExternalOutput").ap()

    NTOK = NCORE * TPC
    din("M_all", [NE0, NTOK], BF16)
    din("xn2_all", [NTOK, D], BF16)
    din("w1", [2 * D, 4096])
    din("w2", [2 * 4096, D])
    din("ident", [128, 128], BF16)
    din("iota_t", [128, 128], I32)
    din("eoff16", [16, 1])
    din("basec", [128, 1])
    din("trashc", [128, 128])
    dout("expert_out", [2 * PAD0, D], BF16)
    dout("gidx_row", [1, NTOK])
    dout("keep_row", [1, NTOK])

    invtab = nc.dram_tensor("invtab", [19072, 1], I32, kind="Internal").ap()

    ctx = ExitStack()
    with ctx:
        kb = KB(nc, ctx)
        ldr_ = kb.ring("ld", 4)
        wlr = kb.ring("wl", 2)
        str_ = kb.ring("st", 4)
        gr = kb.ring("gq", 4)
        v, t, s, g = kb.e["vector"], kb.e["tensor"], kb.e["scalar"], kb.e["gpsimd"]

        ident = ctx.enter_context(nc.sbuf_tensor("identB", [128, 128], BF16))
        itok = kb.dma("sync", ident[:], ap["ident"], ldr_)

        p1 = Pool(nc)
        Mem = p1.sb("Mem", [16, NTOK], BF16)
        cum = p1.sb("cum", [16, NTOK], F32)
        tmp_em = p1.sb("tmp_em", [16, NTOK], F32)
        eoffc = p1.sb("eoffc", [16, 1], F32)
        stg = p1.sb("stg", [1, 1024], F32)
        basec = p1.sb("basec_s", [128, 1], F32)
        iota = p1.sb("iota_s", [128, 128], I32)
        gidxf = p1.sb("gidxf", [128, 128], F32)
        negm = p1.sb("negm", [128, 128], F32)
        idxf2 = p1.sb("idxf2", [128, 128], I32)
        zs = p1.sb("zs", [128, 21], I32)
        trashc = p1.sb("trashc_s", [128, 128], F32)
        onesf = p1.sb("onesf_c", [128, 1], F32)
        pcs = p1.ps("pcs", [1, 512])

        mtok = kb.dma("sync", Mem[:], ap["M_all"], ldr_)
        etok = kb.dma("sync", eoffc[:], ap["eoff16"], ldr_)
        btok = kb.dma("sync", basec[:], ap["basec"], ldr_)
        ittok = kb.dma("sync", iota[:], ap["iota_t"], ldr_)
        kb.op("vector", "memset", onesf[:], 1.0)
        ttok2 = kb.dma("sync", trashc[:], ap["trashc"], ldr_)
        kb.op("gpsimd", "memset", zs[:], 0)
        ztok = kb.dma("gpsimd",
                      invtab[0:2688, :].rearrange("(p k) one -> p (k one)", p=128),
                      zs[:], gr, wait=[g])
        kb.op("vector", "tensor_tensor_scan", cum[:], Mem[:], Mem[:], 0.0,
              ALU.add, ALU.bypass, wait=[mtok])
        kb.op("vector", "tensor_scalar", cum[:], cum[:], -1.0, float(CAP0),
              ALU.add, ALU.min)
        kb.op("vector", "tensor_scalar", tmp_em[:], cum[:], eoffc[0:16, 0:1],
              None, ALU.add, wait=[etok])
        kb.op("vector", "tensor_mul", tmp_em[:], tmp_em[:], Mem[:])
        st_tok = []
        for cg in range(NTOK // 512):
            kb.op("tensor", "matmul", pcs[:], onesf[0:16, 0:1],
                  tmp_em[0:16, cg * 512:(cg + 1) * 512], start=True, stop=True,
                  wait=[(v, v.n)])
            sl = (cg % 2) * 512
            kb.op("vector", "tensor_copy", stg[:, sl:sl + 512], pcs[:],
                  wait=[(t, kb.e["tensor"].n)] + st_tok)
            st_tok = [kb.dma("scalar",
                             ap["gidx_row"][0:1, cg * 512:(cg + 1) * 512],
                             stg[:, sl:sl + 512], str_, wait=[v])]
        kb.op("vector", "tensor_single_scalar", tmp_em[:], cum[:],
              float(CAP0) - 0.5, ALU.is_lt)
        kb.op("vector", "tensor_mul", tmp_em[:], tmp_em[:], Mem[:])
        for cg in range(NTOK // 512):
            kb.op("tensor", "matmul", pcs[:], onesf[0:16, 0:1],
                  tmp_em[0:16, cg * 512:(cg + 1) * 512], start=True, stop=True,
                  wait=[(v, v.n)])
            sl = (cg % 2) * 512
            kb.op("vector", "tensor_copy", stg[:, sl:sl + 512], pcs[:],
                  wait=[(t, kb.e["tensor"].n)] + st_tok)
            st_tok = [kb.dma("scalar",
                             ap["keep_row"][0:1, cg * 512:(cg + 1) * 512],
                             stg[:, sl:sl + 512], str_, wait=[v])]
        gxt = kb.dma("sync", gidxf[:],
                     ap["gidx_row"].rearrange("one (a b) -> (one a) b", a=128),
                     ldr_, wait=st_tok)
        kb.op("vector", "tensor_scalar", gidxf[:], gidxf[:], basec[:, 0:1],
              None, ALU.subtract, wait=[gxt, btok])
        kb.op("vector", "tensor_single_scalar", negm[:], gidxf[:], 0.0,
              ALU.is_lt)
        kb.op("vector", "scalar_tensor_tensor", gidxf[:], negm[:], 1e6,
              gidxf[:], ALU.mult, ALU.add)
        kb.op("vector", "tensor_single_scalar", negm[:], gidxf[:], 2561.5,
              ALU.is_lt)
        kb.op("vector", "tensor_mul", gidxf[:], gidxf[:], negm[:])
        kb.op("vector", "tensor_scalar", negm[:], negm[:], -1.0, 1.0,
              ALU.mult, ALU.add)
        kb.op("vector", "tensor_mul", negm[:], negm[:], trashc[:],
              wait=[ttok2])
        kb.op("vector", "tensor_add", gidxf[:], gidxf[:], negm[:])
        kb.op("vector", "tensor_copy", idxf2[:], gidxf[:])
        vsc = (v, v.n)
        scr = kb.ring("scrC", 8)
        for tt_ in range(128):
            kb.idma("gpsimd", invtab,
                    bass.IndirectOffsetOnAxis(ap=idxf2[:, tt_:tt_ + 1], axis=0),
                    iota[:, tt_:tt_ + 1], None, ring=scr,
                    wait=[vsc, ittok, ztok] if tt_ == 0 else [])
        scb_tok = scr.final_tokens()
        p1.close()

        fence_w = kb.fence() + gr.final_tokens() + ldr_.final_tokens()
        fence = fence_w + scb_tok
        p2 = Pool(nc)
        w1b = p2.sb("w1b", [128, 8 * 4096], BF16)
        w2b = p2.sb("w2b", [128, 32 * D], BF16)
        bufs = dict(
            idxc=[p2.sb(f"idxc{i}", [128, 1], I32) for i in range(4)],
            xg=[p2.sb(f"xg{i}", [128, D], BF16) for i in range(4)],
            xgT=p2.sb("xgT", [128, 8 * 512], BF16),
            hT=p2.sb("hT", [128, 32 * 512], BF16),
            eot=p2.sb("eot", [128, 2 * 512], BF16),
            pT2=[p2.ps(f"pT2{i}", [128, 128], BF16) for i in range(2)],
            ps1=[p2.ps(f"ps1{i}", [128, 512]) for i in range(2)],
            ps2=[p2.ps(f"ps2{i}", [128, 512]) for i in range(2)],
            ident=ident, itok=itok,
        )
        zr = p2.sb("zr", [1, D], BF16)
        kb.op("vector", "memset", zr[:], 0.0, wait=fence)
        kb.dma("scalar", ap["expert_out"][PAD0 - 1:PAD0, :], zr[:], str_,
               wait=[v])
        kb.dma("scalar", ap["expert_out"][2 * PAD0 - 1:2 * PAD0, :], zr[:],
               str_, wait=[v])

        for e in range(2):
            w1tok = emit_wload_g(kb, ap["w1"][e * D:(e + 1) * D, :], w1b,
                                 wlr, 4096, fence + [t])
            w2tok = emit_wload_g(kb, ap["w2"][e * 4096:(e + 1) * 4096, :],
                                 w2b, wlr, D, fence + [t])
            blocks = [(e * PAD0, 4), (e * PAD0 + 512, 4), (e * PAD0 + 1024, 2)]
            emit_ffn_blocks(kb, ap["xn2_all"], invtab, ap["expert_out"],
                            w1b, w2b, bufs, AF.Gelu_apprx_tanh, blocks,
                            w1tok, w2tok, fence, ldr_, gr, str_)
            fence = kb.fence()
        kb.build(final_rings=[ldr_, wlr, str_, gr])
        p2.close()
    return nc, ap


def prep_launch_C(inputs, core, stB):
    d = inputs
    im = {}
    im["M_all"] = np.concatenate([stB[j]["M_em"] for j in range(NCORE)], 1)
    im["xn2_all"] = np.concatenate([stB[j]["xn2"] for j in range(NCORE)], 0)
    im["w1"] = np.ascontiguousarray(
        np.asarray(d["L0_ffn_w1"][2 * core:2 * core + 2]).reshape(2 * D, 4096))
    im["w2"] = np.ascontiguousarray(
        np.asarray(d["L0_ffn_w2"][2 * core:2 * core + 2]).reshape(2 * 4096, D))
    im["ident"] = np.eye(128, dtype=np.float32).astype(BF)
    p = np.arange(128)
    im["iota_t"] = (p[:, None] * 128 + np.arange(128)[None, :]).astype(np.int32)
    im["eoff16"] = (np.arange(NE0) * PAD0).astype(np.float32).reshape(16, 1)
    im["basec"] = np.full((128, 1), 2 * core * PAD0, np.float32)
    im["trashc"] = (2688 + np.arange(128)[:, None] * 128
                    + np.arange(128)[None, :]).astype(np.float32)
    return im




def build_launch_D():
    nc = bass.Bass("TRN2", target_bir_lowering=False, debug=False)
    ap = {}

    def din(name, shape, dt=F32):
        ap[name] = nc.dram_tensor(name, shape, dt, kind="ExternalInput").ap()

    def dout(name, shape, dt=F32):
        ap[name] = nc.dram_tensor(name, shape, dt, kind="ExternalOutput").ap()

    din("x1", [TPC, D])
    din("expert_out_all", [NE0 * PAD0, D], BF16)
    din("gidx", [128, NTT])
    din("keep", [128, NTT])
    din("w_tm", [128, NTT])
    din("WqT1", [D, D]); din("WkT1", [D, D]); din("WvT1", [D, D])
    din("bq", [128, 8]); din("bk", [128, 8]); din("bv", [128, 8])
    din("r1_cos", [128, TPC], BF16); din("r1_sin", [128, TPC], BF16)
    din("r1_cos_q", [128, TPC], BF16); din("r1_sin_q", [128, TPC], BF16)
    din("rot_permT", [128, 128], BF16)
    din("ident", [128, 128], BF16)
    dout("x2", [TPC, D])
    dout("qTd1", [D, TPC], BF16)
    dout("kTd1", [D, TPC], BF16)
    dout("v_aug", [TPC, H1 * 65], BF16)

    ctx = ExitStack()
    with ctx:
        kb = KB(nc, ctx)
        ldr_ = kb.ring("ld", 4)
        wlr = kb.ring("wl", 2)
        str_ = kb.ring("st", 4)
        gr = kb.ring("gq", 4)
        v, t, s, g = kb.e["vector"], kb.e["tensor"], kb.e["scalar"], kb.e["gpsimd"]

        ident = ctx.enter_context(nc.sbuf_tensor("identB", [128, 128], BF16))
        itok = kb.dma("sync", ident[:], ap["ident"], ldr_)
        epsc = ctx.enter_context(nc.sbuf_tensor("epsc", [128, 1], F32))
        kb.op("vector", "memset", epsc[:], EPS)
        xn3T = ctx.enter_context(nc.sbuf_tensor("xn3T", [128, 8 * TPC], BF16))

        p1 = Pool(nc)
        x_a = p1.sb("x_a", [128, D], F32)
        x_b = p1.sb("x_b", [128, D], F32)
        gk = p1.sb("gk", [128, 3 * NTT], F32)
        idxc = p1.sb("idxc", [128, 2], I32)
        yg = p1.sb("yg", [128, 2 * D], BF16)
        x2t = p1.sb("x2t", [128, D], F32)
        xn_tile = p1.sb("xn_tile", [128, D], BF16)
        red = p1.sb("red", [128, 8], F32)
        xs_sc = p1.sb("xs_sc", [128, D], F32)
        wk2 = p1.sb("wk2", [128, 1], F32)
        pT = p1.ps("pT", [128, 128], BF16)

        gtok1 = kb.dma("sync", gk[:, 0:NTT], ap["gidx"], ldr_)
        gtok2 = kb.dma("sync", gk[:, NTT:2 * NTT], ap["keep"], ldr_)
        gtok3 = kb.dma("sync", gk[:, 2 * NTT:3 * NTT], ap["w_tm"], ldr_)
        x2_war = []
        g_tok = {}
        stt_tok = {}

        def emit_gather(i):
            b = i % 2
            kb.op("vector", "tensor_copy", idxc[:, b:b + 1], gk[:, i:i + 1],
                  wait=[gtok1] + stt_tok.get(b, []))
            g_tok[b] = kb.idma("gpsimd", yg[:, b * D:(b + 1) * D], None,
                               ap["expert_out_all"],
                               bass.IndirectOffsetOnAxis(ap=idxc[:, b:b + 1],
                                                         axis=0),
                               gr, wait=[(v, v.n)])

        emit_gather(0)
        tw = {}
        for i in range(NTT):
            b = i % 2
            xb = x_a if i % 2 == 0 else x_b
            xtok = kb.dma("sync", xb[:], ap["x1"][i * 128:(i + 1) * 128, :], ldr_,
                          wait=[v] if i >= 2 else [])
            if i + 1 < NTT:
                emit_gather(i + 1)
            kb.op("vector", "tensor_mul", wk2[:], gk[:, NTT + i:NTT + i + 1],
                  gk[:, 2 * NTT + i:2 * NTT + i + 1], wait=[gtok2, gtok3])
            kb.op("vector", "scalar_tensor_tensor", x2t[:], yg[:, b * D:(b + 1) * D],
                  wk2[:, 0:1], xb[:], ALU.mult, ALU.add,
                  wait=[g_tok[b], xtok, s] + x2_war)
            stt_tok[b] = [(v, v.n)]
            x2_war = [kb.dma("scalar", ap["x2"][i * 128:(i + 1) * 128, :],
                             x2t[:], str_, wait=[v])]
            emit_ln_tile(kb, x2t, xn_tile, red, xs_sc, [], epsc[:, 0:1])
            emit_transpose_to_fm(kb, xn_tile, xn3T, i, ident, pT, itok, war=tw)
        p1.close()

        fence = kb.fence() + ldr_.final_tokens() + gr.final_tokens() \
            + str_.final_tokens()
        p2 = Pool(nc)
        wbuf_d = p2.sb("wbuf", [128, 2 * 8 * 1024], BF16)
        bb = p2.sb("bb", [128, 3 * 8], F32)
        rc = p2.sb("rc", [128, TPC], BF16)
        rs = p2.sb("rs", [128, TPC], BF16)
        rperm = p2.sb("rperm", [128, 128], BF16)
        qb_bf = p2.sb("qb_bf", [128, 2 * 512], BF16)
        t1 = p2.sb("t1", [128, 2 * 512], F32)
        t2 = p2.sb("t2", [128, 2 * 512], F32)
        evb = p2.sb("evb", [128, 4 * 512], BF16)
        vfm = p2.sb("vfm", [128, 8 * TPC], BF16)
        vag = p2.sb("vag", [128, H1 * 65], BF16)
        psq = [p2.ps(f"psq{i}", [128, 512]) for i in range(2)]
        pro = [p2.ps(f"pro{i}", [128, 512]) for i in range(2)]
        pT2 = [p2.ps(f"pT2{i}", [128, 128], BF16) for i in range(2)]

        btok = kb.dma("sync", bb[:, 0:8], ap["bq"], ldr_, wait=fence)
        kb.dma("sync", bb[:, 8:16], ap["bk"], ldr_)
        kb.dma("sync", bb[:, 16:24], ap["bv"], ldr_)
        ptok = kb.dma("sync", rperm[:], ap["rot_permT"], ldr_)

        wslot = [0]

        def load_w(name):
            b = wslot[0] % 2
            wslot[0] += 1
            wb = wbuf_d[:, b * 8192:(b + 1) * 8192]
            tok = kb.dma("gpsimd", wb.rearrange("p (c o) -> p c o", c=8),
                         ap[name].rearrange("(c p) o -> p c o", p=128),
                         wlr, wait=fence + ([t] if wslot[0] > 2 else []))
            return wb, tok

        ev_war = {}

        pq_war = {}
        pr_war = {}
        qb_war = {}

        def proj_rope(cos_ap, sin_ap, out_dram, bcol_off, wb, wtok):
            ctok = kb.dma("sync", rc[:], cos_ap, ldr_, wait=[v])
            stok = kb.dma("sync", rs[:], sin_ap, ldr_, wait=[v])
            for oc in range(8):
                for cg in range(4):
                    b = cg % 2
                    pq = psq[b]
                    pr2 = pro[b]
                    for c in range(8):
                        kb.op("tensor", "matmul", pq[:],
                              wb[:, c * 1024 + oc * 128: c * 1024 + (oc + 1) * 128],
                              xn3T[:, c * TPC + cg * 512: c * TPC + (cg + 1) * 512],
                              start=(c == 0), stop=(c == 7),
                              wait=[wtok] + pq_war.get(b, [(v, v.n)])
                              if c == 0 else [])
                    kb.op("vector", "tensor_scalar",
                          qb_bf[:, b * 512:(b + 1) * 512], pq[:],
                          bb[:, bcol_off + oc:bcol_off + oc + 1],
                          None, ALU.add,
                          wait=[(t, t.n), btok, ctok, stok] + qb_war.get(b, []))
                    pq_war[b] = [(v, v.n)]
                    kb.op("tensor", "matmul", pr2[:], rperm[:],
                          qb_bf[:, b * 512:(b + 1) * 512],
                          start=True, stop=True,
                          wait=[(v, v.n), ptok] + pr_war.get(b, []))
                    kb.op("vector", "tensor_mul", t1[:, b * 512:(b + 1) * 512],
                          qb_bf[:, b * 512:(b + 1) * 512],
                          rc[:, cg * 512:(cg + 1) * 512])
                    kb.op("vector", "tensor_mul", t2[:, b * 512:(b + 1) * 512],
                          pr2[:],
                          rs[:, cg * 512:(cg + 1) * 512], wait=[(t, t.n)])
                    pr_war[b] = [(v, v.n)]
                    sl = b * 512
                    kb.op("vector", "tensor_add", evb[:, sl:sl + 512],
                          t1[:, b * 512:(b + 1) * 512],
                          t2[:, b * 512:(b + 1) * 512],
                          wait=ev_war.get(sl, []))
                    qb_war[b] = [(v, v.n)]
                    d3 = fm_dram(out_dram)
                    ev_war[sl] = [kb.dma("scalar",
                                         d3[:, oc, cg * 512:(cg + 1) * 512],
                                         evb[:, sl:sl + 512], str_, wait=[v])]

        wb, wtok = load_w("WqT1")
        proj_rope(ap["r1_cos_q"], ap["r1_sin_q"], ap["qTd1"], 0, wb, wtok)
        wb, wtok = load_w("WkT1")
        proj_rope(ap["r1_cos"], ap["r1_sin"], ap["kTd1"], 8, wb, wtok)

        wb, wtok = load_w("WvT1")
        for oc in range(8):
            for cg in range(4):
                b = cg % 2
                pq = psq[b]
                for c in range(8):
                    kb.op("tensor", "matmul", pq[:],
                          wb[:, c * 1024 + oc * 128: c * 1024 + (oc + 1) * 128],
                          xn3T[:, c * TPC + cg * 512: c * TPC + (cg + 1) * 512],
                          start=(c == 0), stop=(c == 7),
                          wait=[wtok] + pq_war.get(b, [(v, v.n)])
                          if c == 0 else [])
                kb.op("vector", "tensor_scalar",
                      vfm[:, oc * TPC + cg * 512: oc * TPC + (cg + 1) * 512],
                      pq[:], bb[:, 16 + oc:16 + oc + 1], None, ALU.add,
                      wait=[(t, t.n)])
                pq_war[b] = [(v, v.n)]
        vw_war = []
        tp_war = {}
        for i in range(NTT):
            kb.op("vector", "memset", vag[:, 64::65], 1.0, wait=vw_war)
            for hp in range(8):
                b = hp % 2
                kb.op("tensor", "transpose", pT2[b][:],
                      vfm[:, hp * TPC + i * 128: hp * TPC + (i + 1) * 128],
                      ident[:],
                      wait=[itok] + ([(v, v.n)] if hp < 2
                                     else tp_war.get(b, [(v, v.n)])))
                kb.op("vector", "tensor_copy",
                      vag[:, (2 * hp) * 65:(2 * hp) * 65 + 64], pT2[b][:, 0:64],
                      wait=[(t, t.n)])
                kb.op("vector", "tensor_copy",
                      vag[:, (2 * hp + 1) * 65:(2 * hp + 1) * 65 + 64],
                      pT2[b][:, 64:128])
                tp_war[b] = [(v, v.n)]
            vw_war[:] = [kb.dma("scalar", ap["v_aug"][i * 128:(i + 1) * 128, :],
                                vag[:], str_, wait=[v])]
        kb.build(final_rings=[ldr_, wlr, str_, gr])
        p2.close()
    return nc, ap


def prep_launch_D(inputs, core, stB, stC):
    d = inputs
    im = {}
    im["x1"] = stB[core]["x1"]
    im["expert_out_all"] = np.concatenate(
        [stC[j]["expert_out"] for j in range(NCORE)], 0)
    gr_ = np.asarray(stC[0]["gidx_row"]).reshape(-1)[core * TPC:(core + 1) * TPC]
    im["gidx"] = np.ascontiguousarray(gr_.reshape(NTT, 128).T)
    kr_ = np.asarray(stC[0]["keep_row"]).reshape(-1)[core * TPC:(core + 1) * TPC]
    im["keep"] = np.ascontiguousarray(kr_.reshape(NTT, 128).T)
    im["w_tm"] = stB[core]["w_tm"]
    im["WqT1"] = np.ascontiguousarray(np.asarray(d["L1_attn_Wq"]).T)
    im["WkT1"] = np.ascontiguousarray(np.asarray(d["L1_attn_Wk"]).T)
    im["WvT1"] = np.ascontiguousarray(np.asarray(d["L1_attn_Wv"]).T)
    im["bq"] = np.ascontiguousarray(np.asarray(d["L1_attn_bq"]).reshape(8, 128).T)
    im["bk"] = np.ascontiguousarray(np.asarray(d["L1_attn_bk"]).reshape(8, 128).T)
    im["bv"] = np.ascontiguousarray(np.asarray(d["L1_attn_bv"]).reshape(8, 128).T)
    tloc = np.arange(TPC, dtype=np.float32)
    pos = (core * TPC + tloc) % 8192
    inv1 = 1.0 / 10000.0 ** (np.arange(0, DH1, 2, dtype=np.float32) / DH1)
    fr1 = np.concatenate([inv1, inv1])[:, None] * pos[None, :]
    cos2 = np.concatenate([np.cos(fr1)] * 2, 0)
    sin2 = np.concatenate([np.sin(fr1)] * 2, 0)
    s1 = DH1 ** -0.5
    im["r1_cos"] = cos2.astype(BF)
    im["r1_sin"] = sin2.astype(BF)
    im["r1_cos_q"] = (cos2 * s1).astype(BF)
    im["r1_sin_q"] = (sin2 * s1).astype(BF)
    P = np.zeros((128, 128), np.float32)
    for hb in range(2):
        o = hb * 64
        for j in range(32):
            P[o + j, o + 32 + j] = -1.0
            P[o + 32 + j, o + j] = 1.0
    im["rot_permT"] = P.T.copy().astype(BF)
    im["ident"] = np.eye(128, dtype=np.float32).astype(BF)
    return im




def build_launch_E():
    nc = bass.Bass("TRN2", target_bir_lowering=False, debug=False)
    ap = {}

    def din(name, shape, dt=F32):
        ap[name] = nc.dram_tensor(name, shape, dt, kind="ExternalInput").ap()

    def dout(name, shape, dt=F32):
        ap[name] = nc.dram_tensor(name, shape, dt, kind="ExternalOutput").ap()

    din("qT", [D, TPC], BF16)
    din("kT_all", [D, 8192], BF16)
    din("v_aug_all", [8192, H1 * 65], BF16)
    din("ident", [128, 128], BF16)
    din("identf", [128, 128])
    din("onesf", [128, 128])
    dout("attT", [D, TPC], BF16)

    NKT = 8192 // 128

    ctx = ExitStack()
    with ctx:
        kb = KB(nc, ctx)
        ldr_ = kb.ring("ld", 6)
        str_ = kb.ring("st", 4)
        v, t, s = kb.e["vector"], kb.e["tensor"], kb.e["scalar"]

        ident = ctx.enter_context(nc.sbuf_tensor("identB", [128, 128], BF16))
        itok = kb.dma("sync", ident[:], ap["ident"], ldr_)
        identf = ctx.enter_context(nc.sbuf_tensor("identF", [128, 128], F32))
        iftok = kb.dma("sync", identf[:], ap["identf"], ldr_)
        onesf = ctx.enter_context(nc.sbuf_tensor("onesf_sb", [128, 128], F32))
        otok = kb.dma("sync", onesf[:], ap["onesf"], ldr_)

        p = Pool(nc)
        qsb = [p.sb(f"qsb{i}", [128, TPC], BF16) for i in range(2)]
        ksb = [p.sb(f"ksb{i}", [128, 8192], BF16) for i in range(2)]
        vsb = [p.sb(f"vsb{i}", [128, 64 * 130], BF16) for i in range(2)]
        PTA = [p.sb(f"PTA{i}", [128, 1024], BF16) for i in range(3)]
        PTB = [p.sb(f"PTB{i}", [128, 1024], BF16) for i in range(3)]
        mcol = p.sb("mcol", [128, 8], F32)
        rec_sb = p.sb("rec_sb", [1, TPC], F32)
        rb_sb = p.sb("rb_sb", [64, TPC], F32)
        att_sb = p.sb("att_sb", [128, TPC], BF16)
        psA = p.ps("psA", [128, 1024])
        psB = p.ps("psB", [128, 1024])
        pvA = p.ps("pvA", [128, 1024])
        pvB = p.ps("pvB", [128, 1024])

        q3 = fm_dram(ap["qT"])
        k3 = fm_dram(ap["kT_all"])
        v3 = ap["v_aug_all"].rearrange("(kt p) c -> p kt c", p=128)
        a3 = fm_dram(ap["attT"])

        ld_war = [[], []]
        att_war = []
        ld_toks = {}

        def issue_loads(ch):
            bslot = ch % 2
            t1 = kb.dma("sync", qsb[bslot][:], q3[:, ch, :], ldr_,
                        wait=ld_war[bslot])
            t2 = kb.dma("sync", ksb[bslot][:], k3[:, ch, :], ldr_,
                        wait=ld_war[bslot])
            t3 = kb.dma("sync", vsb[bslot][:].rearrange("p (kt c) -> p kt c", c=130),
                        v3[:, :, 2 * ch * 65:(2 * ch + 2) * 65], ldr_,
                        wait=ld_war[bslot])
            ld_toks[ch] = (t1, t2, t3)

        issue_loads(0)
        for ch in range(H1 // 2):
            bslot = ch % 2
            qtok, ktok, vtok = ld_toks[ch]
            qs, ks, vs = qsb[bslot], ksb[bslot], vsb[bslot]
            if ch + 1 < H1 // 2:
                issue_loads(ch + 1)

            for i, kt in enumerate((0, 32)):
                for half in range(2):
                    kb.op("tensor", "matmul", psA[:, half * 512:(half + 1) * 512],
                          ks[0:64, kt * 128:(kt + 1) * 128],
                          qs[0:64, half * 512:(half + 1) * 512],
                          start=True, stop=True, tile_position=(0, 0),
                          wait=[qtok, ktok, s, v] if half == 0 else [])
                    kb.op("tensor", "matmul", psB[:, half * 512:(half + 1) * 512],
                          ks[64:128, kt * 128:(kt + 1) * 128],
                          qs[64:128, half * 512:(half + 1) * 512],
                          start=True, stop=True, tile_position=(64, 0))
                kb.op("vector", "reduce_max", mcol[:, 2 * i:2 * i + 1], psA[:],
                      AXX, wait=[t])
                kb.op("vector", "reduce_max", mcol[:, 2 * i + 1:2 * i + 2], psB[:],
                      AXX)
            kb.op("vector", "tensor_max", mcol[:, 4:5], mcol[:, 0:1], mcol[:, 2:3])
            kb.op("vector", "tensor_max", mcol[:, 5:6], mcol[:, 1:2], mcol[:, 3:4])
            kb.op("tensor", "transpose", psA[0:2, 0:128], mcol[:, 4:6],
                  identf[:], wait=[v, iftok])
            kb.op("vector", "reduce_max", mcol[0:2, 6:7], psA[0:2, 0:128], AXX,
                  wait=[t])
            kb.op("vector", "tensor_scalar_mul", mcol[0:2, 7:8], mcol[0:2, 6:7],
                  -1.0)
            kb.op("vector", "tensor_scalar", mcol[0:2, 4:6], identf[0:2, 0:2],
                  mcol[0:2, 7:8], None, ALU.mult)
            kb.op("tensor", "matmul", psA[0:128, 0:2], onesf[0:2, 0:128],
                  mcol[0:2, 4:6], start=True, stop=True, wait=[v, otok])
            kb.op("vector", "tensor_copy", mcol[:, 6:8], psA[:, 0:2], wait=[t])

            for qc in range(2):
                qof = qc * 1024
                expA = [0] * NKT
                expB = [0] * NKT
                for kt in range(NKT):
                    wA = [(s, expA[kt - 1])] if kt >= 1 else [v, s]
                    wB = [(s, expB[kt - 1])] if kt >= 1 else []
                    kb.op("tensor", "matmul", psA[:, 0:512],
                          ks[0:64, kt * 128:(kt + 1) * 128],
                          qs[0:64, qof:qof + 512],
                          start=True, stop=True, tile_position=(0, 0), wait=wA)
                    kb.op("tensor", "matmul", psA[:, 512:1024],
                          ks[0:64, kt * 128:(kt + 1) * 128],
                          qs[0:64, qof + 512:qof + 1024],
                          start=True, stop=True, tile_position=(0, 0))
                    tA = kb.e["tensor"].n
                    kb.op("tensor", "matmul", psB[:, 0:512],
                          ks[64:128, kt * 128:(kt + 1) * 128],
                          qs[64:128, qof:qof + 512],
                          start=True, stop=True, tile_position=(64, 0), wait=wB)
                    kb.op("tensor", "matmul", psB[:, 512:1024],
                          ks[64:128, kt * 128:(kt + 1) * 128],
                          qs[64:128, qof + 512:qof + 1024],
                          start=True, stop=True, tile_position=(64, 0))
                    tB = kb.e["tensor"].n
                    kb.op("scalar", "activation", PTA[kt % 3][:], psA[:], AF.Exp,
                          bias=mcol[:, 6:7], wait=[(t, tA)])
                    expA[kt] = s.n
                    kb.op("scalar", "activation", PTB[kt % 3][:], psB[:], AF.Exp,
                          bias=mcol[:, 7:8], wait=[(t, tB)])
                    expB[kt] = s.n
                    if kt >= 1:
                        for half in range(2):
                            kb.op("tensor", "matmul",
                                  pvA[0:65, half * 512:(half + 1) * 512],
                                  vs[:, (kt - 1) * 130:(kt - 1) * 130 + 65],
                                  PTA[(kt - 1) % 3][:, half * 512:(half + 1) * 512],
                                  start=(kt == 1), stop=False,
                                  wait=([vtok, (s, expA[kt - 1])] if kt == 1
                                        else [(s, expA[kt - 1])])
                                  if half == 0 else [])
                        for half in range(2):
                            kb.op("tensor", "matmul",
                                  pvB[0:65, half * 512:(half + 1) * 512],
                                  vs[:, (kt - 1) * 130 + 65:kt * 130],
                                  PTB[(kt - 1) % 3][:, half * 512:(half + 1) * 512],
                                  start=(kt == 1), stop=False,
                                  wait=[(s, expB[kt - 1])] if half == 0 else [])
                for half in range(2):
                    kb.op("tensor", "matmul",
                          pvA[0:65, half * 512:(half + 1) * 512],
                          vs[:, (NKT - 1) * 130:(NKT - 1) * 130 + 65],
                          PTA[(NKT - 1) % 3][:, half * 512:(half + 1) * 512],
                          start=False, stop=True,
                          wait=[(s, expA[NKT - 1])] if half == 0 else [])
                for half in range(2):
                    kb.op("tensor", "matmul",
                          pvB[0:65, half * 512:(half + 1) * 512],
                          vs[:, (NKT - 1) * 130 + 65:NKT * 130],
                          PTB[(NKT - 1) % 3][:, half * 512:(half + 1) * 512],
                          start=False, stop=True,
                          wait=[(s, expB[NKT - 1])] if half == 0 else [])

                for hh, pvX, sub in ((0, pvA, 0), (1, pvB, 64)):
                    kb.op("scalar", "activation", rec_sb[:, qof:qof + 1024],
                          pvX[64:65, :], AF.Ln, wait=[t, v])
                    kb.op("scalar", "activation", rec_sb[:, qof:qof + 1024],
                          rec_sb[:, qof:qof + 1024], AF.Exp, scale=-1.0)
                    for sl in range(2):
                        kb.op("tensor", "matmul",
                              psA[0:64, sl * 512:(sl + 1) * 512],
                              onesf[0:1, 0:64],
                              rec_sb[:, qof + sl * 512:qof + (sl + 1) * 512],
                              start=True, stop=True, wait=[s, v])
                    kb.op("vector", "tensor_copy", rb_sb[:, qof:qof + 1024],
                          psA[0:64, :], wait=[t])
                    kb.op("vector", "tensor_mul",
                          att_sb[sub:sub + 64, qof:qof + 1024],
                          pvX[0:64, :], rb_sb[:, qof:qof + 1024],
                          wait=att_war)
            ld_war[bslot] = [(t, t.n), (s, s.n)]
            att_war = [kb.dma("scalar", a3[:, ch, :], att_sb[:], str_,
                              wait=[v])]

        kb.build(final_rings=[ldr_, str_])
        p.close()
    return nc, ap


def prep_launch_E(inputs, core, stD):
    b0 = (core // 4) * 4
    im = {}
    im["identf"] = np.eye(128, dtype=np.float32)
    im["qT"] = stD[core]["qTd1"]
    im["kT_all"] = np.concatenate(
        [stD[j]["kTd1"] for j in range(b0, b0 + 4)], 1)
    im["v_aug_all"] = np.concatenate(
        [stD[j]["v_aug"] for j in range(b0, b0 + 4)], 0)
    im["ident"] = np.eye(128, dtype=np.float32).astype(BF)
    im["onesf"] = np.ones((128, 128), np.float32)
    return im




def build_launch_F():
    nc = bass.Bass("TRN2", target_bir_lowering=False, debug=False)
    ap = {}

    def din(name, shape, dt=F32):
        ap[name] = nc.dram_tensor(name, shape, dt, kind="ExternalInput").ap()

    def dout(name, shape, dt=F32):
        ap[name] = nc.dram_tensor(name, shape, dt, kind="ExternalOutput").ap()

    din("x2", [TPC, D])
    din("attT", [D, TPC], BF16)
    din("WoT1", [D, D])
    din("bo_b", [128, D])
    din("Wr", [D, NE1])
    din("ident", [128, 128], BF16)
    din("identf", [128, 128])
    dout("x3", [TPC, D])
    dout("xn4", [TPC, D], BF16)
    dout("G_em", [NE1, TPC])
    dout("M1_em", [NE1, TPC])
    dout("M2_em", [NE1, TPC])

    ctx = ExitStack()
    with ctx:
        kb = KB(nc, ctx)
        ldr_ = kb.ring("ld", 4)
        wlr = kb.ring("wl", 2)
        str_ = kb.ring("st", 4)
        v, t, s = kb.e["vector"], kb.e["tensor"], kb.e["scalar"]

        ident = ctx.enter_context(nc.sbuf_tensor("identB", [128, 128], BF16))
        itok = kb.dma("sync", ident[:], ap["ident"], ldr_)
        identf = ctx.enter_context(nc.sbuf_tensor("identF", [128, 128], F32))
        iftok = kb.dma("sync", identf[:], ap["identf"], ldr_)
        epsc = ctx.enter_context(nc.sbuf_tensor("epsc", [128, 1], F32))
        kb.op("vector", "memset", epsc[:], EPS)

        p = Pool(nc)
        wbuf = p.sb("wbuf", [128, 8 * 1024], BF16)
        wrb = p.sb("wrb", [128, 8 * NE1], BF16)
        bob = p.sb("bob", [128, D], F32)
        atile = p.sb("atile", [128, 8 * 128], BF16)
        x_a = p.sb("x_a", [128, D], F32)
        x_b = p.sb("x_b", [128, D], F32)
        x3t = p.sb("x3t", [128, D], F32)
        xn_tile = p.sb("xn_tile", [128, D], BF16)
        red = p.sb("red", [128, 8], F32)
        xs_sc = p.sb("xs_sc", [128, D], F32)
        xnTt = p.sb("xnTt", [128, 8 * 128], BF16)
        lg = p.sb("lg", [128, NE1], F32)
        ex = p.sb("ex", [128, NE1], F32)
        pb = p.sb("pb", [128, NE1], F32)
        t8 = p.sb("t8", [128, 8], F32)
        sc1 = p.sb("sc1", [128, 6 * NE1], F32)
        Gm = p.sb("Gm", [NE1, TPC], F32)
        M1m = p.sb("M1m", [NE1, TPC], F32)
        M2m = p.sb("M2m", [NE1, TPC], F32)
        pj2 = [p.ps(f"pj2{i}", [128, 512]) for i in range(2)]
        pT = p.ps("pT", [128, 128], BF16)
        pr = p.ps("pr", [128, NE1])
        pG = p.ps("pG", [128, 128])

        w3 = ap["WoT1"].rearrange("(c p) o -> p c o", p=128)
        wotok = kb.dma("gpsimd", wbuf[:].rearrange("p (c o) -> p c o", c=8),
                       w3, wlr)
        wr3 = ap["Wr"].rearrange("(c p) e -> p c e", p=128)
        wrtok = kb.dma("gpsimd", wrb[:].rearrange("p (c e) -> p c e", c=8),
                       wr3, wlr)
        botok = kb.dma("sync", bob[:], ap["bo_b"], ldr_)

        a3 = fm_dram(ap["attT"])
        for i in range(NTT):
            xb = x_a if i % 2 == 0 else x_b
            xtok = kb.dma("sync", xb[:], ap["x2"][i * 128:(i + 1) * 128, :], ldr_,
                          wait=[v] if i >= 2 else [])
            atok = kb.dma("sync", atile[:].rearrange("p (c w) -> p c w", c=8),
                          a3[:, :, i * 128:(i + 1) * 128], ldr_, wait=[t])
            for og2 in range(2):
                pp = pj2[og2]
                for c in range(8):
                    kb.op("tensor", "matmul", pp[:],
                          atile[:, c * 128:(c + 1) * 128],
                          wbuf[:, c * 1024 + og2 * 512: c * 1024 + (og2 + 1) * 512],
                          start=(c == 0), stop=(c == 7),
                          wait=[wotok, v, atok] if c == 0 else [])
                kb.op("vector", "scalar_tensor_tensor",
                      x3t[:, og2 * 512:(og2 + 1) * 512],
                      xb[:, og2 * 512:(og2 + 1) * 512], 0.0, pp[:],
                      ALU.bypass, ALU.add, wait=[t, xtok, s])
                kb.op("vector", "tensor_add", x3t[:, og2 * 512:(og2 + 1) * 512],
                      x3t[:, og2 * 512:(og2 + 1) * 512],
                      bob[:, og2 * 512:(og2 + 1) * 512], wait=[botok])
            kb.dma("scalar", ap["x3"][i * 128:(i + 1) * 128, :], x3t[:], str_,
                   wait=[v])
            emit_ln_tile(kb, x3t, xn_tile, red, xs_sc, [], epsc[:, 0:1])
            kb.dma("scalar", ap["xn4"][i * 128:(i + 1) * 128, :], xn_tile[:], str_,
                   wait=[v])
            emit_transpose_to_fm(kb, xn_tile, xnTt, 0, ident, pT, itok, tcols=128)
            for c in range(8):
                kb.op("tensor", "matmul", pr[:], xnTt[:, c * 128:(c + 1) * 128],
                      wrb[:, c * NE1:(c + 1) * NE1],
                      start=(c == 0), stop=(c == 7),
                      wait=[v, wrtok] if c == 0 else [])
            kb.op("vector", "tensor_copy", lg[:], pr[:], wait=[t])
            kb.op("vector", "reduce_max", red[:, 0:1], lg[:], AXX)
            kb.op("vector", "tensor_scalar_mul", red[:, 1:2], red[:, 0:1], -1.0)
            kb.op("scalar", "activation", ex[:], lg[:], AF.Exp,
                  bias=red[:, 1:2], accum_out=red[:, 2:3], wait=[v])
            kb.op("vector", "reciprocal", red[:, 3:4], red[:, 2:3], wait=[s])
            kb.op("vector", "tensor_scalar", pb[:], ex[:], red[:, 3:4], None,
                  ALU.mult)
            kb.op("vector", "max", t8[:], pb[:])
            eq1 = sc1[:, 0:8]; f1 = sc1[:, 8:16]; m1 = sc1[:, 16:24]
            cand = sc1[:, 24:32]; m2 = sc1[:, 32:40]; mor = sc1[:, 40:48]
            kb.op("vector", "tensor_scalar", eq1, pb[:], t8[:, 0:1], None,
                  ALU.is_equal)
            kb.op("vector", "tensor_tensor_scan", f1, eq1, eq1, 0.0,
                  ALU.add, ALU.bypass)
            kb.op("vector", "tensor_single_scalar", f1, f1, 1.0, ALU.is_equal)
            kb.op("vector", "tensor_mul", m1, eq1, f1)
            kb.op("vector", "tensor_scalar", eq1, pb[:], t8[:, 1:2], None,
                  ALU.is_equal)
            kb.op("vector", "tensor_scalar", cand, m1, -1.0, 1.0,
                  ALU.mult, ALU.add)
            kb.op("vector", "tensor_mul", cand, cand, eq1)
            kb.op("vector", "tensor_tensor_scan", f1, cand, cand, 0.0,
                  ALU.add, ALU.bypass)
            kb.op("vector", "tensor_single_scalar", f1, f1, 1.0, ALU.is_equal)
            kb.op("vector", "tensor_mul", m2, cand, f1)
            kb.op("vector", "tensor_add", mor, m1, m2)
            kb.op("vector", "tensor_mul", mor, mor, pb[:])
            kb.op("tensor", "transpose", pG[0:NE1, :], mor, identf[:],
                  wait=[v, iftok])
            kb.op("vector", "tensor_copy", Gm[0:NE1, i * 128:(i + 1) * 128],
                  pG[0:NE1, 0:128], wait=[t])
            kb.op("tensor", "transpose", pG[0:NE1, :], m1, identf[:],
                  wait=[v, iftok])
            kb.op("vector", "tensor_copy", M1m[0:NE1, i * 128:(i + 1) * 128],
                  pG[0:NE1, 0:128], wait=[t])
            kb.op("tensor", "transpose", pG[0:NE1, :], m2, identf[:],
                  wait=[v, iftok])
            kb.op("vector", "tensor_copy", M2m[0:NE1, i * 128:(i + 1) * 128],
                  pG[0:NE1, 0:128], wait=[t])
        kb.dma("scalar", ap["G_em"], Gm[:], str_, wait=[v])
        kb.dma("scalar", ap["M1_em"], M1m[:], str_, wait=[v])
        kb.dma("scalar", ap["M2_em"], M2m[:], str_, wait=[v])
        kb.build(final_rings=[ldr_, wlr, str_])
        p.close()
    return nc, ap


def prep_launch_F(inputs, core, stD, stE):
    d = inputs
    im = {}
    im["x2"] = stD[core]["x2"]
    im["attT"] = stE[core]["attT"]
    im["WoT1"] = np.ascontiguousarray(np.asarray(d["L1_attn_Wo"]).T)
    im["bo_b"] = np.broadcast_to(np.asarray(d["L1_attn_bo"]), (128, D)).copy()
    im["Wr"] = np.ascontiguousarray(np.asarray(d["L1_ffn_Wr"]))
    im["ident"] = np.eye(128, dtype=np.float32).astype(BF)
    im["identf"] = np.eye(128, dtype=np.float32)
    return im



def build_launch_G():
    nc = bass.Bass("TRN2", target_bir_lowering=False, debug=False)
    ap = {}

    def din(name, shape, dt=F32):
        ap[name] = nc.dram_tensor(name, shape, dt, kind="ExternalInput").ap()

    def dout(name, shape, dt=F32):
        ap[name] = nc.dram_tensor(name, shape, dt, kind="ExternalOutput").ap()

    NTOK = NCORE * TPC
    din("g_own", [128, 128])
    din("g_row", [1, NTOK])
    din("xn4_all", [NTOK, D], BF16)
    din("W1", [D, 4096])
    din("W2", [4096, D])
    din("ident", [128, 128], BF16)
    din("onesb", [128, 128], BF16)
    din("iota_t", [128, 128], I32)
    din("trashg", [128, 128])
    dout("expert_out", [PAD1, D], BF16)
    dout("theta", [1, 1])

    invtab = nc.dram_tensor("invtabG", [21632, 1], I32, kind="Internal").ap()
    rowb = nc.dram_tensor("rowbG", [NTOK], F32, kind="Internal").ap()

    ctx = ExitStack()
    with ctx:
        kb = KB(nc, ctx)
        ldr_ = kb.ring("ld", 4)
        wlr = kb.ring("wl", 2)
        str_ = kb.ring("st", 4)
        gr = kb.ring("gq", 4)
        v, t, s, g = kb.e["vector"], kb.e["tensor"], kb.e["scalar"], kb.e["gpsimd"]

        ident = ctx.enter_context(nc.sbuf_tensor("identB", [128, 128], BF16))
        itok = kb.dma("sync", ident[:], ap["ident"], ldr_)
        onesb = ctx.enter_context(nc.sbuf_tensor("onesB", [128, 128], BF16))
        obtok = kb.dma("sync", onesb[:], ap["onesb"], ldr_)

        p1 = Pool(nc)
        go = p1.sb("go", [128, 128], F32)
        mk = p1.sb("mk", [128, 128], F32)
        bi = p1.sb("bi", [128, 8], F32)
        pcb = p1.sb("pcb", [128, 1], BF16)
        selb = p1.sb("selb", [1, 1], BF16)
        selm = p1.sb("selm", [128, 1], I32)
        pcnt = p1.ps("pcnt", [1, 1])
        pbc2 = p1.ps("pbc2", [128, 1])

        gotok = kb.dma("sync", go[:], ap["g_own"], ldr_)
        kb.op("vector", "memset", bi[:, 0:1], 0.0)
        kb.op("vector", "memset", bi[:, 1:2], 1.0)
        kb.op("vector", "memset", bi[:, 2:3], 0.5)
        for it in range(45):
            kb.op("vector", "tensor_scalar", mk[:], go[:], bi[:, 2:3], None,
                  ALU.is_gt, ALU.add, accum_out=bi[:, 4:5], wait=[gotok])
            kb.op("vector", "tensor_copy", pcb[:], bi[:, 4:5])
            kb.op("tensor", "matmul", pcnt[:], pcb[:], onesb[:, 0:1],
                  start=True, stop=True, wait=[v, obtok])
            kb.op("vector", "tensor_single_scalar", bi[0:1, 5:6], pcnt[0:1, 0:1],
                  float(CAP1) + 0.5, ALU.is_le, wait=[t])
            kb.op("vector", "tensor_copy", selb[:], bi[0:1, 5:6])
            kb.op("tensor", "matmul", pbc2[:], onesb[0:1, :], selb[:],
                  start=True, stop=True, wait=[v])
            kb.op("vector", "tensor_copy", selm[:], pbc2[:, 0:1], wait=[t])
            kb.op("vector", "select", bi[:, 7:8], selm[:], bi[:, 2:3],
                  bi[:, 1:2])
            kb.op("vector", "select", bi[:, 0:1], selm[:], bi[:, 0:1],
                  bi[:, 2:3])
            kb.op("vector", "tensor_copy", bi[:, 1:2], bi[:, 7:8])
            kb.op("vector", "tensor_add", bi[:, 2:3], bi[:, 0:1], bi[:, 1:2])
            kb.op("vector", "tensor_scalar_mul", bi[:, 2:3], bi[:, 2:3], 0.5)
        kb.dma("scalar", ap["theta"], bi[0:1, 1:2], str_, wait=[v])

        grow = p1.sb("grow", [1, NTOK], F32)
        mrow = p1.sb("mrow", [1, NTOK], F32)
        crow = p1.sb("crow", [1, NTOK], F32)
        rtok = kb.dma("sync", grow[:], ap["g_row"], ldr_)
        kb.op("vector", "tensor_scalar", mrow[:], grow[:], bi[0:1, 1:2], None,
              ALU.is_gt, wait=[rtok])
        kb.op("vector", "tensor_tensor_scan", crow[:], mrow[:], mrow[:], 0.0,
              ALU.add, ALU.bypass)
        kb.op("vector", "tensor_sub", crow[:], crow[:], mrow[:])
        kb.op("vector", "tensor_scalar", mrow[:], mrow[:], -1e6, 1e6,
              ALU.mult, ALU.add)
        kb.op("vector", "tensor_add", crow[:], crow[:], mrow[:])
        rbtok = kb.dma("scalar", rowb.rearrange("(a x) -> a x", a=1), crow[:],
                       str_, wait=[v])

        iota = p1.sb("iotaG", [128, 128], I32)
        zs = p1.sb("zsG", [128, 41], I32)
        idxf = p1.sb("idxf", [128, 128], F32)
        negi = p1.sb("negiG", [128, 128], F32)
        idxi = p1.sb("idxiG", [128, 128], I32)
        trashg = p1.sb("trashg_s", [128, 128], F32)
        ittok = kb.dma("sync", iota[:], ap["iota_t"], ldr_)
        tgt = kb.dma("sync", trashg[:], ap["trashg"], ldr_)
        kb.op("gpsimd", "memset", zs[:], 0)
        ztok = kb.dma("gpsimd",
                      invtab[0:5248, :].rearrange("(p k) one -> p (k one)", p=128),
                      zs[:], gr, wait=[g])
        ixtok = kb.dma("sync", idxf[:], rowb.rearrange("(a b) -> a b", a=128),
                       ldr_, wait=[rbtok])
        kb.op("vector", "tensor_single_scalar", negi[:], idxf[:], 5120.5,
              ALU.is_lt, wait=[ixtok])
        kb.op("vector", "tensor_mul", idxf[:], idxf[:], negi[:])
        kb.op("vector", "tensor_scalar", negi[:], negi[:], -1.0, 1.0,
              ALU.mult, ALU.add)
        kb.op("vector", "tensor_mul", negi[:], negi[:], trashg[:], wait=[tgt])
        kb.op("vector", "tensor_add", idxf[:], idxf[:], negi[:])
        kb.op("vector", "tensor_copy", idxi[:], idxf[:])
        vsc = (v, v.n)
        scr = kb.ring("scrG", 8)
        for tt_ in range(128):
            kb.idma("gpsimd", invtab,
                    bass.IndirectOffsetOnAxis(ap=idxi[:, tt_:tt_ + 1], axis=0),
                    iota[:, tt_:tt_ + 1], None, ring=scr,
                    wait=[vsc, ittok, ztok] if tt_ == 0 else [])
        scb_tok = scr.final_tokens()
        p1.close()

        fence_w = kb.fence() + gr.final_tokens() + ldr_.final_tokens() \
            + str_.final_tokens()
        fence = fence_w + scb_tok
        p2 = Pool(nc)
        w1b = p2.sb("w1b", [128, 8 * 4096], BF16)
        w2b = p2.sb("w2b", [128, 32 * D], BF16)
        bufs = dict(
            idxc=[p2.sb(f"idxc{i}", [128, 1], I32) for i in range(4)],
            xg=[p2.sb(f"xg{i}", [128, D], BF16) for i in range(4)],
            xgT=p2.sb("xgT", [128, 8 * 512], BF16),
            hT=p2.sb("hT", [128, 32 * 512], BF16),
            eot=p2.sb("eot", [128, 2 * 512], BF16),
            pT2=[p2.ps(f"pT2{i}", [128, 128], BF16) for i in range(2)],
            ps1=[p2.ps(f"ps1{i}", [128, 512]) for i in range(2)],
            ps2=[p2.ps(f"ps2{i}", [128, 512]) for i in range(2)],
            ident=ident, itok=itok,
        )
        zr = p2.sb("zr", [1, D], BF16)
        kb.op("vector", "memset", zr[:], 0.0, wait=fence)
        kb.dma("scalar", ap["expert_out"][PAD1 - 1:PAD1, :], zr[:], str_,
               wait=[v])
        w1tok = emit_wload_g(kb, ap["W1"], w1b, wlr, 4096, fence)
        w2tok = emit_wload_g(kb, ap["W2"], w2b, wlr, D, fence)
        blocks = [(b * 512, 4) for b in range(10)]
        emit_ffn_blocks(kb, ap["xn4_all"], invtab, ap["expert_out"],
                        w1b, w2b, bufs, AF.Relu, blocks,
                        w1tok, w2tok, fence, ldr_, gr, str_)
        kb.build(final_rings=[ldr_, wlr, str_, gr])
        p2.close()
    return nc, ap


def prep_launch_G(inputs, core, stF):
    d = inputs
    im = {}
    G_all = np.concatenate([stF[j]["G_em"] for j in range(NCORE)], 1)
    im["g_own"] = np.ascontiguousarray(G_all[core].reshape(128, 128))
    im["g_row"] = np.ascontiguousarray(G_all[core].reshape(1, -1))
    im["xn4_all"] = np.concatenate([stF[j]["xn4"] for j in range(NCORE)], 0)
    im["W1"] = np.ascontiguousarray(np.asarray(d["L1_ffn_W1"][core]))
    im["W2"] = np.ascontiguousarray(np.asarray(d["L1_ffn_W2"][core]))
    im["ident"] = np.eye(128, dtype=np.float32).astype(BF)
    im["onesb"] = np.ones((128, 128), np.float32).astype(BF)
    p = np.arange(128)
    im["iota_t"] = (p[:, None] * 128 + np.arange(128)[None, :]).astype(np.int32)
    im["trashg"] = (5152 + np.arange(128)[:, None] * 128
                    + np.arange(128)[None, :]).astype(np.float32)
    return im



def build_launch_Ha():
    nc = bass.Bass("TRN2", target_bir_lowering=False, debug=False)
    ap = {}

    def din(name, shape, dt=F32):
        ap[name] = nc.dram_tensor(name, shape, dt, kind="ExternalInput").ap()

    def dout(name, shape, dt=F32):
        ap[name] = nc.dram_tensor(name, shape, dt, kind="ExternalOutput").ap()

    NTOK = NCORE * TPC
    din("G_all", [NE1, NTOK])
    din("M1_all", [NE1, NTOK])
    din("M2_all", [NE1, NTOK])
    din("theta_all", [NE1, 1])
    din("eoff8", [NE1, 1])
    dout("slota_row", [1, NTOK]); dout("wa_row", [1, NTOK])
    dout("slotb_row", [1, NTOK]); dout("wb_row", [1, NTOK])

    SEG = 4096

    ctx = ExitStack()
    with ctx:
        kb = KB(nc, ctx)
        ldr_ = kb.ring("ld", 6)
        str_ = kb.ring("st", 4)
        v, t, s = kb.e["vector"], kb.e["tensor"], kb.e["scalar"]

        p = Pool(nc)
        Gs = p.sb("Gs", [NE1, SEG], F32)
        M1s = p.sb("M1s", [NE1, SEG], F32)
        M2s = p.sb("M2s", [NE1, SEG], F32)
        Mk = p.sb("Mk", [NE1, SEG], F32)
        cum = p.sb("cum", [NE1, SEG], F32)
        wsel = p.sb("wsel", [NE1, SEG], F32)
        tmp = p.sb("tmp", [NE1, SEG], F32)
        thc = p.sb("thc", [NE1, 1], F32)
        eoffc = p.sb("eoffc8", [NE1, 1], F32)
        carry = p.sb("carry", [NE1, 2], F32)
        onesf = p.sb("onesf_h", [NE1, 1], F32)
        stg = p.sb("stg", [1, 1024], F32)
        pcs = p.ps("pcs", [1, 512])

        ttok = kb.dma("sync", thc[:], ap["theta_all"], ldr_)
        etok = kb.dma("sync", eoffc[:], ap["eoff8"], ldr_)
        kb.op("vector", "memset", onesf[:], 1.0)
        kb.op("vector", "memset", carry[:, 0:1], 0.0)
        st_tok = []

        def colsums(src_ap, out_dram, seg):
            nonlocal st_tok
            for cg in range(SEG // 512):
                kb.op("tensor", "matmul", pcs[:], onesf[:],
                      src_ap[0:NE1, cg * 512:(cg + 1) * 512],
                      start=True, stop=True, wait=[(v, v.n)])
                sl = (cg % 2) * 512
                kb.op("vector", "tensor_copy", stg[:, sl:sl + 512], pcs[:],
                      wait=[(t, kb.e["tensor"].n)] + st_tok)
                st_tok = [kb.dma(
                    "scalar",
                    out_dram[0:1, seg * SEG + cg * 512: seg * SEG + (cg + 1) * 512],
                    stg[:, sl:sl + 512], str_, wait=[v])]

        for seg in range(NTOK // SEG):
            gt = kb.dma("sync", Gs[:], ap["G_all"][:, seg * SEG:(seg + 1) * SEG],
                        ldr_, wait=[v, t])
            m1t = kb.dma("sync", M1s[:], ap["M1_all"][:, seg * SEG:(seg + 1) * SEG],
                         ldr_, wait=[v, t])
            m2t = kb.dma("sync", M2s[:], ap["M2_all"][:, seg * SEG:(seg + 1) * SEG],
                         ldr_, wait=[v, t])
            kb.op("vector", "tensor_scalar", Mk[:], Gs[:], thc[:, 0:1], None,
                  ALU.is_gt, wait=[gt, ttok])
            kb.op("vector", "tensor_tensor_scan", cum[:], Mk[:], Mk[:],
                  carry[:, 0:1], ALU.add, ALU.bypass)
            kb.op("vector", "tensor_copy", carry[:, 0:1],
                  cum[:, SEG - 1:SEG])
            kb.op("vector", "tensor_sub", cum[:], cum[:], Mk[:])
            kb.op("vector", "tensor_scalar", cum[:], cum[:], eoffc[:, 0:1],
                  None, ALU.add, wait=[etok])
            kb.op("vector", "tensor_mul", wsel[:], M1s[:], Mk[:], wait=[m1t])
            kb.op("vector", "tensor_mul", tmp[:], wsel[:], cum[:])
            colsums(tmp, ap["slota_row"], seg)
            kb.op("vector", "tensor_mul", tmp[:], wsel[:], Gs[:])
            colsums(tmp, ap["wa_row"], seg)
            kb.op("vector", "tensor_mul", wsel[:], M2s[:], Mk[:], wait=[m2t])
            kb.op("vector", "tensor_mul", tmp[:], wsel[:], cum[:])
            colsums(tmp, ap["slotb_row"], seg)
            kb.op("vector", "tensor_mul", tmp[:], wsel[:], Gs[:])
            colsums(tmp, ap["wb_row"], seg)
        kb.build(final_rings=[ldr_, str_])
        p.close()
    return nc, ap


def prep_launch_Ha(inputs, core, stF, stG):
    im = {}
    im["G_all"] = np.concatenate([stF[j]["G_em"] for j in range(NCORE)], 1)
    im["M1_all"] = np.concatenate([stF[j]["M1_em"] for j in range(NCORE)], 1)
    im["M2_all"] = np.concatenate([stF[j]["M2_em"] for j in range(NCORE)], 1)
    im["theta_all"] = np.concatenate([stG[j]["theta"] for j in range(NCORE)], 0)
    im["eoff8"] = (np.arange(NE1) * PAD1).astype(np.float32).reshape(NE1, 1)
    return im



def build_launch_Hb():
    nc = bass.Bass("TRN2", target_bir_lowering=False, debug=False)
    ap = {}

    def din(name, shape, dt=F32):
        ap[name] = nc.dram_tensor(name, shape, dt, kind="ExternalInput").ap()

    def dout(name, shape, dt=F32):
        ap[name] = nc.dram_tensor(name, shape, dt, kind="ExternalOutput").ap()

    din("x3", [TPC, D])
    din("expert_out_all", [NE1 * PAD1, D], BF16)
    din("slota", [128, NTT]); din("wa", [128, NTT])
    din("slotb", [128, NTT]); din("wb", [128, NTT])
    dout("out", [TPC, D])

    ctx = ExitStack()
    with ctx:
        kb = KB(nc, ctx)
        ldr_ = kb.ring("ld", 4)
        str_ = kb.ring("st", 4)
        gr = kb.ring("gq", 4)
        v, t, s, g = kb.e["vector"], kb.e["tensor"], kb.e["scalar"], kb.e["gpsimd"]

        p = Pool(nc)
        x_a = p.sb("x_a", [128, D], F32)
        x_b = p.sb("x_b", [128, D], F32)
        tabs = p.sb("tabs", [128, 4 * NTT], F32)
        idxa = p.sb("idxa", [128, 1], I32)
        idxb = p.sb("idxb", [128, 1], I32)
        ya = p.sb("ya", [128, D], BF16)
        yb = p.sb("yb", [128, D], BF16)
        acc = p.sb("acc", [128, D], F32)
        outt = p.sb("outt", [128, D], F32)

        ta = kb.dma("sync", tabs[:, 0:NTT], ap["slota"], ldr_)
        tb = kb.dma("sync", tabs[:, NTT:2 * NTT], ap["slotb"], ldr_)
        tc = kb.dma("sync", tabs[:, 2 * NTT:3 * NTT], ap["wa"], ldr_)
        td = kb.dma("sync", tabs[:, 3 * NTT:4 * NTT], ap["wb"], ldr_)
        ga_war = []
        for i in range(NTT):
            xb = x_a if i % 2 == 0 else x_b
            xtok = kb.dma("sync", xb[:], ap["x3"][i * 128:(i + 1) * 128, :], ldr_,
                          wait=[v] if i >= 2 else [])
            kb.op("vector", "tensor_copy", idxa[:], tabs[:, i:i + 1],
                  wait=[ta] + ga_war)
            kb.op("vector", "tensor_copy", idxb[:], tabs[:, NTT + i:NTT + i + 1],
                  wait=[tb])
            gt1 = kb.idma("gpsimd", ya[:], None, ap["expert_out_all"],
                          bass.IndirectOffsetOnAxis(ap=idxa[:], axis=0),
                          gr, wait=[v, s])
            gt2 = kb.idma("gpsimd", yb[:], None, ap["expert_out_all"],
                          bass.IndirectOffsetOnAxis(ap=idxb[:], axis=0),
                          gr, wait=[v, s])
            ga_war = [gt1, gt2]
            kb.op("vector", "tensor_scalar", acc[:], ya[:],
                  tabs[:, 2 * NTT + i:2 * NTT + i + 1], None, ALU.mult,
                  wait=[gt1, tc, s])
            kb.op("vector", "scalar_tensor_tensor", acc[:], yb[:],
                  tabs[:, 3 * NTT + i:3 * NTT + i + 1], acc[:],
                  ALU.mult, ALU.add, wait=[gt2, td])
            kb.op("vector", "tensor_add", outt[:], acc[:], xb[:],
                  wait=[xtok])
            kb.dma("scalar", ap["out"][i * 128:(i + 1) * 128, :], outt[:], str_,
                   wait=[v])
        kb.build(final_rings=[ldr_, str_, gr])
        p.close()
    return nc, ap


def prep_launch_Hb(inputs, core, stF2, stG, tabs):
    im = {}
    im["x3"] = stF2[core]["x3"]
    im["expert_out_all"] = np.concatenate(
        [stG[j]["expert_out"] for j in range(NCORE)], 0)
    for k2, row in (("slota", "slota_row"), ("wa", "wa_row"),
                    ("slotb", "slotb_row"), ("wb", "wb_row")):
        r = np.asarray(tabs[row]).reshape(-1)[core * TPC:(core + 1) * TPC]
        im[k2] = np.ascontiguousarray(r.reshape(NTT, 128).T)
    return im


GAMMA0 = (1.0 - np.exp2(-5.0 - np.arange(H0, dtype=np.float64)))


def prep_launch_B(inputs, core, stA):
    d = inputs
    im = {}
    im["x"] = np.ascontiguousarray(d["L0_attn_x"].reshape(-1, D)[core * TPC:(core + 1) * TPC])
    im["o_part"] = stA[core]["o_part"]
    im["qTd"] = stA[core]["qTd"]
    im["gTd"] = stA[core]["gTd"]
    im["P_all"] = np.concatenate([stA[j]["P"] for j in range(NCORE)], 0)
    coef = np.zeros((128, NCORE * H0), np.float32)
    b0 = (core // 4) * 4
    for j in range(b0, core):
        for h in range(H0):
            coef[:, j * H0 + h] = (GAMMA0[h] ** SC) ** (4 * (core - 1 - j))
    im["coefP"] = coef
    im["WoT"] = np.ascontiguousarray(np.asarray(d["L0_attn_Wo"]).T)
    im["gwT"] = np.ascontiguousarray(np.asarray(d["L0_ffn_gate_w"]).T)
    im["gnw"] = np.ascontiguousarray(np.asarray(d["L0_attn_gnorm_w"]).reshape(4, 128).T)
    tloc = np.arange(TPC, dtype=np.float32)
    cf = np.stack([g ** (tloc + 1) for g in GAMMA0.astype(np.float32)])
    im["crossfullT"] = np.repeat(cf[:, None, :], 128, axis=1).reshape(H0 * 128, TPC).astype(BF)
    im["ident"] = np.eye(128, dtype=np.float32).astype(BF)
    im["onesf"] = np.ones((128, 128), np.float32)
    return im


def kernel(**inputs):
    cores = list(range(NCORE))
    times = []

    def run(nc, ims):
        r = run_bass_kernel_spmd(nc, ims, core_ids=cores)
        times.append(r.exec_time_ns)
        try:
            with open("/tmp/klaunch_times.txt", "w") as f:
                f.write(repr(times))
        except OSError:
            pass
        return r.results

    ncA, _ = build_launch_A()
    stA = run(ncA, [prep_launch_A(inputs, c) for c in cores])
    ncB, _ = build_launch_B()
    stB = run(ncB, [prep_launch_B(inputs, c, stA) for c in cores])
    ncC, _ = build_launch_C()
    stC = run(ncC, [prep_launch_C(inputs, c, stB) for c in cores])
    ncD, _ = build_launch_D()
    stD = run(ncD, [prep_launch_D(inputs, c, stB, stC) for c in cores])
    ncE, _ = build_launch_E()
    stE = run(ncE, [prep_launch_E(inputs, c, stD) for c in cores])
    ncF, _ = build_launch_F()
    stF = run(ncF, [prep_launch_F(inputs, c, stD, stE) for c in cores])
    ncG, _ = build_launch_G()
    stG = run(ncG, [prep_launch_G(inputs, c, stF) for c in cores])
    ncHa, _ = build_launch_Ha()
    stHa = run(ncHa, [prep_launch_Ha(inputs, c, stF, stG) for c in cores])
    tabs = stHa[0]
    ncHb, _ = build_launch_Hb()
    stHb = run(ncHb, [prep_launch_Hb(inputs, c, stF, stG, tabs) for c in cores])
    out = np.concatenate([stHb[c]["out"] for c in cores], 0)
    return np.ascontiguousarray(out.reshape(2, 8192, D).astype(np.float32))


# version 11 (final)
# speedup vs baseline: 1.2327x; 1.0167x over previous
import numpy as np
import ml_dtypes
from contextlib import ExitStack

import concourse.bass as bass
import concourse.mybir as mybir
from concourse.bass_utils import run_bass_kernel_spmd

F32 = mybir.dt.float32
BF16 = mybir.dt.bfloat16
I32 = mybir.dt.int32
AF = mybir.ActivationFunctionType
ALU = mybir.AluOpType
AXX = mybir.AxisListType.X
BF = ml_dtypes.bfloat16

NCORE = 8
TPC = 2048
NTT = TPC // 128
D = 1024
SC = 512
NSC = TPC // SC
H0, HK0, HV0 = 4, 256, 512
H1, DH1 = 16, 64
NE0, CAP0, PAD0 = 16, 1280, 1281
NE1, CAP1, PAD1 = 8, 5120, 5121
EPS = 1e-5

ENGINES = ("sync", "vector", "tensor", "scalar", "gpsimd")


class Ctr:
    __slots__ = ("h", "n", "name")

    def __init__(self, h, name):
        self.h = h
        self.n = 0
        self.name = name


class Ring:
    def __init__(self, ctrs):
        self.ctrs = ctrs
        self.i = 0

    def next(self):
        c = self.ctrs[self.i]
        self.i = (self.i + 1) % len(self.ctrs)
        return c

    def final_tokens(self):
        return [(c, c.n) for c in self.ctrs if c.n > 0]


class KB:

    def __init__(self, nc, ctx):
        self.nc = nc
        self.ctx = ctx
        self.ops = {e: [] for e in ENGINES}
        self.e = {}
        for e in ENGINES:
            self.e[e] = self.sem("c_" + e)

    def sem(self, name):
        return Ctr(self.ctx.enter_context(self.nc.semaphore(name)), name)

    def ring(self, name, n):
        return Ring([self.sem(f"{name}{i}") for i in range(n)])

    def op(self, eng, meth, *args, wait=(), inc="auto", incn=1, **kw):
        wl = list(wait)
        if eng in ("vector", "scalar", "gpsimd") and meth not in (
                "dma_start", "indirect_dma_start"):
            wl.append((self.e[eng], self.e[eng].n))
        best = {}
        for w in wl:
            if isinstance(w, tuple):
                c, val = w
            else:
                c, val = w, w.n
            if val > 0 and val > best.get(id(c), (None, -1))[1]:
                best[id(c)] = (c.h, val)
        waits = list(best.values())
        if inc == "auto":
            inc = self.e[eng]
        if inc is not None:
            inc.n += incn
            inc_spec = (inc.h, incn)
        else:
            inc_spec = None
        self.ops[eng].append((waits, meth, args, kw, inc_spec))

    def dma(self, eng, out, in_, ring, wait=(), **kw):
        ctr = ring.next()
        wl = list(wait)
        if ctr.n > 0:
            wl.append((ctr, ctr.n))
        self.op(eng, "dma_start", wait=wl, inc=ctr, incn=16, out=out, in_=in_, **kw)
        return (ctr, ctr.n)

    def idma(self, eng, out, out_off, in_, in_off, ring=None, wait=(), ctr=None,
             **kw):
        wl = list(wait)
        if ring is not None:
            ctr = ring.next()
            if ctr.n > 0:
                wl.append((ctr, ctr.n))
        self.op(eng, "indirect_dma_start", out, out_off, in_, in_off,
                wait=wl, inc=ctr, incn=16, **kw)
        return (ctr, ctr.n)

    def fence(self):
        return [(c, c.n) for c in self.e.values() if c.n > 0]

    def build(self, final_rings=()):
        waits = []
        for r in final_rings:
            waits += r.final_tokens()
        waits += self.fence()
        self.op("sync", "nop", wait=waits, inc=None)
        with self.nc.Block() as block:
            for eng in ENGINES:
                ops = self.ops[eng]
                if not ops:
                    continue

                def body(q, ops=ops):
                    for waits, meth, args, kw, inc_spec in ops:
                        for h, v in waits:
                            q.wait_ge(h, v)
                        ins = getattr(q, meth)(*args, **kw)
                        if inc_spec is not None:
                            ins.then_inc(inc_spec[0], inc_spec[1])

                getattr(block, eng)(body)


def fm_dram(ap):
    return ap.rearrange("(c p) t -> p c t", p=128)


class Pool:

    def __init__(self, nc):
        self.nc = nc
        self.es = ExitStack()

    def sb(self, name, shape, dt=F32):
        return self.es.enter_context(self.nc.sbuf_tensor(name, shape, dt))

    def ps(self, name, shape, dt=F32):
        return self.es.enter_context(self.nc.psum_tensor(name, shape, dt))

    def close(self):
        self.es.close()



def emit_ln_tile(kb, xb, xn_tile, red, xs_sc, wait_x, epsc, war=()):
    v, s = kb.e["vector"], kb.e["scalar"]
    kb.op("vector", "reduce_sum", red[:, 0:1], xb[:], AXX, wait=list(wait_x) + [s])
    kb.op("vector", "tensor_scalar_mul", red[:, 1:2], red[:, 0:1], -1.0 / D)
    kb.op("scalar", "activation", xs_sc[:], xb[:], AF.Square,
          accum_out=red[:, 2:3], wait=list(wait_x) + [v])
    kb.op("vector", "tensor_mul", red[:, 3:4], red[:, 1:2], red[:, 1:2], wait=[s])
    kb.op("vector", "tensor_scalar", red[:, 4:5], red[:, 2:3], 1.0 / D,
          red[:, 3:4], ALU.mult, ALU.subtract)
    kb.op("scalar", "activation", red[:, 5:6], red[:, 4:5], AF.Ln, bias=epsc, wait=[v])
    kb.op("scalar", "activation", red[:, 6:7], red[:, 5:6], AF.Exp, scale=-0.5)
    kb.op("vector", "tensor_scalar", xn_tile[:], xb[:], red[:, 1:2],
          red[:, 6:7], ALU.add, ALU.mult, wait=[s, kb.e["tensor"]] + list(war))


def emit_transpose_to_fm(kb, src_tile, dstT, tile_idx, ident, pT, itok,
                         nchunks=8, tcols=TPC, war=None):
    v, t = kb.e["vector"], kb.e["tensor"]
    banks = pT if isinstance(pT, list) else [pT]
    if war is None:
        war = {}
    for c in range(nchunks):
        b = c % len(banks)
        bw = [(v, v.n)] if c < len(banks) else war.get(b, [(v, v.n)])
        kb.op("tensor", "transpose", banks[b][:],
              src_tile[:, c * 128:(c + 1) * 128],
              ident[:], wait=[itok] + bw)
        kb.op("vector", "tensor_copy",
              dstT[:, c * tcols + tile_idx * 128: c * tcols + (tile_idx + 1) * 128],
              banks[b][:], wait=[(t, t.n)])
        war[b] = [(v, v.n)]



def build_launch_A():
    nc = bass.Bass("TRN2", target_bir_lowering=False, debug=False)
    ap = {}

    def din(name, shape, dt=F32):
        ap[name] = nc.dram_tensor(name, shape, dt, kind="ExternalInput").ap()

    def dout(name, shape, dt=F32):
        ap[name] = nc.dram_tensor(name, shape, dt, kind="ExternalOutput").ap()

    din("x", [TPC, D])
    din("WqT", [D, D]); din("WkT", [D, D])
    din("WvT", [D, 2048]); din("WgT", [D, 2048])
    din("ident", [128, 128], BF16)
    din("r0_cos", [128, TPC]); din("r0_sin", [128, TPC])
    din("r0_cos_q", [128, TPC]); din("r0_sin_q", [128, TPC])
    din("maskT", [H0 * SC, SC], BF16)
    din("kdec", [128, H0 * 4])
    din("crossT", [H0 * 128, SC], BF16)
    dout("qTd", [D, TPC], BF16)
    dout("kTd", [D, TPC], BF16)
    dout("vd", [TPC, 2048], BF16)
    dout("gTd", [2048, TPC], BF16)
    dout("o_part", [2048, TPC], BF16)
    dout("P", [256, H0 * HV0])

    gamma = (1.0 - np.exp2(-5.0 - np.arange(H0, dtype=np.float64)))
    gSC = [float(g) ** SC for g in gamma]

    ctx = ExitStack()
    with ctx:
        kb = KB(nc, ctx)
        ldr_ = kb.ring("ld", 4)
        wlr = kb.ring("wl", 2)
        str_ = kb.ring("st", 4)
        v, t, s = kb.e["vector"], kb.e["tensor"], kb.e["scalar"]

        ident = ctx.enter_context(nc.sbuf_tensor("identB", [128, 128], BF16))
        itok = kb.dma("sync", ident[:], ap["ident"], ldr_)
        S_f = ctx.enter_context(nc.sbuf_tensor("S_f", [128, H0 * 2 * 512], F32))
        S_b = ctx.enter_context(nc.sbuf_tensor("S_b", [128, 2 * 512], BF16))

        p12 = Pool(nc)
        xnT = p12.sb("xnT", [128, 8 * TPC], BF16)
        wbuf = p12.sb("wbuf", [128, 8 * 2048], BF16)
        ph1 = p12.sb("ph1", [128, 4 * 512], F32)
        x_a = p12.sb("x_a", [128, D], F32)
        x_b = p12.sb("x_b", [128, D], F32)
        xs_sc = p12.sb("xs_sc", [128, D], F32)
        red = p12.sb("red", [128, 8], F32)
        epsc = p12.sb("epsc", [128, 1], F32)
        xn_tile = p12.sb("xn_tile", [128, D], BF16)
        rc = p12.sb("rc", [128, TPC], F32)
        rs = p12.sb("rs", [128, TPC], F32)
        evb = p12.sb("evb", [128, 4 * 512], BF16)
        pT = [p12.ps(f"pT{i}", [128, 128], BF16) for i in range(2)]
        pj = [p12.ps(f"pj{i}", [128, 512], F32) for i in range(4)]
        kb.op("vector", "memset", epsc[:], EPS)

        twar = {}
        for i in range(NTT):
            xb = x_a if i % 2 == 0 else x_b
            xtok = kb.dma("sync", xb[:], ap["x"][i * 128:(i + 1) * 128, :], ldr_,
                          wait=[v] if i >= 2 else [])
            emit_ln_tile(kb, xb, xn_tile, red, xs_sc, [xtok], epsc[:, 0:1])
            emit_transpose_to_fm(kb, xn_tile, xnT, i, ident, pT, itok, war=twar)

        def load_w(name, ncols):
            w3 = ap[name].rearrange("(c p) o -> p c o", p=128)
            dst = wbuf[:].rearrange("p (c o) -> p c o", c=8)[:, :, 0:ncols]
            return kb.dma("gpsimd", dst, w3, wlr, wait=[t])

        def wtile(c, o):
            return wbuf[:, c * 2048 + o * 128: c * 2048 + (o + 1) * 128]

        ev_tok = {}

        def proj_pair_rope(ncols, cos_ap, sin_ap, out_dram, wtok):
            ctok = kb.dma("sync", rc[:], cos_ap, ldr_, wait=[v])
            stok = kb.dma("sync", rs[:], sin_ap, ldr_, wait=[v])
            for pair in range(ncols // 256):
                for cg in range(4):
                    pa, pb = pj[2 * (cg % 2)], pj[2 * (cg % 2) + 1]
                    bw = rope_war.get(cg % 2, [(v, v.n)])
                    for (oc, pp) in ((2 * pair, pa), (2 * pair + 1, pb)):
                        for c in range(8):
                            kb.op("tensor", "matmul", pp[:], wtile(c, oc),
                                  xnT[:, c * TPC + cg * 512: c * TPC + (cg + 1) * 512],
                                  start=(c == 0), stop=(c == 7),
                                  wait=[wtok] + bw if c == 0 else [])
                    cs = rc[:, cg * 512:(cg + 1) * 512]
                    sn = rs[:, cg * 512:(cg + 1) * 512]
                    t1, t2 = ph1[:, 0:512], ph1[:, 512:1024]
                    t3, t4 = ph1[:, 1024:1536], ph1[:, 1536:2048]
                    kb.op("vector", "tensor_mul", t1, pa[:], cs, wait=[t, ctok, stok])
                    kb.op("vector", "tensor_mul", t2, pb[:], sn)
                    kb.op("vector", "tensor_mul", t3, pa[:], sn)
                    kb.op("vector", "tensor_mul", t4, pb[:], cs)
                    sl = (cg % 2) * 1024
                    ea, eb = evb[:, sl:sl + 512], evb[:, sl + 512:sl + 1024]
                    kb.op("vector", "tensor_sub", ea, t1, t2,
                          wait=ev_tok.get(sl, []))
                    kb.op("vector", "tensor_add", eb, t4, t3,
                          wait=ev_tok.get(sl + 512, []))
                    d3 = fm_dram(out_dram)
                    ta = kb.dma("scalar", d3[:, 2 * pair, cg * 512:(cg + 1) * 512],
                                ea, str_, wait=[v])
                    tb = kb.dma("scalar",
                                d3[:, 2 * pair + 1, cg * 512:(cg + 1) * 512],
                                eb, str_, wait=[v])
                    ev_tok[sl] = [ta]
                    ev_tok[sl + 512] = [tb]
                    rope_war[cg % 2] = [(v, v.n)]

        rope_war = {}
        wtok = load_w("WqT", 1024)
        proj_pair_rope(1024, ap["r0_cos_q"], ap["r0_sin_q"], ap["qTd"], wtok)
        wtok = load_w("WkT", 1024)
        proj_pair_rope(1024, ap["r0_cos"], ap["r0_sin"], ap["kTd"], wtok)

        wtok = load_w("WvT", 2048)
        pj_war = {}
        for i in range(NTT):
            for og in range(4):
                pp = pj[og]
                for c in range(8):
                    kb.op("tensor", "matmul", pp[:],
                          xnT[:, c * TPC + i * 128: c * TPC + (i + 1) * 128],
                          wbuf[:, c * 2048 + og * 512: c * 2048 + (og + 1) * 512],
                          start=(c == 0), stop=(c == 7),
                          wait=[wtok] + pj_war.get(og, [(v, v.n)])
                          if c == 0 else [])
                sl = (og % 2) * 512
                ea = evb[:, sl:sl + 512]
                war = ev_tok.get(sl, [])
                kb.op("vector", "tensor_copy", ea, pp[:], wait=[(t, t.n)] + war)
                pj_war[og] = [(v, v.n)]
                ev_tok[sl] = [kb.dma(
                    "scalar", ap["vd"][i * 128:(i + 1) * 128, og * 512:(og + 1) * 512],
                    ea, str_, wait=[v])]

        wtok = load_w("WgT", 2048)
        pj_war = {}
        for oc in range(16):
            for cg in range(4):
                pp = pj[cg % 4]
                for c in range(8):
                    kb.op("tensor", "matmul", pp[:], wtile(c, oc),
                          xnT[:, c * TPC + cg * 512: c * TPC + (cg + 1) * 512],
                          start=(c == 0), stop=(c == 7),
                          wait=[wtok] + pj_war.get(cg % 4, [(v, v.n)])
                          if c == 0 else [])
                sl = (cg % 2) * 512
                ea = evb[:, sl:sl + 512]
                war = ev_tok.get(sl, [])
                kb.op("vector", "tensor_copy", ea, pp[:], wait=[(t, t.n)] + war)
                pj_war[cg % 4] = [(v, v.n)]
                d3 = fm_dram(ap["gTd"])
                ev_tok[sl] = [kb.dma("scalar", d3[:, oc, cg * 512:(cg + 1) * 512],
                                     ea, str_, wait=[v])]

        p12.close()

        fence = kb.fence() + str_.final_tokens() + ldr_.final_tokens() \
            + wlr.final_tokens()
        p3 = Pool(nc)
        qsb_d = p3.sb("qsb", [128, 2 * 2 * 512], BF16)
        ksb_d = p3.sb("ksb", [128, 2 * 2 * 512], BF16)
        vsb_d = p3.sb("vsb", [128, 2 * 4 * 512], BF16)
        khat_d = p3.sb("khat", [128, 2 * 4 * 256], BF16)
        qhat_d = p3.sb("qhat", [128, 2 * 2 * 512], BF16)
        attnT_d = p3.sb("attnT", [128, 2 * 4 * 512], BF16)
        maskb = p3.sb("maskb", [128, H0 * 4 * 512], BF16)
        crossb = p3.sb("crossb", [128, H0 * 512], BF16)
        kdecb = p3.sb("kdecb", [128, H0 * 4], F32)
        oev = p3.sb("oev", [128, 4 * 512], BF16)
        pTr = [p3.ps(f"pTr{i}", [128, 128], BF16) for i in range(2)]
        psc = [p3.ps(f"psc{i}", [128, 512], F32) for i in range(2)]
        po = [p3.ps(f"po{i}", [128, 512], F32) for i in range(2)]
        pA = [p3.ps(f"pA{i}", [128, 512], F32) for i in range(2)]

        mk3 = ap["maskT"].rearrange("(h J p) i -> p (h J) i", p=128, h=H0)
        mtok = kb.dma("sync", maskb[:].rearrange("p (x i) -> p x i", i=512), mk3,
                      ldr_, wait=fence)
        cr3 = ap["crossT"].rearrange("(h p) i -> p h i", p=128)
        crtok = kb.dma("sync", crossb[:].rearrange("p (h i) -> p h i", h=H0), cr3,
                       ldr_, wait=fence)
        kdtok = kb.dma("sync", kdecb[:], ap["kdec"], ldr_, wait=fence)
        kb.op("vector", "memset", S_f[:], 0.0, wait=fence)

        q3 = fm_dram(ap["qTd"])
        k3 = fm_dram(ap["kTd"])
        ost = {}
        tr_war = {}
        sc_war = {}
        po_war = {}
        pA_war = {}
        iter_tok = {}
        for sc in range(NSC):
            for h in range(H0):
                it = sc * H0 + h
                bsel = it % 2
                qsb = qsb_d[:, bsel * 1024:(bsel + 1) * 1024]
                ksb = ksb_d[:, bsel * 1024:(bsel + 1) * 1024]
                vsb = vsb_d[:, bsel * 2048:(bsel + 1) * 2048]
                khat = khat_d[:, bsel * 1024:(bsel + 1) * 1024]
                qhat = qhat_d[:, bsel * 1024:(bsel + 1) * 1024]
                attnT = attnT_d[:, bsel * 2048:(bsel + 1) * 2048]
                lwar = iter_tok.get(bsel, [])
                toks = []
                for c in range(2):
                    toks.append(kb.dma("sync", qsb[:, c * 512:(c + 1) * 512],
                                       q3[:, h * 2 + c, sc * 512:(sc + 1) * 512],
                                       ldr_, wait=lwar))
                    toks.append(kb.dma("sync", ksb[:, c * 512:(c + 1) * 512],
                                       k3[:, h * 2 + c, sc * 512:(sc + 1) * 512],
                                       ldr_, wait=lwar))
                for jt in range(4):
                    toks.append(kb.dma("sync", vsb[:, jt * 512:(jt + 1) * 512],
                                       ap["vd"][sc * 512 + jt * 128:
                                                sc * 512 + (jt + 1) * 128,
                                                h * 512:(h + 1) * 512],
                                       ldr_, wait=lwar))

                for jt in range(4):
                    for c in range(2):
                        b = (jt * 2 + c) % 2
                        kb.op("tensor", "transpose", pTr[b][:],
                              ksb[:, c * 512 + jt * 128: c * 512 + (jt + 1) * 128],
                              ident[:],
                              wait=toks + [itok] + tr_war.get(b, [(v, v.n)]))
                        kb.op("vector", "tensor_scalar_mul",
                              khat[:, jt * 256 + c * 128: jt * 256 + (c + 1) * 128],
                              pTr[b][:], kdecb[:, h * 4 + jt: h * 4 + jt + 1],
                              wait=[(t, t.n), kdtok])
                        tr_war[b] = [(v, v.n)]

                kb.op("vector", "tensor_copy", S_b[:, 0:512],
                      S_f[:, (h * 2) * 512:(h * 2 + 1) * 512], wait=[t])
                kb.op("vector", "tensor_copy", S_b[:, 512:1024],
                      S_f[:, (h * 2 + 1) * 512:(h * 2 + 2) * 512])
                for c in range(2):
                    kb.op("vector", "tensor_mul", qhat[:, c * 512:(c + 1) * 512],
                          qsb[:, c * 512:(c + 1) * 512],
                          crossb[:, h * 512:(h + 1) * 512], wait=toks + [crtok])

                for jt in range(4):
                    b = jt % 2
                    for c in range(2):
                        kb.op("tensor", "matmul", psc[b][:],
                              ksb[:, c * 512 + jt * 128: c * 512 + (jt + 1) * 128],
                              qsb[:, c * 512:(c + 1) * 512],
                              start=(c == 0), stop=(c == 1),
                              wait=sc_war.get(b, [(v, v.n)]) + toks
                              if c == 0 else [])
                    kb.op("vector", "tensor_mul", attnT[:, jt * 512:(jt + 1) * 512],
                          psc[b][:],
                          maskb[:, (h * 4 + jt) * 512:(h * 4 + jt + 1) * 512],
                          wait=[(t, t.n), mtok])
                    sc_war[b] = [(v, v.n)]

                for dvc in range(4):
                    b = dvc % 2
                    pp = po[b]
                    pw = [(v, v.n)] if dvc < 2 else po_war.get(b, [(v, v.n)])
                    for jt in range(4):
                        kb.op("tensor", "matmul", pp[:],
                              vsb[:, jt * 512 + dvc * 128: jt * 512 + (dvc + 1) * 128],
                              attnT[:, jt * 512:(jt + 1) * 512],
                              start=(jt == 0), stop=False,
                              wait=pw if jt == 0 else [])
                    for c in range(2):
                        kb.op("tensor", "matmul", pp[:],
                              S_b[:, c * 512 + dvc * 128: c * 512 + (dvc + 1) * 128],
                              qhat[:, c * 512:(c + 1) * 512],
                              start=False, stop=(c == 1))
                    war = [ost[dvc]] if dvc in ost else []
                    kb.op("vector", "tensor_copy", oev[:, dvc * 512:(dvc + 1) * 512],
                          pp[:], wait=[(t, t.n)] + war)
                    po_war[b] = [(v, v.n)]
                    d3 = fm_dram(ap["o_part"])
                    ost[dvc] = kb.dma("scalar",
                                      d3[:, h * 4 + dvc, sc * 512:(sc + 1) * 512],
                                      oev[:, dvc * 512:(dvc + 1) * 512], str_,
                                      wait=[v])

                for c in range(2):
                    pp = pA[c]
                    for jt in range(4):
                        kb.op("tensor", "matmul", pp[:],
                              khat[:, jt * 256 + c * 128: jt * 256 + (c + 1) * 128],
                              vsb[:, jt * 512:(jt + 1) * 512],
                              start=(jt == 0), stop=(jt == 3),
                              wait=[(v, v.n)] if jt == 0 else [])
                    kb.op("vector", "scalar_tensor_tensor",
                          S_f[:, (h * 2 + c) * 512:(h * 2 + c + 1) * 512],
                          S_f[:, (h * 2 + c) * 512:(h * 2 + c + 1) * 512],
                          gSC[h], pp[:], ALU.mult, ALU.add, wait=[(t, t.n)])
                    pA_war[c] = [(v, v.n)]
                iter_tok[bsel] = [(t, t.n), (v, v.n)]

        pd3 = ap["P"].rearrange("(c p) o -> p c o", p=128)
        for h in range(H0):
            for c in range(2):
                kb.dma("scalar", pd3[:, c, h * 512:(h + 1) * 512],
                       S_f[:, (h * 2 + c) * 512:(h * 2 + c + 1) * 512], str_,
                       wait=[v])

        kb.build(final_rings=[ldr_, wlr, str_])
        p3.close()
    return nc, ap




def build_launch_B():
    nc = bass.Bass("TRN2", target_bir_lowering=False, debug=False)
    ap = {}

    def din(name, shape, dt=F32):
        ap[name] = nc.dram_tensor(name, shape, dt, kind="ExternalInput").ap()

    def dout(name, shape, dt=F32):
        ap[name] = nc.dram_tensor(name, shape, dt, kind="ExternalOutput").ap()

    din("x", [TPC, D])
    din("o_part", [2048, TPC], BF16)
    din("qTd", [D, TPC], BF16)
    din("gTd", [2048, TPC], BF16)
    din("P_all", [NCORE * 256, H0 * HV0])
    din("coefP", [128, NCORE * H0])
    din("WoT", [2048, D])
    din("gwT", [D, NE0])
    din("gnw", [128, 4])
    din("crossfullT", [H0 * 128, TPC], BF16)
    din("ident", [128, 128], BF16)
    din("onesf", [128, 128])
    dout("x1", [TPC, D])
    dout("xn2", [TPC, D], BF16)
    dout("M_em", [NE0, TPC], BF16)
    dout("w_tm", [128, NTT])

    ctx = ExitStack()
    with ctx:
        kb = KB(nc, ctx)
        ldr_ = kb.ring("ld", 4)
        wlr = kb.ring("wl", 2)
        str_ = kb.ring("st", 4)
        v, t, s = kb.e["vector"], kb.e["tensor"], kb.e["scalar"]

        ident = ctx.enter_context(nc.sbuf_tensor("identB", [128, 128], BF16))
        itok = kb.dma("sync", ident[:], ap["ident"], ldr_)
        onesf = ctx.enter_context(nc.sbuf_tensor("onesf_sb", [128, 128], F32))
        otok = kb.dma("sync", onesf[:], ap["onesf"], ldr_)
        epsc = ctx.enter_context(nc.sbuf_tensor("epsc", [128, 1], F32))
        kb.op("vector", "memset", epsc[:], EPS)

        og_d = nc.dram_tensor("og_d", [16 * 128, TPC], BF16, kind="Internal").ap()

        p1 = Pool(nc)
        S_in = p1.sb("S_in", [128, H0 * 2 * 512], F32)
        S_inb_d = p1.sb("S_inb", [128, 2 * 2 * 512], BF16)
        ptmp = p1.sb("ptmp", [128, 2 * 2048], F32)
        coefb = p1.sb("coefb", [128, NCORE * H0], F32)
        gnwb = p1.sb("gnwb", [128, 4], F32)
        cstok = kb.dma("sync", coefb[:], ap["coefP"], ldr_)
        gntok = kb.dma("sync", gnwb[:], ap["gnw"], ldr_)
        kb.op("vector", "memset", S_in[:], 0.0)
        pa3 = ap["P_all"].rearrange("(j c p) o -> p (j c) o", p=128, c=2)
        pt_war = {}
        for j in range(NCORE):
            for c in range(2):
                i2 = j * 2 + c
                half = i2 % 2
                pt = ptmp[:, half * 2048:(half + 1) * 2048]
                ptok = kb.dma("sync" if i2 % 4 < 2 else "scalar", pt,
                              pa3[:, i2, :], ldr_ if i2 % 4 < 2 else str_,
                              wait=pt_war.get(half, []))
                for h in range(H0):
                    kb.op("vector", "scalar_tensor_tensor",
                          S_in[:, (h * 2 + c) * 512:(h * 2 + c + 1) * 512],
                          pt[:, h * 512:(h + 1) * 512],
                          coefb[:, j * H0 + h: j * H0 + h + 1],
                          S_in[:, (h * 2 + c) * 512:(h * 2 + c + 1) * 512],
                          ALU.mult, ALU.add, wait=[ptok, cstok])
                pt_war[half] = [(v, v.n)]

        qhat_d = p1.sb("qhat", [128, 2 * 2 * TPC], BF16)
        crb_d = p1.sb("crb", [128, 2 * TPC], BF16)
        of_d = p1.sb("of", [128, 2 * 4 * TPC], F32)
        sq1 = p1.sb("sq1", [128, 512], F32)
        ogt = p1.sb("ogt", [128, 2 * TPC], BF16)
        opb = p1.sb("opb", [128, 2 * TPC], BF16)
        gob = p1.sb("gob", [128, 2 * TPC], BF16)
        gsil = p1.sb("gsil", [128, TPC], BF16)
        tmpf = p1.sb("tmpf", [128, TPC], F32)
        vr = p1.sb("vr", [1, 2 * TPC], F32)
        varr = vr[0:1, 0:TPC]
        rstd = vr[0:1, TPC:2 * TPC]
        rbc = p1.sb("rbc", [128, TPC], F32)
        pox = [p1.ps(f"pox{i}", [128, 512]) for i in range(4)]
        pvar = p1.ps("pvar", [1, 512])
        pbc = p1.ps("pbc", [128, 512])

        q3 = fm_dram(ap["qTd"])
        cf3 = ap["crossfullT"].rearrange("(h p) t -> p h t", p=128)
        op3 = fm_dram(ap["o_part"])
        g3 = fm_dram(ap["gTd"])
        ogt_war = {}
        qh_war = {}
        si_war = {}
        of_war = {}
        op_war = {}
        go_war = {}
        pox_war = {}
        for h in range(H0):
            hb = h % 2
            qhat = qhat_d[:, hb * 2 * TPC:(hb + 1) * 2 * TPC]
            crb = crb_d[:, hb * TPC:(hb + 1) * TPC]
            S_inb = S_inb_d[:, hb * 1024:(hb + 1) * 1024]
            of = of_d[:, hb * 4 * TPC:(hb + 1) * 4 * TPC]
            qtok = kb.dma("sync", qhat.rearrange("p (c t) -> p c t", c=2),
                          q3[:, h * 2:h * 2 + 2, :], ldr_,
                          wait=qh_war.get(hb, []))
            crtok = kb.dma("sync", crb, cf3[:, h, :], ldr_,
                           wait=qh_war.get(hb, []))
            for c in range(2):
                kb.op("vector", "tensor_mul", qhat[:, c * TPC:(c + 1) * TPC],
                      qhat[:, c * TPC:(c + 1) * TPC], crb,
                      wait=[qtok, crtok])
                kb.op("vector", "tensor_copy", S_inb[:, c * 512:(c + 1) * 512],
                      S_in[:, (h * 2 + c) * 512:(h * 2 + c + 1) * 512],
                      wait=si_war.get(hb, []))
            vq_tok = [(v, v.n)]
            ofw = of_war.get(hb, [])
            for dvc in range(4):
                oh = dvc % 2
                optok = kb.dma("sync", opb[:, oh * TPC:(oh + 1) * TPC],
                               op3[:, h * 4 + dvc, :], ldr_,
                               wait=op_war.get(oh, []))
                for tc in range(4):
                    bk = (dvc * 4 + tc) % 4
                    pp = pox[bk]
                    for c in range(2):
                        kb.op("tensor", "matmul",
                              pp[:],
                              S_inb[:, c * 512 + dvc * 128: c * 512 + (dvc + 1) * 128],
                              qhat[:, c * TPC + tc * 512: c * TPC + (tc + 1) * 512],
                              start=(c == 0), stop=(c == 1),
                              wait=vq_tok + pox_war.get(bk, [])
                              if c == 0 else [])
                    kb.op("vector", "scalar_tensor_tensor",
                          of[:, dvc * TPC + tc * 512: dvc * TPC + (tc + 1) * 512],
                          opb[:, oh * TPC + tc * 512: oh * TPC + (tc + 1) * 512],
                          0.0, pp[:],
                          ALU.bypass, ALU.add, wait=[(t, t.n), optok] + ofw)
                    pox_war[bk] = [(v, v.n)]
                op_war[oh] = [(v, v.n)]
            for tc in range(4):
                for dvc in range(4):
                    kb.op("scalar", "activation", sq1[:],
                          of[:, dvc * TPC + tc * 512: dvc * TPC + (tc + 1) * 512],
                          AF.Square, wait=[(v, v.n), (t, t.n)])
                    kb.op("tensor", "matmul", pvar[:], onesf[:, 0:1], sq1[:],
                          start=(dvc == 0), stop=(dvc == 3),
                          wait=[(s, s.n), otok] + ([(v, v.n)] if dvc == 0 else []))
                kb.op("vector", "tensor_copy", varr[:, tc * 512:(tc + 1) * 512],
                      pvar[:], wait=[(t, t.n)])
            kb.op("scalar", "activation", rstd[:], varr[:], AF.Ln,
                  bias=epsc[0:1, 0:1], scale=1.0 / HV0, wait=[(v, v.n)])
            kb.op("scalar", "activation", varr[:], rstd[:], AF.Exp, scale=-0.5)
            for tc in range(4):
                kb.op("tensor", "matmul", pbc[:], onesf[0:1, :],
                      varr[:, tc * 512:(tc + 1) * 512], start=True, stop=True,
                      wait=[(s, s.n), (v, v.n)])
                kb.op("vector", "tensor_copy", rbc[:, tc * 512:(tc + 1) * 512],
                      pbc[:], wait=[(t, t.n)])
            for dvc in range(4):
                oh = dvc % 2
                gtok = kb.dma("sync", gob[:, oh * TPC:(oh + 1) * TPC],
                              g3[:, h * 4 + dvc, :], ldr_,
                              wait=go_war.get(oh, []))
                kb.op("scalar", "activation", gsil[:],
                      gob[:, oh * TPC:(oh + 1) * TPC], AF.Silu,
                      wait=[gtok])
                go_war[oh] = [(s, s.n)]
                kb.op("vector", "scalar_tensor_tensor", tmpf[:],
                      of[:, dvc * TPC:(dvc + 1) * TPC],
                      gnwb[:, dvc:dvc + 1], rbc[:],
                      ALU.mult, ALU.mult, wait=[gntok])
                kb.op("vector", "tensor_mul", ogt[:, oh * TPC:(oh + 1) * TPC],
                      tmpf[:], gsil[:],
                      wait=ogt_war.get(oh, []) + [(s, s.n)])
                ogt_war[oh] = [kb.dma(
                    "scalar", fm_dram(og_d)[:, h * 4 + dvc, :],
                    ogt[:, oh * TPC:(oh + 1) * TPC], str_, wait=[v])]
            qh_war[hb] = [(t, t.n), (v, v.n)]
            si_war[hb] = [(t, t.n)]
            of_war[hb] = [(s, s.n), (v, v.n)]
        p1.close()

        fence = kb.fence() + ldr_.final_tokens() + str_.final_tokens()
        og_fence = fence
        p2 = Pool(nc)
        wbuf = p2.sb("wbuf", [128, 16 * 1024], BF16)
        ogt2 = p2.sb("ogt2", [128, 16 * 128], BF16)
        gwb = p2.sb("gwb", [128, 8 * NE0], BF16)
        x_a = p2.sb("x_a", [128, D], F32)
        x_b = p2.sb("x_b", [128, D], F32)
        x1t = p2.sb("x1t", [128, D], F32)
        xn_tile = p2.sb("xn_tile", [128, D], BF16)
        red = p2.sb("red", [128, 8], F32)
        xs_sc = p2.sb("xs_sc", [128, D], F32)
        xnTt = p2.sb("xnTt", [128, 8 * 128], BF16)
        lg = p2.sb("lg", [128, NE0], F32)
        eqb = p2.sb("eqb", [128, NE0], F32)
        cumb = p2.sb("cumb", [128, NE0], F32)
        cum1 = p2.sb("cum1", [128, NE0], F32)
        m_bf = p2.sb("m_bf", [128, NE0], BF16)
        wsb = p2.sb("wsb", [128, NTT], F32)
        Mem = p2.sb("Mem", [16, TPC], BF16)
        pj2 = [p2.ps(f"pj2{i}", [128, 512]) for i in range(2)]
        pT = p2.ps("pT", [128, 128], BF16)
        pr = p2.ps("pr", [128, NE0])

        w3 = ap["WoT"].rearrange("(c p) o -> p c o", p=128)
        wotok = kb.dma("gpsimd", wbuf[:].rearrange("p (c o) -> p c o", c=16),
                       w3, wlr, wait=fence)
        gw3 = ap["gwT"].rearrange("(c p) e -> p c e", p=128)
        gwtok = kb.dma("gpsimd", gwb[:].rearrange("p (c e) -> p c e", c=8),
                       gw3, wlr, wait=fence)

        x1_war = []
        xn2_war = []
        for i in range(NTT):
            xb = x_a if i % 2 == 0 else x_b
            xtok = kb.dma("sync", xb[:], ap["x"][i * 128:(i + 1) * 128, :], ldr_,
                          wait=[v] if i >= 2 else fence)
            ogtok = kb.dma("sync", ogt2[:].rearrange("p (c w) -> p c w", c=16),
                           fm_dram(og_d)[:, :, i * 128:(i + 1) * 128], ldr_,
                           wait=[t] + (og_fence if i < 2 else []))
            for og2 in range(2):
                pp = pj2[og2]
                for c in range(16):
                    kb.op("tensor", "matmul", pp[:],
                          ogt2[:, c * 128:(c + 1) * 128],
                          wbuf[:, c * 1024 + og2 * 512: c * 1024 + (og2 + 1) * 512],
                          start=(c == 0), stop=(c == 15),
                          wait=[wotok, v, ogtok] if c == 0 else [])
                kb.op("vector", "scalar_tensor_tensor",
                      x1t[:, og2 * 512:(og2 + 1) * 512],
                      xb[:, og2 * 512:(og2 + 1) * 512], 0.0, pp[:],
                      ALU.bypass, ALU.add, wait=[t, xtok, s] + x1_war)
            x1_war = [kb.dma("scalar", ap["x1"][i * 128:(i + 1) * 128, :],
                             x1t[:], str_, wait=[v])]
            emit_ln_tile(kb, x1t, xn_tile, red, xs_sc, [], epsc[:, 0:1],
                         war=xn2_war)
            xn2_war = [kb.dma("scalar", ap["xn2"][i * 128:(i + 1) * 128, :],
                              xn_tile[:], str_, wait=[v])]
            emit_transpose_to_fm(kb, xn_tile, xnTt, 0, ident, pT, itok, tcols=128)
            for c in range(8):
                kb.op("tensor", "matmul", pr[:], xnTt[:, c * 128:(c + 1) * 128],
                      gwb[:, c * NE0:(c + 1) * NE0],
                      start=(c == 0), stop=(c == 7),
                      wait=[v, gwtok] if c == 0 else [])
            kb.op("vector", "tensor_copy", lg[:], pr[:], wait=[t])
            kb.op("vector", "reduce_max", red[:, 0:1], lg[:], AXX)
            kb.op("vector", "tensor_scalar_mul", red[:, 1:2], red[:, 0:1], -1.0)
            kb.op("scalar", "activation", eqb[:], lg[:], AF.Exp,
                  bias=red[:, 1:2], accum_out=red[:, 2:3], wait=[v])
            kb.op("vector", "reciprocal", wsb[:, i:i + 1], red[:, 2:3], wait=[s])
            kb.op("vector", "tensor_scalar", eqb[:], lg[:], red[:, 0:1], None,
                  ALU.is_equal)
            kb.op("vector", "tensor_tensor_scan", cumb[:], eqb[:], eqb[:], 0.0,
                  ALU.add, ALU.bypass)
            kb.op("vector", "tensor_single_scalar", cum1[:], cumb[:], 1.0,
                  ALU.is_equal)
            kb.op("vector", "tensor_mul", m_bf[:], eqb[:], cum1[:])
            kb.op("tensor", "transpose", pT[0:16, :], m_bf[:], ident[:],
                  wait=[v, itok])
            kb.op("vector", "tensor_copy", Mem[0:16, i * 128:(i + 1) * 128],
                  pT[0:16, :], wait=[t])
        kb.dma("scalar", ap["M_em"], Mem[0:16, :], str_, wait=[v])
        kb.dma("scalar", ap["w_tm"], wsb[:], str_, wait=[v])
        kb.build(final_rings=[ldr_, wlr, str_])
        p2.close()
    return nc, ap


def host_constants(core):
    c = {}
    c["ident"] = np.eye(128, dtype=np.float32).astype(BF)
    tloc = np.arange(TPC, dtype=np.float32)
    pos = (core * TPC + tloc) % 8192

    inv0 = 1.0 / 10000.0 ** (np.arange(0, HK0, 2, dtype=np.float32) / HK0)
    fr = inv0[:, None] * pos[None, :]
    c["r0_cos"] = np.cos(fr).astype(np.float32)
    c["r0_sin"] = np.sin(fr).astype(np.float32)
    s0 = HK0 ** -0.5
    c["r0_cos_q"] = (np.cos(fr) * s0).astype(np.float32)
    c["r0_sin_q"] = (np.sin(fr) * s0).astype(np.float32)

    gamma = (1.0 - np.exp2(-5.0 - np.arange(H0, dtype=np.float32)))
    i512 = np.arange(SC, dtype=np.float32)
    dif = i512[None, :] - i512[:, None]
    mT = np.stack([np.where(dif >= 0, g ** np.clip(dif, 0, None), 0.0) for g in gamma])
    c["maskT"] = mT.reshape(H0 * SC, SC).astype(BF)
    kd = np.zeros((128, H0 * 4), np.float32)
    for h in range(H0):
        for jt in range(4):
            kd[:, h * 4 + jt] = gamma[h] ** (SC - 1 - (jt * 128 + np.arange(128)))
    c["kdec"] = kd
    cr = np.stack([g ** (i512 + 1) for g in gamma])
    c["crossT"] = np.repeat(cr[:, None, :], 128, axis=1).reshape(H0 * 128, SC).astype(BF)
    return c


def prep_launch_A(inputs, core):
    d = inputs
    im = host_constants(core)
    im["x"] = np.ascontiguousarray(d["L0_attn_x"].reshape(-1, D)[core * TPC:(core + 1) * TPC])
    im["WqT"] = np.ascontiguousarray(np.asarray(d["L0_attn_Wq"]).T)
    im["WkT"] = np.ascontiguousarray(np.asarray(d["L0_attn_Wk"]).T)
    im["WvT"] = np.ascontiguousarray(np.asarray(d["L0_attn_Wv"]).T)
    im["WgT"] = np.ascontiguousarray(np.asarray(d["L0_attn_Wg"]).T)
    keep = {"x", "WqT", "WkT", "WvT", "WgT", "ident", "r0_cos", "r0_sin",
            "r0_cos_q", "r0_sin_q", "maskT", "kdec", "crossT"}
    return {k: im[k] for k in keep}





def emit_wload_g(kb, w_src, dst_bf, wlr, n_cols, waits, eng="gpsimd"):
    nch = w_src.shape[0] // 128
    w3 = w_src.rearrange("(c p) f -> p c f", p=128)
    toks = []
    for c0 in range(nch):
        toks.append(kb.dma(eng, dst_bf[:, c0 * n_cols:(c0 + 1) * n_cols],
                           w3[:, c0, :], wlr, wait=waits))
    return toks


def emit_wload_cast(kb, w_src, dst_bf, stage, ldr_, n_cols, tag_toks):
    s = kb.e["scalar"]
    nch = w_src.shape[0] // 128
    w3 = w_src.rearrange("(c p) f -> p c f", p=128)
    W = stage[0].shape[1]
    st_war = [[(s, s.n)], [(s, s.n)]]
    i = 0
    if n_cols <= W:
        per = W // n_cols
        for c0 in range(0, nch, per):
            buf = stage[i % 2]
            cw = min(per, nch - c0)
            tok = kb.dma("sync", buf[:, 0:cw * n_cols].rearrange(
                "p (c f) -> p c f", c=cw), w3[:, c0:c0 + cw, :], ldr_,
                wait=st_war[i % 2] + tag_toks)
            kb.op("scalar", "activation",
                  dst_bf[:, c0 * n_cols:(c0 + cw) * n_cols],
                  buf[:, 0:cw * n_cols], AF.Copy, wait=[tok])
            st_war[i % 2] = [(s, s.n)]
            i += 1
    else:
        for c in range(nch):
            for off in range(0, n_cols, W):
                buf = stage[i % 2]
                w = min(W, n_cols - off)
                tok = kb.dma("sync", buf[:, 0:w], w3[:, c, off:off + w], ldr_,
                             wait=st_war[i % 2] + tag_toks)
                kb.op("scalar", "activation",
                      dst_bf[:, c * n_cols + off: c * n_cols + off + w],
                      buf[:, 0:w], AF.Copy, wait=[tok])
                st_war[i % 2] = [(s, s.n)]
                i += 1
    return (s, s.n)


def emit_ffn_blocks(kb, ap_in, invtab, eo_ap, w1b, w2b, bufs, act_fn,
                    blocks, w1tok, w2tok, fence, ldr_, gr, str_):
    v, t, s, g = kb.e["vector"], kb.e["tensor"], kb.e["scalar"], kb.e["gpsimd"]
    idxc, xg, xgT, hT, eot = (bufs["idxc"], bufs["xg"], bufs["xgT"],
                              bufs["hT"], bufs["eot"])
    pT2, ps1, ps2 = bufs["pT2"], bufs["ps1"], bufs["ps2"]
    ident = bufs["ident"]
    itok = bufs["itok"]
    ga_war = []
    eo_war = bufs.setdefault("_eo_war", {})
    for base, nt in blocks:
        gtoks = []
        for j in range(nt):
            itk = kb.dma("gpsimd", idxc[j][:],
                         invtab[base + j * 128:base + (j + 1) * 128, :], gr,
                         wait=fence + ga_war)
            gtoks.append(kb.idma("gpsimd", xg[j][:], None, ap_in,
                                 bass.IndirectOffsetOnAxis(ap=idxc[j][:], axis=0),
                                 gr, wait=[itk] + bufs.get("_tp", [])))
        ga_war = list(gtoks)
        ns = nt * 128
        evac_tok = bufs.setdefault("_evac_tok", [[], []])
        for j in range(nt):
            for c in range(8):
                bank = (j * 8 + c) % 2
                kb.op("tensor", "transpose", pT2[bank][:],
                      xg[j][:, c * 128:(c + 1) * 128], ident[:],
                      wait=[gtoks[j], itok] + evac_tok[bank])
                kb.op("vector", "tensor_copy",
                      xgT[:, c * 512 + j * 128: c * 512 + (j + 1) * 128],
                      pT2[bank][:], wait=[(t, kb.e["tensor"].n)])
                evac_tok[bank] = [(v, v.n)]
        bufs["_tp"] = [(t, kb.e["tensor"].n)]
        s1war = bufs.setdefault("_s1war", {})
        for fc in range(32):
            b1 = fc % 2
            pp = ps1[b1]
            w0 = list(s1war.get(b1, []))
            if fc < 2:
                w0.append((v, v.n))
                w0.append((s, s.n))
            for c in range(8):
                kb.op("tensor", "matmul", pp[:, 0:ns],
                      w1b[:, c * 4096 + fc * 128: c * 4096 + (fc + 1) * 128],
                      xgT[:, c * 512: c * 512 + ns],
                      start=(c == 0), stop=(c == 7),
                      wait=(w0 + [w1tok[c]]) if c == 0 else [w1tok[c]])
            kb.op("scalar", "activation", hT[:, fc * 512: fc * 512 + ns],
                  pp[:, 0:ns], act_fn, wait=[(t, kb.e["tensor"].n)])
            s1war[b1] = [(s, s.n)]
        s2war = bufs.setdefault("_s2war", {})
        for j in range(nt):
            for og2 in range(2):
                pp = ps2[og2]
                w0 = list(s2war.get(og2, []))
                if j == 0:
                    w0.append((s, s.n))
                for fc in range(32):
                    kb.op("tensor", "matmul", pp[:],
                          hT[:, fc * 512 + j * 128: fc * 512 + (j + 1) * 128],
                          w2b[:, fc * D + og2 * 512: fc * D + (og2 + 1) * 512],
                          start=(fc == 0), stop=(fc == 31),
                          wait=(w0 + [w2tok[fc]]) if fc == 0 else [w2tok[fc]])
                war = eo_war.get(og2, [])
                kb.op("vector", "tensor_copy",
                      eot[:, og2 * 512:(og2 + 1) * 512], pp[:],
                      wait=[(t, kb.e["tensor"].n)] + war)
                s2war[og2] = [(v, v.n)]
                eo_war[og2] = [kb.dma(
                    "scalar",
                    eo_ap[base + j * 128:base + (j + 1) * 128,
                          og2 * 512:(og2 + 1) * 512],
                    eot[:, og2 * 512:(og2 + 1) * 512], str_, wait=[v])]



def build_launch_C():
    nc = bass.Bass("TRN2", target_bir_lowering=False, debug=False)
    ap = {}

    def din(name, shape, dt=F32):
        ap[name] = nc.dram_tensor(name, shape, dt, kind="ExternalInput").ap()

    def dout(name, shape, dt=F32):
        ap[name] = nc.dram_tensor(name, shape, dt, kind="ExternalOutput").ap()

    NTOK = NCORE * TPC
    din("M_all", [NE0, NTOK], BF16)
    din("xn2_all", [NTOK, D], BF16)
    din("w1", [2 * D, 4096])
    din("w2", [2 * 4096, D])
    din("ident", [128, 128], BF16)
    din("iota_t", [128, 128], I32)
    din("eoff16", [16, 1])
    din("basec", [128, 1])
    din("trashc", [128, 128])
    dout("expert_out", [2 * PAD0, D], BF16)
    dout("gidx_row", [1, NTOK])
    dout("keep_row", [1, NTOK])

    invtab = nc.dram_tensor("invtab", [19072, 1], I32, kind="Internal").ap()

    ctx = ExitStack()
    with ctx:
        kb = KB(nc, ctx)
        ldr_ = kb.ring("ld", 4)
        wlr = kb.ring("wl", 2)
        wlr2 = kb.ring("wl2", 2)
        str_ = kb.ring("st", 4)
        gr = kb.ring("gq", 4)
        v, t, s, g = kb.e["vector"], kb.e["tensor"], kb.e["scalar"], kb.e["gpsimd"]

        ident = ctx.enter_context(nc.sbuf_tensor("identB", [128, 128], BF16))
        itok = kb.dma("sync", ident[:], ap["ident"], ldr_)

        p1 = Pool(nc)
        Mem = p1.sb("Mem", [16, NTOK], BF16)
        cum = p1.sb("cum", [16, NTOK], F32)
        tmp_em = p1.sb("tmp_em", [16, NTOK], F32)
        eoffc = p1.sb("eoffc", [16, 1], F32)
        stg = p1.sb("stg", [1, 1024], F32)
        basec = p1.sb("basec_s", [128, 1], F32)
        iota = p1.sb("iota_s", [128, 128], I32)
        gidxf = p1.sb("gidxf", [128, 128], F32)
        negm = p1.sb("negm", [128, 128], F32)
        idxf2 = p1.sb("idxf2", [128, 128], I32)
        zs = p1.sb("zs", [128, 21], I32)
        trashc = p1.sb("trashc_s", [128, 128], F32)
        onesf = p1.sb("onesf_c", [128, 1], F32)
        pcs = p1.ps("pcs", [1, 512])

        mtok = kb.dma("sync", Mem[:], ap["M_all"], ldr_)
        etok = kb.dma("sync", eoffc[:], ap["eoff16"], ldr_)
        btok = kb.dma("sync", basec[:], ap["basec"], ldr_)
        ittok = kb.dma("sync", iota[:], ap["iota_t"], ldr_)
        kb.op("vector", "memset", onesf[:], 1.0)
        ttok2 = kb.dma("sync", trashc[:], ap["trashc"], ldr_)
        kb.op("gpsimd", "memset", zs[:], 0)
        ztok = kb.dma("gpsimd",
                      invtab[0:2688, :].rearrange("(p k) one -> p (k one)", p=128),
                      zs[:], gr, wait=[g])
        kb.op("vector", "tensor_tensor_scan", cum[:], Mem[:], Mem[:], 0.0,
              ALU.add, ALU.bypass, wait=[mtok])
        kb.op("vector", "tensor_scalar", cum[:], cum[:], -1.0, float(CAP0),
              ALU.add, ALU.min)
        kb.op("vector", "tensor_scalar", tmp_em[:], cum[:], eoffc[0:16, 0:1],
              None, ALU.add, wait=[etok])
        kb.op("vector", "tensor_mul", tmp_em[:], tmp_em[:], Mem[:])
        st_tok = []
        for cg in range(NTOK // 512):
            kb.op("tensor", "matmul", pcs[:], onesf[0:16, 0:1],
                  tmp_em[0:16, cg * 512:(cg + 1) * 512], start=True, stop=True,
                  wait=[(v, v.n)])
            sl = (cg % 2) * 512
            kb.op("vector", "tensor_copy", stg[:, sl:sl + 512], pcs[:],
                  wait=[(t, kb.e["tensor"].n)] + st_tok)
            st_tok = [kb.dma("scalar",
                             ap["gidx_row"][0:1, cg * 512:(cg + 1) * 512],
                             stg[:, sl:sl + 512], str_, wait=[v])]
        kb.op("vector", "tensor_single_scalar", tmp_em[:], cum[:],
              float(CAP0) - 0.5, ALU.is_lt)
        kb.op("vector", "tensor_mul", tmp_em[:], tmp_em[:], Mem[:])
        for cg in range(NTOK // 512):
            kb.op("tensor", "matmul", pcs[:], onesf[0:16, 0:1],
                  tmp_em[0:16, cg * 512:(cg + 1) * 512], start=True, stop=True,
                  wait=[(v, v.n)])
            sl = (cg % 2) * 512
            kb.op("vector", "tensor_copy", stg[:, sl:sl + 512], pcs[:],
                  wait=[(t, kb.e["tensor"].n)] + st_tok)
            st_tok = [kb.dma("scalar",
                             ap["keep_row"][0:1, cg * 512:(cg + 1) * 512],
                             stg[:, sl:sl + 512], str_, wait=[v])]
        gxt = kb.dma("sync", gidxf[:],
                     ap["gidx_row"].rearrange("one (a b) -> (one a) b", a=128),
                     ldr_, wait=st_tok)
        kb.op("vector", "tensor_scalar", gidxf[:], gidxf[:], basec[:, 0:1],
              None, ALU.subtract, wait=[gxt, btok])
        kb.op("vector", "tensor_single_scalar", negm[:], gidxf[:], 0.0,
              ALU.is_lt)
        kb.op("vector", "scalar_tensor_tensor", gidxf[:], negm[:], 1e6,
              gidxf[:], ALU.mult, ALU.add)
        kb.op("vector", "tensor_single_scalar", negm[:], gidxf[:], 2561.5,
              ALU.is_lt)
        kb.op("vector", "tensor_mul", gidxf[:], gidxf[:], negm[:])
        kb.op("vector", "tensor_scalar", negm[:], negm[:], -1.0, 1.0,
              ALU.mult, ALU.add)
        kb.op("vector", "tensor_mul", negm[:], negm[:], trashc[:],
              wait=[ttok2])
        kb.op("vector", "tensor_add", gidxf[:], gidxf[:], negm[:])
        kb.op("vector", "tensor_copy", idxf2[:], gidxf[:])
        vsc = (v, v.n)
        scr = kb.ring("scrC", 8)
        for tt_ in range(128):
            kb.idma("gpsimd", invtab,
                    bass.IndirectOffsetOnAxis(ap=idxf2[:, tt_:tt_ + 1], axis=0),
                    iota[:, tt_:tt_ + 1], None, ring=scr,
                    wait=[vsc, ittok, ztok] if tt_ == 0 else [])
        scb_tok = scr.final_tokens()
        p1.close()

        fence_w = kb.fence() + gr.final_tokens() + ldr_.final_tokens()
        fence = fence_w + scb_tok
        p2 = Pool(nc)
        w1b = p2.sb("w1b", [128, 8 * 4096], BF16)
        w2b = p2.sb("w2b", [128, 32 * D], BF16)
        bufs = dict(
            idxc=[p2.sb(f"idxc{i}", [128, 1], I32) for i in range(4)],
            xg=[p2.sb(f"xg{i}", [128, D], BF16) for i in range(4)],
            xgT=p2.sb("xgT", [128, 8 * 512], BF16),
            hT=p2.sb("hT", [128, 32 * 512], BF16),
            eot=p2.sb("eot", [128, 2 * 512], BF16),
            pT2=[p2.ps(f"pT2{i}", [128, 128], BF16) for i in range(2)],
            ps1=[p2.ps(f"ps1{i}", [128, 512]) for i in range(2)],
            ps2=[p2.ps(f"ps2{i}", [128, 512]) for i in range(2)],
            ident=ident, itok=itok,
        )
        zr = p2.sb("zr", [1, D], BF16)
        kb.op("vector", "memset", zr[:], 0.0, wait=fence)
        kb.dma("scalar", ap["expert_out"][PAD0 - 1:PAD0, :], zr[:], str_,
               wait=[v])
        kb.dma("scalar", ap["expert_out"][2 * PAD0 - 1:2 * PAD0, :], zr[:],
               str_, wait=[v])

        for e in range(2):
            w1tok = emit_wload_g(kb, ap["w1"][e * D:(e + 1) * D, :], w1b,
                                 wlr, 4096, fence + [t])
            w2tok = emit_wload_g(kb, ap["w2"][e * 4096:(e + 1) * 4096, :],
                                 w2b, wlr2, D, fence + [t])
            blocks = [(e * PAD0, 4), (e * PAD0 + 512, 4), (e * PAD0 + 1024, 2)]
            emit_ffn_blocks(kb, ap["xn2_all"], invtab, ap["expert_out"],
                            w1b, w2b, bufs, AF.Gelu_apprx_tanh, blocks,
                            w1tok, w2tok, fence, ldr_, gr, str_)
            fence = kb.fence()
        kb.build(final_rings=[ldr_, wlr, wlr2, str_, gr])
        p2.close()
    return nc, ap


def prep_launch_C(inputs, core, stB):
    d = inputs
    im = {}
    im["M_all"] = np.concatenate([stB[j]["M_em"] for j in range(NCORE)], 1)
    im["xn2_all"] = np.concatenate([stB[j]["xn2"] for j in range(NCORE)], 0)
    im["w1"] = np.ascontiguousarray(
        np.asarray(d["L0_ffn_w1"][2 * core:2 * core + 2]).reshape(2 * D, 4096))
    im["w2"] = np.ascontiguousarray(
        np.asarray(d["L0_ffn_w2"][2 * core:2 * core + 2]).reshape(2 * 4096, D))
    im["ident"] = np.eye(128, dtype=np.float32).astype(BF)
    p = np.arange(128)
    im["iota_t"] = (p[:, None] * 128 + np.arange(128)[None, :]).astype(np.int32)
    im["eoff16"] = (np.arange(NE0) * PAD0).astype(np.float32).reshape(16, 1)
    im["basec"] = np.full((128, 1), 2 * core * PAD0, np.float32)
    im["trashc"] = (2688 + np.arange(128)[:, None] * 128
                    + np.arange(128)[None, :]).astype(np.float32)
    return im




def build_launch_D():
    nc = bass.Bass("TRN2", target_bir_lowering=False, debug=False)
    ap = {}

    def din(name, shape, dt=F32):
        ap[name] = nc.dram_tensor(name, shape, dt, kind="ExternalInput").ap()

    def dout(name, shape, dt=F32):
        ap[name] = nc.dram_tensor(name, shape, dt, kind="ExternalOutput").ap()

    din("x1", [TPC, D])
    din("expert_out_all", [NE0 * PAD0, D], BF16)
    din("gidx", [128, NTT])
    din("keep", [128, NTT])
    din("w_tm", [128, NTT])
    din("WqT1", [D, D]); din("WkT1", [D, D]); din("WvT1", [D, D])
    din("bq", [128, 8]); din("bk", [128, 8]); din("bv", [128, 8])
    din("r1_cos", [128, TPC], BF16); din("r1_sin", [128, TPC], BF16)
    din("r1_cos_q", [128, TPC], BF16); din("r1_sin_q", [128, TPC], BF16)
    din("rot_permT", [128, 128], BF16)
    din("ident", [128, 128], BF16)
    dout("x2", [TPC, D])
    dout("qTd1", [D, TPC], BF16)
    dout("kTd1", [D, TPC], BF16)
    dout("v_aug", [TPC, H1 * 65], BF16)

    ctx = ExitStack()
    with ctx:
        kb = KB(nc, ctx)
        ldr_ = kb.ring("ld", 4)
        wlr = kb.ring("wl", 2)
        str_ = kb.ring("st", 4)
        gr = kb.ring("gq", 4)
        v, t, s, g = kb.e["vector"], kb.e["tensor"], kb.e["scalar"], kb.e["gpsimd"]

        ident = ctx.enter_context(nc.sbuf_tensor("identB", [128, 128], BF16))
        itok = kb.dma("sync", ident[:], ap["ident"], ldr_)
        epsc = ctx.enter_context(nc.sbuf_tensor("epsc", [128, 1], F32))
        kb.op("vector", "memset", epsc[:], EPS)
        xn3T = ctx.enter_context(nc.sbuf_tensor("xn3T", [128, 8 * TPC], BF16))

        p1 = Pool(nc)
        x_a = p1.sb("x_a", [128, D], F32)
        x_b = p1.sb("x_b", [128, D], F32)
        gk = p1.sb("gk", [128, 3 * NTT], F32)
        idxc = p1.sb("idxc", [128, 2], I32)
        yg = p1.sb("yg", [128, 2 * D], BF16)
        x2t = p1.sb("x2t", [128, D], F32)
        xn_tile = p1.sb("xn_tile", [128, D], BF16)
        red = p1.sb("red", [128, 8], F32)
        xs_sc = p1.sb("xs_sc", [128, D], F32)
        wk2 = p1.sb("wk2", [128, 1], F32)
        pT = p1.ps("pT", [128, 128], BF16)

        gtok1 = kb.dma("sync", gk[:, 0:NTT], ap["gidx"], ldr_)
        gtok2 = kb.dma("sync", gk[:, NTT:2 * NTT], ap["keep"], ldr_)
        gtok3 = kb.dma("sync", gk[:, 2 * NTT:3 * NTT], ap["w_tm"], ldr_)
        x2_war = []
        g_tok = {}
        stt_tok = {}

        def emit_gather(i):
            b = i % 2
            kb.op("vector", "tensor_copy", idxc[:, b:b + 1], gk[:, i:i + 1],
                  wait=[gtok1] + stt_tok.get(b, []))
            g_tok[b] = kb.idma("gpsimd", yg[:, b * D:(b + 1) * D], None,
                               ap["expert_out_all"],
                               bass.IndirectOffsetOnAxis(ap=idxc[:, b:b + 1],
                                                         axis=0),
                               gr, wait=[(v, v.n)])

        emit_gather(0)
        tw = {}
        for i in range(NTT):
            b = i % 2
            xb = x_a if i % 2 == 0 else x_b
            xtok = kb.dma("sync", xb[:], ap["x1"][i * 128:(i + 1) * 128, :], ldr_,
                          wait=[v] if i >= 2 else [])
            if i + 1 < NTT:
                emit_gather(i + 1)
            kb.op("vector", "tensor_mul", wk2[:], gk[:, NTT + i:NTT + i + 1],
                  gk[:, 2 * NTT + i:2 * NTT + i + 1], wait=[gtok2, gtok3])
            kb.op("vector", "scalar_tensor_tensor", x2t[:], yg[:, b * D:(b + 1) * D],
                  wk2[:, 0:1], xb[:], ALU.mult, ALU.add,
                  wait=[g_tok[b], xtok, s] + x2_war)
            stt_tok[b] = [(v, v.n)]
            x2_war = [kb.dma("scalar", ap["x2"][i * 128:(i + 1) * 128, :],
                             x2t[:], str_, wait=[v])]
            emit_ln_tile(kb, x2t, xn_tile, red, xs_sc, [], epsc[:, 0:1])
            emit_transpose_to_fm(kb, xn_tile, xn3T, i, ident, pT, itok, war=tw)
        p1.close()

        fence = kb.fence() + ldr_.final_tokens() + gr.final_tokens() \
            + str_.final_tokens()
        p2 = Pool(nc)
        wbuf_d = p2.sb("wbuf", [128, 2 * 8 * 1024], BF16)
        bb = p2.sb("bb", [128, 3 * 8], F32)
        rc = p2.sb("rc", [128, TPC], BF16)
        rs = p2.sb("rs", [128, TPC], BF16)
        rperm = p2.sb("rperm", [128, 128], BF16)
        qb_bf = p2.sb("qb_bf", [128, 2 * 512], BF16)
        t1 = p2.sb("t1", [128, 2 * 512], F32)
        t2 = p2.sb("t2", [128, 2 * 512], F32)
        evb = p2.sb("evb", [128, 4 * 512], BF16)
        vfm = p2.sb("vfm", [128, 8 * TPC], BF16)
        vag = p2.sb("vag", [128, H1 * 65], BF16)
        psq = [p2.ps(f"psq{i}", [128, 512]) for i in range(2)]
        pro = [p2.ps(f"pro{i}", [128, 512]) for i in range(2)]
        pT2 = [p2.ps(f"pT2{i}", [128, 128], BF16) for i in range(2)]

        btok = kb.dma("sync", bb[:, 0:8], ap["bq"], ldr_, wait=fence)
        kb.dma("sync", bb[:, 8:16], ap["bk"], ldr_)
        kb.dma("sync", bb[:, 16:24], ap["bv"], ldr_)
        ptok = kb.dma("sync", rperm[:], ap["rot_permT"], ldr_)

        wslot = [0]

        def load_w(name):
            b = wslot[0] % 2
            wslot[0] += 1
            wb = wbuf_d[:, b * 8192:(b + 1) * 8192]
            tok = kb.dma("gpsimd", wb.rearrange("p (c o) -> p c o", c=8),
                         ap[name].rearrange("(c p) o -> p c o", p=128),
                         wlr, wait=fence + ([t] if wslot[0] > 2 else []))
            return wb, tok

        ev_war = {}

        pq_war = {}
        pr_war = {}
        qb_war = {}

        def proj_rope(cos_ap, sin_ap, out_dram, bcol_off, wb, wtok):
            ctok = kb.dma("sync", rc[:], cos_ap, ldr_, wait=[v])
            stok = kb.dma("sync", rs[:], sin_ap, ldr_, wait=[v])
            for oc in range(8):
                for cg in range(4):
                    b = cg % 2
                    pq = psq[b]
                    pr2 = pro[b]
                    for c in range(8):
                        kb.op("tensor", "matmul", pq[:],
                              wb[:, c * 1024 + oc * 128: c * 1024 + (oc + 1) * 128],
                              xn3T[:, c * TPC + cg * 512: c * TPC + (cg + 1) * 512],
                              start=(c == 0), stop=(c == 7),
                              wait=[wtok] + pq_war.get(b, [(v, v.n)])
                              if c == 0 else [])
                    kb.op("vector", "tensor_scalar",
                          qb_bf[:, b * 512:(b + 1) * 512], pq[:],
                          bb[:, bcol_off + oc:bcol_off + oc + 1],
                          None, ALU.add,
                          wait=[(t, t.n), btok, ctok, stok] + qb_war.get(b, []))
                    pq_war[b] = [(v, v.n)]
                    kb.op("tensor", "matmul", pr2[:], rperm[:],
                          qb_bf[:, b * 512:(b + 1) * 512],
                          start=True, stop=True,
                          wait=[(v, v.n), ptok] + pr_war.get(b, []))
                    kb.op("vector", "tensor_mul", t1[:, b * 512:(b + 1) * 512],
                          qb_bf[:, b * 512:(b + 1) * 512],
                          rc[:, cg * 512:(cg + 1) * 512])
                    kb.op("vector", "tensor_mul", t2[:, b * 512:(b + 1) * 512],
                          pr2[:],
                          rs[:, cg * 512:(cg + 1) * 512], wait=[(t, t.n)])
                    pr_war[b] = [(v, v.n)]
                    sl = b * 512
                    kb.op("vector", "tensor_add", evb[:, sl:sl + 512],
                          t1[:, b * 512:(b + 1) * 512],
                          t2[:, b * 512:(b + 1) * 512],
                          wait=ev_war.get(sl, []))
                    qb_war[b] = [(v, v.n)]
                    d3 = fm_dram(out_dram)
                    ev_war[sl] = [kb.dma("scalar",
                                         d3[:, oc, cg * 512:(cg + 1) * 512],
                                         evb[:, sl:sl + 512], str_, wait=[v])]

        wb, wtok = load_w("WqT1")
        proj_rope(ap["r1_cos_q"], ap["r1_sin_q"], ap["qTd1"], 0, wb, wtok)
        wb, wtok = load_w("WkT1")
        proj_rope(ap["r1_cos"], ap["r1_sin"], ap["kTd1"], 8, wb, wtok)

        wb, wtok = load_w("WvT1")
        for oc in range(8):
            for cg in range(4):
                b = cg % 2
                pq = psq[b]
                for c in range(8):
                    kb.op("tensor", "matmul", pq[:],
                          wb[:, c * 1024 + oc * 128: c * 1024 + (oc + 1) * 128],
                          xn3T[:, c * TPC + cg * 512: c * TPC + (cg + 1) * 512],
                          start=(c == 0), stop=(c == 7),
                          wait=[wtok] + pq_war.get(b, [(v, v.n)])
                          if c == 0 else [])
                kb.op("vector", "tensor_scalar",
                      vfm[:, oc * TPC + cg * 512: oc * TPC + (cg + 1) * 512],
                      pq[:], bb[:, 16 + oc:16 + oc + 1], None, ALU.add,
                      wait=[(t, t.n)])
                pq_war[b] = [(v, v.n)]
        vw_war = []
        tp_war = {}
        for i in range(NTT):
            kb.op("vector", "memset", vag[:, 64::65], 1.0, wait=vw_war)
            for hp in range(8):
                b = hp % 2
                kb.op("tensor", "transpose", pT2[b][:],
                      vfm[:, hp * TPC + i * 128: hp * TPC + (i + 1) * 128],
                      ident[:],
                      wait=[itok] + ([(v, v.n)] if hp < 2
                                     else tp_war.get(b, [(v, v.n)])))
                kb.op("vector", "tensor_copy",
                      vag[:, (2 * hp) * 65:(2 * hp) * 65 + 64], pT2[b][:, 0:64],
                      wait=[(t, t.n)])
                kb.op("vector", "tensor_copy",
                      vag[:, (2 * hp + 1) * 65:(2 * hp + 1) * 65 + 64],
                      pT2[b][:, 64:128])
                tp_war[b] = [(v, v.n)]
            vw_war[:] = [kb.dma("scalar", ap["v_aug"][i * 128:(i + 1) * 128, :],
                                vag[:], str_, wait=[v])]
        kb.build(final_rings=[ldr_, wlr, str_, gr])
        p2.close()
    return nc, ap


def prep_launch_D(inputs, core, stB, stC):
    d = inputs
    im = {}
    im["x1"] = stB[core]["x1"]
    im["expert_out_all"] = np.concatenate(
        [stC[j]["expert_out"] for j in range(NCORE)], 0)
    gr_ = np.asarray(stC[0]["gidx_row"]).reshape(-1)[core * TPC:(core + 1) * TPC]
    im["gidx"] = np.ascontiguousarray(gr_.reshape(NTT, 128).T)
    kr_ = np.asarray(stC[0]["keep_row"]).reshape(-1)[core * TPC:(core + 1) * TPC]
    im["keep"] = np.ascontiguousarray(kr_.reshape(NTT, 128).T)
    im["w_tm"] = stB[core]["w_tm"]
    im["WqT1"] = np.ascontiguousarray(np.asarray(d["L1_attn_Wq"]).T)
    im["WkT1"] = np.ascontiguousarray(np.asarray(d["L1_attn_Wk"]).T)
    im["WvT1"] = np.ascontiguousarray(np.asarray(d["L1_attn_Wv"]).T)
    im["bq"] = np.ascontiguousarray(np.asarray(d["L1_attn_bq"]).reshape(8, 128).T)
    im["bk"] = np.ascontiguousarray(np.asarray(d["L1_attn_bk"]).reshape(8, 128).T)
    im["bv"] = np.ascontiguousarray(np.asarray(d["L1_attn_bv"]).reshape(8, 128).T)
    tloc = np.arange(TPC, dtype=np.float32)
    pos = (core * TPC + tloc) % 8192
    inv1 = 1.0 / 10000.0 ** (np.arange(0, DH1, 2, dtype=np.float32) / DH1)
    fr1 = np.concatenate([inv1, inv1])[:, None] * pos[None, :]
    cos2 = np.concatenate([np.cos(fr1)] * 2, 0)
    sin2 = np.concatenate([np.sin(fr1)] * 2, 0)
    s1 = DH1 ** -0.5
    im["r1_cos"] = cos2.astype(BF)
    im["r1_sin"] = sin2.astype(BF)
    im["r1_cos_q"] = (cos2 * s1).astype(BF)
    im["r1_sin_q"] = (sin2 * s1).astype(BF)
    P = np.zeros((128, 128), np.float32)
    for hb in range(2):
        o = hb * 64
        for j in range(32):
            P[o + j, o + 32 + j] = -1.0
            P[o + 32 + j, o + j] = 1.0
    im["rot_permT"] = P.T.copy().astype(BF)
    im["ident"] = np.eye(128, dtype=np.float32).astype(BF)
    return im




def build_launch_E():
    nc = bass.Bass("TRN2", target_bir_lowering=False, debug=False)
    ap = {}

    def din(name, shape, dt=F32):
        ap[name] = nc.dram_tensor(name, shape, dt, kind="ExternalInput").ap()

    def dout(name, shape, dt=F32):
        ap[name] = nc.dram_tensor(name, shape, dt, kind="ExternalOutput").ap()

    din("qT", [D, TPC], BF16)
    din("kT_all", [D, 8192], BF16)
    din("v_aug_all", [8192, H1 * 65], BF16)
    din("ident", [128, 128], BF16)
    din("identf", [128, 128])
    din("onesf", [128, 128])
    dout("attT", [D, TPC], BF16)

    NKT = 8192 // 128

    ctx = ExitStack()
    with ctx:
        kb = KB(nc, ctx)
        ldr_ = kb.ring("ld", 6)
        str_ = kb.ring("st", 4)
        v, t, s = kb.e["vector"], kb.e["tensor"], kb.e["scalar"]

        ident = ctx.enter_context(nc.sbuf_tensor("identB", [128, 128], BF16))
        itok = kb.dma("sync", ident[:], ap["ident"], ldr_)
        identf = ctx.enter_context(nc.sbuf_tensor("identF", [128, 128], F32))
        iftok = kb.dma("sync", identf[:], ap["identf"], ldr_)
        onesf = ctx.enter_context(nc.sbuf_tensor("onesf_sb", [128, 128], F32))
        otok = kb.dma("sync", onesf[:], ap["onesf"], ldr_)

        p = Pool(nc)
        qsb = [p.sb(f"qsb{i}", [128, TPC], BF16) for i in range(2)]
        ksb = [p.sb(f"ksb{i}", [128, 8192], BF16) for i in range(2)]
        vsb = [p.sb(f"vsb{i}", [128, 64 * 130], BF16) for i in range(2)]
        PTA = [p.sb(f"PTA{i}", [128, 1024], BF16) for i in range(3)]
        PTB = [p.sb(f"PTB{i}", [128, 1024], BF16) for i in range(3)]
        mcol = p.sb("mcol", [128, 8], F32)
        rec_sb = p.sb("rec_sb", [1, TPC], F32)
        rb_sb = p.sb("rb_sb", [64, TPC], F32)
        att_sb = p.sb("att_sb", [128, TPC], BF16)
        psA = p.ps("psA", [128, 1024])
        psB = p.ps("psB", [128, 1024])
        pvA = p.ps("pvA", [128, 1024])
        pvB = p.ps("pvB", [128, 1024])

        q3 = fm_dram(ap["qT"])
        k3 = fm_dram(ap["kT_all"])
        v3 = ap["v_aug_all"].rearrange("(kt p) c -> p kt c", p=128)
        a3 = fm_dram(ap["attT"])

        ld_war = [[], []]
        att_war = []
        ld_toks = {}

        def issue_loads(ch):
            bslot = ch % 2
            t1 = kb.dma("sync", qsb[bslot][:], q3[:, ch, :], ldr_,
                        wait=ld_war[bslot])
            t2 = kb.dma("sync", ksb[bslot][:], k3[:, ch, :], ldr_,
                        wait=ld_war[bslot])
            t3 = kb.dma("sync", vsb[bslot][:].rearrange("p (kt c) -> p kt c", c=130),
                        v3[:, :, 2 * ch * 65:(2 * ch + 2) * 65], ldr_,
                        wait=ld_war[bslot])
            ld_toks[ch] = (t1, t2, t3)

        issue_loads(0)
        for ch in range(H1 // 2):
            bslot = ch % 2
            qtok, ktok, vtok = ld_toks[ch]
            qs, ks, vs = qsb[bslot], ksb[bslot], vsb[bslot]
            if ch + 1 < H1 // 2:
                issue_loads(ch + 1)

            for i, kt in enumerate((0, 32)):
                for half in range(2):
                    kb.op("tensor", "matmul", psA[:, half * 512:(half + 1) * 512],
                          ks[0:64, kt * 128:(kt + 1) * 128],
                          qs[0:64, half * 512:(half + 1) * 512],
                          start=True, stop=True, tile_position=(0, 0),
                          wait=[qtok, ktok, s, v] if half == 0 else [])
                    kb.op("tensor", "matmul", psB[:, half * 512:(half + 1) * 512],
                          ks[64:128, kt * 128:(kt + 1) * 128],
                          qs[64:128, half * 512:(half + 1) * 512],
                          start=True, stop=True, tile_position=(64, 0))
                kb.op("vector", "reduce_max", mcol[:, 2 * i:2 * i + 1], psA[:],
                      AXX, wait=[t])
                kb.op("vector", "reduce_max", mcol[:, 2 * i + 1:2 * i + 2], psB[:],
                      AXX)
            kb.op("vector", "tensor_max", mcol[:, 4:5], mcol[:, 0:1], mcol[:, 2:3])
            kb.op("vector", "tensor_max", mcol[:, 5:6], mcol[:, 1:2], mcol[:, 3:4])
            kb.op("tensor", "transpose", psA[0:2, 0:128], mcol[:, 4:6],
                  identf[:], wait=[v, iftok])
            kb.op("vector", "reduce_max", mcol[0:2, 6:7], psA[0:2, 0:128], AXX,
                  wait=[t])
            kb.op("vector", "tensor_scalar_mul", mcol[0:2, 7:8], mcol[0:2, 6:7],
                  -1.0)
            kb.op("vector", "tensor_scalar", mcol[0:2, 4:6], identf[0:2, 0:2],
                  mcol[0:2, 7:8], None, ALU.mult)
            kb.op("tensor", "matmul", psA[0:128, 0:2], onesf[0:2, 0:128],
                  mcol[0:2, 4:6], start=True, stop=True, wait=[v, otok])
            kb.op("vector", "tensor_copy", mcol[:, 6:8], psA[:, 0:2], wait=[t])

            for qc in range(2):
                qof = qc * 1024
                expA = [0] * NKT
                expB = [0] * NKT
                for kt in range(NKT):
                    wA = [(s, expA[kt - 1])] if kt >= 1 else [v, s]
                    wB = [(s, expB[kt - 1])] if kt >= 1 else []
                    kb.op("tensor", "matmul", psA[:, 0:512],
                          ks[0:64, kt * 128:(kt + 1) * 128],
                          qs[0:64, qof:qof + 512],
                          start=True, stop=True, tile_position=(0, 0), wait=wA)
                    kb.op("tensor", "matmul", psA[:, 512:1024],
                          ks[0:64, kt * 128:(kt + 1) * 128],
                          qs[0:64, qof + 512:qof + 1024],
                          start=True, stop=True, tile_position=(0, 0))
                    tA = kb.e["tensor"].n
                    kb.op("tensor", "matmul", psB[:, 0:512],
                          ks[64:128, kt * 128:(kt + 1) * 128],
                          qs[64:128, qof:qof + 512],
                          start=True, stop=True, tile_position=(64, 0), wait=wB)
                    kb.op("tensor", "matmul", psB[:, 512:1024],
                          ks[64:128, kt * 128:(kt + 1) * 128],
                          qs[64:128, qof + 512:qof + 1024],
                          start=True, stop=True, tile_position=(64, 0))
                    tB = kb.e["tensor"].n
                    kb.op("scalar", "activation", PTA[kt % 3][:], psA[:], AF.Exp,
                          bias=mcol[:, 6:7], wait=[(t, tA)])
                    expA[kt] = s.n
                    kb.op("scalar", "activation", PTB[kt % 3][:], psB[:], AF.Exp,
                          bias=mcol[:, 7:8], wait=[(t, tB)])
                    expB[kt] = s.n
                    if kt >= 1:
                        for half in range(2):
                            kb.op("tensor", "matmul",
                                  pvA[0:65, half * 512:(half + 1) * 512],
                                  vs[:, (kt - 1) * 130:(kt - 1) * 130 + 65],
                                  PTA[(kt - 1) % 3][:, half * 512:(half + 1) * 512],
                                  start=(kt == 1), stop=False,
                                  wait=([vtok, (s, expA[kt - 1])] if kt == 1
                                        else [(s, expA[kt - 1])])
                                  if half == 0 else [])
                        for half in range(2):
                            kb.op("tensor", "matmul",
                                  pvB[0:65, half * 512:(half + 1) * 512],
                                  vs[:, (kt - 1) * 130 + 65:kt * 130],
                                  PTB[(kt - 1) % 3][:, half * 512:(half + 1) * 512],
                                  start=(kt == 1), stop=False,
                                  wait=[(s, expB[kt - 1])] if half == 0 else [])
                for half in range(2):
                    kb.op("tensor", "matmul",
                          pvA[0:65, half * 512:(half + 1) * 512],
                          vs[:, (NKT - 1) * 130:(NKT - 1) * 130 + 65],
                          PTA[(NKT - 1) % 3][:, half * 512:(half + 1) * 512],
                          start=False, stop=True,
                          wait=[(s, expA[NKT - 1])] if half == 0 else [])
                for half in range(2):
                    kb.op("tensor", "matmul",
                          pvB[0:65, half * 512:(half + 1) * 512],
                          vs[:, (NKT - 1) * 130 + 65:NKT * 130],
                          PTB[(NKT - 1) % 3][:, half * 512:(half + 1) * 512],
                          start=False, stop=True,
                          wait=[(s, expB[NKT - 1])] if half == 0 else [])

                for hh, pvX, sub in ((0, pvA, 0), (1, pvB, 64)):
                    kb.op("scalar", "activation", rec_sb[:, qof:qof + 1024],
                          pvX[64:65, :], AF.Ln, wait=[t, v])
                    kb.op("scalar", "activation", rec_sb[:, qof:qof + 1024],
                          rec_sb[:, qof:qof + 1024], AF.Exp, scale=-1.0)
                    for sl in range(2):
                        kb.op("tensor", "matmul",
                              psA[0:64, sl * 512:(sl + 1) * 512],
                              onesf[0:1, 0:64],
                              rec_sb[:, qof + sl * 512:qof + (sl + 1) * 512],
                              start=True, stop=True, wait=[s, v])
                    kb.op("vector", "tensor_copy", rb_sb[:, qof:qof + 1024],
                          psA[0:64, :], wait=[t])
                    kb.op("vector", "tensor_mul",
                          att_sb[sub:sub + 64, qof:qof + 1024],
                          pvX[0:64, :], rb_sb[:, qof:qof + 1024],
                          wait=att_war)
            ld_war[bslot] = [(t, t.n), (s, s.n)]
            att_war = [kb.dma("scalar", a3[:, ch, :], att_sb[:], str_,
                              wait=[v])]

        kb.build(final_rings=[ldr_, str_])
        p.close()
    return nc, ap


def prep_launch_E(inputs, core, stD):
    b0 = (core // 4) * 4
    im = {}
    im["identf"] = np.eye(128, dtype=np.float32)
    im["qT"] = stD[core]["qTd1"]
    im["kT_all"] = np.concatenate(
        [stD[j]["kTd1"] for j in range(b0, b0 + 4)], 1)
    im["v_aug_all"] = np.concatenate(
        [stD[j]["v_aug"] for j in range(b0, b0 + 4)], 0)
    im["ident"] = np.eye(128, dtype=np.float32).astype(BF)
    im["onesf"] = np.ones((128, 128), np.float32)
    return im




def build_launch_F():
    nc = bass.Bass("TRN2", target_bir_lowering=False, debug=False)
    ap = {}

    def din(name, shape, dt=F32):
        ap[name] = nc.dram_tensor(name, shape, dt, kind="ExternalInput").ap()

    def dout(name, shape, dt=F32):
        ap[name] = nc.dram_tensor(name, shape, dt, kind="ExternalOutput").ap()

    din("x2", [TPC, D])
    din("attT", [D, TPC], BF16)
    din("WoT1", [D, D])
    din("bo_b", [128, D])
    din("Wr", [D, NE1])
    din("ident", [128, 128], BF16)
    din("identf", [128, 128])
    dout("x3", [TPC, D])
    dout("xn4", [TPC, D], BF16)
    dout("G_em", [NE1, TPC])
    dout("M1_em", [NE1, TPC])
    dout("M2_em", [NE1, TPC])

    ctx = ExitStack()
    with ctx:
        kb = KB(nc, ctx)
        ldr_ = kb.ring("ld", 4)
        wlr = kb.ring("wl", 2)
        str_ = kb.ring("st", 4)
        v, t, s = kb.e["vector"], kb.e["tensor"], kb.e["scalar"]

        ident = ctx.enter_context(nc.sbuf_tensor("identB", [128, 128], BF16))
        itok = kb.dma("sync", ident[:], ap["ident"], ldr_)
        identf = ctx.enter_context(nc.sbuf_tensor("identF", [128, 128], F32))
        iftok = kb.dma("sync", identf[:], ap["identf"], ldr_)
        epsc = ctx.enter_context(nc.sbuf_tensor("epsc", [128, 1], F32))
        kb.op("vector", "memset", epsc[:], EPS)

        p = Pool(nc)
        wbuf = p.sb("wbuf", [128, 8 * 1024], BF16)
        wrb = p.sb("wrb", [128, 8 * NE1], BF16)
        bob = p.sb("bob", [128, D], F32)
        atile = p.sb("atile", [128, 8 * 128], BF16)
        x_a = p.sb("x_a", [128, D], F32)
        x_b = p.sb("x_b", [128, D], F32)
        x3t = p.sb("x3t", [128, D], F32)
        xn_tile = p.sb("xn_tile", [128, D], BF16)
        red = p.sb("red", [128, 8], F32)
        xs_sc = p.sb("xs_sc", [128, D], F32)
        xnTt = p.sb("xnTt", [128, 8 * 128], BF16)
        lg = p.sb("lg", [128, NE1], F32)
        ex = p.sb("ex", [128, NE1], F32)
        pb = p.sb("pb", [128, NE1], F32)
        t8 = p.sb("t8", [128, 8], F32)
        sc1 = p.sb("sc1", [128, 6 * NE1], F32)
        Gm = p.sb("Gm", [NE1, TPC], F32)
        M1m = p.sb("M1m", [NE1, TPC], F32)
        M2m = p.sb("M2m", [NE1, TPC], F32)
        pj2 = [p.ps(f"pj2{i}", [128, 512]) for i in range(2)]
        pT = p.ps("pT", [128, 128], BF16)
        pr = p.ps("pr", [128, NE1])
        pG = p.ps("pG", [128, 128])

        w3 = ap["WoT1"].rearrange("(c p) o -> p c o", p=128)
        wotok = kb.dma("gpsimd", wbuf[:].rearrange("p (c o) -> p c o", c=8),
                       w3, wlr)
        wr3 = ap["Wr"].rearrange("(c p) e -> p c e", p=128)
        wrtok = kb.dma("gpsimd", wrb[:].rearrange("p (c e) -> p c e", c=8),
                       wr3, wlr)
        botok = kb.dma("sync", bob[:], ap["bo_b"], ldr_)

        a3 = fm_dram(ap["attT"])
        for i in range(NTT):
            xb = x_a if i % 2 == 0 else x_b
            xtok = kb.dma("sync", xb[:], ap["x2"][i * 128:(i + 1) * 128, :], ldr_,
                          wait=[v] if i >= 2 else [])
            atok = kb.dma("sync", atile[:].rearrange("p (c w) -> p c w", c=8),
                          a3[:, :, i * 128:(i + 1) * 128], ldr_, wait=[t])
            for og2 in range(2):
                pp = pj2[og2]
                for c in range(8):
                    kb.op("tensor", "matmul", pp[:],
                          atile[:, c * 128:(c + 1) * 128],
                          wbuf[:, c * 1024 + og2 * 512: c * 1024 + (og2 + 1) * 512],
                          start=(c == 0), stop=(c == 7),
                          wait=[wotok, v, atok] if c == 0 else [])
                kb.op("vector", "scalar_tensor_tensor",
                      x3t[:, og2 * 512:(og2 + 1) * 512],
                      xb[:, og2 * 512:(og2 + 1) * 512], 0.0, pp[:],
                      ALU.bypass, ALU.add, wait=[t, xtok, s])
                kb.op("vector", "tensor_add", x3t[:, og2 * 512:(og2 + 1) * 512],
                      x3t[:, og2 * 512:(og2 + 1) * 512],
                      bob[:, og2 * 512:(og2 + 1) * 512], wait=[botok])
            kb.dma("scalar", ap["x3"][i * 128:(i + 1) * 128, :], x3t[:], str_,
                   wait=[v])
            emit_ln_tile(kb, x3t, xn_tile, red, xs_sc, [], epsc[:, 0:1])
            kb.dma("scalar", ap["xn4"][i * 128:(i + 1) * 128, :], xn_tile[:], str_,
                   wait=[v])
            emit_transpose_to_fm(kb, xn_tile, xnTt, 0, ident, pT, itok, tcols=128)
            for c in range(8):
                kb.op("tensor", "matmul", pr[:], xnTt[:, c * 128:(c + 1) * 128],
                      wrb[:, c * NE1:(c + 1) * NE1],
                      start=(c == 0), stop=(c == 7),
                      wait=[v, wrtok] if c == 0 else [])
            kb.op("vector", "tensor_copy", lg[:], pr[:], wait=[t])
            kb.op("vector", "reduce_max", red[:, 0:1], lg[:], AXX)
            kb.op("vector", "tensor_scalar_mul", red[:, 1:2], red[:, 0:1], -1.0)
            kb.op("scalar", "activation", ex[:], lg[:], AF.Exp,
                  bias=red[:, 1:2], accum_out=red[:, 2:3], wait=[v])
            kb.op("vector", "reciprocal", red[:, 3:4], red[:, 2:3], wait=[s])
            kb.op("vector", "tensor_scalar", pb[:], ex[:], red[:, 3:4], None,
                  ALU.mult)
            kb.op("vector", "max", t8[:], pb[:])
            eq1 = sc1[:, 0:8]; f1 = sc1[:, 8:16]; m1 = sc1[:, 16:24]
            cand = sc1[:, 24:32]; m2 = sc1[:, 32:40]; mor = sc1[:, 40:48]
            kb.op("vector", "tensor_scalar", eq1, pb[:], t8[:, 0:1], None,
                  ALU.is_equal)
            kb.op("vector", "tensor_tensor_scan", f1, eq1, eq1, 0.0,
                  ALU.add, ALU.bypass)
            kb.op("vector", "tensor_single_scalar", f1, f1, 1.0, ALU.is_equal)
            kb.op("vector", "tensor_mul", m1, eq1, f1)
            kb.op("vector", "tensor_scalar", eq1, pb[:], t8[:, 1:2], None,
                  ALU.is_equal)
            kb.op("vector", "tensor_scalar", cand, m1, -1.0, 1.0,
                  ALU.mult, ALU.add)
            kb.op("vector", "tensor_mul", cand, cand, eq1)
            kb.op("vector", "tensor_tensor_scan", f1, cand, cand, 0.0,
                  ALU.add, ALU.bypass)
            kb.op("vector", "tensor_single_scalar", f1, f1, 1.0, ALU.is_equal)
            kb.op("vector", "tensor_mul", m2, cand, f1)
            kb.op("vector", "tensor_add", mor, m1, m2)
            kb.op("vector", "tensor_mul", mor, mor, pb[:])
            kb.op("tensor", "transpose", pG[0:NE1, :], mor, identf[:],
                  wait=[v, iftok])
            kb.op("vector", "tensor_copy", Gm[0:NE1, i * 128:(i + 1) * 128],
                  pG[0:NE1, 0:128], wait=[t])
            kb.op("tensor", "transpose", pG[0:NE1, :], m1, identf[:],
                  wait=[v, iftok])
            kb.op("vector", "tensor_copy", M1m[0:NE1, i * 128:(i + 1) * 128],
                  pG[0:NE1, 0:128], wait=[t])
            kb.op("tensor", "transpose", pG[0:NE1, :], m2, identf[:],
                  wait=[v, iftok])
            kb.op("vector", "tensor_copy", M2m[0:NE1, i * 128:(i + 1) * 128],
                  pG[0:NE1, 0:128], wait=[t])
        kb.dma("scalar", ap["G_em"], Gm[:], str_, wait=[v])
        kb.dma("scalar", ap["M1_em"], M1m[:], str_, wait=[v])
        kb.dma("scalar", ap["M2_em"], M2m[:], str_, wait=[v])
        kb.build(final_rings=[ldr_, wlr, str_])
        p.close()
    return nc, ap


def prep_launch_F(inputs, core, stD, stE):
    d = inputs
    im = {}
    im["x2"] = stD[core]["x2"]
    im["attT"] = stE[core]["attT"]
    im["WoT1"] = np.ascontiguousarray(np.asarray(d["L1_attn_Wo"]).T)
    im["bo_b"] = np.broadcast_to(np.asarray(d["L1_attn_bo"]), (128, D)).copy()
    im["Wr"] = np.ascontiguousarray(np.asarray(d["L1_ffn_Wr"]))
    im["ident"] = np.eye(128, dtype=np.float32).astype(BF)
    im["identf"] = np.eye(128, dtype=np.float32)
    return im



def build_launch_G():
    nc = bass.Bass("TRN2", target_bir_lowering=False, debug=False)
    ap = {}

    def din(name, shape, dt=F32):
        ap[name] = nc.dram_tensor(name, shape, dt, kind="ExternalInput").ap()

    def dout(name, shape, dt=F32):
        ap[name] = nc.dram_tensor(name, shape, dt, kind="ExternalOutput").ap()

    NTOK = NCORE * TPC
    din("g_own", [128, 128])
    din("g_row", [1, NTOK])
    din("xn4_all", [NTOK, D], BF16)
    din("W1", [D, 4096])
    din("W2", [4096, D])
    din("ident", [128, 128], BF16)
    din("onesb", [128, 128], BF16)
    din("iota_t", [128, 128], I32)
    din("trashg", [128, 128])
    dout("expert_out", [PAD1, D], BF16)
    dout("theta", [1, 1])

    invtab = nc.dram_tensor("invtabG", [21632, 1], I32, kind="Internal").ap()
    rowb = nc.dram_tensor("rowbG", [NTOK], F32, kind="Internal").ap()

    ctx = ExitStack()
    with ctx:
        kb = KB(nc, ctx)
        ldr_ = kb.ring("ld", 4)
        wlr = kb.ring("wl", 2)
        wlr2 = kb.ring("wl2", 2)
        str_ = kb.ring("st", 4)
        gr = kb.ring("gq", 4)
        v, t, s, g = kb.e["vector"], kb.e["tensor"], kb.e["scalar"], kb.e["gpsimd"]

        ident = ctx.enter_context(nc.sbuf_tensor("identB", [128, 128], BF16))
        itok = kb.dma("sync", ident[:], ap["ident"], ldr_)
        onesb = ctx.enter_context(nc.sbuf_tensor("onesB", [128, 128], BF16))
        obtok = kb.dma("sync", onesb[:], ap["onesb"], ldr_)

        p1 = Pool(nc)
        go = p1.sb("go", [128, 128], F32)
        mk = p1.sb("mk", [128, 128], F32)
        bi = p1.sb("bi", [128, 8], F32)
        pcb = p1.sb("pcb", [128, 1], BF16)
        selb = p1.sb("selb", [1, 1], BF16)
        selm = p1.sb("selm", [128, 1], I32)
        pcnt = p1.ps("pcnt", [1, 1])
        pbc2 = p1.ps("pbc2", [128, 1])

        gotok = kb.dma("sync", go[:], ap["g_own"], ldr_)
        kb.op("vector", "memset", bi[:, 0:1], 0.0)
        kb.op("vector", "memset", bi[:, 1:2], 1.0)
        kb.op("vector", "memset", bi[:, 2:3], 0.5)
        for it in range(45):
            kb.op("vector", "tensor_scalar", mk[:], go[:], bi[:, 2:3], None,
                  ALU.is_gt, ALU.add, accum_out=bi[:, 4:5], wait=[gotok])
            kb.op("vector", "tensor_copy", pcb[:], bi[:, 4:5])
            kb.op("tensor", "matmul", pcnt[:], pcb[:], onesb[:, 0:1],
                  start=True, stop=True, wait=[v, obtok])
            kb.op("vector", "tensor_single_scalar", bi[0:1, 5:6], pcnt[0:1, 0:1],
                  float(CAP1) + 0.5, ALU.is_le, wait=[t])
            kb.op("vector", "tensor_copy", selb[:], bi[0:1, 5:6])
            kb.op("tensor", "matmul", pbc2[:], onesb[0:1, :], selb[:],
                  start=True, stop=True, wait=[v])
            kb.op("vector", "tensor_copy", selm[:], pbc2[:, 0:1], wait=[t])
            kb.op("vector", "select", bi[:, 7:8], selm[:], bi[:, 2:3],
                  bi[:, 1:2])
            kb.op("vector", "select", bi[:, 0:1], selm[:], bi[:, 0:1],
                  bi[:, 2:3])
            kb.op("vector", "tensor_copy", bi[:, 1:2], bi[:, 7:8])
            kb.op("vector", "tensor_add", bi[:, 2:3], bi[:, 0:1], bi[:, 1:2])
            kb.op("vector", "tensor_scalar_mul", bi[:, 2:3], bi[:, 2:3], 0.5)
        kb.dma("scalar", ap["theta"], bi[0:1, 1:2], str_, wait=[v])

        grow = p1.sb("grow", [1, NTOK], F32)
        mrow = p1.sb("mrow", [1, NTOK], F32)
        crow = p1.sb("crow", [1, NTOK], F32)
        rtok = kb.dma("sync", grow[:], ap["g_row"], ldr_)
        kb.op("vector", "tensor_scalar", mrow[:], grow[:], bi[0:1, 1:2], None,
              ALU.is_gt, wait=[rtok])
        kb.op("vector", "tensor_tensor_scan", crow[:], mrow[:], mrow[:], 0.0,
              ALU.add, ALU.bypass)
        kb.op("vector", "tensor_sub", crow[:], crow[:], mrow[:])
        kb.op("vector", "tensor_scalar", mrow[:], mrow[:], -1e6, 1e6,
              ALU.mult, ALU.add)
        kb.op("vector", "tensor_add", crow[:], crow[:], mrow[:])
        rbtok = kb.dma("scalar", rowb.rearrange("(a x) -> a x", a=1), crow[:],
                       str_, wait=[v])

        iota = p1.sb("iotaG", [128, 128], I32)
        zs = p1.sb("zsG", [128, 41], I32)
        idxf = p1.sb("idxf", [128, 128], F32)
        negi = p1.sb("negiG", [128, 128], F32)
        idxi = p1.sb("idxiG", [128, 128], I32)
        trashg = p1.sb("trashg_s", [128, 128], F32)
        ittok = kb.dma("sync", iota[:], ap["iota_t"], ldr_)
        tgt = kb.dma("sync", trashg[:], ap["trashg"], ldr_)
        kb.op("gpsimd", "memset", zs[:], 0)
        ztok = kb.dma("gpsimd",
                      invtab[0:5248, :].rearrange("(p k) one -> p (k one)", p=128),
                      zs[:], gr, wait=[g])
        ixtok = kb.dma("sync", idxf[:], rowb.rearrange("(a b) -> a b", a=128),
                       ldr_, wait=[rbtok])
        kb.op("vector", "tensor_single_scalar", negi[:], idxf[:], 5120.5,
              ALU.is_lt, wait=[ixtok])
        kb.op("vector", "tensor_mul", idxf[:], idxf[:], negi[:])
        kb.op("vector", "tensor_scalar", negi[:], negi[:], -1.0, 1.0,
              ALU.mult, ALU.add)
        kb.op("vector", "tensor_mul", negi[:], negi[:], trashg[:], wait=[tgt])
        kb.op("vector", "tensor_add", idxf[:], idxf[:], negi[:])
        kb.op("vector", "tensor_copy", idxi[:], idxf[:])
        vsc = (v, v.n)
        scr = kb.ring("scrG", 8)
        for tt_ in range(128):
            kb.idma("gpsimd", invtab,
                    bass.IndirectOffsetOnAxis(ap=idxi[:, tt_:tt_ + 1], axis=0),
                    iota[:, tt_:tt_ + 1], None, ring=scr,
                    wait=[vsc, ittok, ztok] if tt_ == 0 else [])
        scb_tok = scr.final_tokens()
        p1.close()

        fence_w = kb.fence() + gr.final_tokens() + ldr_.final_tokens() \
            + str_.final_tokens()
        fence = fence_w + scb_tok
        p2 = Pool(nc)
        w1b = p2.sb("w1b", [128, 8 * 4096], BF16)
        w2b = p2.sb("w2b", [128, 32 * D], BF16)
        bufs = dict(
            idxc=[p2.sb(f"idxc{i}", [128, 1], I32) for i in range(4)],
            xg=[p2.sb(f"xg{i}", [128, D], BF16) for i in range(4)],
            xgT=p2.sb("xgT", [128, 8 * 512], BF16),
            hT=p2.sb("hT", [128, 32 * 512], BF16),
            eot=p2.sb("eot", [128, 2 * 512], BF16),
            pT2=[p2.ps(f"pT2{i}", [128, 128], BF16) for i in range(2)],
            ps1=[p2.ps(f"ps1{i}", [128, 512]) for i in range(2)],
            ps2=[p2.ps(f"ps2{i}", [128, 512]) for i in range(2)],
            ident=ident, itok=itok,
        )
        zr = p2.sb("zr", [1, D], BF16)
        kb.op("vector", "memset", zr[:], 0.0, wait=fence)
        kb.dma("scalar", ap["expert_out"][PAD1 - 1:PAD1, :], zr[:], str_,
               wait=[v])
        w1tok = emit_wload_g(kb, ap["W1"], w1b, wlr, 4096, fence)
        w2tok = emit_wload_g(kb, ap["W2"], w2b, wlr2, D, fence)
        blocks = [(b * 512, 4) for b in range(10)]
        emit_ffn_blocks(kb, ap["xn4_all"], invtab, ap["expert_out"],
                        w1b, w2b, bufs, AF.Relu, blocks,
                        w1tok, w2tok, fence, ldr_, gr, str_)
        kb.build(final_rings=[ldr_, wlr, wlr2, str_, gr])
        p2.close()
    return nc, ap


def prep_launch_G(inputs, core, stF):
    d = inputs
    im = {}
    G_all = np.concatenate([stF[j]["G_em"] for j in range(NCORE)], 1)
    im["g_own"] = np.ascontiguousarray(G_all[core].reshape(128, 128))
    im["g_row"] = np.ascontiguousarray(G_all[core].reshape(1, -1))
    im["xn4_all"] = np.concatenate([stF[j]["xn4"] for j in range(NCORE)], 0)
    im["W1"] = np.ascontiguousarray(np.asarray(d["L1_ffn_W1"][core]))
    im["W2"] = np.ascontiguousarray(np.asarray(d["L1_ffn_W2"][core]))
    im["ident"] = np.eye(128, dtype=np.float32).astype(BF)
    im["onesb"] = np.ones((128, 128), np.float32).astype(BF)
    p = np.arange(128)
    im["iota_t"] = (p[:, None] * 128 + np.arange(128)[None, :]).astype(np.int32)
    im["trashg"] = (5152 + np.arange(128)[:, None] * 128
                    + np.arange(128)[None, :]).astype(np.float32)
    return im



def build_launch_Ha():
    nc = bass.Bass("TRN2", target_bir_lowering=False, debug=False)
    ap = {}

    def din(name, shape, dt=F32):
        ap[name] = nc.dram_tensor(name, shape, dt, kind="ExternalInput").ap()

    def dout(name, shape, dt=F32):
        ap[name] = nc.dram_tensor(name, shape, dt, kind="ExternalOutput").ap()

    NTOK = NCORE * TPC
    din("G_all", [NE1, NTOK])
    din("M1_all", [NE1, NTOK])
    din("M2_all", [NE1, NTOK])
    din("theta_all", [NE1, 1])
    din("eoff8", [NE1, 1])
    dout("slota_row", [1, NTOK]); dout("wa_row", [1, NTOK])
    dout("slotb_row", [1, NTOK]); dout("wb_row", [1, NTOK])

    SEG = 4096

    ctx = ExitStack()
    with ctx:
        kb = KB(nc, ctx)
        ldr_ = kb.ring("ld", 6)
        str_ = kb.ring("st", 4)
        v, t, s = kb.e["vector"], kb.e["tensor"], kb.e["scalar"]

        p = Pool(nc)
        Gs = p.sb("Gs", [NE1, SEG], F32)
        M1s = p.sb("M1s", [NE1, SEG], F32)
        M2s = p.sb("M2s", [NE1, SEG], F32)
        Mk = p.sb("Mk", [NE1, SEG], F32)
        cum = p.sb("cum", [NE1, SEG], F32)
        wsel = p.sb("wsel", [NE1, SEG], F32)
        tmp = p.sb("tmp", [NE1, SEG], F32)
        thc = p.sb("thc", [NE1, 1], F32)
        eoffc = p.sb("eoffc8", [NE1, 1], F32)
        carry = p.sb("carry", [NE1, 2], F32)
        onesf = p.sb("onesf_h", [NE1, 1], F32)
        stg = p.sb("stg", [1, 1024], F32)
        pcs = p.ps("pcs", [1, 512])

        ttok = kb.dma("sync", thc[:], ap["theta_all"], ldr_)
        etok = kb.dma("sync", eoffc[:], ap["eoff8"], ldr_)
        kb.op("vector", "memset", onesf[:], 1.0)
        kb.op("vector", "memset", carry[:, 0:1], 0.0)
        st_tok = []

        def colsums(src_ap, out_dram, seg):
            nonlocal st_tok
            for cg in range(SEG // 512):
                kb.op("tensor", "matmul", pcs[:], onesf[:],
                      src_ap[0:NE1, cg * 512:(cg + 1) * 512],
                      start=True, stop=True, wait=[(v, v.n)])
                sl = (cg % 2) * 512
                kb.op("vector", "tensor_copy", stg[:, sl:sl + 512], pcs[:],
                      wait=[(t, kb.e["tensor"].n)] + st_tok)
                st_tok = [kb.dma(
                    "scalar",
                    out_dram[0:1, seg * SEG + cg * 512: seg * SEG + (cg + 1) * 512],
                    stg[:, sl:sl + 512], str_, wait=[v])]

        for seg in range(NTOK // SEG):
            gt = kb.dma("sync", Gs[:], ap["G_all"][:, seg * SEG:(seg + 1) * SEG],
                        ldr_, wait=[v, t])
            m1t = kb.dma("sync", M1s[:], ap["M1_all"][:, seg * SEG:(seg + 1) * SEG],
                         ldr_, wait=[v, t])
            m2t = kb.dma("sync", M2s[:], ap["M2_all"][:, seg * SEG:(seg + 1) * SEG],
                         ldr_, wait=[v, t])
            kb.op("vector", "tensor_scalar", Mk[:], Gs[:], thc[:, 0:1], None,
                  ALU.is_gt, wait=[gt, ttok])
            kb.op("vector", "tensor_tensor_scan", cum[:], Mk[:], Mk[:],
                  carry[:, 0:1], ALU.add, ALU.bypass)
            kb.op("vector", "tensor_copy", carry[:, 0:1],
                  cum[:, SEG - 1:SEG])
            kb.op("vector", "tensor_sub", cum[:], cum[:], Mk[:])
            kb.op("vector", "tensor_scalar", cum[:], cum[:], eoffc[:, 0:1],
                  None, ALU.add, wait=[etok])
            kb.op("vector", "tensor_mul", wsel[:], M1s[:], Mk[:], wait=[m1t])
            kb.op("vector", "tensor_mul", tmp[:], wsel[:], cum[:])
            colsums(tmp, ap["slota_row"], seg)
            kb.op("vector", "tensor_mul", tmp[:], wsel[:], Gs[:])
            colsums(tmp, ap["wa_row"], seg)
            kb.op("vector", "tensor_mul", wsel[:], M2s[:], Mk[:], wait=[m2t])
            kb.op("vector", "tensor_mul", tmp[:], wsel[:], cum[:])
            colsums(tmp, ap["slotb_row"], seg)
            kb.op("vector", "tensor_mul", tmp[:], wsel[:], Gs[:])
            colsums(tmp, ap["wb_row"], seg)
        kb.build(final_rings=[ldr_, str_])
        p.close()
    return nc, ap


def prep_launch_Ha(inputs, core, stF, stG):
    im = {}
    im["G_all"] = np.concatenate([stF[j]["G_em"] for j in range(NCORE)], 1)
    im["M1_all"] = np.concatenate([stF[j]["M1_em"] for j in range(NCORE)], 1)
    im["M2_all"] = np.concatenate([stF[j]["M2_em"] for j in range(NCORE)], 1)
    im["theta_all"] = np.concatenate([stG[j]["theta"] for j in range(NCORE)], 0)
    im["eoff8"] = (np.arange(NE1) * PAD1).astype(np.float32).reshape(NE1, 1)
    return im



def build_launch_Hb():
    nc = bass.Bass("TRN2", target_bir_lowering=False, debug=False)
    ap = {}

    def din(name, shape, dt=F32):
        ap[name] = nc.dram_tensor(name, shape, dt, kind="ExternalInput").ap()

    def dout(name, shape, dt=F32):
        ap[name] = nc.dram_tensor(name, shape, dt, kind="ExternalOutput").ap()

    din("x3", [TPC, D])
    din("expert_out_all", [NE1 * PAD1, D], BF16)
    din("slota", [128, NTT]); din("wa", [128, NTT])
    din("slotb", [128, NTT]); din("wb", [128, NTT])
    dout("out", [TPC, D])

    ctx = ExitStack()
    with ctx:
        kb = KB(nc, ctx)
        ldr_ = kb.ring("ld", 4)
        str_ = kb.ring("st", 4)
        gr = kb.ring("gq", 4)
        v, t, s, g = kb.e["vector"], kb.e["tensor"], kb.e["scalar"], kb.e["gpsimd"]

        p = Pool(nc)
        x_a = p.sb("x_a", [128, D], F32)
        x_b = p.sb("x_b", [128, D], F32)
        tabs = p.sb("tabs", [128, 4 * NTT], F32)
        idxa = p.sb("idxa", [128, 1], I32)
        idxb = p.sb("idxb", [128, 1], I32)
        ya = p.sb("ya", [128, D], BF16)
        yb = p.sb("yb", [128, D], BF16)
        acc = p.sb("acc", [128, D], F32)
        outt = p.sb("outt", [128, D], F32)

        ta = kb.dma("sync", tabs[:, 0:NTT], ap["slota"], ldr_)
        tb = kb.dma("sync", tabs[:, NTT:2 * NTT], ap["slotb"], ldr_)
        tc = kb.dma("sync", tabs[:, 2 * NTT:3 * NTT], ap["wa"], ldr_)
        td = kb.dma("sync", tabs[:, 3 * NTT:4 * NTT], ap["wb"], ldr_)
        ga_war = []
        for i in range(NTT):
            xb = x_a if i % 2 == 0 else x_b
            xtok = kb.dma("sync", xb[:], ap["x3"][i * 128:(i + 1) * 128, :], ldr_,
                          wait=[v] if i >= 2 else [])
            kb.op("vector", "tensor_copy", idxa[:], tabs[:, i:i + 1],
                  wait=[ta] + ga_war)
            kb.op("vector", "tensor_copy", idxb[:], tabs[:, NTT + i:NTT + i + 1],
                  wait=[tb])
            gt1 = kb.idma("gpsimd", ya[:], None, ap["expert_out_all"],
                          bass.IndirectOffsetOnAxis(ap=idxa[:], axis=0),
                          gr, wait=[v, s])
            gt2 = kb.idma("gpsimd", yb[:], None, ap["expert_out_all"],
                          bass.IndirectOffsetOnAxis(ap=idxb[:], axis=0),
                          gr, wait=[v, s])
            ga_war = [gt1, gt2]
            kb.op("vector", "tensor_scalar", acc[:], ya[:],
                  tabs[:, 2 * NTT + i:2 * NTT + i + 1], None, ALU.mult,
                  wait=[gt1, tc, s])
            kb.op("vector", "scalar_tensor_tensor", acc[:], yb[:],
                  tabs[:, 3 * NTT + i:3 * NTT + i + 1], acc[:],
                  ALU.mult, ALU.add, wait=[gt2, td])
            kb.op("vector", "tensor_add", outt[:], acc[:], xb[:],
                  wait=[xtok])
            kb.dma("scalar", ap["out"][i * 128:(i + 1) * 128, :], outt[:], str_,
                   wait=[v])
        kb.build(final_rings=[ldr_, str_, gr])
        p.close()
    return nc, ap


def prep_launch_Hb(inputs, core, stF2, stG, tabs):
    im = {}
    im["x3"] = stF2[core]["x3"]
    im["expert_out_all"] = np.concatenate(
        [stG[j]["expert_out"] for j in range(NCORE)], 0)
    for k2, row in (("slota", "slota_row"), ("wa", "wa_row"),
                    ("slotb", "slotb_row"), ("wb", "wb_row")):
        r = np.asarray(tabs[row]).reshape(-1)[core * TPC:(core + 1) * TPC]
        im[k2] = np.ascontiguousarray(r.reshape(NTT, 128).T)
    return im


GAMMA0 = (1.0 - np.exp2(-5.0 - np.arange(H0, dtype=np.float64)))


def prep_launch_B(inputs, core, stA):
    d = inputs
    im = {}
    im["x"] = np.ascontiguousarray(d["L0_attn_x"].reshape(-1, D)[core * TPC:(core + 1) * TPC])
    im["o_part"] = stA[core]["o_part"]
    im["qTd"] = stA[core]["qTd"]
    im["gTd"] = stA[core]["gTd"]
    im["P_all"] = np.concatenate([stA[j]["P"] for j in range(NCORE)], 0)
    coef = np.zeros((128, NCORE * H0), np.float32)
    b0 = (core // 4) * 4
    for j in range(b0, core):
        for h in range(H0):
            coef[:, j * H0 + h] = (GAMMA0[h] ** SC) ** (4 * (core - 1 - j))
    im["coefP"] = coef
    im["WoT"] = np.ascontiguousarray(np.asarray(d["L0_attn_Wo"]).T)
    im["gwT"] = np.ascontiguousarray(np.asarray(d["L0_ffn_gate_w"]).T)
    im["gnw"] = np.ascontiguousarray(np.asarray(d["L0_attn_gnorm_w"]).reshape(4, 128).T)
    tloc = np.arange(TPC, dtype=np.float32)
    cf = np.stack([g ** (tloc + 1) for g in GAMMA0.astype(np.float32)])
    im["crossfullT"] = np.repeat(cf[:, None, :], 128, axis=1).reshape(H0 * 128, TPC).astype(BF)
    im["ident"] = np.eye(128, dtype=np.float32).astype(BF)
    im["onesf"] = np.ones((128, 128), np.float32)
    return im


def kernel(**inputs):
    cores = list(range(NCORE))
    times = []

    def run(nc, ims):
        r = run_bass_kernel_spmd(nc, ims, core_ids=cores)
        times.append(r.exec_time_ns)
        try:
            with open("/tmp/klaunch_times.txt", "w") as f:
                f.write(repr(times))
        except OSError:
            pass
        return r.results

    ncA, _ = build_launch_A()
    stA = run(ncA, [prep_launch_A(inputs, c) for c in cores])
    ncB, _ = build_launch_B()
    stB = run(ncB, [prep_launch_B(inputs, c, stA) for c in cores])
    ncC, _ = build_launch_C()
    stC = run(ncC, [prep_launch_C(inputs, c, stB) for c in cores])
    ncD, _ = build_launch_D()
    stD = run(ncD, [prep_launch_D(inputs, c, stB, stC) for c in cores])
    ncE, _ = build_launch_E()
    stE = run(ncE, [prep_launch_E(inputs, c, stD) for c in cores])
    ncF, _ = build_launch_F()
    stF = run(ncF, [prep_launch_F(inputs, c, stD, stE) for c in cores])
    ncG, _ = build_launch_G()
    stG = run(ncG, [prep_launch_G(inputs, c, stF) for c in cores])
    ncHa, _ = build_launch_Ha()
    stHa = run(ncHa, [prep_launch_Ha(inputs, c, stF, stG) for c in cores])
    tabs = stHa[0]
    ncHb, _ = build_launch_Hb()
    stHb = run(ncHb, [prep_launch_Hb(inputs, c, stF, stG, tabs) for c in cores])
    out = np.concatenate([stHb[c]["out"] for c in cores], 0)
    return np.ascontiguousarray(out.reshape(2, 8192, D).astype(np.float32))
